# version 11
# baseline (speedup 1.0000x reference)
import os
import numpy as np
import ml_dtypes
from contextlib import ExitStack

KA_STAGE = int(os.environ.get("KA_STAGE", "99"))
KA_SUB = int(os.environ.get("KA_SUB", "99"))
KA_TRUNC = int(os.environ.get("KA_TRUNC", "0"))
KA_SBN = int(os.environ.get("KA_SBN", "16"))

import concourse.bass as bass
import concourse.mybir as mybir
from concourse.bass_utils import run_bass_kernel_spmd

F32 = mybir.dt.float32
BF16 = mybir.dt.bfloat16
AF = mybir.ActivationFunctionType
ALU = mybir.AluOpType
AX = mybir.AxisListType

NCORES = 8
D = 1024
NCH = 8
TOK = 2048
SEQ = 8192
BF = ml_dtypes.bfloat16

ENGS = ("tensor", "vector", "scalar", "gpsimd", "sync")


class P:

    RING = 4

    def __init__(self, nc):
        self.nc = nc
        self.ev = {e: [] for e in ENGS}
        self.cnt = {e: 0 for e in ENGS}
        self.dmacnt = {e: 0 for e in ENGS}
        self.semnames = list(ENGS) + [f"{e}D{i}" for e in ("sync", "gpsimd", "scalar", "vector", "tensor") for i in range(self.RING)]
        self.semtot = {n: 0 for n in self.semnames}
        self.lastw = {}
        self.readers = {}

    def _dep(self, need, tok):
        sem, val = tok
        if val > need.get(sem, 0):
            need[sem] = val

    def op(self, eng, emit, reads=(), writes=(), inc=1, tag="", dma=False):
        ag = getattr(self, "alias_groups", None)
        if ag:
            def _ex(keys):
                out = []
                for k in keys:
                    out.extend(ag.get(k, [k]))
                return out
            reads = _ex(list(reads))
            writes = _ex(list(writes))
        etag0 = ("dma", eng, self.dmacnt[eng]) if dma else eng
        need = {}
        for b in reads:
            for (sem, val, weng) in self.lastw.get(b, ()):
                if weng == etag0 == "tensor":
                    continue
                if weng != etag0:
                    self._dep(need, (sem, val))
                elif etag0 != "tensor":
                    self._dep(need, (sem, val))
        for b in writes:
            for (sem, val, weng) in self.lastw.get(b, ()):
                if weng == etag0 == "tensor":
                    continue
                self._dep(need, (sem, val))
            for (sem, val, reng) in self.readers.get(b, ()):
                if reng == etag0 == "tensor":
                    continue
                self._dep(need, (sem, val))
        if dma:
            k = self.dmacnt[eng]
            self.dmacnt[eng] = k + 1
            sem = f"{eng}D{k % self.RING}"
            occ = k // self.RING
            if occ > 0:
                self._dep(need, (sem, 16 * occ))
            myval = 16 * (occ + 1)
            etag = ("dma", eng, k)
        else:
            sem = eng
            self.cnt[eng] += inc
            myval = self.cnt[eng]
            etag = eng
        self.semtot[sem] = max(self.semtot[sem], myval)
        self.seq = getattr(self, "seq", 0) + 1
        self.ev[eng].append((emit, need, sem, 16 if dma else inc, tag, self.seq))
        tok = (sem, myval, etag)
        for b in reads:
            self.readers.setdefault(b, []).append(tok)
        for b in writes:
            self.lastw.setdefault(b, []).append(tok)
            self.readers[b] = []
        return tok

    def t(self, emit, **kw):
        return self.op("tensor", emit, **kw)

    def v(self, emit, **kw):
        return self.op("vector", emit, **kw)

    def s(self, emit, **kw):
        return self.op("scalar", emit, **kw)

    def g(self, emit, **kw):
        return self.op("gpsimd", emit, **kw)

    def dma(self, eng, emit, reads=(), writes=(), tag=""):
        return self.op(eng, emit, reads=reads, writes=writes, tag=tag, dma=True)

    def finalize(self, ctx):
        if os.environ.get("KA_PRINT"):
            print("TOTAL_EVENTS:", getattr(self, "seq", 0))
        if KA_TRUNC > 0:
            for e in ENGS:
                self.ev[e] = [ev for ev in self.ev[e] if ev[5] <= KA_TRUNC]
            self.semtot = {n: 0 for n in self.semnames}
            for e in ENGS:
                for ev in self.ev[e]:
                    sem, inc = ev[2], ev[3]
                    if sem == e:
                        self.semtot[sem] = max(self.semtot[sem], ev[1].get(sem, 0))
            self.semtot = {n: 0 for n in self.semnames}
            for e in ENGS:
                cnt = {}
                for ev in self.ev[e]:
                    sem = ev[2]
                    cnt[sem] = cnt.get(sem, 0) + ev[3]
                for snm, v in cnt.items():
                    self.semtot[snm] = self.semtot.get(snm, 0) + v
        nc = self.nc
        sems = {n: ctx.enter_context(nc.semaphore(n)) for n in self.semnames}
        block = ctx.enter_context(nc.Block())
        totals = dict(self.semtot)
        evs = self.ev

        def runner(ename, engobj):
            done = {n: 0 for n in self.semnames}
            for (emit, need, incsem, inc, tag, _seq) in evs[ename]:
                for n, c2 in need.items():
                    if c2 > done[n]:
                        engobj.wait_ge(sems[n], c2)
                        done[n] = c2
                inst = emit(engobj)
                inst.then_inc(sems[incsem], inc)
            if ename == "sync":
                for n in self.semnames:
                    if totals[n] > done.get(n, 0):
                        engobj.wait_ge(sems[n], totals[n])

        @block.tensor
        def _(e):
            runner("tensor", e)

        @block.vector
        def _(e):
            runner("vector", e)

        @block.scalar
        def _(e):
            runner("scalar", e)

        @block.gpsimd
        def _(e):
            runner("gpsimd", e)

        @block.sync
        def _(e):
            runner("sync", e)


def ln_rows(nc, ctx, prefix, width=544):
    sb = lambda name, shape: ctx.enter_context(nc.sbuf_tensor(name, shape, F32))
    t = {}
    for nm in ("mu", "rstd", "murs", "var"):
        t[nm] = sb(f"{prefix}_{nm}", [1, width])
        t["k" + nm] = f"{prefix}_{nm}"
    t["eps"] = sb(f"{prefix}_eps", [1, 1])
    t["keps"] = f"{prefix}_eps"
    return t


def emit_ln(p, x_ap, xkey, n_tok, sq_ap, sqkey, rows, psA, psAkey, onescol, onescolkey, ones1f, ones1fkey, out_emit, d_model=1024, eps=1e-5):
    nch = x_ap.shape[1]

    def mk_sum(e):
        last = None
        for c in range(nch):
            last = e.matmul(psA[0:1, 0:n_tok], onescol[:, :], x_ap[:, c, :], start=(c == 0), stop=(c == nch - 1))
        return last

    p.t(mk_sum, reads=[xkey, onescolkey], writes=[psAkey])
    p.s(lambda e: e.activation(out=sq_ap, in_=x_ap, func=AF.Square), reads=[xkey], writes=[sqkey])

    def mk_sum2(e):
        last = None
        for c in range(nch):
            last = e.matmul(
                psA[0:1, 512 : 512 + n_tok], onescol_f_of(rows), sq_ap[:, c, :], start=(c == 0), stop=(c == nch - 1)
            )
        return last

    p.t(mk_sum2, reads=[sqkey, rows["konesf_col"]], writes=[psAkey])
    mu, rstd, murs, var = rows["mu"], rows["rstd"], rows["murs"], rows["var"]
    kmu, krstd, kmurs, kvar = rows["kmu"], rows["krstd"], rows["kmurs"], rows["kvar"]
    sc = 1.0 / d_model
    p.s(lambda e: e.activation(out=mu[:, 0:n_tok], in_=psA[0:1, 0:n_tok], func=AF.Copy, scale=sc), reads=[psAkey], writes=[kmu])
    p.s(lambda e: e.activation(out=var[:, 0:n_tok], in_=mu[:, 0:n_tok], func=AF.Square), reads=[kmu], writes=[kvar])
    p.v(
        lambda e: e.scalar_tensor_tensor(var[:, 0:n_tok], psA[0:1, 512 : 512 + n_tok], sc, var[:, 0:n_tok], ALU.mult, ALU.subtract),
        reads=[psAkey, kvar],
        writes=[kvar],
    )
    p.v(lambda e: e.memset(rows["eps"][:], float(eps)), writes=[rows["keps"]])
    p.s(lambda e: e.activation(out=rstd[:, 0:n_tok], in_=var[:, 0:n_tok], func=AF.Sqrt, bias=rows["eps"][:, :]), reads=[kvar, rows["keps"]], writes=[krstd])
    p.v(lambda e: e.reciprocal(rstd[:, 0:n_tok], rstd[:, 0:n_tok]), reads=[krstd], writes=[krstd])
    p.v(lambda e: e.tensor_mul(murs[:, 0:n_tok], mu[:, 0:n_tok], rstd[:, 0:n_tok]), reads=[kmu, krstd], writes=[kmurs])
    p.t(
        lambda e: e.matmul(psA[:, 0:n_tok], ones1f[:, 0:128], rstd[:, 0:n_tok], start=True, stop=True),
        reads=[krstd, ones1fkey],
        writes=[psAkey],
    )
    p.t(
        lambda e: e.matmul(psA[:, 512 : 512 + n_tok], ones1f[:, 0:128], murs[:, 0:n_tok], start=True, stop=True),
        reads=[kmurs, ones1fkey],
        writes=[psAkey],
    )
    out_emit(psA[:, 0:n_tok], psA[:, 512 : 512 + n_tok], [psAkey, psAkey])


def onescol_f_of(rows):
    return rows["onesf_col"][:, :]


KEXT = 2176
NREG = 4
REG = 512


def build_launch_a():
    nc = bass.Bass("TRN2", target_bir_lowering=False)
    ap = lambda t: t.ap()
    x0e = ap(nc.dram_tensor("x0e", [D, KEXT], F32, kind="ExternalInput"))
    wqkvT = ap(nc.dram_tensor("wqkvT", [D, 3 * D], F32, kind="ExternalInput"))
    woutT = ap(nc.dram_tensor("woutT", [D, D], F32, kind="ExternalInput"))
    bqkv24 = ap(nc.dram_tensor("bqkv24", [24, 128], F32, kind="ExternalInput"))
    bqkv_row = ap(nc.dram_tensor("bqkv_row", [1, 3 * D], F32, kind="ExternalInput"))
    bout8 = ap(nc.dram_tensor("bout8", [8, 128], F32, kind="ExternalInput"))
    maskA = ap(nc.dram_tensor("maskA", [16, 128, 128], F32, kind="ExternalInput"))
    maskB = ap(nc.dram_tensor("maskB", [16, 16, 16], F32, kind="ExternalInput"))
    s16 = ap(nc.dram_tensor("s16", [8, 16, 128], F32, kind="ExternalInput"))
    hs16 = ap(nc.dram_tensor("hs16", [8, 128, 16], F32, kind="ExternalInput"))
    identw = ap(nc.dram_tensor("identw", [128, 128], F32, kind="ExternalInput"))
    x1out = ap(nc.dram_tensor("x1out", [D, TOK], F32, kind="ExternalOutput"))

    with ExitStack() as ctx:
        sb = lambda name, shape, dt=F32: ctx.enter_context(nc.sbuf_tensor(name, shape, dt))
        pst = lambda name, shape: ctx.enter_context(nc.psum_tensor(name, shape, F32))
        p = P(nc)

        x0r = sb("x0r", [128, NCH, 544])
        sq = sb("sq", [128, NCH, 544])
        xh = sb("xh", [128, NCH, 544], BF16)
        wq = sb("wq", [128, NCH, 3 * D], BF16)
        wo = sb("wo", [128, NCH, D], BF16)
        bq64 = sb("bq64", [64, 48])
        bout_c = sb("bout_c", [128, 8])
        b48 = sb("b48", [48, 64])
        b8 = sb("b8", [8, 128])
        bvrow = sb("bvrow", [1, D], BF16)
        identf = sb("identf", [128, 128])
        maskA_sb = sb("maskA_sb", [128, 16, 128], BF16)
        maskB_sb = sb("maskB_sb", [16, 16, 16])
        s16_sb = sb("s16_sb", [16, NCH, 128])

        qr = sb("qr", [64, 16, REG], BF16)
        kr = sb("kr", [64, 16, 528], BF16)
        scr = sb("scr", [64, 4, 528])
        vr = sb("vr", [128, 5, 16, 64], BF16)
        eA = sb("eA", [128, 2048], BF16)
        eB = sb("eB", [16, 256], BF16)
        ctxr = sb("ctxr", [128, NCH, REG], BF16)

        yr = sb("yr", [128, NCH, REG], BF16)
        rowmax = sb("rowmax", [128, 1])
        nB = sb("nB", [1, 4])
        negB = sb("negB", [1, 1], BF16)
        negBc = sb("negBc", [128, 1])
        sums_tm = sb("sums_tm", [128, 16])
        bsums_tm = sb("bsums_tm", [16, 16])
        shm = sb("shm", [16, 128])
        recip_hm = sb("recip_hm", [16, REG])
        ones1f = sb("ones1f", [1, 128])
        ones1b = sb("ones1b", [1, 128], BF16)
        onescol_f = sb("onescol_f", [128, 1])
        onescol_b = sb("onescol_b", [128, 1], BF16)
        rows = ln_rows(nc, ctx, "lnA")
        rows2 = rows
        for rr in (rows,):
            rr["onesf_col"] = onescol_f
            rr["konesf_col"] = "onescol_f"

        psO = pst("psO", [128, 2048])
        psA = pst("psA", [128, 1024])
        ps1 = pst("ps1", [128, 512])
        ps2 = pst("ps2", [128, 512])

        p.v(lambda e: e.memset(ones1f[:], 1.0), writes=["ones1f"])
        p.v(lambda e: e.memset(ones1b[:], 1.0), writes=["ones1b"])
        p.v(lambda e: e.memset(onescol_f[:], 1.0), writes=["onescol_f"])
        p.v(lambda e: e.memset(onescol_b[:], 1.0), writes=["onescol_b"])

        p.dma("sync", lambda e: e.dma_start(out=b48[:], in_=bqkv24.rearrange("a (b c) -> (a b) c", b=2)), writes=["b48"])
        p.dma("sync", lambda e: e.dma_start(out=b8[:], in_=bout8[:, :]), writes=["b8"])
        p.dma("gpsimd", lambda e: e.dma_start(out=bvrow[:], in_=bqkv_row[:, 2048:3072]), writes=["bvrow"])
        p.dma("sync", lambda e: e.dma_start(out=identf[:], in_=identw[:, :]), writes=["identf"])
        p.dma("gpsimd", lambda e: e.dma_start(out=maskA_sb[:], in_=maskA.rearrange("j p t -> p j t")), writes=["maskA_sb"])
        p.dma("sync", lambda e: e.dma_start(out=maskB_sb[:], in_=maskB.rearrange("j a b -> a j b")), writes=["maskB_sb"])
        p.dma("sync", lambda e: e.dma_start(out=s16_sb[:], in_=s16.rearrange("c h t -> h c t")), writes=["s16_sb"])

        for cc in range(NCH):
            p.dma(
                "gpsimd",
                lambda e, cc=cc: e.dma_start(out=wq[:, cc, :], in_=wqkvT[128 * cc : 128 * cc + 128, :]),
                writes=["wq"],
                tag=f"wq{cc}",
            )
            p.dma(
                "gpsimd",
                lambda e, cc=cc: e.dma_start(out=wo[:, cc, :], in_=woutT[128 * cc : 128 * cc + 128, :]),
                writes=["wo"],
                tag=f"wo{cc}",
            )
        p.t(lambda e: e.transpose(ps1[0:64, 0:48], b48[:, :], identf[0:48, 0:48]), reads=["b48", "identf"], writes=["ps1"])
        p.v(lambda e: e.tensor_copy(bq64[:], ps1[0:64, 0:48]), reads=["ps1"], writes=["bq64"])
        p.v(lambda e: e.tensor_scalar_mul(bq64[:, 0:16], bq64[:, 0:16], 0.125), reads=["bq64"], writes=["bq64"])
        p.t(lambda e: e.transpose(ps1[:, 0:8], b8[:, :], identf[0:8, 0:8]), reads=["b8", "identf"], writes=["ps1"])
        p.v(lambda e: e.tensor_copy(bout_c[:], ps1[:, 0:8]), reads=["ps1"], writes=["bout_c"])

        for r in range(NREG if KA_STAGE >= 8 else 1):
            r0 = r * REG

            p.dma(
                "sync",
                lambda e, r0=r0: e.dma_start(
                    out=x0r[:, :, :], in_=x0e.rearrange("(c p) t -> p c t", p=128)[:, :, r0 : r0 + 544]
                ),
                writes=["x0r"],
                tag=f"x0r{r}",
            )

            if KA_STAGE < 2:
                continue

            def apply_ln(R, MR, rdeps, off, n):
                p.v(
                    lambda e: e.tensor_tensor(
                        sq[:, :, off : off + n], x0r[:, :, off : off + n],
                        R.unsqueeze(1).broadcast_to([128, NCH, n]), ALU.mult,
                    ),
                    reads=["x0r"] + rdeps[:1],
                    writes=["sq"],
                )
                p.v(
                    lambda e: e.tensor_tensor(
                        xh[:, :, off : off + n], sq[:, :, off : off + n],
                        MR.unsqueeze(1).broadcast_to([128, NCH, n]), ALU.subtract,
                    ),
                    reads=["sq"] + rdeps[1:],
                    writes=["xh"],
                )

            for (off, n) in [(0, 512), (512, 32)]:
                emit_ln(
                    p, x0r.ap()[:, :, off : off + n], "x0r", n, sq.ap()[:, :, off : off + n], "sq",
                    rows, psA, "psA", onescol_f, "onescol_f", ones1f, "ones1f",
                    lambda R, MR, rd, off=off, n=n: apply_ln(R, MR, rd, off, n),
                )

            for om in range(0, 16 if KA_STAGE >= 3 else 0, 2):
                bk = ps1 if (om // 2) % 2 == 0 else ps2
                bkk = "ps1" if (om // 2) % 2 == 0 else "ps2"

                def mk_q(e, om=om, bk=bk):
                    last = None
                    for ci in range(NCH):
                        last = e.matmul(
                            bk[0:128, 0:512], wq[:, ci, 64 * om : 64 * om + 128], xh[:, ci, 8:520],
                            start=(ci == 0), stop=(ci == NCH - 1),
                        )
                    return last

                p.t(mk_q, reads=["xh", "wq"], writes=[bkk], tag=f"q{r}_{om}")
                p.s(
                    lambda e, om=om, bk=bk: e.activation(out=qr[:, om, :], in_=bk[0:64, 0:512], func=AF.Identity, scale=0.125, bias=bq64[:, om : om + 1]),
                    reads=[bkk, "bq64"],
                    writes=["qr"],
                )
                p.s(
                    lambda e, om=om, bk=bk: e.activation(out=qr[:, om + 1, :], in_=bk[64:128, 0:512], func=AF.Identity, scale=0.125, bias=bq64[:, om + 1 : om + 2]),
                    reads=[bkk, "bq64"],
                    writes=["qr"],
                )
            for om in range(0, 16 if KA_STAGE >= 3 else 0, 2):
                for ti, (toff, tn) in enumerate([(0, 512), (512, 16)]):
                    bk = ps1 if (om // 2 + ti) % 2 == 0 else ps2
                    bkk = "ps1" if (om // 2 + ti) % 2 == 0 else "ps2"

                    def mk_k(e, om=om, toff=toff, tn=tn, bk=bk):
                        last = None
                        for ci in range(NCH):
                            last = e.matmul(
                                bk[0:128, 0:tn], wq[:, ci, 1024 + 64 * om : 1024 + 64 * om + 128], xh[:, ci, toff : toff + tn],
                                start=(ci == 0), stop=(ci == NCH - 1),
                            )
                        return last

                    p.t(mk_k, reads=["xh", "wq"], writes=[bkk], tag=f"k{r}_{om}_{toff}")
                    p.s(
                        lambda e, om=om, toff=toff, tn=tn, bk=bk: e.activation(
                            out=kr[:, om, toff : toff + tn], in_=bk[0:64, 0:tn], func=AF.Identity, bias=bq64[:, 16 + om : 17 + om]
                        ),
                        reads=[bkk, "bq64"],
                        writes=["kr"],
                    )
                    p.s(
                        lambda e, om=om, toff=toff, tn=tn, bk=bk: e.activation(
                            out=kr[:, om + 1, toff : toff + tn], in_=bk[64:128, 0:tn], func=AF.Identity, bias=bq64[:, 17 + om : 18 + om]
                        ),
                        reads=[bkk, "bq64"],
                        writes=["kr"],
                    )
            for vt in range(5 if KA_STAGE >= 4 else 0):
                tn = 128 if vt < 4 else 16
                toff = vt * 128
                for ofj in range(2):
                    bk = ps1 if (2 * vt + ofj) % 2 == 0 else ps2
                    bkk = "ps1" if (2 * vt + ofj) % 2 == 0 else "ps2"

                    def mk_v(e, vt=vt, tn=tn, toff=toff, ofj=ofj, bk=bk):
                        for ci in range(NCH):
                            e.matmul(
                                bk[0:tn, 0:512], xh[:, ci, toff : toff + tn],
                                wq[:, ci, 2048 + 512 * ofj : 2048 + 512 * ofj + 512],
                                start=(ci == 0), stop=False,
                            )
                        return e.matmul(
                            bk[0:tn, 0:512], ones1b[:, 0:tn], bvrow[:, 512 * ofj : 512 * ofj + 512],
                            start=False, stop=True,
                        )

                    p.t(mk_v, reads=["xh", "wq", "ones1b", "bvrow"], writes=[bkk], tag=f"v{r}_{vt}_{ofj}")
                    p.v(
                        lambda e, vt=vt, tn=tn, ofj=ofj, bk=bk: e.tensor_copy(
                            vr.ap()[0:tn, vt, 8 * ofj : 8 * ofj + 8, :],
                            bk.ap()[0:tn, 0:512].rearrange("t (h d) -> t h d", h=8),
                        ),
                        reads=[bkk],
                        writes=["vr"],
                    )

            def emit_norm2(src, srckey, ncols, ntiles, into_col):
                for hg in range(4):
                    p.s(
                        lambda e, hg=hg: e.activation(
                            out=scr[:, :, 0:ncols],
                            in_=src[0:64, 4 * hg : 4 * hg + 4, 0:ncols],
                            func=AF.Square,
                        ),
                        reads=[srckey],
                        writes=["scr"],
                    )

                    def mk_n2(e, hg=hg):
                        last = None
                        for hh in range(4):
                            for tt in range(ntiles):
                                tn = 128 if tt < 4 else 16
                                cc = 20 * (4 * hg + hh) + tt
                                last = e.matmul(
                                    ps2[0:tn, cc : cc + 1],
                                    scr[:, hh, 128 * tt : 128 * tt + tn],
                                    onescol_f[0:64, :],
                                    start=True, stop=True,
                                )
                        return last

                    p.t(mk_n2, reads=["scr", "onescol_f"], writes=["ps2"], tag=f"n2_{r}_{hg}_{into_col}")
                p.v(lambda e: e.reduce_max(rowmax[:], ps2[:, 0:320], axis=AX.X), reads=["ps2"], writes=["rowmax"])
                p.t(
                    lambda e: e.transpose(ps2[0:1, 0:128], rowmax[:, :], identf[:, :]),
                    reads=["rowmax", "identf"],
                    writes=["ps2"],
                )
                p.v(
                    lambda e, into_col=into_col: e.reduce_max(nB[:, into_col : into_col + 1], ps2[0:1, 0:128], axis=AX.X),
                    reads=["ps2"],
                    writes=["nB"],
                )

            if KA_STAGE < 5:
                continue
            p.v(lambda e: e.memset(ps2[:, 0:320], 0.0), writes=["ps2"])
            emit_norm2(qr.ap(), "qr", 512, 4, 0)
            p.v(lambda e: e.memset(ps2[:, 0:320], 0.0), writes=["ps2"])
            emit_norm2(kr.ap(), "kr", 528, 5, 1)
            p.s(lambda e: e.activation(out=nB[:, 0:2], in_=nB[:, 0:2], func=AF.Sqrt), reads=["nB"], writes=["nB"])
            p.v(lambda e: e.tensor_mul(nB[:, 2:3], nB[:, 0:1], nB[:, 1:2]), reads=["nB"], writes=["nB"])
            p.s(
                lambda e: e.activation(out=negB[:, :], in_=nB[:, 2:3], func=AF.Copy, scale=-1.0, bias=-1.0),
                reads=["nB"],
                writes=["negB"],
            )
            p.t(
                lambda e: e.matmul(ps2[:, 0:1], ones1b[:, 0:128], negB[:, :], start=True, stop=True),
                reads=["negB", "ones1b"],
                writes=["ps2"],
            )
            p.v(lambda e: e.tensor_copy(negBc[:], ps2[:, 0:1]), reads=["ps2"], writes=["negBc"])

            for j in range(4 if KA_STAGE >= 6 else 0):
                jq = r * 4 + j
                q0 = 128 * j
                k0 = 128 * j

                def mk_sb(e, q0=q0, k0=k0):
                    last = None
                    for h in range(16):
                        last = e.matmul(
                            ps2[0:16, 16 * h : 16 * h + 16],
                            kr[:, h, k0 + 128 : k0 + 144],
                            qr[:, h, q0 + 112 : q0 + 128],
                            start=True, stop=True,
                        )
                    return last

                p.t(mk_sb, reads=["kr", "qr"], writes=["ps2"], tag=f"sB{r}_{j}")
                if KA_SUB < 2:
                    continue
                p.s(
                    lambda e: e.activation(out=eB[:, :], in_=ps2[0:16, 0:256], func=AF.Exp, bias=negBc[0:16, :]),
                    reads=["ps2", "negBc"],
                    writes=["eB"],
                )
                p.v(
                    lambda e, jq=jq: e.tensor_tensor(
                        eB.ap().rearrange("a (h b) -> a h b", h=16),
                        eB.ap().rearrange("a (h b) -> a h b", h=16),
                        maskB_sb.ap()[:, jq, :].unsqueeze(1).broadcast_to([16, 16, 16]),
                        ALU.mult,
                    ),
                    reads=["eB", "maskB_sb"],
                    writes=["eB"],
                )
                if KA_SUB < 3:
                    continue
                for gg in range(2):
                    def mk_sa(e, gg=gg, q0=q0, k0=k0):
                        last = None
                        for hh in range(8):
                            h = 8 * gg + hh
                            last = e.matmul(
                                psA[:, 128 * hh : 128 * hh + 128],
                                kr[:, h, k0 : k0 + 128],
                                qr[:, h, q0 : q0 + 128],
                                start=True, stop=True,
                            )
                        return last

                    p.t(mk_sa, reads=["kr", "qr"], writes=["psA"], tag=f"sA{r}_{j}_{gg}")
                    p.s(
                        lambda e, gg=gg: e.activation(
                            out=eA[:, 1024 * gg : 1024 * gg + 1024], in_=psA[:, 0:1024], func=AF.Exp, bias=negBc[:, :]
                        ),
                        reads=["psA", "negBc"],
                        writes=["eA"],
                    )
                p.v(
                    lambda e, jq=jq: e.tensor_tensor(
                        eA.ap().rearrange("a (h b) -> a h b", h=16),
                        eA.ap().rearrange("a (h b) -> a h b", h=16),
                        maskA_sb.ap()[:, jq, :].unsqueeze(1).broadcast_to([128, 16, 128]),
                        ALU.mult,
                    ),
                    reads=["eA", "maskA_sb"],
                    writes=["eA"],
                )

                if KA_SUB < 4:
                    continue

                def mk_pv(e, j=j):
                    last = None
                    for h in range(16):
                        e.matmul(
                            psO[0:64, 128 * h : 128 * h + 128], vr.ap()[:, j, h, :], eA[:, 128 * h : 128 * h + 128],
                            start=True, stop=False,
                        )
                        last = e.matmul(
                            psO[0:64, 128 * h + 112 : 128 * h + 128], vr.ap()[0:16, j + 1, h, :], eB[:, 16 * h : 16 * h + 16],
                            start=False, stop=True,
                        )
                    return last

                p.t(mk_pv, reads=["vr", "eA", "eB"], writes=["psO"], tag=f"pv{r}_{j}")

                if KA_SUB < 5:
                    continue

                def mk_sums(e):
                    last = None
                    for h in range(16):
                        last = e.matmul(
                            ps2[:, 16 + h : 17 + h], eA[:, 128 * h : 128 * h + 128], onescol_b[:, :], start=True, stop=True
                        )
                    for h in range(16):
                        last = e.matmul(
                            ps2[0:16, 32 + h : 33 + h], eB[:, 16 * h : 16 * h + 16], onescol_b[0:16, :], start=True, stop=True
                        )
                    return last

                p.t(mk_sums, reads=["eA", "eB", "onescol_b"], writes=["ps2"], tag=f"sums{r}_{j}")
                p.v(lambda e: e.tensor_copy(sums_tm[:], ps2[:, 16:32]), reads=["ps2"], writes=["sums_tm"])
                p.v(lambda e: e.tensor_copy(bsums_tm[:], ps2[0:16, 32:48]), reads=["ps2"], writes=["bsums_tm"])
                p.t(
                    lambda e: e.transpose(ps2[0:16, 64:192], sums_tm[:, :], identf[:, :]),
                    reads=["sums_tm", "identf"],
                    writes=["ps2"],
                )
                p.v(lambda e: e.tensor_copy(shm[:, :], ps2[0:16, 64:192]), reads=["ps2"], writes=["shm"])
                p.t(
                    lambda e: e.transpose(ps2[0:16, 192:208], bsums_tm[:, :], identf[0:16, 0:16]),
                    reads=["bsums_tm", "identf"],
                    writes=["ps2"],
                )
                p.v(
                    lambda e: e.tensor_tensor(shm[:, 112:128], shm[:, 112:128], ps2[0:16, 192:208], ALU.add),
                    reads=["ps2", "shm"],
                    writes=["shm"],
                )
                p.v(
                    lambda e, q0=q0: e.reciprocal(recip_hm[:, q0 : q0 + 128], shm[:, :]),
                    reads=["shm"],
                    writes=["recip_hm"],
                )
                if KA_SUB < 6:
                    continue
                p.v(
                    lambda e, q0=q0: e.tensor_copy(
                        ctxr.ap()[0:64, :, q0 : q0 + 128],
                        psO.ap()[0:64, 0:2048].rearrange("p (c two t) -> p c two t", two=2, t=128)[:, :, 0, :],
                    ),
                    reads=["psO"],
                    writes=["ctxr"],
                )
                p.v(
                    lambda e, q0=q0: e.tensor_copy(
                        ctxr.ap()[64:128, :, q0 : q0 + 128],
                        psO.ap()[0:64, 0:2048].rearrange("p (c two t) -> p c two t", two=2, t=128)[:, :, 1, :],
                    ),
                    reads=["psO"],
                    writes=["ctxr"],
                )

            if KA_STAGE < 7:
                continue
            for cch in range(NCH):
                p.t(
                    lambda e, cch=cch: e.matmul(psA[:, 0:512], s16_sb[:, cch, :], recip_hm[:, :], start=True, stop=True),
                    reads=["s16_sb", "recip_hm"],
                    writes=["psA"],
                    tag=f"rb{r}_{cch}",
                )
                p.v(
                    lambda e, cch=cch: e.tensor_tensor(ctxr[:, cch, :], ctxr[:, cch, :], psA[:, 0:512], ALU.mult),
                    reads=["ctxr", "psA"],
                    writes=["ctxr"],
                )

            for oj in range(NCH):
                def mk_y(e, oj=oj):
                    last = None
                    for ci in range(NCH):
                        last = e.matmul(
                            ps1[:, 0:512], wo[:, ci, 128 * oj : 128 * oj + 128], ctxr[:, ci, :],
                            start=(ci == 0), stop=(ci == NCH - 1),
                        )
                    return last

                p.t(mk_y, reads=["ctxr", "wo"], writes=["ps1"], tag=f"y{r}_{oj}")
                p.s(
                    lambda e, oj=oj: e.activation(out=yr[:, oj, :], in_=ps1[:, 0:512], func=AF.Identity, bias=bout_c[:, oj : oj + 1]),
                    reads=["ps1", "bout_c"],
                    writes=["yr"],
                )

            def apply_post(R, MR, rdeps):
                p.v(
                    lambda e: e.tensor_tensor(
                        sq[:, :, 0:512], yr[:, :, :], R.unsqueeze(1).broadcast_to([128, NCH, 512]), ALU.mult
                    ),
                    reads=["yr"] + rdeps[:1],
                    writes=["sq"],
                )
                p.v(
                    lambda e: e.tensor_tensor(
                        sq[:, :, 0:512], sq[:, :, 0:512], MR.unsqueeze(1).broadcast_to([128, NCH, 512]), ALU.subtract
                    ),
                    reads=["sq"] + rdeps[1:],
                    writes=["sq"],
                )
                p.v(
                    lambda e: e.tensor_tensor(x0r[:, :, 8:520], x0r[:, :, 8:520], sq[:, :, 0:512], ALU.add),
                    reads=["sq", "x0r"],
                    writes=["x0r"],
                )

            emit_ln(
                p, yr.ap()[:, :, :], "yr", 512, sq.ap()[:, :, 0:512], "sq",
                rows2, psA, "psA", onescol_b, "onescol_b", ones1f, "ones1f", apply_post,
            )

            p.dma(
                "sync",
                lambda e, r0=r0: e.dma_start(
                    out=x1out.rearrange("(c p) t -> p c t", p=128)[:, :, r0 : r0 + 512], in_=x0r[:, :, 8:520]
                ),
                reads=["x0r"],
                writes=[("x1out", r)],
                tag=f"x1out{r}",
            )

        p.finalize(ctx)
    return nc





def emit_ln_g(p, x_chunks, n_tok, square_jobs, sq_chunks, rows, psA, psAkey,
              onescol, onescolkey, onescol_f, onescol_f_key, ones1f, ones1fkey,
              out_emit, d_model=1024, eps=1e-5):
    assert n_tok <= 512

    def mk_sum(e):
        last = None
        n = len(x_chunks)
        for i, (ap, _k) in enumerate(x_chunks):
            last = e.matmul(psA[0:1, 0:n_tok], onescol[:, :], ap, start=(i == 0), stop=(i == n - 1))
        return last

    p.t(mk_sum, reads=[k for _, k in x_chunks] + [onescolkey], writes=[psAkey])
    for (oap, iap, rkey, wkey) in square_jobs:
        p.s(lambda e, oap=oap, iap=iap: e.activation(out=oap, in_=iap, func=AF.Square), reads=[rkey], writes=[wkey])

    def mk_sum2(e):
        last = None
        n = len(sq_chunks)
        for i, (ap, _k) in enumerate(sq_chunks):
            last = e.matmul(psA[0:1, 512 : 512 + n_tok], onescol_f[:, :], ap, start=(i == 0), stop=(i == n - 1))
        return last

    p.t(mk_sum2, reads=[k for _, k in sq_chunks] + [onescol_f_key], writes=[psAkey])
    mu, rstd, murs, var = rows["mu"], rows["rstd"], rows["murs"], rows["var"]
    kmu, krstd, kmurs, kvar = rows["kmu"], rows["krstd"], rows["kmurs"], rows["kvar"]
    sc = 1.0 / d_model
    p.s(lambda e: e.activation(out=mu[:, 0:n_tok], in_=psA[0:1, 0:n_tok], func=AF.Copy, scale=sc), reads=[psAkey], writes=[kmu])
    p.s(lambda e: e.activation(out=var[:, 0:n_tok], in_=mu[:, 0:n_tok], func=AF.Square), reads=[kmu], writes=[kvar])
    p.v(
        lambda e: e.scalar_tensor_tensor(var[:, 0:n_tok], psA[0:1, 512 : 512 + n_tok], sc, var[:, 0:n_tok], ALU.mult, ALU.subtract),
        reads=[psAkey, kvar],
        writes=[kvar],
    )
    p.v(lambda e: e.memset(rows["eps"][:], float(eps)), writes=[rows["keps"]])
    p.s(lambda e: e.activation(out=rstd[:, 0:n_tok], in_=var[:, 0:n_tok], func=AF.Sqrt, bias=rows["eps"][:, :]), reads=[kvar, rows["keps"]], writes=[krstd])
    p.v(lambda e: e.reciprocal(rstd[:, 0:n_tok], rstd[:, 0:n_tok]), reads=[krstd], writes=[krstd])
    p.v(lambda e: e.tensor_mul(murs[:, 0:n_tok], mu[:, 0:n_tok], rstd[:, 0:n_tok]), reads=[kmu, krstd], writes=[kmurs])
    p.t(
        lambda e: e.matmul(psA[:, 0:n_tok], ones1f[:, 0:128], rstd[:, 0:n_tok], start=True, stop=True),
        reads=[krstd, ones1fkey],
        writes=[psAkey],
    )
    p.t(
        lambda e: e.matmul(psA[:, 512 : 512 + n_tok], ones1f[:, 0:128], murs[:, 0:n_tok], start=True, stop=True),
        reads=[kmurs, ones1fkey],
        writes=[psAkey],
    )
    out_emit(psA[:, 0:n_tok], psA[:, 512 : 512 + n_tok], [psAkey, psAkey])


FA_EXT = 520


def build_launch_fa(entry_ln):
    nc = bass.Bass("TRN2", target_bir_lowering=False)
    ap = lambda t: t.ap()
    fxe = ap(nc.dram_tensor("fxe", [D, FA_EXT + TOK], F32, kind="ExternalInput"))
    wqkv = ap(nc.dram_tensor("wqkv", [D, 1536], F32, kind="ExternalInput"))
    wo_in = ap(nc.dram_tensor("wo_in", [512, D], F32, kind="ExternalInput"))
    qs_row = ap(nc.dram_tensor("qs_row", [1, 64], F32, kind="ExternalInput"))
    ks_row = ap(nc.dram_tensor("ks_row", [1, 64], F32, kind="ExternalInput"))
    tqc_in = ap(nc.dram_tensor("tqc_in", [64, TOK], F32, kind="ExternalInput"))
    tqs_in = ap(nc.dram_tensor("tqs_in", [64, TOK], F32, kind="ExternalInput"))
    tkc_in = ap(nc.dram_tensor("tkc_in", [64, FA_EXT + TOK], F32, kind="ExternalInput"))
    tks_in = ap(nc.dram_tensor("tks_in", [64, FA_EXT + TOK], F32, kind="ExternalInput"))
    dead_in = ap(nc.dram_tensor("dead_in", [1, FA_EXT + TOK], F32, kind="ExternalInput"))
    rot_in = ap(nc.dram_tensor("rot_in", [64, 64], F32, kind="ExternalInput"))
    tri_in = ap(nc.dram_tensor("tri_in", [2, 128, 128], F32, kind="ExternalInput"))
    identw = ap(nc.dram_tensor("identw", [128, 128], F32, kind="ExternalInput"))
    xmid_out = ap(nc.dram_tensor("xmid_out", [D, TOK], F32, kind="ExternalOutput"))

    with ExitStack() as ctx:
        sb = lambda name, shape, dt=F32: ctx.enter_context(nc.sbuf_tensor(name, shape, dt))
        pst = lambda name, shape, dt=F32: ctx.enter_context(nc.psum_tensor(name, shape, dt))
        p = P(nc)
        p.alias_groups = {"psS": [("psS", 0), ("psS", 1)], "sq": [("sq", 0), ("sq", 1)]}

        fxr = sb("fxr", [128, NCH, 1032])
        fh = sb("fh", [128, NCH, 1032], BF16)
        sq = sb("sq", [128, NCH, 512])
        wq_sb = sb("wq_sb", [128, NCH, 1536], BF16)
        wo_sb = sb("wo_sb", [128, 4, 1024], BF16)
        tqc = sb("tqc", [64, 512])
        tqs = sb("tqs", [64, 512])
        tkc = sb("tkc", [64, 1024])
        tks = sb("tks", [64, 1024])
        dead_r = sb("dead_r", [1, 1024])
        rot64 = sb("rot64", [64, 64], BF16)
        triL = sb("triL", [128, 128])
        triD = sb("triD", [128, 128])
        identf = sb("identf", [128, 128])
        identb = sb("identb", [128, 128], BF16)
        qs_col = sb("qs_col", [64, 1])
        ks_col = sb("ks_col", [64, 1])
        qraw = sb("qraw", [64, 8, 512], BF16)
        kraw = sb("kraw", [64, 8, 1024], BF16)
        qfin = sb("qfin", [65, 8, 512], BF16)
        kfin = sb("kfin", [65, 8, 1024], BF16)
        scr = sb("scr", [1, 128])
        u_sbR = [sb(f"u_sbR{i}", [64, 1024], BF16) for i in range(2)]
        rrowR = [sb(f"rrowR{i}", [1, 1024]) for i in range(2)]
        vr = sb("vr", [128, 8, 8, 64], BF16)
        P_sb = sb("P_sb", [128, 4, 640], BF16)
        PT_sb = sb("PT_sb", [128, 2560], BF16)
        negm = sb("negm", [128, 4])
        ssum = sb("ssum", [128, 4])
        o_fm = sb("o_fm", [128, 4, 512], BF16)
        ones1f = sb("ones1f", [1, 128])
        onescol_f = sb("onescol_f", [128, 1])
        onescol_b = sb("onescol_b", [128, 1], BF16)
        rows = ln_rows(nc, ctx, "lnF", width=512)
        rows["onesf_col"] = onescol_f
        rows["konesf_col"] = "onescol_f"

        psS = pst("psS", [128, 2560])
        psPT = pst("psPT", [128, 2048], BF16)
        psO = pst("psO", [128, 512])
        psA = psS.ap()[:, 0:1024]
        psN = psS.ap()[:, 1024:2048]

        p.v(lambda e: e.memset(ones1f[:], 1.0), writes=["ones1f"])
        p.v(lambda e: e.memset(onescol_f[:], 1.0), writes=["onescol_f"])
        p.v(lambda e: e.memset(onescol_b[:], 1.0), writes=["onescol_b"])
        p.v(lambda e: e.memset(qfin.ap()[64:65, :, :], 1.0), writes=[("qfin", "ones")])

        p.dma("sync", lambda e: e.dma_start(out=identf[:], in_=identw[:, :]), writes=["identf"])
        p.dma("gpsimd", lambda e: e.dma_start(out=identb[:], in_=identw[:, :]), writes=["identb"])
        p.dma("gpsimd", lambda e: e.dma_start(out=rot64[:], in_=rot_in[:, :]), writes=["rot64"])
        p.dma("sync", lambda e: e.dma_start(out=triL[:], in_=tri_in[0, :, :]), writes=["triL"])
        p.dma("sync", lambda e: e.dma_start(out=triD[:], in_=tri_in[1, :, :]), writes=["triD"])

        for cc in range(NCH):
            p.dma("gpsimd", lambda e, cc=cc: e.dma_start(out=wq_sb[:, cc, :], in_=wqkv[128 * cc : 128 * cc + 128, :]), writes=["wq_sb"], tag=f"wq{cc}")
        for cc in range(4):
            p.dma("gpsimd", lambda e, cc=cc: e.dma_start(out=wo_sb[:, cc, :], in_=wo_in[128 * cc : 128 * cc + 128, :]), writes=["wo_sb"], tag=f"wo{cc}")
        p.dma("sync", lambda e: e.dma_start(out=scr[0:1, 0:64], in_=qs_row[:, :]), writes=["scr"])
        p.dma("sync", lambda e: e.dma_start(out=scr[0:1, 64:128], in_=ks_row[:, :]), writes=["scr"])
        p.t(lambda e: e.transpose(psN[0:64, 0:1], scr[0:1, 0:64], identf[0:1, 0:1]), reads=["scr", "identf"], writes=["psS"])
        p.v(lambda e: e.tensor_scalar_mul(qs_col[:], psN[0:64, 0:1], 8.0), reads=["psS"], writes=["qs_col"])
        p.t(lambda e: e.transpose(psN[0:64, 0:1], scr[0:1, 64:128], identf[0:1, 0:1]), reads=["scr", "identf"], writes=["psS"])
        p.v(lambda e: e.tensor_copy(ks_col[:], psN[0:64, 0:1]), reads=["psS"], writes=["ks_col"])

        for r in range(4):
            B = 512 * r
            c0 = B

            p.dma(
                "sync",
                lambda e, c0=c0: e.dma_start(out=fxr[:, :, :], in_=fxe.rearrange("(c p) t -> p c t", p=128)[:, :, c0 : c0 + 1032]),
                writes=["fxr"],
                tag=f"fxr{r}",
            )
            p.dma("sync", lambda e, B=B: e.dma_start(out=tqc[:, :], in_=tqc_in[:, B : B + 512]), writes=["tqc"], tag=f"tqc{r}")
            p.dma("sync", lambda e, B=B: e.dma_start(out=tqs[:, :], in_=tqs_in[:, B : B + 512]), writes=["tqs"], tag=f"tqs{r}")
            p.dma("sync", lambda e, c0=c0: e.dma_start(out=tkc[:, :], in_=tkc_in[:, c0 + 8 : c0 + 1032]), writes=["tkc"], tag=f"tkc{r}")
            p.dma("sync", lambda e, c0=c0: e.dma_start(out=tks[:, :], in_=tks_in[:, c0 + 8 : c0 + 1032]), writes=["tks"], tag=f"tks{r}")
            p.dma("sync", lambda e, c0=c0: e.dma_start(out=dead_r[:, :], in_=dead_in[:, c0 + 8 : c0 + 1032]), writes=["dead_r"], tag=f"dead{r}")

            if entry_ln:
                def apply_lnF(R, MR, rdeps, off, n):
                    p.v(
                        lambda e: e.tensor_tensor(
                            sq[:, :, 0:n], fxr[:, :, off : off + n], R.unsqueeze(1).broadcast_to([128, NCH, n]), ALU.mult
                        ),
                        reads=["fxr", "psS"],
                        writes=["sq"],
                    )
                    p.v(
                        lambda e: e.tensor_tensor(
                            fxr[:, :, off : off + n], sq[:, :, 0:n], MR.unsqueeze(1).broadcast_to([128, NCH, n]), ALU.subtract
                        ),
                        reads=["sq", "psS"],
                        writes=["fxr"],
                    )

                for (off, n) in [(0, 512), (512, 512), (1024, 8)]:
                    emit_ln_g(
                        p,
                        [(fxr.ap()[:, c, off : off + n], "fxr") for c in range(NCH)],
                        n,
                        [(sq.ap()[:, :, 0:n], fxr.ap()[:, :, off : off + n], "fxr", "sq")],
                        [(sq.ap()[:, c, 0:n], "sq") for c in range(NCH)],
                        rows, psA, "psS", onescol_f, "onescol_f", onescol_f, "onescol_f",
                        ones1f, "ones1f",
                        lambda R, MR, rd, off=off, n=n: apply_lnF(R, MR, rd, off, n),
                    )
            p.v(lambda e: e.tensor_copy(fh[:, :, :], fxr[:, :, :]), reads=["fxr"], writes=["fh"])

            def hap(c, a, n):
                off = a if c < 4 else a - 1
                return fh.ap()[:, c, off : off + n]

            for vt in range(8):
                def mk_v(e, vt=vt):
                    last = None
                    for ci in range(NCH):
                        last = e.matmul(
                            psS[0:128, 0:512], hap(ci, 8 + 128 * vt, 128), wq_sb[:, ci, 1024:1536],
                            start=(ci == 0), stop=(ci == NCH - 1),
                        )
                    return last

                p.t(mk_v, reads=["fh", "wq_sb"], writes=["psS"], tag=f"v{r}_{vt}")
                p.v(
                    lambda e, vt=vt: e.tensor_copy(
                        vr.ap()[:, vt, :, :], psS.ap()[0:128, 0:512].rearrange("t (h d) -> t h d", h=8)
                    ),
                    reads=["psS"],
                    writes=["vr"],
                )
            for om in range(0, 8, 2):
                def mk_qr(e, om=om):
                    last = None
                    for ci in range(NCH):
                        last = e.matmul(
                            psN[0:128, 0:512], wq_sb[:, ci, 64 * om : 64 * om + 128], hap(ci, 520, 512),
                            start=(ci == 0), stop=(ci == NCH - 1),
                        )
                    return last

                p.t(mk_qr, reads=["fh", "wq_sb"], writes=["psS"], tag=f"qr{r}_{om}")
                p.s(lambda e, om=om: e.activation(out=qraw[:, om, :], in_=psN[0:64, 0:512], func=AF.Copy), reads=["psS"], writes=["qraw"])
                p.s(lambda e, om=om: e.activation(out=qraw[:, om + 1, :], in_=psN[64:128, 0:512], func=AF.Copy), reads=["psS"], writes=["qraw"])
                for tt in range(2):
                    def mk_kr(e, om=om, tt=tt):
                        last = None
                        for ci in range(NCH):
                            last = e.matmul(
                                psN[0:128, 0:512], wq_sb[:, ci, 512 + 64 * om : 512 + 64 * om + 128], hap(ci, 8 + 512 * tt, 512),
                                start=(ci == 0), stop=(ci == NCH - 1),
                            )
                        return last

                    p.t(mk_kr, reads=["fh", "wq_sb"], writes=["psS"], tag=f"kr{r}_{om}_{tt}")
                    p.s(
                        lambda e, om=om, tt=tt: e.activation(out=kraw[:, om, 512 * tt : 512 * tt + 512], in_=psN[0:64, 0:512], func=AF.Copy),
                        reads=["psS"],
                        writes=["kraw"],
                    )
                    p.s(
                        lambda e, om=om, tt=tt: e.activation(out=kraw[:, om + 1, 512 * tt : 512 * tt + 512], in_=psN[64:128, 0:512], func=AF.Copy),
                        reads=["psS"],
                        writes=["kraw"],
                    )

            sqflat = sq.ap()[0:64, :, :].rearrange("p c t -> p (c t)")
            scrR = [sqflat[:, 0:1024], sqflat[:, 1024:2048]]

            def rot_pipe2(idx, src3, srckey, nh, n, sc_col, sc_key, tcos, tcos_k, tsin, tsin_k, dst3, dstkey):
                pp = idx % 2
                W = nh * n
                scr2, u2, rr2 = scrR[pp], u_sbR[pp], rrowR[pp]
                kscr, ku, krr = ("sq", pp), f"u_sbR{pp}", f"rrowR{pp}"
                pcol = 1024 * pp
                kps = ("psS", pp)
                scr2v = scr2[:, 0:W].rearrange("p (c t) -> p c t", c=nh)
                u2v = u_sbR[pp].ap()[:, 0:W].rearrange("p (c t) -> p c t", c=nh)
                psmatv = psS.ap()[0:64, pcol : pcol + W].rearrange("p (c t) -> p c t", c=nh)
                src2 = src3.rearrange("p c t -> p (c t)")
                cosb = tcos.ap()[:, 0:n].unsqueeze(1).broadcast_to([64, nh, n])
                sinb = tsin.ap()[:, 0:n].unsqueeze(1).broadcast_to([64, nh, n])

                p.s(lambda e: e.activation(out=scr2v, in_=src3, func=AF.Square), reads=[srckey], writes=[kscr])

                def mk_ss(e):
                    last = None
                    for j2 in range(2):
                        last = e.matmul(psS[0:1, pcol + 512 * j2 : pcol + 512 * j2 + 512], onescol_f[0:64, :], scr2[:, 512 * j2 : 512 * j2 + 512], start=True, stop=True)
                    return last

                p.t(mk_ss, reads=[kscr, "onescol_f"], writes=[kps], tag=f"ss{r}_{idx}{srckey}")
                p.s(lambda e: e.activation(out=rr2[:, 0:W], in_=psS[0:1, pcol : pcol + W], func=AF.Sqrt), reads=[kps], writes=[krr])
                p.v(lambda e: e.tensor_scalar_max(rr2[:, 0:W], rr2[:, 0:W], 1e-12), reads=[krr], writes=[krr])
                p.v(lambda e: e.reciprocal(rr2[:, 0:W], rr2[:, 0:W]), reads=[krr], writes=[krr])

                def mk_bc(e):
                    last = None
                    for j2 in range(2):
                        last = e.matmul(psS[0:64, pcol + 512 * j2 : pcol + 512 * j2 + 512], ones1f[:, 0:64], rr2[:, 512 * j2 : 512 * j2 + 512], start=True, stop=True)
                    return last

                p.t(mk_bc, reads=[krr, "ones1f"], writes=[kps], tag=f"bc{r}_{idx}{srckey}")
                p.v(
                    lambda e: e.scalar_tensor_tensor(u2[:, 0:W], src2, sc_col[:, :], psS[0:64, pcol : pcol + W], ALU.mult, ALU.mult),
                    reads=[srckey, sc_key, kps],
                    writes=[ku],
                )

                def mk_rot(e):
                    last = None
                    for j2 in range(2):
                        last = e.matmul(psS[0:64, pcol + 512 * j2 : pcol + 512 * j2 + 512], rot64[:, :], u2[:, 512 * j2 : 512 * j2 + 512], start=True, stop=True)
                    return last

                p.t(mk_rot, reads=[ku, "rot64"], writes=[kps], tag=f"rot{r}_{idx}{srckey}")
                p.v(lambda e: e.tensor_tensor(scr2v, u2v, cosb, ALU.mult), reads=[ku, tcos_k], writes=[kscr])
                p.v(lambda e: e.tensor_tensor(u2v, psmatv, sinb, ALU.mult), reads=[kps, tsin_k], writes=[ku])
                p.v(lambda e: e.tensor_tensor(dst3, scr2v, u2v, ALU.add), reads=[kscr, ku], writes=[dstkey])

            pipes = []
            for hp in range(4):
                pipes.append((qraw.ap()[:, 2 * hp : 2 * hp + 2, :], "qraw", 2, 512, qs_col, "qs_col", tqc, "tqc", tqs, "tqs", qfin.ap()[0:64, 2 * hp : 2 * hp + 2, :], "qfin"))
            for h in range(8):
                pipes.append((kraw.ap()[:, h : h + 1, :], "kraw", 1, 1024, ks_col, "ks_col", tkc, "tkc", tks, "tks", kfin.ap()[0:64, h : h + 1, :], "kfin"))
            for idx, args in enumerate(pipes):
                rot_pipe2(idx, *args)
            p.v(
                lambda e: e.tensor_copy(kfin.ap()[64:65, :, :], dead_r.ap()[:, :].unsqueeze(1).broadcast_to([1, 8, 1024])),
                reads=["dead_r"],
                writes=[("kfin", "dead")],
            )

            for j in range(4):
                q0 = 128 * j
                for hg in range(2):
                    def mk_S(e, j=j, hg=hg, q0=q0):
                        last = None
                        for hh in range(4):
                            h = 4 * hg + hh
                            a = 640 * hh
                            while a < 640 * hh + 640:
                                n = min(512 - a % 512, 640 * hh + 640 - a)
                                last = e.matmul(
                                    psS[:, a : a + n],
                                    qfin[:, h, q0 : q0 + 128],
                                    kfin[:, h, 128 * j + (a - 640 * hh) : 128 * j + (a - 640 * hh) + n],
                                    start=True, stop=True,
                                )
                                a += n
                        return last

                    p.t(mk_S, reads=["qfin", ("qfin", "ones"), "kfin", ("kfin", "dead")], writes=["psS"], tag=f"S{r}_{j}_{hg}")
                    p.v(
                        lambda e: e.tensor_tensor(
                            psS.ap().rearrange("p (hh k) -> p hh k", hh=4)[:, :, 0:128],
                            psS.ap().rearrange("p (hh k) -> p hh k", hh=4)[:, :, 0:128],
                            triL.ap().unsqueeze(1).broadcast_to([128, 4, 128]),
                            ALU.add,
                        ),
                        reads=["psS", "triL"],
                        writes=["psS"],
                    )
                    p.v(
                        lambda e: e.tensor_tensor(
                            psS.ap().rearrange("p (hh k) -> p hh k", hh=4)[:, :, 512:640],
                            psS.ap().rearrange("p (hh k) -> p hh k", hh=4)[:, :, 512:640],
                            triD.ap().unsqueeze(1).broadcast_to([128, 4, 128]),
                            ALU.add,
                        ),
                        reads=["psS", "triD"],
                        writes=["psS"],
                    )
                    p.v(
                        lambda e: e.reduce_max(negm[:, :], psS.ap().rearrange("p (hh k) -> p hh k", hh=4), axis=AX.X, negate=True),
                        reads=["psS"],
                        writes=["negm"],
                    )
                    for hh in range(4):
                        p.s(
                            lambda e, hh=hh: e.activation(
                                out=P_sb[:, hh, :], in_=psS[:, 640 * hh : 640 * hh + 640], func=AF.Exp, bias=negm[:, hh : hh + 1]
                            ),
                            reads=["psS", "negm"],
                            writes=["P_sb"],
                        )
                    p.v(
                        lambda e: e.reduce_sum(ssum[:, :], P_sb.ap(), axis=AX.X),
                        reads=["P_sb"],
                        writes=["ssum"],
                    )
                    p.v(lambda e: e.reciprocal(ssum[:, :], ssum[:, :]), reads=["ssum"], writes=["ssum"])
                    p.v(
                        lambda e: e.tensor_tensor(
                            P_sb.ap(), P_sb.ap(), ssum.ap()[:, :].unsqueeze(2).broadcast_to([128, 4, 640]), ALU.mult
                        ),
                        reads=["P_sb", "ssum"],
                        writes=["P_sb"],
                    )
                    for wv in range(2):
                        def mk_T(e, wv=wv):
                            last = None
                            for i in range(10):
                                g = 10 * wv + i
                                hh, kc = g // 5, g % 5
                                last = e.transpose(
                                    psPT[:, 128 * i : 128 * i + 128],
                                    P_sb[:, hh, 128 * kc : 128 * kc + 128],
                                    identb[:, :],
                                )
                            return last

                        p.t(mk_T, reads=["P_sb", "identb"], writes=["psPT"], tag=f"T{r}_{j}_{hg}_{wv}")
                        p.v(
                            lambda e, wv=wv: e.tensor_copy(PT_sb[:, 1280 * wv : 1280 * wv + 1280], psPT[:, 0:1280]),
                            reads=["psPT"],
                            writes=["PT_sb"],
                        )

                    def mk_PV(e, j=j, hg=hg):
                        last = None
                        for hh in range(4):
                            h = 4 * hg + hh
                            for kc in range(5):
                                g = 5 * hh + kc
                                last = e.matmul(
                                    psO[0:64, 128 * hh : 128 * hh + 128],
                                    vr.ap()[:, j + kc, h, :],
                                    PT_sb[:, 128 * g : 128 * g + 128],
                                    start=(kc == 0), stop=(kc == 4),
                                )
                        return last

                    p.t(mk_PV, reads=["vr", "PT_sb"], writes=["psO"], tag=f"PV{r}_{j}_{hg}")
                    for po in range(2):
                        p.v(
                            lambda e, po=po, hg=hg, q0=q0: e.tensor_copy(
                                o_fm.ap()[64 * po : 64 * po + 64, 2 * hg : 2 * hg + 2, q0 : q0 + 128],
                                psO.ap()[0:64, 0:512].rearrange("p (c two t) -> p c two t", c=2, two=2, t=128)[:, :, po, :],
                            ),
                            reads=["psO"],
                            writes=["o_fm"],
                        )

            for oj in range(NCH):
                def mk_wo(e, oj=oj):
                    last = None
                    for ci in range(4):
                        last = e.matmul(
                            psN[:, 0:512], wo_sb[:, ci, 128 * oj : 128 * oj + 128], o_fm[:, ci, :],
                            start=(ci == 0), stop=(ci == 3),
                        )
                    return last

                p.t(mk_wo, reads=["o_fm", "wo_sb"], writes=["psS"], tag=f"wo{r}_{oj}")
                p.v(
                    lambda e, oj=oj: e.tensor_tensor(fxr[:, oj, 520:1032], fxr[:, oj, 520:1032], psN[:, 0:512], ALU.add),
                    reads=["fxr", "psS"],
                    writes=["fxr"],
                )
            p.dma(
                "sync",
                lambda e, B=B: e.dma_start(
                    out=xmid_out.rearrange("(c p) t -> p c t", p=128)[:, :, B : B + 512], in_=fxr[:, :, 520:1032]
                ),
                reads=["fxr"],
                writes=[("xmid_out", r)],
                tag=f"xmo{r}",
            )

        p.finalize(ctx)
    return nc




def build_launch_fb(final_combine):
    nc = bass.Bass("TRN2", target_bir_lowering=False)
    ap = lambda t: t.ap()
    xme = ap(nc.dram_tensor("xme", [D, TOK + 1], F32, kind="ExternalInput"))
    w1_in = ap(nc.dram_tensor("w1_in", [D, 5460], F32, kind="ExternalInput"))
    w2_in = ap(nc.dram_tensor("w2_in", [2816, D], F32, kind="ExternalInput"))
    b1a_in = ap(nc.dram_tensor("b1a_in", [22, 128], F32, kind="ExternalInput"))
    b1g_in = ap(nc.dram_tensor("b1g_in", [22, 128], F32, kind="ExternalInput"))
    b2_in = ap(nc.dram_tensor("b2_in", [8, 128], F32, kind="ExternalInput"))
    lng_in = ap(nc.dram_tensor("lng_in", [1, D], F32, kind="ExternalInput"))
    lnb_in = ap(nc.dram_tensor("lnb_in", [1, D], F32, kind="ExternalInput"))
    identw = ap(nc.dram_tensor("identw", [128, 128], F32, kind="ExternalInput"))
    neg1_in = ap(nc.dram_tensor("neg1_in", [1, 544], F32, kind="ExternalInput"))
    if final_combine:
        x1own = ap(nc.dram_tensor("x1own", [D, TOK], F32, kind="ExternalInput"))
    out_t = ap(nc.dram_tensor("out_t", [D, TOK], F32, kind="ExternalOutput"))

    with ExitStack() as ctx:
        sb = lambda name, shape, dt=F32: ctx.enter_context(nc.sbuf_tensor(name, shape, dt))
        pst = lambda name, shape, dt=F32: ctx.enter_context(nc.psum_tensor(name, shape, dt))
        p = P(nc)

        xmr = sb("xmr", [128, NCH, 513])
        sq = sb("sq", [128, NCH, 512])
        xh = sb("xh", [128, NCH, 512], BF16)
        w1ta = [sb(f"w1ta{i}", [128, NCH, 128], BF16) for i in range(2)]
        w1tg = [sb(f"w1tg{i}", [128, NCH, 128], BF16) for i in range(2)]
        w2t = [sb(f"w2t{i}", [128, 22, 128], BF16) for i in range(2)]
        h_sb = sb("h_sb", [128, 22, 512], BF16)
        b1a_c = sb("b1a_c", [128, 22])
        b1g_c = sb("b1g_c", [128, 22])
        b2_c = sb("b2_c", [128, 8])
        gb2 = sb("gb2", [2, NCH, 128])
        gcol = sb("gcol", [128, 8])
        rows2v = sb("rows2v", [2, 544])
        btmp = sb("btmp", [22, 128])
        identf = sb("identf", [128, 128])
        xnr = sb("xnr", [128, NCH, 512])
        x1r = sb("x1r", [128, NCH, 512])
        ones1f = sb("ones1f", [1, 128])
        onescol_f = sb("onescol_f", [128, 1])
        rows = ln_rows(nc, ctx, "lnB")
        rows["onesf_col"] = onescol_f
        rows["konesf_col"] = "onescol_f"

        ps1 = pst("ps1", [128, 512])
        ps2 = pst("ps2", [128, 512])
        psA = pst("psA", [128, 1024])
        psB = pst("psB", [128, 512])

        p.v(lambda e: e.memset(ones1f[:], 1.0), writes=["ones1f"])
        p.v(lambda e: e.memset(onescol_f[:], 1.0), writes=["onescol_f"])
        p.dma("sync", lambda e: e.dma_start(out=rows2v.ap()[1:2, :], in_=neg1_in[:, :]), writes=[("rows2v", "neg")])

        p.dma("sync", lambda e: e.dma_start(out=identf[:], in_=identw[:, :]), writes=["identf"])
        p.dma("sync", lambda e: e.dma_start(out=btmp[:], in_=b1a_in[:, :]), writes=["btmp"])
        p.t(lambda e: e.transpose(psA[:, 0:22], btmp[:, :], identf[0:22, 0:22]), reads=["btmp", "identf"], writes=["psA"])
        p.v(lambda e: e.tensor_copy(b1a_c[:], psA[:, 0:22]), reads=["psA"], writes=["b1a_c"])
        p.dma("sync", lambda e: e.dma_start(out=btmp[:], in_=b1g_in[:, :]), writes=["btmp"])
        p.t(lambda e: e.transpose(psA[:, 0:22], btmp[:, :], identf[0:22, 0:22]), reads=["btmp", "identf"], writes=["psA"])
        p.v(lambda e: e.tensor_copy(b1g_c[:], psA[:, 0:22]), reads=["psA"], writes=["b1g_c"])
        p.dma("sync", lambda e: e.dma_start(out=btmp[0:8, :], in_=b2_in[:, :]), writes=["btmp"])
        p.t(lambda e: e.transpose(psA[:, 0:8], btmp[0:8, :], identf[0:8, 0:8]), reads=["btmp", "identf"], writes=["psA"])
        p.v(lambda e: e.tensor_copy(b2_c[:], psA[:, 0:8]), reads=["psA"], writes=["b2_c"])
        p.dma("sync", lambda e: e.dma_start(out=gb2.ap()[0:1, :, :], in_=lng_in.rearrange("o (c m) -> o c m", c=NCH)), writes=[("gb2", "g")])
        p.dma("sync", lambda e: e.dma_start(out=gb2.ap()[1:2, :, :], in_=lnb_in.rearrange("o (c m) -> o c m", c=NCH)), writes=[("gb2", "b")])
        lng8 = sb("lng8", [8, 128])
        p.dma("sync", lambda e: e.dma_start(out=lng8[:], in_=lng_in.rearrange("o (c m) -> (o c) m", c=NCH)), writes=["lng8"])
        p.t(lambda e: e.transpose(psA[:, 0:8], lng8[:, :], identf[0:8, 0:8]), reads=["lng8", "identf"], writes=["psA"])
        p.v(lambda e: e.tensor_copy(gcol[:], psA[:, 0:8]), reads=["psA"], writes=["gcol"])

        for r in range(4):
            B = 512 * r
            p.dma(
                "sync",
                lambda e, B=B: e.dma_start(out=xmr[:, :, :], in_=xme.rearrange("(c p) t -> p c t", p=128)[:, :, B : B + 513]),
                writes=["xmr"],
                tag=f"xmr{r}",
            )
            if final_combine:
                p.dma(
                    "sync",
                    lambda e, B=B: e.dma_start(out=x1r[:, :, :], in_=x1own.rearrange("(c p) t -> p c t", p=128)[:, :, B : B + 512]),
                    writes=["x1r"],
                    tag=f"x1r{r}",
                )

            def h2ap(c, n=512):
                off = 1 if c < 4 else 0
                return xmr.ap()[:, c, off : off + n]

            def apply_lnG(R, MR, rdeps):
                p.v(
                    lambda e: e.tensor_tensor(sq[:, 0:4, :], xmr[:, 0:4, 1:513], R.unsqueeze(1).broadcast_to([128, 4, 512]), ALU.mult),
                    reads=["xmr", "psA"],
                    writes=["sq"],
                )
                p.v(
                    lambda e: e.tensor_tensor(sq[:, 4:8, :], xmr[:, 4:8, 0:512], R.unsqueeze(1).broadcast_to([128, 4, 512]), ALU.mult),
                    reads=["xmr", "psA"],
                    writes=["sq"],
                )
                p.v(lambda e: e.tensor_copy(rows2v[0:1, 0:512], rows["murs"][:, 0:512]), reads=[rows["kmurs"]], writes=[("rows2v", "m")])
                for cch in range(NCH):
                    p.t(
                        lambda e, cch=cch: e.matmul(psB[:, 0:512], gb2[:, cch, :], rows2v[:, 0:512], start=True, stop=True),
                        reads=[("gb2", "g"), ("gb2", "b"), ("rows2v", "m"), ("rows2v", "neg")],
                        writes=["psB"],
                        tag=f"gbc{r}_{cch}",
                    )
                    p.v(
                        lambda e, cch=cch: e.scalar_tensor_tensor(
                            xh[:, cch, :], sq[:, cch, :], gcol[:, cch : cch + 1], psB[:, 0:512], ALU.mult, ALU.subtract
                        ),
                        reads=["sq", "gcol", "psB"],
                        writes=["xh"],
                    )

            emit_ln_g(
                p,
                [(h2ap(c), "xmr") for c in range(NCH)],
                512,
                [
                    (sq.ap()[:, 0:4, :], xmr.ap()[:, 0:4, 1:513], "xmr", "sq"),
                    (sq.ap()[:, 4:8, :], xmr.ap()[:, 4:8, 0:512], "xmr", "sq"),
                ],
                [(sq.ap()[:, c, :], "sq") for c in range(NCH)],
                rows, psA, "psA", onescol_f, "onescol_f", onescol_f, "onescol_f",
                ones1f, "ones1f", apply_lnG,
            )

            for oj in range(22):
                mm = 128 if oj < 21 else 42
                bb = oj % 2
                p.dma(
                    "gpsimd",
                    lambda e, oj=oj, mm=mm, bb=bb: e.dma_start(
                        out=w1ta[bb].ap()[:, :, 0:mm],
                        in_=w1_in.rearrange("(c p) m -> p c m", p=128)[:, :, 128 * oj : 128 * oj + mm],
                    ),
                    writes=[("w1ta", bb)],
                    tag=f"w1a{r}_{oj}",
                )
                p.dma(
                    "gpsimd",
                    lambda e, oj=oj, mm=mm, bb=bb: e.dma_start(
                        out=w1tg[bb].ap()[:, :, 0:mm],
                        in_=w1_in.rearrange("(c p) m -> p c m", p=128)[:, :, 2730 + 128 * oj : 2730 + 128 * oj + mm],
                    ),
                    writes=[("w1tg", bb)],
                    tag=f"w1g{r}_{oj}",
                )

                def mk_a(e, oj=oj, mm=mm, bb=bb):
                    last = None
                    for ci in range(NCH):
                        last = e.matmul(
                            ps1[0:mm, 0:512], w1ta[bb][:, ci, 0:mm], xh[:, ci, :],
                            start=(ci == 0), stop=(ci == NCH - 1),
                        )
                    return last

                def mk_g(e, oj=oj, mm=mm, bb=bb):
                    last = None
                    for ci in range(NCH):
                        last = e.matmul(
                            ps2[0:mm, 0:512], w1tg[bb][:, ci, 0:mm], xh[:, ci, :],
                            start=(ci == 0), stop=(ci == NCH - 1),
                        )
                    return last

                p.t(mk_a, reads=["xh", ("w1ta", bb)], writes=["ps1"], tag=f"a{r}_{oj}")
                p.t(mk_g, reads=["xh", ("w1tg", bb)], writes=["ps2"], tag=f"g{r}_{oj}")
                p.s(
                    lambda e, oj=oj, mm=mm: e.activation(out=sq[0:mm, 0, :], in_=ps1[0:mm, 0:512], func=AF.Identity, bias=b1a_c[0:mm, oj : oj + 1]),
                    reads=["ps1", "b1a_c"],
                    writes=["sq"],
                )
                p.s(
                    lambda e, oj=oj, mm=mm: e.activation(out=sq[0:mm, 1, :], in_=ps2[0:mm, 0:512], func=AF.Gelu, bias=b1g_c[0:mm, oj : oj + 1]),
                    reads=["ps2", "b1g_c"],
                    writes=["sq"],
                )
                p.v(
                    lambda e, oj=oj, mm=mm: e.tensor_tensor(h_sb[0:mm, oj, :], sq[0:mm, 0, :], sq[0:mm, 1, :], ALU.mult),
                    reads=["sq"],
                    writes=["h_sb"],
                )

            for oj in range(NCH):
                bb = oj % 2
                p.dma(
                    "gpsimd",
                    lambda e, oj=oj, bb=bb: e.dma_start(
                        out=w2t[bb].ap()[:, 0:22, :],
                        in_=w2_in.rearrange("(c p) m -> p c m", p=128)[:, :, 128 * oj : 128 * oj + 128],
                    ),
                    writes=[("w2t", bb)],
                    tag=f"w2l{r}_{oj}",
                )

                def mk_w2(e, oj=oj, bb=bb):
                    last = None
                    for ci in range(22):
                        rr2 = 128 if ci < 21 else 42
                        last = e.matmul(
                            ps1[:, 0:512], w2t[bb][0:rr2, ci, :], h_sb[0:rr2, ci, :],
                            start=(ci == 0), stop=(ci == 21),
                        )
                    return last

                p.t(mk_w2, reads=["h_sb", ("w2t", bb)], writes=["ps1"], tag=f"w2{r}_{oj}")
                p.s(
                    lambda e, oj=oj: e.activation(out=sq[:, 0, :], in_=ps1[:, 0:512], func=AF.Identity, bias=b2_c[:, oj : oj + 1]),
                    reads=["ps1", "b2_c"],
                    writes=["sq"],
                )
                p.v(
                    lambda e, oj=oj: e.tensor_tensor(xnr[:, oj, :], xmr[:, oj, 1:513], sq[:, 0, :], ALU.add),
                    reads=["xmr", "sq"],
                    writes=["xnr"],
                )

            if final_combine:
                def apply_fin(R, MR, rdeps):
                    p.v(
                        lambda e: e.tensor_tensor(sq[:, :, :], xnr[:, :, :], R.unsqueeze(1).broadcast_to([128, NCH, 512]), ALU.mult),
                        reads=["xnr", "psA"],
                        writes=["sq"],
                    )
                    p.v(
                        lambda e: e.tensor_tensor(sq[:, :, :], sq[:, :, :], MR.unsqueeze(1).broadcast_to([128, NCH, 512]), ALU.subtract),
                        reads=["sq", "psA"],
                        writes=["sq"],
                    )
                    p.v(
                        lambda e: e.tensor_tensor(xnr[:, :, :], x1r[:, :, :], sq[:, :, :], ALU.add),
                        reads=["sq", "x1r"],
                        writes=["xnr"],
                    )

                emit_ln_g(
                    p,
                    [(xnr.ap()[:, c, :], "xnr") for c in range(NCH)],
                    512,
                    [(sq.ap()[:, :, :], xnr.ap()[:, :, :], "xnr", "sq")],
                    [(sq.ap()[:, c, :], "sq") for c in range(NCH)],
                    rows, psA, "psA", onescol_f, "onescol_f", onescol_f, "onescol_f",
                    ones1f, "ones1f", apply_fin,
                )

            p.dma(
                "sync",
                lambda e, B=B: e.dma_start(
                    out=out_t.rearrange("(c p) t -> p c t", p=128)[:, :, B : B + 512], in_=xnr[:, :, :]
                ),
                reads=["xnr"],
                writes=[("out_t", r)],
                tag=f"out{r}",
            )

        p.finalize(ctx)
    return nc




def fm(x):
    return np.ascontiguousarray(x.T)


def build_attn0_masks(s):
    mA = np.zeros((16, 128, 128), np.float32)
    mB = np.zeros((16, 16, 16), np.float32)
    for j in range(16):
        qt = s + 128 * j
        kk = np.arange(128)[:, None]
        qq = np.arange(128)[None, :]
        key = qt - 8 + kk
        q = qt + qq
        mA[j] = ((np.abs(key - q) <= 8) & (key >= 0) & (key < SEQ)).astype(np.float32)
        kk2 = np.arange(16)[:, None]
        qi = np.arange(16)[None, :]
        key2 = qt + 120 + kk2
        q2 = qt + 112 + qi
        mB[j] = ((np.abs(key2 - q2) <= 8) & (key2 >= 0) & (key2 < SEQ)).astype(np.float32)
    return mA, mB


def build_selectors():
    s16 = np.zeros((8, 16, 128), np.float32)
    hs16 = np.zeros((8, 128, 16), np.float32)
    for c in range(8):
        for pp in range(128):
            h = 2 * c + (1 if pp >= 64 else 0)
            s16[c, h, pp] = 1.0
            hs16[c, pp, h] = 1.0
    return s16, hs16


def run_launch_a(inputs, x0_full):
    nc = build_launch_a()
    w_qkv = inputs["L0_attn_w_qkv"]
    w_out = inputs["L0_attn_w_out"]
    wqkvT = np.ascontiguousarray(w_qkv.T.astype(np.float32))
    woutT = np.ascontiguousarray(w_out.T.astype(np.float32))
    bqkv = inputs["L0_attn_b_qkv"].astype(np.float32)
    bout = inputs["L0_attn_b_out"].astype(np.float32)
    s16, hs16 = build_selectors()
    ident = np.eye(128, dtype=np.float32)
    in_maps = []
    for c in range(NCORES):
        b, s = c // 4, TOK * (c % 4)
        lo, hi = s - 8, s - 8 + KEXT
        x0e = np.zeros((KEXT, D), np.float32)
        a0, a1 = max(lo, 0), min(hi, SEQ)
        x0e[a0 - lo : a1 - lo] = x0_full[b, a0:a1]
        mA, mB = build_attn0_masks(s)
        in_maps.append(
            {
                "x0e": fm(x0e),
                "wqkvT": wqkvT,
                "woutT": woutT,
                "bqkv24": bqkv.reshape(24, 128),
                "bqkv_row": bqkv.reshape(1, 3 * D),
                "bout8": bout.reshape(8, 128),
                "maskA": mA,
                "maskB": mB,
                "s16": s16,
                "hs16": hs16,
                "identw": ident,
            }
        )
    res = run_bass_kernel_spmd(nc, in_maps, core_ids=list(range(NCORES)))
    x1 = np.zeros((2, SEQ, D), np.float32)
    for c in range(NCORES):
        b, s = c // 4, TOK * (c % 4)
        x1[b, s : s + TOK] = res.results[c]["x1out"].T
    return x1





def build_launch_l1():
    nc = bass.Bass("TRN2", target_bir_lowering=False)
    ap = lambda t: t.ap()
    x2own = ap(nc.dram_tensor("x2own", [D, TOK], F32, kind="ExternalInput"))
    x2rot = ap(nc.dram_tensor("x2rot", [D, SEQ], F32, kind="ExternalInput"))
    wq_in = ap(nc.dram_tensor("wq_in", [D, D], F32, kind="ExternalInput"))
    wk_in = ap(nc.dram_tensor("wk_in", [D, 256], F32, kind="ExternalInput"))
    wv_in = ap(nc.dram_tensor("wv_in", [D, 256], F32, kind="ExternalInput"))
    wo_in = ap(nc.dram_tensor("wo_in", [D, D], F32, kind="ExternalInput"))
    bq16 = ap(nc.dram_tensor("bq16", [16, 64], F32, kind="ExternalInput"))
    bk4 = ap(nc.dram_tensor("bk4", [4, 64], F32, kind="ExternalInput"))
    bvrow_in = ap(nc.dram_tensor("bvrow_in", [1, 256], F32, kind="ExternalInput"))
    bo8 = ap(nc.dram_tensor("bo8", [8, 128], F32, kind="ExternalInput"))
    maskw_in = ap(nc.dram_tensor("maskw_in", [128, 4, 128], F32, kind="ExternalInput"))
    s16 = ap(nc.dram_tensor("s16", [8, 16, 128], F32, kind="ExternalInput"))
    identw = ap(nc.dram_tensor("identw", [128, 128], F32, kind="ExternalInput"))
    x3out = ap(nc.dram_tensor("x3out", [D, TOK], F32, kind="ExternalOutput"))
    dbgout = ap(nc.dram_tensor("dbgout", [16, 256], F32, kind="ExternalOutput"))

    with ExitStack() as ctx:
        sb = lambda name, shape, dt=F32: ctx.enter_context(nc.sbuf_tensor(name, shape, dt))
        pst = lambda name, shape, dt=F32: ctx.enter_context(nc.psum_tensor(name, shape, dt))
        p = P(nc)
        p.alias_groups = {
            "psS": [("psS", 0), ("psS", 1), ("psS", ("q", 0)), ("psS", ("q", 1)), ("psS", ("n", 0)), ("psS", ("n", 1))],
            "psO": [("psO", 0), ("psO", 1)],
            "sqt": [("sqt", 0), ("sqt", 1)],
        }

        xt = sb("xt", [128, NCH, 256])
        xht = sb("xht", [128, NCH, 256], BF16)
        sqt = sb("sqt", [128, NCH, 256])
        k_sb = sb("k_sb", [64, 4, SEQ], BF16)
        v_sb = sb("v_sb", [128, 64, 4, 65], BF16)
        q_t = sb("q_t", [64, 16, 128], BF16)
        e_sb = sb("e_sb", [128, 4096], BF16)
        ctxr = sb("ctxr", [128, NCH, 512], BF16)
        x2r = sb("x2r", [128, NCH, 512])
        yr = ctxr
        wq_sb = sb("wq_sb", [128, NCH, D], BF16)
        wk_sb = sb("wk_sb", [128, NCH, 256], BF16)
        wv_sb = sb("wv_sb", [128, NCH, 256], BF16)
        wo_sb = sb("wo_sb", [128, NCH, D], BF16)
        bq_c = sb("bq_c", [64, 16])
        bk_c = sb("bk_c", [64, 4])
        bo_c = sb("bo_c", [128, 8])
        bvrow = sb("bvrow", [1, 256], BF16)
        btmp = sb("btmp", [16, 128])
        maskw = sb("maskw", [128, 4, 128])
        s16_sb = sb("s16_sb", [16, NCH, 128], BF16)
        identf = sb("identf", [128, 128])
        negBc = sb("negBc", [128, 1])
        nB = sb("nB", [1, 4])
        negB = sb("negB", [1, 1], BF16)
        rowmax = sb("rowmax", [128, 1])
        n2acc = sb("n2acc", [128, 16])
        scr = sb("scr", [64, 4, 128], BF16)
        srow = sb("srow", [1, 512])
        sums_tm = sb("sums_tm", [128, 16])
        recip_hm = sb("recip_hm", [16, 512], BF16)
        ones1f = sb("ones1f", [1, 128])
        ones1b = sb("ones1b", [1, 128], BF16)
        onescol_f = sb("onescol_f", [128, 1])
        onescol_b = sb("onescol_b", [128, 1], BF16)
        rows = ln_rows(nc, ctx, "lnL", width=256)
        rows["onesf_col"] = onescol_f
        rows["konesf_col"] = "onescol_f"

        psS = pst("psS", [128, 2048])
        psO = pst("psO", [128, 2048])
        psA = psS.ap()[:, 0:1024]

        p.v(lambda e: e.memset(ones1f[:], 1.0), writes=["ones1f"])
        p.v(lambda e: e.memset(ones1b[:], 1.0), writes=["ones1b"])
        p.v(lambda e: e.memset(onescol_f[:], 1.0), writes=["onescol_f"])
        p.v(lambda e: e.memset(onescol_b[:], 1.0), writes=["onescol_b"])
        p.v(lambda e: e.memset(v_sb.ap()[:, :, :, 64:65], 1.0), writes=[("v_sb", "ones")])
        p.v(lambda e: e.memset(recip_hm[:], 0.0), writes=["recip_hm"])
        p.v(lambda e: e.memset(ctxr[:, :, :], 0.0), writes=["ctxr"])

        p.dma("sync", lambda e: e.dma_start(out=identf[:], in_=identw[:, :]), writes=["identf"])
        p.dma("sync", lambda e: e.dma_start(out=maskw[:], in_=maskw_in[:, :, :]), writes=["maskw"])
        p.dma("gpsimd", lambda e: e.dma_start(out=s16_sb[:], in_=s16.rearrange("c h t -> h c t")), writes=["s16_sb"])
        p.dma("gpsimd", lambda e: e.dma_start(out=bvrow[:], in_=bvrow_in[:, :]), writes=["bvrow"])
        for cc in range(NCH):
            p.dma("gpsimd", lambda e, cc=cc: e.dma_start(out=wq_sb[:, cc, :], in_=wq_in[128 * cc : 128 * cc + 128, :]), writes=["wq_sb"], tag=f"wq{cc}")
            p.dma("gpsimd", lambda e, cc=cc: e.dma_start(out=wo_sb[:, cc, :], in_=wo_in[128 * cc : 128 * cc + 128, :]), writes=["wo_sb"], tag=f"wo{cc}")
            p.dma("gpsimd", lambda e, cc=cc: e.dma_start(out=wk_sb[:, cc, :], in_=wk_in[128 * cc : 128 * cc + 128, :]), writes=["wk_sb"], tag=f"wk{cc}")
            p.dma("gpsimd", lambda e, cc=cc: e.dma_start(out=wv_sb[:, cc, :], in_=wv_in[128 * cc : 128 * cc + 128, :]), writes=["wv_sb"], tag=f"wv{cc}")
        p.dma("sync", lambda e: e.dma_start(out=btmp[0:16, 0:64], in_=bq16[:, :]), writes=["btmp"])
        p.t(lambda e: e.transpose(psS[0:64, 0:16], btmp[0:16, 0:64], identf[0:16, 0:16]), reads=["btmp", "identf"], writes=["psS"])
        p.v(lambda e: e.tensor_scalar_mul(bq_c[:], psS[0:64, 0:16], 0.125), reads=["psS"], writes=["bq_c"])
        p.dma("sync", lambda e: e.dma_start(out=btmp[0:4, 0:64], in_=bk4[:, :]), writes=["btmp"])
        p.t(lambda e: e.transpose(psS[0:64, 0:4], btmp[0:4, 0:64], identf[0:4, 0:4]), reads=["btmp", "identf"], writes=["psS"])
        p.v(lambda e: e.tensor_copy(bk_c[:], psS[0:64, 0:4]), reads=["psS"], writes=["bk_c"])
        p.dma("sync", lambda e: e.dma_start(out=btmp[0:8, :], in_=bo8[:, :]), writes=["btmp"])
        p.t(lambda e: e.transpose(psS[:, 0:8], btmp[0:8, :], identf[0:8, 0:8]), reads=["btmp", "identf"], writes=["psS"])
        p.v(lambda e: e.tensor_copy(bo_c[:], psS[:, 0:8]), reads=["psS"], writes=["bo_c"])

        for kt4 in range(32):
            p.dma(
                "sync",
                lambda e, kt4=kt4: e.dma_start(out=xt[:, :, :], in_=x2rot.rearrange("(c p) t -> p c t", p=128)[:, :, 256 * kt4 : 256 * kt4 + 256]),
                writes=["xt"],
                tag=f"xt{kt4}",
            )

            def apply_lnL(R, MR, rdeps):
                p.v(
                    lambda e: e.tensor_tensor(sqt[:, :, :], xt[:, :, :], R.unsqueeze(1).broadcast_to([128, NCH, 256]), ALU.mult),
                    reads=["xt", "psS"],
                    writes=["sqt"],
                )
                p.v(
                    lambda e: e.tensor_tensor(xht[:, :, :], sqt[:, :, :], MR.unsqueeze(1).broadcast_to([128, NCH, 256]), ALU.subtract),
                    reads=["sqt", "psS"],
                    writes=["xht"],
                )

            emit_ln_g(
                p,
                [(xt.ap()[:, c, :], "xt") for c in range(NCH)],
                256,
                [(sqt.ap()[:, :, :], xt.ap()[:, :, :], "xt", "sqt")],
                [(sqt.ap()[:, c, :], "sqt") for c in range(NCH)],
                rows, psA, "psS", onescol_f, "onescol_f", onescol_f, "onescol_f",
                ones1f, "ones1f", apply_lnL,
            )
            def mk_k(e, kt4=kt4):
                last = None
                for om in range(4):
                    for ci in range(NCH):
                        last = e.matmul(
                            psO[0:64, 256 * om : 256 * om + 256], wk_sb[:, ci, 64 * om : 64 * om + 64], xht[:, ci, :],
                            start=(ci == 0), stop=(ci == NCH - 1),
                        )
                return last

            p.t(mk_k, reads=["xht", "wk_sb"], writes=["psO"], tag=f"k{kt4}")
            for om in range(4):
                p.s(
                    lambda e, kt4=kt4, om=om: e.activation(
                        out=k_sb[:, om, 256 * kt4 : 256 * kt4 + 256], in_=psO[0:64, 256 * om : 256 * om + 256],
                        func=AF.Identity, bias=bk_c[:, om : om + 1],
                    ),
                    reads=["psO", "bk_c"],
                    writes=["k_sb"],
                )
            for sub in range(2):
                def mk_v(e, kt4=kt4, sub=sub):
                    for ci in range(NCH):
                        e.matmul(psS[:, 1024 + 256 * (sub % 2) : 1280 + 256 * (sub % 2)], xht[:, ci, 128 * sub : 128 * sub + 128], wv_sb[:, ci, :], start=(ci == 0), stop=False)
                    return e.matmul(psS[:, 1024 + 256 * (sub % 2) : 1280 + 256 * (sub % 2)], ones1b[:, 0:128], bvrow[:, :], start=False, stop=True)

                p.t(mk_v, reads=["xht", "wv_sb", "ones1b", "bvrow"], writes=[("psS", 1)], tag=f"v{kt4}_{sub}")
                p.v(
                    lambda e, kt4=kt4, sub=sub: e.tensor_copy(
                        v_sb.ap()[:, 2 * kt4 + sub, :, 0:64],
                        psS.ap()[:, 1024 + 256 * (sub % 2) : 1280 + 256 * (sub % 2)].rearrange("t (g d) -> t g d", g=4),
                    ),
                    reads=[("psS", 1)],
                    writes=[("v_sb", "dat")],
                )

        p.v(lambda e: e.memset(n2acc[:], 0.0), writes=["n2acc"])
        for it in range(8):
            p.dma(
                "sync",
                lambda e, it=it: e.dma_start(out=xt[:, :, :], in_=x2own.rearrange("(c p) t -> p c t", p=128)[:, :, 256 * it : 256 * it + 256]),
                writes=["xt"],
                tag=f"xtq{it}",
            )

            def apply_lnq(R, MR, rdeps):
                p.v(
                    lambda e: e.tensor_tensor(sqt[:, :, :], xt[:, :, :], R.unsqueeze(1).broadcast_to([128, NCH, 256]), ALU.mult),
                    reads=["xt", "psS"],
                    writes=["sqt"],
                )
                p.v(
                    lambda e: e.tensor_tensor(xht[:, :, :], sqt[:, :, :], MR.unsqueeze(1).broadcast_to([128, NCH, 256]), ALU.subtract),
                    reads=["sqt", "psS"],
                    writes=["xht"],
                )

            emit_ln_g(
                p,
                [(xt.ap()[:, c, :], "xt") for c in range(NCH)],
                256,
                [(sqt.ap()[:, :, :], xt.ap()[:, :, :], "xt", "sqt")],
                [(sqt.ap()[:, c, :], "sqt") for c in range(NCH)],
                rows, psA, "psS", onescol_f, "onescol_f", onescol_f, "onescol_f",
                ones1f, "ones1f", apply_lnq,
            )
            for om in range(0, 16, 2):
                par = (om // 2) % 2
                qc = 512 * par

                def mk_q(e, om=om, qc=qc):
                    last = None
                    for ci in range(NCH):
                        last = e.matmul(
                            psO[0:128, qc : qc + 256], wq_sb[:, ci, 64 * om : 64 * om + 128], xht[:, ci, :],
                            start=(ci == 0), stop=(ci == NCH - 1),
                        )
                    return last

                p.t(mk_q, reads=["xht", "wq_sb"], writes=[("psO", par)], tag=f"qn{it}_{om}")
                sc = 2 * par
                p.s(
                    lambda e, om=om, qc=qc, sc=sc: e.activation(
                        out=sqt[0:64, sc, 0:256], in_=psO[0:64, qc : qc + 256], func=AF.Square, scale=0.125, bias=bq_c[:, om : om + 1]
                    ),
                    reads=[("psO", par), "bq_c"],
                    writes=[("sqt", par)],
                )
                p.s(
                    lambda e, om=om, qc=qc, sc=sc: e.activation(
                        out=sqt[0:64, sc + 1, 0:256], in_=psO[64:128, qc : qc + 256], func=AF.Square, scale=0.125, bias=bq_c[:, om + 1 : om + 2]
                    ),
                    reads=[("psO", par), "bq_c"],
                    writes=[("sqt", par)],
                )

                def mk_n2(e, om=om, sc=sc, par=par):
                    last = None
                    base = 768 if par == 0 else 832
                    g4 = (om // 2) // 2
                    for hh in range(2):
                        for half in range(2):
                            col = base + 4 * g4 + 2 * hh + half
                            last = e.matmul(
                                psS[0:128, col : col + 1],
                                sqt[0:64, sc + hh, 128 * half : 128 * half + 128],
                                onescol_f[0:64, :],
                                start=True, stop=True,
                            )
                    return last

                p.t(mk_n2, reads=[("sqt", par), "onescol_f"], writes=[("psS", ("n", par))], tag=f"qn2_{it}_{om}")
            p.v(
                lambda e: e.tensor_tensor(n2acc[:, 0:16], n2acc[:, 0:16], psS[:, 768:784], ALU.max),
                reads=[("psS", ("n", 0)), "n2acc"],
                writes=["n2acc"],
            )
            p.v(
                lambda e: e.tensor_tensor(n2acc[:, 0:16], n2acc[:, 0:16], psS[:, 832:848], ALU.max),
                reads=[("psS", ("n", 1)), "n2acc"],
                writes=["n2acc"],
            )
        p.v(lambda e: e.reduce_max(rowmax[:], n2acc[:], axis=AX.X), reads=["n2acc"], writes=["rowmax"])
        p.t(lambda e: e.transpose(psS[0:1, 256:384], rowmax[:, :], identf[:, :]), reads=["rowmax", "identf"], writes=["psS"])
        p.v(lambda e: e.reduce_max(nB[:, 0:1], psS[0:1, 256:384], axis=AX.X), reads=["psS"], writes=["nB"])
        p.v(lambda e: e.memset(n2acc[:], 0.0), writes=["n2acc"])
        for kt in range(64):
            p.s(lambda e, kt=kt: e.activation(out=scr[:, :, :], in_=k_sb.ap()[:, :, 128 * kt : 128 * kt + 128], func=AF.Square), reads=["k_sb"], writes=["scr"])

            def mk_kn2(e):
                last = None
                for om in range(4):
                    last = e.matmul(psS[0:128, 128 + om : 129 + om], scr[:, om, :], onescol_b[0:64, :], start=True, stop=True)
                return last

            p.t(mk_kn2, reads=["scr", "onescol_b"], writes=["psS"], tag=f"kn2{kt}")
            p.v(lambda e: e.tensor_tensor(n2acc[:, 0:4], n2acc[:, 0:4], psS[:, 128:132], ALU.max), reads=["psS", "n2acc"], writes=["n2acc"])
        p.v(lambda e: e.reduce_max(rowmax[:], n2acc[:, 0:4], axis=AX.X), reads=["n2acc"], writes=["rowmax"])
        p.t(lambda e: e.transpose(psS[0:1, 256:384], rowmax[:, :], identf[:, :]), reads=["rowmax", "identf"], writes=["psS"])
        p.v(lambda e: e.reduce_max(nB[:, 1:2], psS[0:1, 256:384], axis=AX.X), reads=["psS"], writes=["nB"])
        p.s(lambda e: e.activation(out=nB[:, 0:2], in_=nB[:, 0:2], func=AF.Sqrt), reads=["nB"], writes=["nB"])
        p.v(lambda e: e.tensor_mul(nB[:, 2:3], nB[:, 0:1], nB[:, 1:2]), reads=["nB"], writes=["nB"])
        p.s(lambda e: e.activation(out=negB[:, :], in_=nB[:, 2:3], func=AF.Copy, scale=-1.0, bias=-1.0), reads=["nB"], writes=["negB"])
        p.t(lambda e: e.matmul(psS[:, 384:385], ones1b[:, 0:128], negB[:, :], start=True, stop=True), reads=["negB", "ones1b"], writes=["psS"])
        p.v(lambda e: e.tensor_copy(negBc[:], psS[:, 384:385]), reads=["psS"], writes=["negBc"])
        for r in range(int(os.environ.get("L1_REGIONS", "4"))):
            p.dma(
                "sync",
                lambda e, r=r: e.dma_start(out=x2r[:, :, :], in_=x2own.rearrange("(c p) t -> p c t", p=128)[:, :, 512 * r : 512 * r + 512]),
                writes=["x2r"],
                tag=f"x2r{r}",
            )
            for jj in range(int(os.environ.get("L1_QSUBS", "4"))):
                qs = 4 * r + jj
                skip2 = os.environ.get("L1_SKIP2", "") if qs == 1 else ""
                if "xt" not in skip2:
                    p.dma(
                        "sync",
                        lambda e, qs=qs: e.dma_start(out=xt[:, :, 0:128], in_=x2own.rearrange("(c p) t -> p c t", p=128)[:, :, 128 * qs : 128 * qs + 128]),
                        writes=["xt"],
                        tag=f"xtq2_{qs}",
                    )

                def apply_lnq2(R, MR, rdeps):
                    p.v(
                        lambda e: e.tensor_tensor(sqt[:, :, 0:128], xt[:, :, 0:128], R.unsqueeze(1).broadcast_to([128, NCH, 128]), ALU.mult),
                        reads=["xt", "psS"],
                        writes=["sqt"],
                    )
                    p.v(
                        lambda e: e.tensor_tensor(xht[:, :, 0:128], sqt[:, :, 0:128], MR.unsqueeze(1).broadcast_to([128, NCH, 128]), ALU.subtract),
                        reads=["sqt", "psS"],
                        writes=["xht"],
                    )

                if "ln" not in skip2:
                    emit_ln_g(
                        p,
                        [(xt.ap()[:, c, 0:128], "xt") for c in range(NCH)],
                        128,
                        [(sqt.ap()[:, :, 0:128], xt.ap()[:, :, 0:128], "xt", "sqt")],
                        [(sqt.ap()[:, c, 0:128], "sqt") for c in range(NCH)],
                        rows, psA, "psS", onescol_f, "onescol_f", onescol_f, "onescol_f",
                        ones1f, "ones1f", apply_lnq2,
                    )
                for om in range(0, 0 if "qp" in skip2 else 16, 2):
                    qoff = 1024 + 512 * ((om // 2) % 2)

                    def mk_q2(e, om=om, qoff=qoff):
                        last = None
                        for ci in range(NCH):
                            last = e.matmul(
                                psS[0:128, qoff : qoff + 128], wq_sb[:, ci, 64 * om : 64 * om + 128], xht[:, ci, 0:128],
                                start=(ci == 0), stop=(ci == NCH - 1),
                            )
                        return last

                    p.t(mk_q2, reads=["xht", "wq_sb"], writes=[("psS", ("q", (om // 2) % 2))], tag=f"q2{qs}_{om}")
                    p.s(
                        lambda e, om=om, qoff=qoff: e.activation(out=q_t[:, om, :], in_=psS[0:64, qoff : qoff + 128], func=AF.Identity, scale=0.125, bias=bq_c[:, om : om + 1]),
                        reads=[("psS", ("q", (om // 2) % 2)), "bq_c"],
                        writes=["q_t"],
                    )
                    p.s(
                        lambda e, om=om, qoff=qoff: e.activation(out=q_t[:, om + 1, :], in_=psS[64:128, qoff : qoff + 128], func=AF.Identity, scale=0.125, bias=bq_c[:, om + 1 : om + 2]),
                        reads=[("psS", ("q", (om // 2) % 2)), "bq_c"],
                        writes=["q_t"],
                    )
                if "ms" not in skip2:
                    p.v(lambda e: e.memset(psO[:, 0:2048], 0.0), writes=["psO"])
                NKT = 4 * qs + 4

                def mk_st(e, kt):
                    last = None
                    for g in range(4):
                        last = e.matmul(
                            psS[:, 512 * g : 512 * g + 512],
                            k_sb[:, g, 128 * kt : 128 * kt + 128],
                            q_t.ap()[:, 4 * g : 4 * g + 4, :].rearrange("p h t -> p (h t)"),
                            start=True, stop=True,
                        )
                    return last

                def emit_S(kt):
                    eo = 2048 * (kt % 2)
                    p.t(lambda e, kt=kt, f=mk_st: f(e, kt), reads=["k_sb", "q_t"], writes=["psS"], tag=f"st{qs}_{kt}")
                    p.s(
                        lambda e, kt=kt, eo=eo: e.activation(
                            out=e_sb[:, eo : eo + 2048], in_=psS[:, 0:2048], func=AF.Exp, bias=negBc[:, 0:1],
                        ),
                        reads=["psS", "negBc"],
                        writes=[("e_sb", kt % 2)],
                    )
                    if kt >= 4 * qs:
                        p.v(
                            lambda e, kt=kt, qs=qs, eo=eo: e.tensor_tensor(
                                e_sb.ap()[:, eo : eo + 2048].rearrange("a (h b) -> a h b", h=16),
                                e_sb.ap()[:, eo : eo + 2048].rearrange("a (h b) -> a h b", h=16),
                                maskw.ap()[:, kt - 4 * qs, :].unsqueeze(1).broadcast_to([128, 16, 128]),
                                ALU.mult,
                            ),
                            reads=[("e_sb", kt % 2), "maskw"],
                            writes=[("e_sb", kt % 2)],
                        )

                def emit_PV(kt):
                    eo = 2048 * (kt % 2)

                    def mk_pv(e, kt=kt, qs=qs, eo=eo):
                        last = None
                        for g in range(4):
                            last = e.matmul(
                                psO[0:65, 512 * g : 512 * g + 512],
                                v_sb.ap()[:, kt, g, :],
                                e_sb[:, eo + 512 * g : eo + 512 * g + 512],
                                start=(kt == 0), stop=(kt == 4 * qs + 3),
                            )
                        return last

                    p.t(mk_pv, reads=[("v_sb", "dat"), ("v_sb", "ones"), ("e_sb", kt % 2)], writes=["psO"], tag=f"pv{qs}_{kt}")

                emit_S(0)
                for kt in range(1, NKT):
                    emit_S(kt)
                    emit_PV(kt - 1)
                emit_PV(NKT - 1)
                if os.environ.get("L1_DBG") and r == 0 and jj == 0:
                    p.v(lambda e: e.tensor_copy(srow[:, :], psO[64:65, 0:512]), reads=["psO"], writes=["srow"])
                    p.dma(
                        "gpsimd",
                        lambda e: e.dma_start(out=dbgout[0:1, 160:176], in_=srow[0:1, 0:16]),
                        reads=["srow"],
                        writes=["dbgout"],
                    )
                for half in range(4):
                    p.v(
                        lambda e, half=half: e.tensor_copy(srow[:, :], psO[64:65, 512 * half : 512 * half + 512]),
                        reads=["psO"],
                        writes=["srow"],
                    )

                    def mk_str(e, half=half):
                        last = None
                        for h in range(4):
                            last = e.transpose(
                                psS[:, 1024 + 4 * half + h : 1025 + 4 * half + h], srow[0:1, 128 * h : 128 * h + 128], identf[0:1, 0:1]
                            )
                        return last

                    p.t(mk_str, reads=["srow", "identf"], writes=["psS"], tag=f"str{qs}_{half}")
                p.v(lambda e: e.tensor_copy(sums_tm[:, :], psS[:, 1024:1040]), reads=["psS"], writes=["sums_tm"])
                p.v(lambda e: e.reciprocal(sums_tm[:, :], sums_tm[:, :]), reads=["sums_tm"], writes=["sums_tm"])
                p.t(
                    lambda e: e.transpose(psS[0:16, 1024:1152], sums_tm[:, :], identf[:, :]),
                    reads=["sums_tm", "identf"],
                    writes=["psS"],
                )
                p.v(
                    lambda e, jj=jj: e.tensor_copy(recip_hm[:, 128 * jj : 128 * jj + 128], psS[0:16, 1024:1152]),
                    reads=["psS"],
                    writes=["recip_hm"],
                )
                if os.environ.get("L1_DBG") and r == 0 and jj == 0:
                    p.dma(
                        "gpsimd",
                        lambda e: e.dma_start(out=dbgout[0:16, 0:128], in_=recip_hm[:, 0:128]),
                        reads=["recip_hm"],
                        writes=["dbgout"],
                    )
                    p.dma(
                        "gpsimd",
                        lambda e: e.dma_start(out=dbgout[0:16, 128:144], in_=sums_tm[0:16, :]),
                        reads=["sums_tm"],
                        writes=["dbgout"],
                    )
                    p.dma(
                        "gpsimd",
                        lambda e: e.dma_start(out=dbgout[0:1, 144:160], in_=srow[0:1, 0:16]),
                        reads=["srow"],
                        writes=["dbgout"],
                    )
                p.v(
                    lambda e, jj=jj: e.tensor_copy(
                        ctxr.ap()[0:64, :, 128 * jj : 128 * jj + 128],
                        psO.ap()[0:64, 0:2048].rearrange("p (c two t) -> p c two t", two=2, t=128)[:, :, 0, :],
                    ),
                    reads=["psO"],
                    writes=["ctxr"],
                )
                p.v(
                    lambda e, jj=jj: e.tensor_copy(
                        ctxr.ap()[64:128, :, 128 * jj : 128 * jj + 128],
                        psO.ap()[0:64, 0:2048].rearrange("p (c two t) -> p c two t", two=2, t=128)[:, :, 1, :],
                    ),
                    reads=["psO"],
                    writes=["ctxr"],
                )
            for cch in range(NCH):
                p.t(
                    lambda e, cch=cch: e.matmul(psS[:, 0:512], s16_sb[:, cch, :], recip_hm[:, :], start=True, stop=True),
                    reads=["s16_sb", "recip_hm"],
                    writes=["psS"],
                    tag=f"rb{r}_{cch}",
                )
                p.v(
                    lambda e, cch=cch: e.tensor_tensor(ctxr[:, cch, :], ctxr[:, cch, :], psS[:, 0:512], ALU.mult),
                    reads=["ctxr", "psS"],
                    writes=["ctxr"],
                )
            for oj in range(NCH):
                def mk_y(e, oj=oj):
                    bank = psS if oj < 4 else psO
                    last = None
                    for ci in range(NCH):
                        last = e.matmul(
                            bank[:, 512 * (oj % 4) : 512 * (oj % 4) + 512], wo_sb[:, ci, 128 * oj : 128 * oj + 128], ctxr[:, ci, :],
                            start=(ci == 0), stop=(ci == NCH - 1),
                        )
                    return last

                p.t(mk_y, reads=["ctxr", "wo_sb"], writes=["psS" if oj < 4 else "psO"], tag=f"y{r}_{oj}")
            for oj in range(NCH):
                p.s(
                    lambda e, oj=oj: e.activation(
                        out=yr[:, oj, :], in_=(psS if oj < 4 else psO)[:, 512 * (oj % 4) : 512 * (oj % 4) + 512],
                        func=AF.Identity, bias=bo_c[:, oj : oj + 1],
                    ),
                    reads=["psS" if oj < 4 else "psO", "bo_c"],
                    writes=["ctxr"],
                )

            def apply_post_tile(R, MR, off, n):
                p.v(
                    lambda e, off=off, n=n: e.tensor_tensor(
                        sqt[:, :, 0:n], yr[:, :, off : off + n], R.unsqueeze(1).broadcast_to([128, NCH, n]), ALU.mult
                    ),
                    reads=["ctxr", "psS"],
                    writes=["sqt"],
                )
                p.v(
                    lambda e, off=off, n=n: e.tensor_tensor(
                        sqt[:, :, 0:n], sqt[:, :, 0:n], MR.unsqueeze(1).broadcast_to([128, NCH, n]), ALU.subtract
                    ),
                    reads=["sqt", "psS"],
                    writes=["sqt"],
                )
                p.v(
                    lambda e, off=off, n=n: e.tensor_tensor(
                        x2r[:, :, off : off + n], x2r[:, :, off : off + n], sqt[:, :, 0:n], ALU.add
                    ),
                    reads=["sqt", "x2r"],
                    writes=["x2r"],
                )

            for (off, n) in [(0, 128), (128, 128), (256, 128), (384, 128)]:
                emit_ln_g(
                    p,
                    [(yr.ap()[:, c, off : off + n], "ctxr") for c in range(NCH)],
                    n,
                    [(sqt.ap()[:, :, 0:128], yr.ap()[:, :, off : off + n], "ctxr", "sqt")],
                    [(sqt.ap()[:, c, 0:128], "sqt") for c in range(NCH)],
                    rows, psA, "psS", onescol_b, "onescol_b", onescol_f, "onescol_f",
                    ones1f, "ones1f",
                    lambda R, MR, rd, off=off, n=n: apply_post_tile(R, MR, off, n),
                )
            p.dma(
                "sync",
                lambda e, r=r: e.dma_start(
                    out=x3out.rearrange("(c p) t -> p c t", p=128)[:, :, 512 * r : 512 * r + 512], in_=x2r[:, :, :]
                ),
                reads=["x2r"],
                writes=[("x3out", r)],
                tag=f"x3o{r}",
            )

        p.finalize(ctx)
    return nc


def l1_in_maps(inputs, x2_full):
    s16, _ = build_selectors()
    ident = np.eye(128, dtype=np.float32)
    maps = []
    wq = np.ascontiguousarray(inputs["L1_attn_Wq"].astype(np.float32))
    wk = np.ascontiguousarray(inputs["L1_attn_Wk"].astype(np.float32))
    wv = np.ascontiguousarray(inputs["L1_attn_Wv"].astype(np.float32))
    wo = np.ascontiguousarray(inputs["L1_attn_Wo"].astype(np.float32))
    tri = (np.arange(128)[:, None] <= np.arange(128)[None, :]).astype(np.float32)
    for c in range(NCORES):
        b, j = c // 4, c % 4
        qtiles = [4 * i + j for i in range(16)]
        xo = x2_full[b].reshape(64, 128, D)
        x2own = fm(np.ascontiguousarray(xo[qtiles].reshape(TOK, D)))
        maskw = np.zeros((128, 4, 128), np.float32)
        for w in range(4):
            if w < j:
                maskw[:, w, :] = 1.0
            elif w == j:
                maskw[:, w, :] = tri
        maps.append(
            {
                "x2own": x2own,
                "x2rot": fm(np.ascontiguousarray(x2_full[b])),
                "wq_in": wq, "wk_in": wk, "wv_in": wv, "wo_in": wo,
                "bq16": inputs["L1_attn_bq"].astype(np.float32).reshape(16, 64),
                "bk4": inputs["L1_attn_bk"].astype(np.float32).reshape(4, 64),
                "bvrow_in": inputs["L1_attn_bv"].astype(np.float32).reshape(1, 256),
                "bo8": inputs["L1_attn_bo"].astype(np.float32).reshape(8, 128),
                "maskw_in": maskw,
                "s16": s16,
                "identw": ident,
            }
        )
    return maps


def run_launch_l1(inputs, x2_full):
    nc = build_launch_l1()
    maps = l1_in_maps(inputs, x2_full)
    res = run_bass_kernel_spmd(nc, maps, core_ids=list(range(NCORES)))
    out = np.zeros((2, SEQ, D), np.float32)
    for c in range(NCORES):
        b, j = c // 4, c % 4
        qtiles = [4 * i + j for i in range(16)]
        ov = out[b].reshape(64, 128, D)
        ov[qtiles] = res.results[c]["x3out"].T.reshape(16, 128, D)
    return out




def build_launch_d():
    nc = bass.Bass("TRN2", target_bir_lowering=False)
    ap = lambda t: t.ap()
    x3own = ap(nc.dram_tensor("x3own", [D, TOK], F32, kind="ExternalInput"))
    rw_in = ap(nc.dram_tensor("rw_in", [D, 8], F32, kind="ExternalInput"))
    rb_in = ap(nc.dram_tensor("rb_in", [1, 8], F32, kind="ExternalInput"))
    w1_in = ap(nc.dram_tensor("w1_in", [10, D, 512], F32, kind="ExternalInput"))
    w3_in = ap(nc.dram_tensor("w3_in", [10, D, 512], F32, kind="ExternalInput"))
    w2_in = ap(nc.dram_tensor("w2_in", [10, 512, D], F32, kind="ExternalInput"))
    identw = ap(nc.dram_tensor("identw", [128, 128], F32, kind="ExternalInput"))
    out_t = ap(nc.dram_tensor("out_t", [D, TOK], F32, kind="ExternalOutput"))

    with ExitStack() as ctx:
        sb = lambda name, shape, dt=F32: ctx.enter_context(nc.sbuf_tensor(name, shape, dt))
        pst = lambda name, shape, dt=F32: ctx.enter_context(nc.psum_tensor(name, shape, dt))
        p = P(nc)

        x3rs = sb("x3rs", [128, NCH, 512])
        xhA = [sb(f"xh{i}", [128, NCH, 512], BF16) for i in range(2)]
        yacA = [sb(f"yac{i}", [128, NCH, 512]) for i in range(2)]
        growA = [sb(f"grow{i}", [1, 4096], BF16) for i in range(2)]
        sq = sb("sq", [128, NCH, 512])
        h_sb = sb("h_sb", [128, 4, 512], BF16)
        scrab = sb("scrab", [128, 1024], BF16)
        rw_sb = sb("rw_sb", [128, NCH, 8], BF16)
        rb_c = sb("rb_c", [8, 1])
        lg_sb = sb("lg_sb", [8, 512])
        lg_tm = sb("lg_tm", [128, 8])
        mr8 = sb("mr8", [128, 8])
        repl = sb("repl", [128, 8])
        m2 = sb("m2", [128, 1])
        negm1 = sb("negm1", [128, 1])
        sel = sb("sel", [128, 8])
        evals = sb("evals", [128, 8])
        denom = sb("denom", [128, 1])
        w1t = [sb(f"w1t{i}", [128, NCH, 512], BF16) for i in range(2)]
        w3t = [sb(f"w3t{i}", [128, NCH, 512], BF16) for i in range(2)]
        w2t = [sb(f"w2t{i}", [128, 4, D], BF16) for i in range(2)]
        wstg = [sb(f"wstg{i}", [128, NCH, 512]) for i in range(2)]
        identf = sb("identf", [128, 128])
        ones1f = sb("ones1f", [1, 128])
        ones1b = sb("ones1b", [1, 128], BF16)
        onescol_f = sb("onescol_f", [128, 1])
        rows = ln_rows(nc, ctx, "lnD", width=512)
        rows["onesf_col"] = onescol_f
        rows["konesf_col"] = "onescol_f"

        ps1 = pst("ps1", [128, 1024])
        ps2 = pst("ps2", [128, 1024])
        psB = pst("psB", [128, 512])
        psA = pst("psA", [128, 1024])
        psT = pst("psT", [128, 512])

        p.v(lambda e: e.memset(ones1f[:], 1.0), writes=["ones1f"])
        p.v(lambda e: e.memset(ones1b[:], 1.0), writes=["ones1b"])
        p.v(lambda e: e.memset(onescol_f[:], 1.0), writes=["onescol_f"])
        p.dma("sync", lambda e: e.dma_start(out=identf[:], in_=identw[:, :]), writes=["identf"])
        p.dma("gpsimd", lambda e: e.dma_start(out=rw_sb[:], in_=rw_in.rearrange("(c p) m -> p c m", p=128)), writes=["rw_sb"])
        p.dma("sync", lambda e: e.dma_start(out=sq[0:1, 0, 0:8], in_=rb_in[:, :]), writes=["sq"])
        p.t(lambda e: e.transpose(psT[0:8, 0:1], sq.ap()[0:1, 0, 0:8], identf[0:1, 0:1]), reads=["sq", "identf"], writes=["psT"])
        p.v(lambda e: e.tensor_copy(rb_c[:], psT[0:8, 0:1]), reads=["psT"], writes=["rb_c"])

        for rp in range(2):
            for r2 in range(2):
                r = 2 * rp + r2
                B = 512 * r
                x3r, xh, grow = x3rs, xhA[r2], growA[r2]
                kx3, kxh, kgrow = "x3rs", f"xh{r2}", f"grow{r2}"
                p.dma(
                    "sync",
                    lambda e, B=B, x3r=x3r: e.dma_start(out=x3r[:, :, :], in_=x3own.rearrange("(c p) t -> p c t", p=128)[:, :, B : B + 512]),
                    writes=[kx3],
                    tag=f"x3r{r}",
                )

                def apply_lnD(R, MR, rdeps, x3r=x3r, xh=xh, kx3=kx3, kxh=kxh):
                    p.v(
                        lambda e: e.tensor_tensor(sq[:, :, :], x3r[:, :, :], R.unsqueeze(1).broadcast_to([128, NCH, 512]), ALU.mult),
                        reads=[kx3, "psA"],
                        writes=["sq"],
                    )
                    p.v(
                        lambda e: e.tensor_tensor(xh[:, :, :], sq[:, :, :], MR.unsqueeze(1).broadcast_to([128, NCH, 512]), ALU.subtract),
                        reads=["sq", "psA"],
                        writes=[kxh],
                    )

                emit_ln_g(
                    p,
                    [(x3r.ap()[:, c, :], kx3) for c in range(NCH)],
                    512,
                    [(sq.ap()[:, :, :], x3r.ap()[:, :, :], kx3, "sq")],
                    [(sq.ap()[:, c, :], "sq") for c in range(NCH)],
                    rows, psA, "psA", onescol_f, "onescol_f", onescol_f, "onescol_f",
                    ones1f, "ones1f", apply_lnD,
                )

                def mk_lg(e, xh=xh):
                    last = None
                    for ci in range(NCH):
                        last = e.matmul(psT[0:8, 0:512], rw_sb[:, ci, :], xh[:, ci, :], start=(ci == 0), stop=(ci == NCH - 1))
                    return last

                p.t(mk_lg, reads=[kxh, "rw_sb"], writes=["psT"], tag=f"lg{r}")
                p.s(
                    lambda e: e.activation(out=lg_sb[:, :], in_=psT[0:8, 0:512], func=AF.Identity, bias=rb_c[:, :]),
                    reads=["psT", "rb_c"],
                    writes=["lg_sb"],
                )
                for tt in range(4):
                    p.t(
                        lambda e, tt=tt: e.transpose(psT[:, 0:8], lg_sb[:, 128 * tt : 128 * tt + 128], identf[0:8, 0:8]),
                        reads=["lg_sb", "identf"],
                        writes=["psT"],
                    )
                    p.v(lambda e: e.tensor_copy(lg_tm[:, :], psT[:, 0:8]), reads=["psT"], writes=["lg_tm"])
                    p.v(lambda e: e.memset(mr8[:], 3.0e38), writes=["mr8"])
                    p.v(lambda e: e.reduce_max(mr8[:, 0:1], lg_tm[:, :], axis=AX.X), reads=["lg_tm"], writes=["mr8"])
                    p.v(lambda e: e.match_replace(repl[:, :], mr8[:, :], lg_tm[:, :], -1.0e30), reads=["mr8", "lg_tm"], writes=["repl"])
                    p.v(lambda e: e.reduce_max(m2[:, :], repl[:, :], axis=AX.X), reads=["repl"], writes=["m2"])
                    p.v(lambda e: e.tensor_scalar(sel[:, :], lg_tm[:, :], m2[:, :], None, ALU.is_ge), reads=["lg_tm", "m2"], writes=["sel"])
                    p.v(lambda e: e.reduce_max(negm1[:, :], lg_tm[:, :], axis=AX.X, negate=True), reads=["lg_tm"], writes=["negm1"])
                    p.s(
                        lambda e: e.activation(out=evals[:, :], in_=lg_tm[:, :], func=AF.Exp, bias=negm1[:, :]),
                        reads=["lg_tm", "negm1"],
                        writes=["evals"],
                    )
                    p.v(lambda e: e.tensor_mul(evals[:, :], evals[:, :], sel[:, :]), reads=["evals", "sel"], writes=["evals"])
                    p.v(lambda e: e.reduce_sum(denom[:, :], evals[:, :], axis=AX.X), reads=["evals"], writes=["denom"])
                    p.v(lambda e: e.reciprocal(denom[:, :], denom[:, :]), reads=["denom"], writes=["denom"])
                    p.v(lambda e: e.tensor_scalar(evals[:, :], evals[:, :], denom[:, :], None, ALU.mult), reads=["evals", "denom"], writes=["evals"])

                    def mk_gt4(e):
                        last = None
                        for ee in range(8):
                            last = e.transpose(psA[0:1, 128 * ee : 128 * ee + 128], evals[:, ee : ee + 1], identf[:, :])
                        return last

                    p.t(mk_gt4, reads=["evals", "identf"], writes=["psA"], tag=f"gt{r}_{tt}")
                    p.v(
                        lambda e, tt=tt, grow=grow: e.tensor_copy(
                            grow.ap()[:, :].rearrange("o (e t) -> o e t", e=8)[:, :, 128 * tt : 128 * tt + 128],
                            psA.ap()[0:1, 0:1024].rearrange("o (e t) -> o e t", e=8),
                        ),
                        reads=["psA"],
                        writes=[kgrow],
                    )
                p.v(lambda e, yac=yacA[r2]: e.memset(yac[:, :, :], 0.0), writes=[f"yac{r2}"])

            for ee in range(10):
                bb = ee % 2
                p.dma(
                    "gpsimd",
                    lambda e, ee=ee, bb=bb: e.dma_start(out=w1t[bb][:], in_=w1_in[ee].rearrange("(c p) m -> p c m", p=128)),
                    writes=[("w1t", bb)],
                    tag=f"w1l{rp}_{ee}",
                )
                p.dma(
                    "sync",
                    lambda e, ee=ee: e.dma_start(out=wstg[0][:], in_=w3_in[ee].rearrange("(c p) m -> p c m", p=128)),
                    writes=[("wstg", 0)],
                    tag=f"w3l{rp}_{ee}",
                )
                p.s(
                    lambda e, bb=bb: e.activation(out=w3t[bb][:], in_=wstg[0][:], func=AF.Copy),
                    reads=[("wstg", 0)],
                    writes=[("w3t", bb)],
                )
                p.dma(
                    "scalar",
                    lambda e, ee=ee: e.dma_start(
                        out=wstg[1].ap().rearrange("p c m -> p (c m)").rearrange("p (d n) -> p d n", d=4),
                        in_=w2_in[ee].rearrange("(c p) m -> p c m", p=128),
                    ),
                    writes=[("wstg", 1)],
                    tag=f"w2l{rp}_{ee}",
                )
                p.s(
                    lambda e, bb=bb: e.activation(
                        out=w2t[bb][:, :, :],
                        in_=wstg[1].ap().rearrange("p c m -> p (c m)").rearrange("p (d n) -> p d n", d=4),
                        func=AF.Copy,
                    ),
                    reads=[("wstg", 1)],
                    writes=[("w2t", bb)],
                )
                for r2 in range(2):
                    xh, grow, yac = xhA[r2], growA[r2], yacA[r2]
                    kxh, kgrow, kyac = f"xh{r2}", f"grow{r2}", f"yac{r2}"
                    for op2 in range(2):
                        def mk_a(e, op2=op2, bb=bb, xh=xh):
                            last = None
                            for of2 in range(2):
                                of = 2 * op2 + of2
                                for ci in range(NCH):
                                    last = e.matmul(
                                        ps1[:, 512 * of2 : 512 * of2 + 512], w1t[bb][:, ci, 128 * of : 128 * of + 128], xh[:, ci, :],
                                        start=(ci == 0), stop=(ci == NCH - 1),
                                    )
                            return last

                        def mk_b(e, op2=op2, bb=bb, xh=xh):
                            last = None
                            for of2 in range(2):
                                of = 2 * op2 + of2
                                for ci in range(NCH):
                                    last = e.matmul(
                                        ps2[:, 512 * of2 : 512 * of2 + 512], w3t[bb][:, ci, 128 * of : 128 * of + 128], xh[:, ci, :],
                                        start=(ci == 0), stop=(ci == NCH - 1),
                                    )
                            return last

                        p.t(mk_a, reads=[kxh, ("w1t", bb)], writes=["ps1"], tag=f"a{rp}_{ee}_{r2}_{op2}")
                        p.t(mk_b, reads=[kxh, ("w3t", bb)], writes=["ps2"], tag=f"b{rp}_{ee}_{r2}_{op2}")
                        p.s(
                            lambda e: e.activation(out=scrab[:, :], in_=ps1[:, 0:1024], func=AF.Silu),
                            reads=["ps1"],
                            writes=["scrab"],
                        )
                        p.v(
                            lambda e, op2=op2: e.tensor_tensor(
                                h_sb.ap()[:, 2 * op2 : 2 * op2 + 2, :],
                                scrab.ap()[:, :].rearrange("p (c t) -> p c t", c=2),
                                ps2.ap()[:, 0:1024].rearrange("p (c t) -> p c t", c=2),
                                ALU.mult,
                            ),
                            reads=["scrab", "ps2"],
                            writes=["h_sb"],
                        )
                    if ee < 8:
                        def mk_gb(e, ee=ee, grow=grow):
                            return e.matmul(psA[:, 0:512], ones1b[:, 0:128], grow.ap()[:, 512 * ee : 512 * ee + 512], start=True, stop=True)

                        p.t(mk_gb, reads=[kgrow, "ones1b"], writes=["psA"], tag=f"gb{rp}_{ee}_{r2}")
                        p.v(
                            lambda e: e.tensor_tensor(
                                h_sb.ap(), h_sb.ap(), psA.ap()[:, 0:512].unsqueeze(1).broadcast_to([128, 4, 512]), ALU.mult
                            ),
                            reads=["h_sb", "psA"],
                            writes=["h_sb"],
                        )
                    for oj in range(NCH):
                        bk = psB if oj % 2 == 0 else psT
                        bkk = "psB" if oj % 2 == 0 else "psT"

                        def mk_w2(e, oj=oj, bb=bb, bk=bk):
                            last = None
                            for ci in range(4):
                                last = e.matmul(
                                    bk[:, 0:512], w2t[bb][:, ci, 128 * oj : 128 * oj + 128], h_sb[:, ci, :],
                                    start=(ci == 0), stop=(ci == 3),
                                )
                            return last

                        p.t(mk_w2, reads=["h_sb", ("w2t", bb)], writes=[bkk], tag=f"w2{rp}_{ee}_{r2}_{oj}")
                        p.v(
                            lambda e, oj=oj, bk=bk, yac=yac: e.tensor_tensor(yac[:, oj, :], yac[:, oj, :], bk[:, 0:512], ALU.add),
                            reads=[kyac, bkk],
                            writes=[kyac],
                        )

            for r2 in range(2):
                r = 2 * rp + r2
                B = 512 * r
                x3r, yac = x3rs, yacA[r2]
                kx3, kyac = "x3rs", f"yac{r2}"

                p.dma(
                    "sync",
                    lambda e, B=B, x3r=x3r: e.dma_start(out=x3r[:, :, :], in_=x3own.rearrange("(c p) t -> p c t", p=128)[:, :, B : B + 512]),
                    writes=[kx3],
                    tag=f"x3rf{r}",
                )

                def apply_fin(R, MR, rdeps, x3r=x3r, yac=yac, kx3=kx3, kyac=kyac):
                    p.v(
                        lambda e: e.tensor_tensor(sq[:, :, :], yac[:, :, :], R.unsqueeze(1).broadcast_to([128, NCH, 512]), ALU.mult),
                        reads=[kyac, "psA"],
                        writes=["sq"],
                    )
                    p.v(
                        lambda e: e.tensor_tensor(sq[:, :, :], sq[:, :, :], MR.unsqueeze(1).broadcast_to([128, NCH, 512]), ALU.subtract),
                        reads=["sq", "psA"],
                        writes=["sq"],
                    )
                    p.v(
                        lambda e: e.tensor_tensor(x3r[:, :, :], x3r[:, :, :], sq[:, :, :], ALU.add),
                        reads=["sq", kx3],
                        writes=[kx3],
                    )

                emit_ln_g(
                    p,
                    [(yac.ap()[:, c, :], kyac) for c in range(NCH)],
                    512,
                    [(sq.ap()[:, :, :], yac.ap()[:, :, :], kyac, "sq")],
                    [(sq.ap()[:, c, :], "sq") for c in range(NCH)],
                    rows, psA, "psA", onescol_f, "onescol_f", onescol_f, "onescol_f",
                    ones1f, "ones1f", apply_fin,
                )
                p.dma(
                    "sync",
                    lambda e, B=B, x3r=x3r: e.dma_start(
                        out=out_t.rearrange("(c p) t -> p c t", p=128)[:, :, B : B + 512], in_=x3r[:, :, :]
                    ),
                    reads=[kx3],
                    writes=[("out_t", r)],
                    tag=f"o{r}",
                )

        p.finalize(ctx)
    return nc


def d_in_maps(inputs, x3_full):
    ident = np.eye(128, dtype=np.float32)
    w1 = np.concatenate([inputs["L1_ffn_routed_w1"], inputs["L1_ffn_shared_w1"]], axis=0).astype(np.float32)
    w3 = np.concatenate([inputs["L1_ffn_routed_w3"], inputs["L1_ffn_shared_w3"]], axis=0).astype(np.float32)
    w2 = np.concatenate([inputs["L1_ffn_routed_w2"], inputs["L1_ffn_shared_w2"]], axis=0).astype(np.float32)
    maps = []
    for c in range(NCORES):
        b, sown = c // 4, TOK * (c % 4)
        maps.append(
            {
                "x3own": fm(np.ascontiguousarray(x3_full[b, sown : sown + TOK])),
                "rw_in": np.ascontiguousarray(inputs["L1_ffn_router_w"].astype(np.float32)),
                "rb_in": inputs["L1_ffn_router_b"].astype(np.float32).reshape(1, 8),
                "w1_in": np.ascontiguousarray(w1),
                "w3_in": np.ascontiguousarray(w3),
                "w2_in": np.ascontiguousarray(w2),
                "identw": ident,
            }
        )
    return maps


def run_launch_d(inputs, x3_full):
    nc = build_launch_d()
    maps = d_in_maps(inputs, x3_full)
    res = run_bass_kernel_spmd(nc, maps, core_ids=list(range(NCORES)))
    out = np.zeros((2, SEQ, D), np.float32)
    for c in range(NCORES):
        b, sown = c // 4, TOK * (c % 4)
        out[b, sown : sown + TOK] = res.results[c]["out_t"].T
    return out


def build_xpos_tables(positions):
    half = np.arange(0, 64, 2, dtype=np.float64)
    inv_freq = 1.0 / 10000.0 ** (half / 64.0)
    t = positions.astype(np.float64)
    freqs = t[:, None] * inv_freq[None, :]
    freqs = np.concatenate([freqs, freqs], axis=1)
    base = (half + 0.4 * 64) / (1.4 * 64)
    power = (t - 8192 // 2) / 256.0
    scale = base[None, :] ** power[:, None]
    scale = np.concatenate([scale, scale], axis=1)
    cos, sin = np.cos(freqs), np.sin(freqs)
    return (
        np.ascontiguousarray((cos * scale).T.astype(np.float32)),
        np.ascontiguousarray((sin * scale).T.astype(np.float32)),
        np.ascontiguousarray((cos / scale).T.astype(np.float32)),
        np.ascontiguousarray((sin / scale).T.astype(np.float32)),
    )


def build_rot64():
    r = np.zeros((64, 64), np.float32)
    for m in range(32):
        r[m + 32, m] = -1.0
    for m in range(32, 64):
        r[m - 32, m] = 1.0
    return r


def build_tris():
    qq = np.arange(128)[:, None]
    kk = np.arange(128)[None, :]
    triL = np.where(kk < qq, -1.0e30, 0.0).astype(np.float32)
    triD = np.where(kk > qq, -1.0e30, 0.0).astype(np.float32)
    return np.stack([triL, triD])


def fa_in_maps(inputs, stream_full, l):
    wqkv = np.ascontiguousarray(inputs["L0_ffn_wqkv"][l].astype(np.float32))
    wo = np.ascontiguousarray(inputs["L0_ffn_wo"][l].astype(np.float32))
    qs = inputs["L0_ffn_q_scale"][l].astype(np.float32).reshape(1, 64)
    ks = inputs["L0_ffn_k_scale"][l].astype(np.float32).reshape(1, 64)
    rot = build_rot64()
    tris = build_tris()
    ident = np.eye(128, dtype=np.float32)
    maps = []
    for c in range(NCORES):
        b, sown = c // 4, TOK * (c % 4)
        lo = sown - FA_EXT
        fxe = np.zeros((FA_EXT + TOK, D), np.float32)
        a0 = max(lo, 0)
        fxe[a0 - lo :] = stream_full[b, a0 : sown + TOK]
        pos_own = np.arange(sown, sown + TOK)
        pos_ext = np.arange(lo, sown + TOK)
        tqc, tqs_, _, _ = build_xpos_tables(pos_own)
        _, _, tkc, tks_ = build_xpos_tables(pos_ext)
        dead = np.where(pos_ext < 0, -1.0e30, 0.0).astype(np.float32).reshape(1, -1)
        maps.append(
            {
                "fxe": fm(fxe), "wqkv": wqkv, "wo_in": wo, "qs_row": qs, "ks_row": ks,
                "tqc_in": tqc, "tqs_in": tqs_, "tkc_in": tkc, "tks_in": tks_,
                "dead_in": dead, "rot_in": rot, "tri_in": tris, "identw": ident,
            }
        )
    return maps


def fb_in_maps(inputs, xmid_full, l, x1_full=None):
    w1 = np.ascontiguousarray(inputs["L0_ffn_w1"][l].astype(np.float32))
    w2 = np.ascontiguousarray(np.pad(inputs["L0_ffn_w2"][l].astype(np.float32), ((0, 86), (0, 0))))
    b1 = inputs["L0_ffn_b1"][l].astype(np.float32)
    b2 = inputs["L0_ffn_b2"][l].astype(np.float32)
    lng = inputs["L0_ffn_ln_g"][l].astype(np.float32).reshape(1, D)
    lnb = inputs["L0_ffn_ln_b"][l].astype(np.float32).reshape(1, D)
    ident = np.eye(128, dtype=np.float32)
    b1a = np.pad(b1[0:2730], (0, 86)).reshape(22, 128)
    b1g = np.pad(b1[2730:5460], (0, 86)).reshape(22, 128)
    maps = []
    for c in range(NCORES):
        b, sown = c // 4, TOK * (c % 4)
        xme = np.zeros((TOK + 1, D), np.float32)
        a0 = max(sown - 1, 0)
        xme[a0 - (sown - 1) :] = xmid_full[b, a0 : sown + TOK]
        m = {
            "xme": fm(xme), "w1_in": w1, "w2_in": w2, "b1a_in": b1a, "b1g_in": b1g,
            "b2_in": b2.reshape(8, 128), "lng_in": lng, "lnb_in": lnb, "identw": ident,
            "neg1_in": np.full((1, 544), -1.0, np.float32),
        }
        if x1_full is not None:
            m["x1own"] = fm(np.ascontiguousarray(x1_full[b, sown : sown + TOK]))
        maps.append(m)
    return maps


def run_launch_fa(inputs, stream_full, l):
    nc = build_launch_fa(l == 0)
    maps = fa_in_maps(inputs, stream_full, l)
    res = run_bass_kernel_spmd(nc, maps, core_ids=list(range(NCORES)))
    out = np.zeros((2, SEQ, D), np.float32)
    for c in range(NCORES):
        b, sown = c // 4, TOK * (c % 4)
        out[b, sown : sown + TOK] = res.results[c]["xmid_out"].T
    return out


def run_launch_fb(inputs, xmid_full, l, x1_full=None):
    nc = build_launch_fb(x1_full is not None)
    maps = fb_in_maps(inputs, xmid_full, l, x1_full)
    res = run_bass_kernel_spmd(nc, maps, core_ids=list(range(NCORES)))
    out = np.zeros((2, SEQ, D), np.float32)
    for c in range(NCORES):
        b, sown = c // 4, TOK * (c % 4)
        out[b, sown : sown + TOK] = res.results[c]["out_t"].T
    return out


def kernel(**inputs):
    _ = np.asarray(inputs["L0_attn_token_ids"])
    x0 = np.asarray(inputs["L0_attn_input_embeddings"], np.float32)
    x1 = run_launch_a(inputs, x0)
    xm0 = run_launch_fa(inputs, x1, 0)
    xn0 = run_launch_fb(inputs, xm0, 0)
    xm1 = run_launch_fa(inputs, xn0, 1)
    x2 = run_launch_fb(inputs, xm1, 1, x1_full=x1)
    x3 = run_launch_l1(inputs, x2)
    out = run_launch_d(inputs, x3)
    return out


# version 12
# speedup vs baseline: 1.0052x; 1.0052x over previous
import os
import numpy as np
import ml_dtypes
from contextlib import ExitStack

KA_STAGE = int(os.environ.get("KA_STAGE", "99"))
KA_SUB = int(os.environ.get("KA_SUB", "99"))
KA_TRUNC = int(os.environ.get("KA_TRUNC", "0"))
KA_SBN = int(os.environ.get("KA_SBN", "16"))

import concourse.bass as bass
import concourse.mybir as mybir
from concourse.bass_utils import run_bass_kernel_spmd

F32 = mybir.dt.float32
BF16 = mybir.dt.bfloat16
AF = mybir.ActivationFunctionType
ALU = mybir.AluOpType
AX = mybir.AxisListType

NCORES = 8
D = 1024
NCH = 8
TOK = 2048
SEQ = 8192
BF = ml_dtypes.bfloat16

ENGS = ("tensor", "vector", "scalar", "gpsimd", "sync")


class P:

    RING = 4

    def __init__(self, nc):
        self.nc = nc
        self.ev = {e: [] for e in ENGS}
        self.cnt = {e: 0 for e in ENGS}
        self.dmacnt = {e: 0 for e in ENGS}
        self.semnames = list(ENGS) + [f"{e}D{i}" for e in ("sync", "gpsimd", "scalar", "vector", "tensor") for i in range(self.RING)]
        self.semtot = {n: 0 for n in self.semnames}
        self.lastw = {}
        self.readers = {}

    def _dep(self, need, tok):
        sem, val = tok
        if val > need.get(sem, 0):
            need[sem] = val

    def op(self, eng, emit, reads=(), writes=(), inc=1, tag="", dma=False):
        ag = getattr(self, "alias_groups", None)
        if ag:
            def _ex(keys):
                out = []
                for k in keys:
                    out.extend(ag.get(k, [k]))
                return out
            reads = _ex(list(reads))
            writes = _ex(list(writes))
        etag0 = ("dma", eng, self.dmacnt[eng]) if dma else eng
        need = {}
        for b in reads:
            for (sem, val, weng) in self.lastw.get(b, ()):
                if weng == etag0 == "tensor":
                    continue
                if weng != etag0:
                    self._dep(need, (sem, val))
                elif etag0 != "tensor":
                    self._dep(need, (sem, val))
        for b in writes:
            for (sem, val, weng) in self.lastw.get(b, ()):
                if weng == etag0 == "tensor":
                    continue
                self._dep(need, (sem, val))
            for (sem, val, reng) in self.readers.get(b, ()):
                if reng == etag0 == "tensor":
                    continue
                self._dep(need, (sem, val))
        if dma:
            k = self.dmacnt[eng]
            self.dmacnt[eng] = k + 1
            sem = f"{eng}D{k % self.RING}"
            occ = k // self.RING
            if occ > 0:
                self._dep(need, (sem, 16 * occ))
            myval = 16 * (occ + 1)
            etag = ("dma", eng, k)
        else:
            sem = eng
            self.cnt[eng] += inc
            myval = self.cnt[eng]
            etag = eng
        self.semtot[sem] = max(self.semtot[sem], myval)
        self.seq = getattr(self, "seq", 0) + 1
        self.ev[eng].append((emit, need, sem, 16 if dma else inc, tag, self.seq))
        tok = (sem, myval, etag)
        for b in reads:
            self.readers.setdefault(b, []).append(tok)
        for b in writes:
            self.lastw.setdefault(b, []).append(tok)
            self.readers[b] = []
        return tok

    def t(self, emit, **kw):
        return self.op("tensor", emit, **kw)

    def v(self, emit, **kw):
        return self.op("vector", emit, **kw)

    def s(self, emit, **kw):
        return self.op("scalar", emit, **kw)

    def g(self, emit, **kw):
        return self.op("gpsimd", emit, **kw)

    def dma(self, eng, emit, reads=(), writes=(), tag=""):
        return self.op(eng, emit, reads=reads, writes=writes, tag=tag, dma=True)

    def finalize(self, ctx):
        if os.environ.get("KA_PRINT"):
            print("TOTAL_EVENTS:", getattr(self, "seq", 0))
        if KA_TRUNC > 0:
            for e in ENGS:
                self.ev[e] = [ev for ev in self.ev[e] if ev[5] <= KA_TRUNC]
            self.semtot = {n: 0 for n in self.semnames}
            for e in ENGS:
                for ev in self.ev[e]:
                    sem, inc = ev[2], ev[3]
                    if sem == e:
                        self.semtot[sem] = max(self.semtot[sem], ev[1].get(sem, 0))
            self.semtot = {n: 0 for n in self.semnames}
            for e in ENGS:
                cnt = {}
                for ev in self.ev[e]:
                    sem = ev[2]
                    cnt[sem] = cnt.get(sem, 0) + ev[3]
                for snm, v in cnt.items():
                    self.semtot[snm] = self.semtot.get(snm, 0) + v
        nc = self.nc
        sems = {n: ctx.enter_context(nc.semaphore(n)) for n in self.semnames}
        block = ctx.enter_context(nc.Block())
        totals = dict(self.semtot)
        evs = self.ev

        def runner(ename, engobj):
            done = {n: 0 for n in self.semnames}
            for (emit, need, incsem, inc, tag, _seq) in evs[ename]:
                for n, c2 in need.items():
                    if c2 > done[n]:
                        engobj.wait_ge(sems[n], c2)
                        done[n] = c2
                inst = emit(engobj)
                inst.then_inc(sems[incsem], inc)
            if ename == "sync":
                for n in self.semnames:
                    if totals[n] > done.get(n, 0):
                        engobj.wait_ge(sems[n], totals[n])

        @block.tensor
        def _(e):
            runner("tensor", e)

        @block.vector
        def _(e):
            runner("vector", e)

        @block.scalar
        def _(e):
            runner("scalar", e)

        @block.gpsimd
        def _(e):
            runner("gpsimd", e)

        @block.sync
        def _(e):
            runner("sync", e)


def ln_rows(nc, ctx, prefix, width=544):
    sb = lambda name, shape: ctx.enter_context(nc.sbuf_tensor(name, shape, F32))
    t = {}
    for nm in ("mu", "rstd", "murs", "var"):
        t[nm] = sb(f"{prefix}_{nm}", [1, width])
        t["k" + nm] = f"{prefix}_{nm}"
    t["eps"] = sb(f"{prefix}_eps", [1, 1])
    t["keps"] = f"{prefix}_eps"
    return t


def emit_ln(p, x_ap, xkey, n_tok, sq_ap, sqkey, rows, psA, psAkey, onescol, onescolkey, ones1f, ones1fkey, out_emit, d_model=1024, eps=1e-5):
    nch = x_ap.shape[1]

    def mk_sum(e):
        last = None
        for c in range(nch):
            last = e.matmul(psA[0:1, 0:n_tok], onescol[:, :], x_ap[:, c, :], start=(c == 0), stop=(c == nch - 1))
        return last

    p.t(mk_sum, reads=[xkey, onescolkey], writes=[psAkey])
    p.s(lambda e: e.activation(out=sq_ap, in_=x_ap, func=AF.Square), reads=[xkey], writes=[sqkey])

    def mk_sum2(e):
        last = None
        for c in range(nch):
            last = e.matmul(
                psA[0:1, 512 : 512 + n_tok], onescol_f_of(rows), sq_ap[:, c, :], start=(c == 0), stop=(c == nch - 1)
            )
        return last

    p.t(mk_sum2, reads=[sqkey, rows["konesf_col"]], writes=[psAkey])
    mu, rstd, murs, var = rows["mu"], rows["rstd"], rows["murs"], rows["var"]
    kmu, krstd, kmurs, kvar = rows["kmu"], rows["krstd"], rows["kmurs"], rows["kvar"]
    sc = 1.0 / d_model
    p.s(lambda e: e.activation(out=mu[:, 0:n_tok], in_=psA[0:1, 0:n_tok], func=AF.Copy, scale=sc), reads=[psAkey], writes=[kmu])
    p.s(lambda e: e.activation(out=var[:, 0:n_tok], in_=mu[:, 0:n_tok], func=AF.Square), reads=[kmu], writes=[kvar])
    p.v(
        lambda e: e.scalar_tensor_tensor(var[:, 0:n_tok], psA[0:1, 512 : 512 + n_tok], sc, var[:, 0:n_tok], ALU.mult, ALU.subtract),
        reads=[psAkey, kvar],
        writes=[kvar],
    )
    p.v(lambda e: e.memset(rows["eps"][:], float(eps)), writes=[rows["keps"]])
    p.s(lambda e: e.activation(out=rstd[:, 0:n_tok], in_=var[:, 0:n_tok], func=AF.Sqrt, bias=rows["eps"][:, :]), reads=[kvar, rows["keps"]], writes=[krstd])
    p.v(lambda e: e.reciprocal(rstd[:, 0:n_tok], rstd[:, 0:n_tok]), reads=[krstd], writes=[krstd])
    p.v(lambda e: e.tensor_mul(murs[:, 0:n_tok], mu[:, 0:n_tok], rstd[:, 0:n_tok]), reads=[kmu, krstd], writes=[kmurs])
    p.t(
        lambda e: e.matmul(psA[:, 0:n_tok], ones1f[:, 0:128], rstd[:, 0:n_tok], start=True, stop=True),
        reads=[krstd, ones1fkey],
        writes=[psAkey],
    )
    p.t(
        lambda e: e.matmul(psA[:, 512 : 512 + n_tok], ones1f[:, 0:128], murs[:, 0:n_tok], start=True, stop=True),
        reads=[kmurs, ones1fkey],
        writes=[psAkey],
    )
    out_emit(psA[:, 0:n_tok], psA[:, 512 : 512 + n_tok], [psAkey, psAkey])


def onescol_f_of(rows):
    return rows["onesf_col"][:, :]


KEXT = 2176
NREG = 4
REG = 512


def build_launch_a():
    nc = bass.Bass("TRN2", target_bir_lowering=False)
    ap = lambda t: t.ap()
    x0e = ap(nc.dram_tensor("x0e", [D, KEXT], F32, kind="ExternalInput"))
    wqkvT = ap(nc.dram_tensor("wqkvT", [D, 3 * D], F32, kind="ExternalInput"))
    woutT = ap(nc.dram_tensor("woutT", [D, D], F32, kind="ExternalInput"))
    bqkv24 = ap(nc.dram_tensor("bqkv24", [24, 128], F32, kind="ExternalInput"))
    bqkv_row = ap(nc.dram_tensor("bqkv_row", [1, 3 * D], F32, kind="ExternalInput"))
    bout8 = ap(nc.dram_tensor("bout8", [8, 128], F32, kind="ExternalInput"))
    maskA = ap(nc.dram_tensor("maskA", [16, 128, 128], F32, kind="ExternalInput"))
    maskB = ap(nc.dram_tensor("maskB", [16, 16, 16], F32, kind="ExternalInput"))
    s16 = ap(nc.dram_tensor("s16", [8, 16, 128], F32, kind="ExternalInput"))
    hs16 = ap(nc.dram_tensor("hs16", [8, 128, 16], F32, kind="ExternalInput"))
    identw = ap(nc.dram_tensor("identw", [128, 128], F32, kind="ExternalInput"))
    x1out = ap(nc.dram_tensor("x1out", [D, TOK], F32, kind="ExternalOutput"))

    with ExitStack() as ctx:
        sb = lambda name, shape, dt=F32: ctx.enter_context(nc.sbuf_tensor(name, shape, dt))
        pst = lambda name, shape: ctx.enter_context(nc.psum_tensor(name, shape, F32))
        p = P(nc)

        x0r = sb("x0r", [128, NCH, 544])
        sq = sb("sq", [128, NCH, 544])
        xh = sb("xh", [128, NCH, 544], BF16)
        wq = sb("wq", [128, NCH, 3 * D], BF16)
        wo = sb("wo", [128, NCH, D], BF16)
        bq64 = sb("bq64", [64, 48])
        bout_c = sb("bout_c", [128, 8])
        b48 = sb("b48", [48, 64])
        b8 = sb("b8", [8, 128])
        bvrow = sb("bvrow", [1, D], BF16)
        identf = sb("identf", [128, 128])
        maskA_sb = sb("maskA_sb", [128, 16, 128], BF16)
        maskB_sb = sb("maskB_sb", [16, 16, 16])
        s16_sb = sb("s16_sb", [16, NCH, 128])

        qr = sb("qr", [64, 16, REG], BF16)
        kr = sb("kr", [64, 16, 528], BF16)
        scr = sb("scr", [64, 4, 528])
        vr = sb("vr", [128, 5, 16, 64], BF16)
        eA = sb("eA", [128, 2048], BF16)
        eB = sb("eB", [16, 256], BF16)
        ctxr = sb("ctxr", [128, NCH, REG], BF16)

        yr = sb("yr", [128, NCH, REG], BF16)
        rowmax = sb("rowmax", [128, 1])
        nB = sb("nB", [1, 4])
        negB = sb("negB", [1, 1], BF16)
        negBc = sb("negBc", [128, 1])
        sums_tm = sb("sums_tm", [128, 16])
        bsums_tm = sb("bsums_tm", [16, 16])
        shm = sb("shm", [16, 128])
        recip_hm = sb("recip_hm", [16, REG])
        ones1f = sb("ones1f", [1, 128])
        ones1b = sb("ones1b", [1, 128], BF16)
        onescol_f = sb("onescol_f", [128, 1])
        onescol_b = sb("onescol_b", [128, 1], BF16)
        rows = ln_rows(nc, ctx, "lnA")
        rows2 = rows
        for rr in (rows,):
            rr["onesf_col"] = onescol_f
            rr["konesf_col"] = "onescol_f"

        psO = pst("psO", [128, 2048])
        psA = pst("psA", [128, 1024])
        ps1 = pst("ps1", [128, 512])
        ps2 = pst("ps2", [128, 512])

        p.v(lambda e: e.memset(ones1f[:], 1.0), writes=["ones1f"])
        p.v(lambda e: e.memset(ones1b[:], 1.0), writes=["ones1b"])
        p.v(lambda e: e.memset(onescol_f[:], 1.0), writes=["onescol_f"])
        p.v(lambda e: e.memset(onescol_b[:], 1.0), writes=["onescol_b"])

        p.dma("sync", lambda e: e.dma_start(out=b48[:], in_=bqkv24.rearrange("a (b c) -> (a b) c", b=2)), writes=["b48"])
        p.dma("sync", lambda e: e.dma_start(out=b8[:], in_=bout8[:, :]), writes=["b8"])
        p.dma("gpsimd", lambda e: e.dma_start(out=bvrow[:], in_=bqkv_row[:, 2048:3072]), writes=["bvrow"])
        p.dma("sync", lambda e: e.dma_start(out=identf[:], in_=identw[:, :]), writes=["identf"])
        p.dma("gpsimd", lambda e: e.dma_start(out=maskA_sb[:], in_=maskA.rearrange("j p t -> p j t")), writes=["maskA_sb"])
        p.dma("sync", lambda e: e.dma_start(out=maskB_sb[:], in_=maskB.rearrange("j a b -> a j b")), writes=["maskB_sb"])
        p.dma("sync", lambda e: e.dma_start(out=s16_sb[:], in_=s16.rearrange("c h t -> h c t")), writes=["s16_sb"])

        for cc in range(NCH):
            p.dma(
                "gpsimd",
                lambda e, cc=cc: e.dma_start(out=wq[:, cc, :], in_=wqkvT[128 * cc : 128 * cc + 128, :]),
                writes=["wq"],
                tag=f"wq{cc}",
            )
            p.dma(
                "gpsimd",
                lambda e, cc=cc: e.dma_start(out=wo[:, cc, :], in_=woutT[128 * cc : 128 * cc + 128, :]),
                writes=["wo"],
                tag=f"wo{cc}",
            )
        p.t(lambda e: e.transpose(ps1[0:64, 0:48], b48[:, :], identf[0:48, 0:48]), reads=["b48", "identf"], writes=["ps1"])
        p.v(lambda e: e.tensor_copy(bq64[:], ps1[0:64, 0:48]), reads=["ps1"], writes=["bq64"])
        p.v(lambda e: e.tensor_scalar_mul(bq64[:, 0:16], bq64[:, 0:16], 0.125), reads=["bq64"], writes=["bq64"])
        p.t(lambda e: e.transpose(ps1[:, 0:8], b8[:, :], identf[0:8, 0:8]), reads=["b8", "identf"], writes=["ps1"])
        p.v(lambda e: e.tensor_copy(bout_c[:], ps1[:, 0:8]), reads=["ps1"], writes=["bout_c"])

        for r in range(NREG if KA_STAGE >= 8 else 1):
            r0 = r * REG

            p.dma(
                "sync",
                lambda e, r0=r0: e.dma_start(
                    out=x0r[:, :, :], in_=x0e.rearrange("(c p) t -> p c t", p=128)[:, :, r0 : r0 + 544]
                ),
                writes=["x0r"],
                tag=f"x0r{r}",
            )

            if KA_STAGE < 2:
                continue

            def apply_ln(R, MR, rdeps, off, n):
                p.v(
                    lambda e: e.tensor_tensor(
                        sq[:, :, off : off + n], x0r[:, :, off : off + n],
                        R.unsqueeze(1).broadcast_to([128, NCH, n]), ALU.mult,
                    ),
                    reads=["x0r"] + rdeps[:1],
                    writes=["sq"],
                )
                p.v(
                    lambda e: e.tensor_tensor(
                        xh[:, :, off : off + n], sq[:, :, off : off + n],
                        MR.unsqueeze(1).broadcast_to([128, NCH, n]), ALU.subtract,
                    ),
                    reads=["sq"] + rdeps[1:],
                    writes=["xh"],
                )

            for (off, n) in [(0, 512), (512, 32)]:
                emit_ln(
                    p, x0r.ap()[:, :, off : off + n], "x0r", n, sq.ap()[:, :, off : off + n], "sq",
                    rows, psA, "psA", onescol_f, "onescol_f", ones1f, "ones1f",
                    lambda R, MR, rd, off=off, n=n: apply_ln(R, MR, rd, off, n),
                )

            for om in range(0, 16 if KA_STAGE >= 3 else 0, 2):
                bk = ps1 if (om // 2) % 2 == 0 else ps2
                bkk = "ps1" if (om // 2) % 2 == 0 else "ps2"

                def mk_q(e, om=om, bk=bk):
                    last = None
                    for ci in range(NCH):
                        last = e.matmul(
                            bk[0:128, 0:512], wq[:, ci, 64 * om : 64 * om + 128], xh[:, ci, 8:520],
                            start=(ci == 0), stop=(ci == NCH - 1),
                        )
                    return last

                p.t(mk_q, reads=["xh", "wq"], writes=[bkk], tag=f"q{r}_{om}")
                p.s(
                    lambda e, om=om, bk=bk: e.activation(out=qr[:, om, :], in_=bk[0:64, 0:512], func=AF.Identity, scale=0.125, bias=bq64[:, om : om + 1]),
                    reads=[bkk, "bq64"],
                    writes=["qr"],
                )
                p.s(
                    lambda e, om=om, bk=bk: e.activation(out=qr[:, om + 1, :], in_=bk[64:128, 0:512], func=AF.Identity, scale=0.125, bias=bq64[:, om + 1 : om + 2]),
                    reads=[bkk, "bq64"],
                    writes=["qr"],
                )
            for om in range(0, 16 if KA_STAGE >= 3 else 0, 2):
                for ti, (toff, tn) in enumerate([(0, 512), (512, 16)]):
                    bk = ps1 if (om // 2 + ti) % 2 == 0 else ps2
                    bkk = "ps1" if (om // 2 + ti) % 2 == 0 else "ps2"

                    def mk_k(e, om=om, toff=toff, tn=tn, bk=bk):
                        last = None
                        for ci in range(NCH):
                            last = e.matmul(
                                bk[0:128, 0:tn], wq[:, ci, 1024 + 64 * om : 1024 + 64 * om + 128], xh[:, ci, toff : toff + tn],
                                start=(ci == 0), stop=(ci == NCH - 1),
                            )
                        return last

                    p.t(mk_k, reads=["xh", "wq"], writes=[bkk], tag=f"k{r}_{om}_{toff}")
                    p.s(
                        lambda e, om=om, toff=toff, tn=tn, bk=bk: e.activation(
                            out=kr[:, om, toff : toff + tn], in_=bk[0:64, 0:tn], func=AF.Identity, bias=bq64[:, 16 + om : 17 + om]
                        ),
                        reads=[bkk, "bq64"],
                        writes=["kr"],
                    )
                    p.s(
                        lambda e, om=om, toff=toff, tn=tn, bk=bk: e.activation(
                            out=kr[:, om + 1, toff : toff + tn], in_=bk[64:128, 0:tn], func=AF.Identity, bias=bq64[:, 17 + om : 18 + om]
                        ),
                        reads=[bkk, "bq64"],
                        writes=["kr"],
                    )
            for vt in range(5 if KA_STAGE >= 4 else 0):
                tn = 128 if vt < 4 else 16
                toff = vt * 128
                for ofj in range(2):
                    bk = ps1 if (2 * vt + ofj) % 2 == 0 else ps2
                    bkk = "ps1" if (2 * vt + ofj) % 2 == 0 else "ps2"

                    def mk_v(e, vt=vt, tn=tn, toff=toff, ofj=ofj, bk=bk):
                        for ci in range(NCH):
                            e.matmul(
                                bk[0:tn, 0:512], xh[:, ci, toff : toff + tn],
                                wq[:, ci, 2048 + 512 * ofj : 2048 + 512 * ofj + 512],
                                start=(ci == 0), stop=False,
                            )
                        return e.matmul(
                            bk[0:tn, 0:512], ones1b[:, 0:tn], bvrow[:, 512 * ofj : 512 * ofj + 512],
                            start=False, stop=True,
                        )

                    p.t(mk_v, reads=["xh", "wq", "ones1b", "bvrow"], writes=[bkk], tag=f"v{r}_{vt}_{ofj}")
                    p.v(
                        lambda e, vt=vt, tn=tn, ofj=ofj, bk=bk: e.tensor_copy(
                            vr.ap()[0:tn, vt, 8 * ofj : 8 * ofj + 8, :],
                            bk.ap()[0:tn, 0:512].rearrange("t (h d) -> t h d", h=8),
                        ),
                        reads=[bkk],
                        writes=["vr"],
                    )

            def emit_norm2(src, srckey, ncols, ntiles, into_col):
                for hg in range(4):
                    p.s(
                        lambda e, hg=hg: e.activation(
                            out=scr[:, :, 0:ncols],
                            in_=src[0:64, 4 * hg : 4 * hg + 4, 0:ncols],
                            func=AF.Square,
                        ),
                        reads=[srckey],
                        writes=["scr"],
                    )

                    def mk_n2(e, hg=hg):
                        last = None
                        for hh in range(4):
                            for tt in range(ntiles):
                                tn = 128 if tt < 4 else 16
                                cc = 20 * (4 * hg + hh) + tt
                                last = e.matmul(
                                    ps2[0:tn, cc : cc + 1],
                                    scr[:, hh, 128 * tt : 128 * tt + tn],
                                    onescol_f[0:64, :],
                                    start=True, stop=True,
                                )
                        return last

                    p.t(mk_n2, reads=["scr", "onescol_f"], writes=["ps2"], tag=f"n2_{r}_{hg}_{into_col}")
                p.v(lambda e: e.reduce_max(rowmax[:], ps2[:, 0:320], axis=AX.X), reads=["ps2"], writes=["rowmax"])
                p.t(
                    lambda e: e.transpose(ps2[0:1, 0:128], rowmax[:, :], identf[:, :]),
                    reads=["rowmax", "identf"],
                    writes=["ps2"],
                )
                p.v(
                    lambda e, into_col=into_col: e.reduce_max(nB[:, into_col : into_col + 1], ps2[0:1, 0:128], axis=AX.X),
                    reads=["ps2"],
                    writes=["nB"],
                )

            if KA_STAGE < 5:
                continue
            p.v(lambda e: e.memset(ps2[:, 0:320], 0.0), writes=["ps2"])
            emit_norm2(qr.ap(), "qr", 512, 4, 0)
            p.v(lambda e: e.memset(ps2[:, 0:320], 0.0), writes=["ps2"])
            emit_norm2(kr.ap(), "kr", 528, 5, 1)
            p.s(lambda e: e.activation(out=nB[:, 0:2], in_=nB[:, 0:2], func=AF.Sqrt), reads=["nB"], writes=["nB"])
            p.v(lambda e: e.tensor_mul(nB[:, 2:3], nB[:, 0:1], nB[:, 1:2]), reads=["nB"], writes=["nB"])
            p.s(
                lambda e: e.activation(out=negB[:, :], in_=nB[:, 2:3], func=AF.Copy, scale=-1.0, bias=-1.0),
                reads=["nB"],
                writes=["negB"],
            )
            p.t(
                lambda e: e.matmul(ps2[:, 0:1], ones1b[:, 0:128], negB[:, :], start=True, stop=True),
                reads=["negB", "ones1b"],
                writes=["ps2"],
            )
            p.v(lambda e: e.tensor_copy(negBc[:], ps2[:, 0:1]), reads=["ps2"], writes=["negBc"])

            for j in range(4 if KA_STAGE >= 6 else 0):
                jq = r * 4 + j
                q0 = 128 * j
                k0 = 128 * j

                def mk_sb(e, q0=q0, k0=k0):
                    last = None
                    for h in range(16):
                        last = e.matmul(
                            ps2[0:16, 16 * h : 16 * h + 16],
                            kr[:, h, k0 + 128 : k0 + 144],
                            qr[:, h, q0 + 112 : q0 + 128],
                            start=True, stop=True,
                        )
                    return last

                p.t(mk_sb, reads=["kr", "qr"], writes=["ps2"], tag=f"sB{r}_{j}")
                if KA_SUB < 2:
                    continue
                p.s(
                    lambda e: e.activation(out=eB[:, :], in_=ps2[0:16, 0:256], func=AF.Exp, bias=negBc[0:16, :]),
                    reads=["ps2", "negBc"],
                    writes=["eB"],
                )
                p.v(
                    lambda e, jq=jq: e.tensor_tensor(
                        eB.ap().rearrange("a (h b) -> a h b", h=16),
                        eB.ap().rearrange("a (h b) -> a h b", h=16),
                        maskB_sb.ap()[:, jq, :].unsqueeze(1).broadcast_to([16, 16, 16]),
                        ALU.mult,
                    ),
                    reads=["eB", "maskB_sb"],
                    writes=["eB"],
                )
                if KA_SUB < 3:
                    continue
                for gg in range(2):
                    def mk_sa(e, gg=gg, q0=q0, k0=k0):
                        last = None
                        for hh in range(8):
                            h = 8 * gg + hh
                            last = e.matmul(
                                psA[:, 128 * hh : 128 * hh + 128],
                                kr[:, h, k0 : k0 + 128],
                                qr[:, h, q0 : q0 + 128],
                                start=True, stop=True,
                            )
                        return last

                    p.t(mk_sa, reads=["kr", "qr"], writes=["psA"], tag=f"sA{r}_{j}_{gg}")
                    p.s(
                        lambda e, gg=gg: e.activation(
                            out=eA[:, 1024 * gg : 1024 * gg + 1024], in_=psA[:, 0:1024], func=AF.Exp, bias=negBc[:, :]
                        ),
                        reads=["psA", "negBc"],
                        writes=["eA"],
                    )
                p.v(
                    lambda e, jq=jq: e.tensor_tensor(
                        eA.ap().rearrange("a (h b) -> a h b", h=16),
                        eA.ap().rearrange("a (h b) -> a h b", h=16),
                        maskA_sb.ap()[:, jq, :].unsqueeze(1).broadcast_to([128, 16, 128]),
                        ALU.mult,
                    ),
                    reads=["eA", "maskA_sb"],
                    writes=["eA"],
                )

                if KA_SUB < 4:
                    continue

                def mk_pv(e, j=j):
                    last = None
                    for h in range(16):
                        e.matmul(
                            psO[0:64, 128 * h : 128 * h + 128], vr.ap()[:, j, h, :], eA[:, 128 * h : 128 * h + 128],
                            start=True, stop=False,
                        )
                        last = e.matmul(
                            psO[0:64, 128 * h + 112 : 128 * h + 128], vr.ap()[0:16, j + 1, h, :], eB[:, 16 * h : 16 * h + 16],
                            start=False, stop=True,
                        )
                    return last

                p.t(mk_pv, reads=["vr", "eA", "eB"], writes=["psO"], tag=f"pv{r}_{j}")

                if KA_SUB < 5:
                    continue

                def mk_sums(e):
                    last = None
                    for h in range(16):
                        last = e.matmul(
                            ps2[:, 16 + h : 17 + h], eA[:, 128 * h : 128 * h + 128], onescol_b[:, :], start=True, stop=True
                        )
                    for h in range(16):
                        last = e.matmul(
                            ps2[0:16, 32 + h : 33 + h], eB[:, 16 * h : 16 * h + 16], onescol_b[0:16, :], start=True, stop=True
                        )
                    return last

                p.t(mk_sums, reads=["eA", "eB", "onescol_b"], writes=["ps2"], tag=f"sums{r}_{j}")
                p.v(lambda e: e.tensor_copy(sums_tm[:], ps2[:, 16:32]), reads=["ps2"], writes=["sums_tm"])
                p.v(lambda e: e.tensor_copy(bsums_tm[:], ps2[0:16, 32:48]), reads=["ps2"], writes=["bsums_tm"])
                p.t(
                    lambda e: e.transpose(ps2[0:16, 64:192], sums_tm[:, :], identf[:, :]),
                    reads=["sums_tm", "identf"],
                    writes=["ps2"],
                )
                p.v(lambda e: e.tensor_copy(shm[:, :], ps2[0:16, 64:192]), reads=["ps2"], writes=["shm"])
                p.t(
                    lambda e: e.transpose(ps2[0:16, 192:208], bsums_tm[:, :], identf[0:16, 0:16]),
                    reads=["bsums_tm", "identf"],
                    writes=["ps2"],
                )
                p.v(
                    lambda e: e.tensor_tensor(shm[:, 112:128], shm[:, 112:128], ps2[0:16, 192:208], ALU.add),
                    reads=["ps2", "shm"],
                    writes=["shm"],
                )
                p.v(
                    lambda e, q0=q0: e.reciprocal(recip_hm[:, q0 : q0 + 128], shm[:, :]),
                    reads=["shm"],
                    writes=["recip_hm"],
                )
                if KA_SUB < 6:
                    continue
                p.v(
                    lambda e, q0=q0: e.tensor_copy(
                        ctxr.ap()[0:64, :, q0 : q0 + 128],
                        psO.ap()[0:64, 0:2048].rearrange("p (c two t) -> p c two t", two=2, t=128)[:, :, 0, :],
                    ),
                    reads=["psO"],
                    writes=["ctxr"],
                )
                p.v(
                    lambda e, q0=q0: e.tensor_copy(
                        ctxr.ap()[64:128, :, q0 : q0 + 128],
                        psO.ap()[0:64, 0:2048].rearrange("p (c two t) -> p c two t", two=2, t=128)[:, :, 1, :],
                    ),
                    reads=["psO"],
                    writes=["ctxr"],
                )

            if KA_STAGE < 7:
                continue
            for cch in range(NCH):
                p.t(
                    lambda e, cch=cch: e.matmul(psA[:, 0:512], s16_sb[:, cch, :], recip_hm[:, :], start=True, stop=True),
                    reads=["s16_sb", "recip_hm"],
                    writes=["psA"],
                    tag=f"rb{r}_{cch}",
                )
                p.v(
                    lambda e, cch=cch: e.tensor_tensor(ctxr[:, cch, :], ctxr[:, cch, :], psA[:, 0:512], ALU.mult),
                    reads=["ctxr", "psA"],
                    writes=["ctxr"],
                )

            for oj in range(NCH):
                def mk_y(e, oj=oj):
                    last = None
                    for ci in range(NCH):
                        last = e.matmul(
                            ps1[:, 0:512], wo[:, ci, 128 * oj : 128 * oj + 128], ctxr[:, ci, :],
                            start=(ci == 0), stop=(ci == NCH - 1),
                        )
                    return last

                p.t(mk_y, reads=["ctxr", "wo"], writes=["ps1"], tag=f"y{r}_{oj}")
                p.s(
                    lambda e, oj=oj: e.activation(out=yr[:, oj, :], in_=ps1[:, 0:512], func=AF.Identity, bias=bout_c[:, oj : oj + 1]),
                    reads=["ps1", "bout_c"],
                    writes=["yr"],
                )

            def apply_post(R, MR, rdeps):
                p.v(
                    lambda e: e.tensor_tensor(
                        sq[:, :, 0:512], yr[:, :, :], R.unsqueeze(1).broadcast_to([128, NCH, 512]), ALU.mult
                    ),
                    reads=["yr"] + rdeps[:1],
                    writes=["sq"],
                )
                p.v(
                    lambda e: e.tensor_tensor(
                        sq[:, :, 0:512], sq[:, :, 0:512], MR.unsqueeze(1).broadcast_to([128, NCH, 512]), ALU.subtract
                    ),
                    reads=["sq"] + rdeps[1:],
                    writes=["sq"],
                )
                p.v(
                    lambda e: e.tensor_tensor(x0r[:, :, 8:520], x0r[:, :, 8:520], sq[:, :, 0:512], ALU.add),
                    reads=["sq", "x0r"],
                    writes=["x0r"],
                )

            emit_ln(
                p, yr.ap()[:, :, :], "yr", 512, sq.ap()[:, :, 0:512], "sq",
                rows2, psA, "psA", onescol_b, "onescol_b", ones1f, "ones1f", apply_post,
            )

            p.dma(
                "sync",
                lambda e, r0=r0: e.dma_start(
                    out=x1out.rearrange("(c p) t -> p c t", p=128)[:, :, r0 : r0 + 512], in_=x0r[:, :, 8:520]
                ),
                reads=["x0r"],
                writes=[("x1out", r)],
                tag=f"x1out{r}",
            )

        p.finalize(ctx)
    return nc





def emit_ln_g(p, x_chunks, n_tok, square_jobs, sq_chunks, rows, psA, psAkey,
              onescol, onescolkey, onescol_f, onescol_f_key, ones1f, ones1fkey,
              out_emit, d_model=1024, eps=1e-5):
    assert n_tok <= 512

    def mk_sum(e):
        last = None
        n = len(x_chunks)
        for i, (ap, _k) in enumerate(x_chunks):
            last = e.matmul(psA[0:1, 0:n_tok], onescol[:, :], ap, start=(i == 0), stop=(i == n - 1))
        return last

    p.t(mk_sum, reads=[k for _, k in x_chunks] + [onescolkey], writes=[psAkey])
    for (oap, iap, rkey, wkey) in square_jobs:
        p.s(lambda e, oap=oap, iap=iap: e.activation(out=oap, in_=iap, func=AF.Square), reads=[rkey], writes=[wkey])

    def mk_sum2(e):
        last = None
        n = len(sq_chunks)
        for i, (ap, _k) in enumerate(sq_chunks):
            last = e.matmul(psA[0:1, 512 : 512 + n_tok], onescol_f[:, :], ap, start=(i == 0), stop=(i == n - 1))
        return last

    p.t(mk_sum2, reads=[k for _, k in sq_chunks] + [onescol_f_key], writes=[psAkey])
    mu, rstd, murs, var = rows["mu"], rows["rstd"], rows["murs"], rows["var"]
    kmu, krstd, kmurs, kvar = rows["kmu"], rows["krstd"], rows["kmurs"], rows["kvar"]
    sc = 1.0 / d_model
    p.s(lambda e: e.activation(out=mu[:, 0:n_tok], in_=psA[0:1, 0:n_tok], func=AF.Copy, scale=sc), reads=[psAkey], writes=[kmu])
    p.s(lambda e: e.activation(out=var[:, 0:n_tok], in_=mu[:, 0:n_tok], func=AF.Square), reads=[kmu], writes=[kvar])
    p.v(
        lambda e: e.scalar_tensor_tensor(var[:, 0:n_tok], psA[0:1, 512 : 512 + n_tok], sc, var[:, 0:n_tok], ALU.mult, ALU.subtract),
        reads=[psAkey, kvar],
        writes=[kvar],
    )
    p.v(lambda e: e.memset(rows["eps"][:], float(eps)), writes=[rows["keps"]])
    p.s(lambda e: e.activation(out=rstd[:, 0:n_tok], in_=var[:, 0:n_tok], func=AF.Sqrt, bias=rows["eps"][:, :]), reads=[kvar, rows["keps"]], writes=[krstd])
    p.v(lambda e: e.reciprocal(rstd[:, 0:n_tok], rstd[:, 0:n_tok]), reads=[krstd], writes=[krstd])
    p.v(lambda e: e.tensor_mul(murs[:, 0:n_tok], mu[:, 0:n_tok], rstd[:, 0:n_tok]), reads=[kmu, krstd], writes=[kmurs])
    p.t(
        lambda e: e.matmul(psA[:, 0:n_tok], ones1f[:, 0:128], rstd[:, 0:n_tok], start=True, stop=True),
        reads=[krstd, ones1fkey],
        writes=[psAkey],
    )
    p.t(
        lambda e: e.matmul(psA[:, 512 : 512 + n_tok], ones1f[:, 0:128], murs[:, 0:n_tok], start=True, stop=True),
        reads=[kmurs, ones1fkey],
        writes=[psAkey],
    )
    out_emit(psA[:, 0:n_tok], psA[:, 512 : 512 + n_tok], [psAkey, psAkey])


FA_EXT = 520


def build_launch_fa(entry_ln):
    nc = bass.Bass("TRN2", target_bir_lowering=False)
    ap = lambda t: t.ap()
    fxe = ap(nc.dram_tensor("fxe", [D, FA_EXT + TOK], F32, kind="ExternalInput"))
    wqkv = ap(nc.dram_tensor("wqkv", [D, 1536], F32, kind="ExternalInput"))
    wo_in = ap(nc.dram_tensor("wo_in", [512, D], F32, kind="ExternalInput"))
    qs_row = ap(nc.dram_tensor("qs_row", [1, 64], F32, kind="ExternalInput"))
    ks_row = ap(nc.dram_tensor("ks_row", [1, 64], F32, kind="ExternalInput"))
    tqc_in = ap(nc.dram_tensor("tqc_in", [64, TOK], F32, kind="ExternalInput"))
    tqs_in = ap(nc.dram_tensor("tqs_in", [64, TOK], F32, kind="ExternalInput"))
    tkc_in = ap(nc.dram_tensor("tkc_in", [64, FA_EXT + TOK], F32, kind="ExternalInput"))
    tks_in = ap(nc.dram_tensor("tks_in", [64, FA_EXT + TOK], F32, kind="ExternalInput"))
    dead_in = ap(nc.dram_tensor("dead_in", [1, FA_EXT + TOK], F32, kind="ExternalInput"))
    rot_in = ap(nc.dram_tensor("rot_in", [64, 64], F32, kind="ExternalInput"))
    tri_in = ap(nc.dram_tensor("tri_in", [2, 128, 128], F32, kind="ExternalInput"))
    identw = ap(nc.dram_tensor("identw", [128, 128], F32, kind="ExternalInput"))
    xmid_out = ap(nc.dram_tensor("xmid_out", [D, TOK], F32, kind="ExternalOutput"))

    with ExitStack() as ctx:
        sb = lambda name, shape, dt=F32: ctx.enter_context(nc.sbuf_tensor(name, shape, dt))
        pst = lambda name, shape, dt=F32: ctx.enter_context(nc.psum_tensor(name, shape, dt))
        p = P(nc)
        p.alias_groups = {"psS": [("psS", 0), ("psS", 1)], "sq": [("sq", 0), ("sq", 1), ("sq", "P2"), ("sq", "PT2")]}

        fxr = sb("fxr", [128, NCH, 1032])
        fh = sb("fh", [128, NCH, 1032], BF16)
        sq = sb("sq", [128, NCH, 512])
        wq_sb = sb("wq_sb", [128, NCH, 1536], BF16)
        wo_sb = sb("wo_sb", [128, 4, 1024], BF16)
        tqc = sb("tqc", [64, 512])
        tqs = sb("tqs", [64, 512])
        tkc = sb("tkc", [64, 1024])
        tks = sb("tks", [64, 1024])
        dead_r = sb("dead_r", [1, 1024])
        rot64 = sb("rot64", [64, 64], BF16)
        triL = sb("triL", [128, 128])
        triD = sb("triD", [128, 128])
        identf = sb("identf", [128, 128])
        identb = sb("identb", [128, 128], BF16)
        qs_col = sb("qs_col", [64, 1])
        ks_col = sb("ks_col", [64, 1])
        qraw = sb("qraw", [64, 8, 512], BF16)
        kraw = sb("kraw", [64, 8, 1024], BF16)
        qfin = sb("qfin", [65, 8, 512], BF16)
        kfin = sb("kfin", [65, 8, 1024], BF16)
        scr = sb("scr", [1, 128])
        u_sbR = [sb(f"u_sbR{i}", [64, 1024], BF16) for i in range(2)]
        rrowR = [sb(f"rrowR{i}", [1, 1024]) for i in range(2)]
        vr = sb("vr", [128, 8, 8, 64], BF16)
        P_sb = sb("P_sb", [128, 4, 640], BF16)
        PT_sb = sb("PT_sb", [128, 2560], BF16)
        negm = sb("negm", [128, 4])
        ssum = sb("ssum", [128, 4])
        negmB = sb("negmB", [128, 4])
        ssumB = sb("ssumB", [128, 4])
        o_fm = sb("o_fm", [128, 4, 512], BF16)
        ones1f = sb("ones1f", [1, 128])
        onescol_f = sb("onescol_f", [128, 1])
        onescol_b = sb("onescol_b", [128, 1], BF16)
        rows = ln_rows(nc, ctx, "lnF", width=512)
        rows["onesf_col"] = onescol_f
        rows["konesf_col"] = "onescol_f"

        psS = pst("psS", [128, 2560])
        psPT = pst("psPT", [128, 2048], BF16)
        psO = pst("psO", [128, 512])
        psA = psS.ap()[:, 0:1024]
        psN = psS.ap()[:, 1024:2048]

        p.v(lambda e: e.memset(ones1f[:], 1.0), writes=["ones1f"])
        p.v(lambda e: e.memset(onescol_f[:], 1.0), writes=["onescol_f"])
        p.v(lambda e: e.memset(onescol_b[:], 1.0), writes=["onescol_b"])
        p.v(lambda e: e.memset(qfin.ap()[64:65, :, :], 1.0), writes=[("qfin", "ones")])

        p.dma("sync", lambda e: e.dma_start(out=identf[:], in_=identw[:, :]), writes=["identf"])
        p.dma("gpsimd", lambda e: e.dma_start(out=identb[:], in_=identw[:, :]), writes=["identb"])
        p.dma("gpsimd", lambda e: e.dma_start(out=rot64[:], in_=rot_in[:, :]), writes=["rot64"])
        p.dma("sync", lambda e: e.dma_start(out=triL[:], in_=tri_in[0, :, :]), writes=["triL"])
        p.dma("sync", lambda e: e.dma_start(out=triD[:], in_=tri_in[1, :, :]), writes=["triD"])

        for cc in range(NCH):
            p.dma("gpsimd", lambda e, cc=cc: e.dma_start(out=wq_sb[:, cc, :], in_=wqkv[128 * cc : 128 * cc + 128, :]), writes=["wq_sb"], tag=f"wq{cc}")
        for cc in range(4):
            p.dma("gpsimd", lambda e, cc=cc: e.dma_start(out=wo_sb[:, cc, :], in_=wo_in[128 * cc : 128 * cc + 128, :]), writes=["wo_sb"], tag=f"wo{cc}")
        p.dma("sync", lambda e: e.dma_start(out=scr[0:1, 0:64], in_=qs_row[:, :]), writes=["scr"])
        p.dma("sync", lambda e: e.dma_start(out=scr[0:1, 64:128], in_=ks_row[:, :]), writes=["scr"])
        p.t(lambda e: e.transpose(psN[0:64, 0:1], scr[0:1, 0:64], identf[0:1, 0:1]), reads=["scr", "identf"], writes=["psS"])
        p.v(lambda e: e.tensor_scalar_mul(qs_col[:], psN[0:64, 0:1], 8.0), reads=["psS"], writes=["qs_col"])
        p.t(lambda e: e.transpose(psN[0:64, 0:1], scr[0:1, 64:128], identf[0:1, 0:1]), reads=["scr", "identf"], writes=["psS"])
        p.v(lambda e: e.tensor_copy(ks_col[:], psN[0:64, 0:1]), reads=["psS"], writes=["ks_col"])

        for r in range(4):
            B = 512 * r
            c0 = B

            p.dma(
                "sync",
                lambda e, c0=c0: e.dma_start(out=fxr[:, :, :], in_=fxe.rearrange("(c p) t -> p c t", p=128)[:, :, c0 : c0 + 1032]),
                writes=["fxr"],
                tag=f"fxr{r}",
            )
            p.dma("sync", lambda e, B=B: e.dma_start(out=tqc[:, :], in_=tqc_in[:, B : B + 512]), writes=["tqc"], tag=f"tqc{r}")
            p.dma("sync", lambda e, B=B: e.dma_start(out=tqs[:, :], in_=tqs_in[:, B : B + 512]), writes=["tqs"], tag=f"tqs{r}")
            p.dma("sync", lambda e, c0=c0: e.dma_start(out=tkc[:, :], in_=tkc_in[:, c0 + 8 : c0 + 1032]), writes=["tkc"], tag=f"tkc{r}")
            p.dma("sync", lambda e, c0=c0: e.dma_start(out=tks[:, :], in_=tks_in[:, c0 + 8 : c0 + 1032]), writes=["tks"], tag=f"tks{r}")
            p.dma("sync", lambda e, c0=c0: e.dma_start(out=dead_r[:, :], in_=dead_in[:, c0 + 8 : c0 + 1032]), writes=["dead_r"], tag=f"dead{r}")

            if entry_ln:
                def apply_lnF(R, MR, rdeps, off, n):
                    p.v(
                        lambda e: e.tensor_tensor(
                            sq[:, :, 0:n], fxr[:, :, off : off + n], R.unsqueeze(1).broadcast_to([128, NCH, n]), ALU.mult
                        ),
                        reads=["fxr", "psS"],
                        writes=["sq"],
                    )
                    p.v(
                        lambda e: e.tensor_tensor(
                            fxr[:, :, off : off + n], sq[:, :, 0:n], MR.unsqueeze(1).broadcast_to([128, NCH, n]), ALU.subtract
                        ),
                        reads=["sq", "psS"],
                        writes=["fxr"],
                    )

                for (off, n) in [(0, 512), (512, 512), (1024, 8)]:
                    emit_ln_g(
                        p,
                        [(fxr.ap()[:, c, off : off + n], "fxr") for c in range(NCH)],
                        n,
                        [(sq.ap()[:, :, 0:n], fxr.ap()[:, :, off : off + n], "fxr", "sq")],
                        [(sq.ap()[:, c, 0:n], "sq") for c in range(NCH)],
                        rows, psA, "psS", onescol_f, "onescol_f", onescol_f, "onescol_f",
                        ones1f, "ones1f",
                        lambda R, MR, rd, off=off, n=n: apply_lnF(R, MR, rd, off, n),
                    )
            p.v(lambda e: e.tensor_copy(fh[:, :, :], fxr[:, :, :]), reads=["fxr"], writes=["fh"])

            def hap(c, a, n):
                off = a if c < 4 else a - 1
                return fh.ap()[:, c, off : off + n]

            for vt in range(8):
                def mk_v(e, vt=vt):
                    last = None
                    for ci in range(NCH):
                        last = e.matmul(
                            psS[0:128, 0:512], hap(ci, 8 + 128 * vt, 128), wq_sb[:, ci, 1024:1536],
                            start=(ci == 0), stop=(ci == NCH - 1),
                        )
                    return last

                p.t(mk_v, reads=["fh", "wq_sb"], writes=["psS"], tag=f"v{r}_{vt}")
                p.v(
                    lambda e, vt=vt: e.tensor_copy(
                        vr.ap()[:, vt, :, :], psS.ap()[0:128, 0:512].rearrange("t (h d) -> t h d", h=8)
                    ),
                    reads=["psS"],
                    writes=["vr"],
                )
            for om in range(0, 8, 2):
                def mk_qr(e, om=om):
                    last = None
                    for ci in range(NCH):
                        last = e.matmul(
                            psN[0:128, 0:512], wq_sb[:, ci, 64 * om : 64 * om + 128], hap(ci, 520, 512),
                            start=(ci == 0), stop=(ci == NCH - 1),
                        )
                    return last

                p.t(mk_qr, reads=["fh", "wq_sb"], writes=["psS"], tag=f"qr{r}_{om}")
                p.s(lambda e, om=om: e.activation(out=qraw[:, om, :], in_=psN[0:64, 0:512], func=AF.Copy), reads=["psS"], writes=["qraw"])
                p.s(lambda e, om=om: e.activation(out=qraw[:, om + 1, :], in_=psN[64:128, 0:512], func=AF.Copy), reads=["psS"], writes=["qraw"])
                for tt in range(2):
                    def mk_kr(e, om=om, tt=tt):
                        last = None
                        for ci in range(NCH):
                            last = e.matmul(
                                psN[0:128, 0:512], wq_sb[:, ci, 512 + 64 * om : 512 + 64 * om + 128], hap(ci, 8 + 512 * tt, 512),
                                start=(ci == 0), stop=(ci == NCH - 1),
                            )
                        return last

                    p.t(mk_kr, reads=["fh", "wq_sb"], writes=["psS"], tag=f"kr{r}_{om}_{tt}")
                    p.s(
                        lambda e, om=om, tt=tt: e.activation(out=kraw[:, om, 512 * tt : 512 * tt + 512], in_=psN[0:64, 0:512], func=AF.Copy),
                        reads=["psS"],
                        writes=["kraw"],
                    )
                    p.s(
                        lambda e, om=om, tt=tt: e.activation(out=kraw[:, om + 1, 512 * tt : 512 * tt + 512], in_=psN[64:128, 0:512], func=AF.Copy),
                        reads=["psS"],
                        writes=["kraw"],
                    )

            sqflat = sq.ap()[0:64, :, :].rearrange("p c t -> p (c t)")
            scrR = [sqflat[:, 0:1024], sqflat[:, 1024:2048]]

            def rot_pipe2(idx, src3, srckey, nh, n, sc_col, sc_key, tcos, tcos_k, tsin, tsin_k, dst3, dstkey):
                pp = idx % 2
                W = nh * n
                scr2, u2, rr2 = scrR[pp], u_sbR[pp], rrowR[pp]
                kscr, ku, krr = ("sq", pp), f"u_sbR{pp}", f"rrowR{pp}"
                pcol = 1024 * pp
                kps = ("psS", pp)
                scr2v = scr2[:, 0:W].rearrange("p (c t) -> p c t", c=nh)
                u2v = u_sbR[pp].ap()[:, 0:W].rearrange("p (c t) -> p c t", c=nh)
                psmatv = psS.ap()[0:64, pcol : pcol + W].rearrange("p (c t) -> p c t", c=nh)
                src2 = src3.rearrange("p c t -> p (c t)")
                cosb = tcos.ap()[:, 0:n].unsqueeze(1).broadcast_to([64, nh, n])
                sinb = tsin.ap()[:, 0:n].unsqueeze(1).broadcast_to([64, nh, n])

                p.s(lambda e: e.activation(out=scr2v, in_=src3, func=AF.Square), reads=[srckey], writes=[kscr])

                def mk_ss(e):
                    last = None
                    for j2 in range(2):
                        last = e.matmul(psS[0:1, pcol + 512 * j2 : pcol + 512 * j2 + 512], onescol_f[0:64, :], scr2[:, 512 * j2 : 512 * j2 + 512], start=True, stop=True)
                    return last

                p.t(mk_ss, reads=[kscr, "onescol_f"], writes=[kps], tag=f"ss{r}_{idx}{srckey}")
                p.s(lambda e: e.activation(out=rr2[:, 0:W], in_=psS[0:1, pcol : pcol + W], func=AF.Sqrt), reads=[kps], writes=[krr])
                p.v(lambda e: e.tensor_scalar_max(rr2[:, 0:W], rr2[:, 0:W], 1e-12), reads=[krr], writes=[krr])
                p.v(lambda e: e.reciprocal(rr2[:, 0:W], rr2[:, 0:W]), reads=[krr], writes=[krr])

                def mk_bc(e):
                    last = None
                    for j2 in range(2):
                        last = e.matmul(psS[0:64, pcol + 512 * j2 : pcol + 512 * j2 + 512], ones1f[:, 0:64], rr2[:, 512 * j2 : 512 * j2 + 512], start=True, stop=True)
                    return last

                p.t(mk_bc, reads=[krr, "ones1f"], writes=[kps], tag=f"bc{r}_{idx}{srckey}")
                p.v(
                    lambda e: e.scalar_tensor_tensor(u2[:, 0:W], src2, sc_col[:, :], psS[0:64, pcol : pcol + W], ALU.mult, ALU.mult),
                    reads=[srckey, sc_key, kps],
                    writes=[ku],
                )

                def mk_rot(e):
                    last = None
                    for j2 in range(2):
                        last = e.matmul(psS[0:64, pcol + 512 * j2 : pcol + 512 * j2 + 512], rot64[:, :], u2[:, 512 * j2 : 512 * j2 + 512], start=True, stop=True)
                    return last

                p.t(mk_rot, reads=[ku, "rot64"], writes=[kps], tag=f"rot{r}_{idx}{srckey}")
                p.v(lambda e: e.tensor_tensor(scr2v, u2v, cosb, ALU.mult), reads=[ku, tcos_k], writes=[kscr])
                p.v(lambda e: e.tensor_tensor(u2v, psmatv, sinb, ALU.mult), reads=[kps, tsin_k], writes=[ku])
                p.v(lambda e: e.tensor_tensor(dst3, scr2v, u2v, ALU.add), reads=[kscr, ku], writes=[dstkey])

            pipes = []
            for hp in range(4):
                pipes.append((qraw.ap()[:, 2 * hp : 2 * hp + 2, :], "qraw", 2, 512, qs_col, "qs_col", tqc, "tqc", tqs, "tqs", qfin.ap()[0:64, 2 * hp : 2 * hp + 2, :], "qfin"))
            for h in range(8):
                pipes.append((kraw.ap()[:, h : h + 1, :], "kraw", 1, 1024, ks_col, "ks_col", tkc, "tkc", tks, "tks", kfin.ap()[0:64, h : h + 1, :], "kfin"))
            for idx, args in enumerate(pipes):
                rot_pipe2(idx, *args)
            p.v(
                lambda e: e.tensor_copy(kfin.ap()[64:65, :, :], dead_r.ap()[:, :].unsqueeze(1).broadcast_to([1, 8, 1024])),
                reads=["dead_r"],
                writes=[("kfin", "dead")],
            )

            sqb = sq.ap().rearrange("p c t -> p (c t)").bitcast(BF16)
            P_sb2 = sqb[:, 0:2560].rearrange("p (c t) -> p c t", c=4)
            PT_sb2 = sqb[:, 2560:5120]
            for j in range(4):
                q0 = 128 * j
                for hg in range(2):
                    par = (2 * j + hg) % 2
                    Pq = P_sb.ap() if par == 0 else P_sb2
                    PTq = PT_sb.ap() if par == 0 else PT_sb2
                    kP = "P_sb" if par == 0 else ("sq", "P2")
                    kPT = "PT_sb" if par == 0 else ("sq", "PT2")
                    negmq = negm if par == 0 else negmB
                    ssumq = ssum if par == 0 else ssumB
                    kneg = "negm" if par == 0 else "negmB"
                    ksum = "ssum" if par == 0 else "ssumB"
                    def mk_S(e, j=j, hg=hg, q0=q0):
                        last = None
                        for hh in range(4):
                            h = 4 * hg + hh
                            a = 640 * hh
                            while a < 640 * hh + 640:
                                n = min(512 - a % 512, 640 * hh + 640 - a)
                                last = e.matmul(
                                    psS[:, a : a + n],
                                    qfin[:, h, q0 : q0 + 128],
                                    kfin[:, h, 128 * j + (a - 640 * hh) : 128 * j + (a - 640 * hh) + n],
                                    start=True, stop=True,
                                )
                                a += n
                        return last

                    p.t(mk_S, reads=["qfin", ("qfin", "ones"), "kfin", ("kfin", "dead")], writes=["psS"], tag=f"S{r}_{j}_{hg}")
                    p.v(
                        lambda e: e.tensor_tensor(
                            psS.ap().rearrange("p (hh k) -> p hh k", hh=4)[:, :, 0:128],
                            psS.ap().rearrange("p (hh k) -> p hh k", hh=4)[:, :, 0:128],
                            triL.ap().unsqueeze(1).broadcast_to([128, 4, 128]),
                            ALU.add,
                        ),
                        reads=["psS", "triL"],
                        writes=["psS"],
                    )
                    p.v(
                        lambda e: e.tensor_tensor(
                            psS.ap().rearrange("p (hh k) -> p hh k", hh=4)[:, :, 512:640],
                            psS.ap().rearrange("p (hh k) -> p hh k", hh=4)[:, :, 512:640],
                            triD.ap().unsqueeze(1).broadcast_to([128, 4, 128]),
                            ALU.add,
                        ),
                        reads=["psS", "triD"],
                        writes=["psS"],
                    )
                    p.v(
                        lambda e, negmq=negmq: e.reduce_max(negmq[:, :], psS.ap().rearrange("p (hh k) -> p hh k", hh=4), axis=AX.X, negate=True),
                        reads=["psS"],
                        writes=[kneg],
                    )
                    for hh in range(4):
                        p.s(
                            lambda e, hh=hh, Pq=Pq, negmq=negmq: e.activation(
                                out=Pq[:, hh, :], in_=psS[:, 640 * hh : 640 * hh + 640], func=AF.Exp, bias=negmq[:, hh : hh + 1]
                            ),
                            reads=["psS", kneg],
                            writes=[kP],
                        )
                    p.v(
                        lambda e, Pq=Pq, ssumq=ssumq: e.reduce_sum(ssumq[:, :], Pq, axis=AX.X),
                        reads=[kP],
                        writes=[ksum],
                    )
                    p.v(lambda e, ssumq=ssumq: e.reciprocal(ssumq[:, :], ssumq[:, :]), reads=[ksum], writes=[ksum])
                    p.v(
                        lambda e, Pq=Pq, ssumq=ssumq: e.tensor_tensor(
                            Pq, Pq, ssumq.ap()[:, :].unsqueeze(2).broadcast_to([128, 4, 640]), ALU.mult
                        ),
                        reads=[kP, ksum],
                        writes=[kP],
                    )
                    for wv in range(2):
                        def mk_T(e, wv=wv, Pq=Pq):
                            last = None
                            for i in range(10):
                                g = 10 * wv + i
                                hh, kc = g // 5, g % 5
                                last = e.transpose(
                                    psPT[:, 128 * i : 128 * i + 128],
                                    Pq[:, hh, 128 * kc : 128 * kc + 128],
                                    identb[:, :],
                                )
                            return last

                        p.t(mk_T, reads=[kP, "identb"], writes=["psPT"], tag=f"T{r}_{j}_{hg}_{wv}")
                        p.v(
                            lambda e, wv=wv, PTq=PTq: e.tensor_copy(PTq[:, 1280 * wv : 1280 * wv + 1280], psPT[:, 0:1280]),
                            reads=["psPT"],
                            writes=[kPT],
                        )

                    def mk_PV(e, j=j, hg=hg, PTq=PTq):
                        last = None
                        for hh in range(4):
                            h = 4 * hg + hh
                            for kc in range(5):
                                g = 5 * hh + kc
                                last = e.matmul(
                                    psO[0:64, 128 * hh : 128 * hh + 128],
                                    vr.ap()[:, j + kc, h, :],
                                    PTq[:, 128 * g : 128 * g + 128],
                                    start=(kc == 0), stop=(kc == 4),
                                )
                        return last

                    p.t(mk_PV, reads=["vr", kPT], writes=["psO"], tag=f"PV{r}_{j}_{hg}")
                    for po in range(2):
                        p.v(
                            lambda e, po=po, hg=hg, q0=q0: e.tensor_copy(
                                o_fm.ap()[64 * po : 64 * po + 64, 2 * hg : 2 * hg + 2, q0 : q0 + 128],
                                psO.ap()[0:64, 0:512].rearrange("p (c two t) -> p c two t", c=2, two=2, t=128)[:, :, po, :],
                            ),
                            reads=["psO"],
                            writes=["o_fm"],
                        )

            for oj in range(NCH):
                def mk_wo(e, oj=oj):
                    last = None
                    for ci in range(4):
                        last = e.matmul(
                            psN[:, 0:512], wo_sb[:, ci, 128 * oj : 128 * oj + 128], o_fm[:, ci, :],
                            start=(ci == 0), stop=(ci == 3),
                        )
                    return last

                p.t(mk_wo, reads=["o_fm", "wo_sb"], writes=["psS"], tag=f"wo{r}_{oj}")
                p.v(
                    lambda e, oj=oj: e.tensor_tensor(fxr[:, oj, 520:1032], fxr[:, oj, 520:1032], psN[:, 0:512], ALU.add),
                    reads=["fxr", "psS"],
                    writes=["fxr"],
                )
            p.dma(
                "sync",
                lambda e, B=B: e.dma_start(
                    out=xmid_out.rearrange("(c p) t -> p c t", p=128)[:, :, B : B + 512], in_=fxr[:, :, 520:1032]
                ),
                reads=["fxr"],
                writes=[("xmid_out", r)],
                tag=f"xmo{r}",
            )

        p.finalize(ctx)
    return nc




def build_launch_fb(final_combine):
    nc = bass.Bass("TRN2", target_bir_lowering=False)
    ap = lambda t: t.ap()
    xme = ap(nc.dram_tensor("xme", [D, TOK + 1], F32, kind="ExternalInput"))
    w1_in = ap(nc.dram_tensor("w1_in", [D, 5460], F32, kind="ExternalInput"))
    w2_in = ap(nc.dram_tensor("w2_in", [2816, D], F32, kind="ExternalInput"))
    b1a_in = ap(nc.dram_tensor("b1a_in", [22, 128], F32, kind="ExternalInput"))
    b1g_in = ap(nc.dram_tensor("b1g_in", [22, 128], F32, kind="ExternalInput"))
    b2_in = ap(nc.dram_tensor("b2_in", [8, 128], F32, kind="ExternalInput"))
    lng_in = ap(nc.dram_tensor("lng_in", [1, D], F32, kind="ExternalInput"))
    lnb_in = ap(nc.dram_tensor("lnb_in", [1, D], F32, kind="ExternalInput"))
    identw = ap(nc.dram_tensor("identw", [128, 128], F32, kind="ExternalInput"))
    neg1_in = ap(nc.dram_tensor("neg1_in", [1, 544], F32, kind="ExternalInput"))
    if final_combine:
        x1own = ap(nc.dram_tensor("x1own", [D, TOK], F32, kind="ExternalInput"))
    out_t = ap(nc.dram_tensor("out_t", [D, TOK], F32, kind="ExternalOutput"))

    with ExitStack() as ctx:
        sb = lambda name, shape, dt=F32: ctx.enter_context(nc.sbuf_tensor(name, shape, dt))
        pst = lambda name, shape, dt=F32: ctx.enter_context(nc.psum_tensor(name, shape, dt))
        p = P(nc)

        xmr = sb("xmr", [128, NCH, 513])
        sq = sb("sq", [128, NCH, 512])
        xh = sb("xh", [128, NCH, 512], BF16)
        w1ta = [sb(f"w1ta{i}", [128, NCH, 128], BF16) for i in range(2)]
        w1tg = [sb(f"w1tg{i}", [128, NCH, 128], BF16) for i in range(2)]
        w2t = [sb(f"w2t{i}", [128, 22, 128], BF16) for i in range(2)]
        h_sb = sb("h_sb", [128, 22, 512], BF16)
        b1a_c = sb("b1a_c", [128, 22])
        b1g_c = sb("b1g_c", [128, 22])
        b2_c = sb("b2_c", [128, 8])
        gb2 = sb("gb2", [2, NCH, 128])
        gcol = sb("gcol", [128, 8])
        rows2v = sb("rows2v", [2, 544])
        btmp = sb("btmp", [22, 128])
        identf = sb("identf", [128, 128])
        xnr = sb("xnr", [128, NCH, 512])
        x1r = sb("x1r", [128, NCH, 512])
        ones1f = sb("ones1f", [1, 128])
        onescol_f = sb("onescol_f", [128, 1])
        rows = ln_rows(nc, ctx, "lnB")
        rows["onesf_col"] = onescol_f
        rows["konesf_col"] = "onescol_f"

        ps1 = pst("ps1", [128, 512])
        ps2 = pst("ps2", [128, 512])
        psA = pst("psA", [128, 1024])
        psB = pst("psB", [128, 512])

        p.v(lambda e: e.memset(ones1f[:], 1.0), writes=["ones1f"])
        p.v(lambda e: e.memset(onescol_f[:], 1.0), writes=["onescol_f"])
        p.dma("sync", lambda e: e.dma_start(out=rows2v.ap()[1:2, :], in_=neg1_in[:, :]), writes=[("rows2v", "neg")])

        p.dma("sync", lambda e: e.dma_start(out=identf[:], in_=identw[:, :]), writes=["identf"])
        p.dma("sync", lambda e: e.dma_start(out=btmp[:], in_=b1a_in[:, :]), writes=["btmp"])
        p.t(lambda e: e.transpose(psA[:, 0:22], btmp[:, :], identf[0:22, 0:22]), reads=["btmp", "identf"], writes=["psA"])
        p.v(lambda e: e.tensor_copy(b1a_c[:], psA[:, 0:22]), reads=["psA"], writes=["b1a_c"])
        p.dma("sync", lambda e: e.dma_start(out=btmp[:], in_=b1g_in[:, :]), writes=["btmp"])
        p.t(lambda e: e.transpose(psA[:, 0:22], btmp[:, :], identf[0:22, 0:22]), reads=["btmp", "identf"], writes=["psA"])
        p.v(lambda e: e.tensor_copy(b1g_c[:], psA[:, 0:22]), reads=["psA"], writes=["b1g_c"])
        p.dma("sync", lambda e: e.dma_start(out=btmp[0:8, :], in_=b2_in[:, :]), writes=["btmp"])
        p.t(lambda e: e.transpose(psA[:, 0:8], btmp[0:8, :], identf[0:8, 0:8]), reads=["btmp", "identf"], writes=["psA"])
        p.v(lambda e: e.tensor_copy(b2_c[:], psA[:, 0:8]), reads=["psA"], writes=["b2_c"])
        p.dma("sync", lambda e: e.dma_start(out=gb2.ap()[0:1, :, :], in_=lng_in.rearrange("o (c m) -> o c m", c=NCH)), writes=[("gb2", "g")])
        p.dma("sync", lambda e: e.dma_start(out=gb2.ap()[1:2, :, :], in_=lnb_in.rearrange("o (c m) -> o c m", c=NCH)), writes=[("gb2", "b")])
        lng8 = sb("lng8", [8, 128])
        p.dma("sync", lambda e: e.dma_start(out=lng8[:], in_=lng_in.rearrange("o (c m) -> (o c) m", c=NCH)), writes=["lng8"])
        p.t(lambda e: e.transpose(psA[:, 0:8], lng8[:, :], identf[0:8, 0:8]), reads=["lng8", "identf"], writes=["psA"])
        p.v(lambda e: e.tensor_copy(gcol[:], psA[:, 0:8]), reads=["psA"], writes=["gcol"])

        for r in range(4):
            B = 512 * r
            p.dma(
                "sync",
                lambda e, B=B: e.dma_start(out=xmr[:, :, :], in_=xme.rearrange("(c p) t -> p c t", p=128)[:, :, B : B + 513]),
                writes=["xmr"],
                tag=f"xmr{r}",
            )
            if final_combine:
                p.dma(
                    "sync",
                    lambda e, B=B: e.dma_start(out=x1r[:, :, :], in_=x1own.rearrange("(c p) t -> p c t", p=128)[:, :, B : B + 512]),
                    writes=["x1r"],
                    tag=f"x1r{r}",
                )

            def h2ap(c, n=512):
                off = 1 if c < 4 else 0
                return xmr.ap()[:, c, off : off + n]

            def apply_lnG(R, MR, rdeps):
                p.v(
                    lambda e: e.tensor_tensor(sq[:, 0:4, :], xmr[:, 0:4, 1:513], R.unsqueeze(1).broadcast_to([128, 4, 512]), ALU.mult),
                    reads=["xmr", "psA"],
                    writes=["sq"],
                )
                p.v(
                    lambda e: e.tensor_tensor(sq[:, 4:8, :], xmr[:, 4:8, 0:512], R.unsqueeze(1).broadcast_to([128, 4, 512]), ALU.mult),
                    reads=["xmr", "psA"],
                    writes=["sq"],
                )
                p.v(lambda e: e.tensor_copy(rows2v[0:1, 0:512], rows["murs"][:, 0:512]), reads=[rows["kmurs"]], writes=[("rows2v", "m")])
                for cch in range(NCH):
                    p.t(
                        lambda e, cch=cch: e.matmul(psB[:, 0:512], gb2[:, cch, :], rows2v[:, 0:512], start=True, stop=True),
                        reads=[("gb2", "g"), ("gb2", "b"), ("rows2v", "m"), ("rows2v", "neg")],
                        writes=["psB"],
                        tag=f"gbc{r}_{cch}",
                    )
                    p.v(
                        lambda e, cch=cch: e.scalar_tensor_tensor(
                            xh[:, cch, :], sq[:, cch, :], gcol[:, cch : cch + 1], psB[:, 0:512], ALU.mult, ALU.subtract
                        ),
                        reads=["sq", "gcol", "psB"],
                        writes=["xh"],
                    )

            emit_ln_g(
                p,
                [(h2ap(c), "xmr") for c in range(NCH)],
                512,
                [
                    (sq.ap()[:, 0:4, :], xmr.ap()[:, 0:4, 1:513], "xmr", "sq"),
                    (sq.ap()[:, 4:8, :], xmr.ap()[:, 4:8, 0:512], "xmr", "sq"),
                ],
                [(sq.ap()[:, c, :], "sq") for c in range(NCH)],
                rows, psA, "psA", onescol_f, "onescol_f", onescol_f, "onescol_f",
                ones1f, "ones1f", apply_lnG,
            )

            for oj in range(22):
                mm = 128 if oj < 21 else 42
                bb = oj % 2
                p.dma(
                    "gpsimd",
                    lambda e, oj=oj, mm=mm, bb=bb: e.dma_start(
                        out=w1ta[bb].ap()[:, :, 0:mm],
                        in_=w1_in.rearrange("(c p) m -> p c m", p=128)[:, :, 128 * oj : 128 * oj + mm],
                    ),
                    writes=[("w1ta", bb)],
                    tag=f"w1a{r}_{oj}",
                )
                p.dma(
                    "gpsimd",
                    lambda e, oj=oj, mm=mm, bb=bb: e.dma_start(
                        out=w1tg[bb].ap()[:, :, 0:mm],
                        in_=w1_in.rearrange("(c p) m -> p c m", p=128)[:, :, 2730 + 128 * oj : 2730 + 128 * oj + mm],
                    ),
                    writes=[("w1tg", bb)],
                    tag=f"w1g{r}_{oj}",
                )

                def mk_a(e, oj=oj, mm=mm, bb=bb):
                    last = None
                    for ci in range(NCH):
                        last = e.matmul(
                            ps1[0:mm, 0:512], w1ta[bb][:, ci, 0:mm], xh[:, ci, :],
                            start=(ci == 0), stop=(ci == NCH - 1),
                        )
                    return last

                def mk_g(e, oj=oj, mm=mm, bb=bb):
                    last = None
                    for ci in range(NCH):
                        last = e.matmul(
                            ps2[0:mm, 0:512], w1tg[bb][:, ci, 0:mm], xh[:, ci, :],
                            start=(ci == 0), stop=(ci == NCH - 1),
                        )
                    return last

                p.t(mk_a, reads=["xh", ("w1ta", bb)], writes=["ps1"], tag=f"a{r}_{oj}")
                p.t(mk_g, reads=["xh", ("w1tg", bb)], writes=["ps2"], tag=f"g{r}_{oj}")
                p.s(
                    lambda e, oj=oj, mm=mm: e.activation(out=sq[0:mm, 0, :], in_=ps1[0:mm, 0:512], func=AF.Identity, bias=b1a_c[0:mm, oj : oj + 1]),
                    reads=["ps1", "b1a_c"],
                    writes=["sq"],
                )
                p.s(
                    lambda e, oj=oj, mm=mm: e.activation(out=sq[0:mm, 1, :], in_=ps2[0:mm, 0:512], func=AF.Gelu, bias=b1g_c[0:mm, oj : oj + 1]),
                    reads=["ps2", "b1g_c"],
                    writes=["sq"],
                )
                p.v(
                    lambda e, oj=oj, mm=mm: e.tensor_tensor(h_sb[0:mm, oj, :], sq[0:mm, 0, :], sq[0:mm, 1, :], ALU.mult),
                    reads=["sq"],
                    writes=["h_sb"],
                )

            for oj in range(NCH):
                bb = oj % 2
                p.dma(
                    "gpsimd",
                    lambda e, oj=oj, bb=bb: e.dma_start(
                        out=w2t[bb].ap()[:, 0:22, :],
                        in_=w2_in.rearrange("(c p) m -> p c m", p=128)[:, :, 128 * oj : 128 * oj + 128],
                    ),
                    writes=[("w2t", bb)],
                    tag=f"w2l{r}_{oj}",
                )

                def mk_w2(e, oj=oj, bb=bb):
                    last = None
                    for ci in range(22):
                        rr2 = 128 if ci < 21 else 42
                        last = e.matmul(
                            ps1[:, 0:512], w2t[bb][0:rr2, ci, :], h_sb[0:rr2, ci, :],
                            start=(ci == 0), stop=(ci == 21),
                        )
                    return last

                p.t(mk_w2, reads=["h_sb", ("w2t", bb)], writes=["ps1"], tag=f"w2{r}_{oj}")
                p.s(
                    lambda e, oj=oj: e.activation(out=sq[:, 0, :], in_=ps1[:, 0:512], func=AF.Identity, bias=b2_c[:, oj : oj + 1]),
                    reads=["ps1", "b2_c"],
                    writes=["sq"],
                )
                p.v(
                    lambda e, oj=oj: e.tensor_tensor(xnr[:, oj, :], xmr[:, oj, 1:513], sq[:, 0, :], ALU.add),
                    reads=["xmr", "sq"],
                    writes=["xnr"],
                )

            if final_combine:
                def apply_fin(R, MR, rdeps):
                    p.v(
                        lambda e: e.tensor_tensor(sq[:, :, :], xnr[:, :, :], R.unsqueeze(1).broadcast_to([128, NCH, 512]), ALU.mult),
                        reads=["xnr", "psA"],
                        writes=["sq"],
                    )
                    p.v(
                        lambda e: e.tensor_tensor(sq[:, :, :], sq[:, :, :], MR.unsqueeze(1).broadcast_to([128, NCH, 512]), ALU.subtract),
                        reads=["sq", "psA"],
                        writes=["sq"],
                    )
                    p.v(
                        lambda e: e.tensor_tensor(xnr[:, :, :], x1r[:, :, :], sq[:, :, :], ALU.add),
                        reads=["sq", "x1r"],
                        writes=["xnr"],
                    )

                emit_ln_g(
                    p,
                    [(xnr.ap()[:, c, :], "xnr") for c in range(NCH)],
                    512,
                    [(sq.ap()[:, :, :], xnr.ap()[:, :, :], "xnr", "sq")],
                    [(sq.ap()[:, c, :], "sq") for c in range(NCH)],
                    rows, psA, "psA", onescol_f, "onescol_f", onescol_f, "onescol_f",
                    ones1f, "ones1f", apply_fin,
                )

            p.dma(
                "sync",
                lambda e, B=B: e.dma_start(
                    out=out_t.rearrange("(c p) t -> p c t", p=128)[:, :, B : B + 512], in_=xnr[:, :, :]
                ),
                reads=["xnr"],
                writes=[("out_t", r)],
                tag=f"out{r}",
            )

        p.finalize(ctx)
    return nc




def fm(x):
    return np.ascontiguousarray(x.T)


def build_attn0_masks(s):
    mA = np.zeros((16, 128, 128), np.float32)
    mB = np.zeros((16, 16, 16), np.float32)
    for j in range(16):
        qt = s + 128 * j
        kk = np.arange(128)[:, None]
        qq = np.arange(128)[None, :]
        key = qt - 8 + kk
        q = qt + qq
        mA[j] = ((np.abs(key - q) <= 8) & (key >= 0) & (key < SEQ)).astype(np.float32)
        kk2 = np.arange(16)[:, None]
        qi = np.arange(16)[None, :]
        key2 = qt + 120 + kk2
        q2 = qt + 112 + qi
        mB[j] = ((np.abs(key2 - q2) <= 8) & (key2 >= 0) & (key2 < SEQ)).astype(np.float32)
    return mA, mB


def build_selectors():
    s16 = np.zeros((8, 16, 128), np.float32)
    hs16 = np.zeros((8, 128, 16), np.float32)
    for c in range(8):
        for pp in range(128):
            h = 2 * c + (1 if pp >= 64 else 0)
            s16[c, h, pp] = 1.0
            hs16[c, pp, h] = 1.0
    return s16, hs16


def run_launch_a(inputs, x0_full):
    nc = build_launch_a()
    w_qkv = inputs["L0_attn_w_qkv"]
    w_out = inputs["L0_attn_w_out"]
    wqkvT = np.ascontiguousarray(w_qkv.T.astype(np.float32))
    woutT = np.ascontiguousarray(w_out.T.astype(np.float32))
    bqkv = inputs["L0_attn_b_qkv"].astype(np.float32)
    bout = inputs["L0_attn_b_out"].astype(np.float32)
    s16, hs16 = build_selectors()
    ident = np.eye(128, dtype=np.float32)
    in_maps = []
    for c in range(NCORES):
        b, s = c // 4, TOK * (c % 4)
        lo, hi = s - 8, s - 8 + KEXT
        x0e = np.zeros((KEXT, D), np.float32)
        a0, a1 = max(lo, 0), min(hi, SEQ)
        x0e[a0 - lo : a1 - lo] = x0_full[b, a0:a1]
        mA, mB = build_attn0_masks(s)
        in_maps.append(
            {
                "x0e": fm(x0e),
                "wqkvT": wqkvT,
                "woutT": woutT,
                "bqkv24": bqkv.reshape(24, 128),
                "bqkv_row": bqkv.reshape(1, 3 * D),
                "bout8": bout.reshape(8, 128),
                "maskA": mA,
                "maskB": mB,
                "s16": s16,
                "hs16": hs16,
                "identw": ident,
            }
        )
    res = run_bass_kernel_spmd(nc, in_maps, core_ids=list(range(NCORES)))
    x1 = np.zeros((2, SEQ, D), np.float32)
    for c in range(NCORES):
        b, s = c // 4, TOK * (c % 4)
        x1[b, s : s + TOK] = res.results[c]["x1out"].T
    return x1





def build_launch_l1():
    nc = bass.Bass("TRN2", target_bir_lowering=False)
    ap = lambda t: t.ap()
    x2own = ap(nc.dram_tensor("x2own", [D, TOK], F32, kind="ExternalInput"))
    x2rot = ap(nc.dram_tensor("x2rot", [D, SEQ], F32, kind="ExternalInput"))
    wq_in = ap(nc.dram_tensor("wq_in", [D, D], F32, kind="ExternalInput"))
    wk_in = ap(nc.dram_tensor("wk_in", [D, 256], F32, kind="ExternalInput"))
    wv_in = ap(nc.dram_tensor("wv_in", [D, 256], F32, kind="ExternalInput"))
    wo_in = ap(nc.dram_tensor("wo_in", [D, D], F32, kind="ExternalInput"))
    bq16 = ap(nc.dram_tensor("bq16", [16, 64], F32, kind="ExternalInput"))
    bk4 = ap(nc.dram_tensor("bk4", [4, 64], F32, kind="ExternalInput"))
    bvrow_in = ap(nc.dram_tensor("bvrow_in", [1, 256], F32, kind="ExternalInput"))
    bo8 = ap(nc.dram_tensor("bo8", [8, 128], F32, kind="ExternalInput"))
    maskw_in = ap(nc.dram_tensor("maskw_in", [128, 4, 128], F32, kind="ExternalInput"))
    s16 = ap(nc.dram_tensor("s16", [8, 16, 128], F32, kind="ExternalInput"))
    identw = ap(nc.dram_tensor("identw", [128, 128], F32, kind="ExternalInput"))
    x3out = ap(nc.dram_tensor("x3out", [D, TOK], F32, kind="ExternalOutput"))
    dbgout = ap(nc.dram_tensor("dbgout", [16, 256], F32, kind="ExternalOutput"))

    with ExitStack() as ctx:
        sb = lambda name, shape, dt=F32: ctx.enter_context(nc.sbuf_tensor(name, shape, dt))
        pst = lambda name, shape, dt=F32: ctx.enter_context(nc.psum_tensor(name, shape, dt))
        p = P(nc)
        p.alias_groups = {
            "psS": [("psS", 0), ("psS", 1), ("psS", ("q", 0)), ("psS", ("q", 1)), ("psS", ("n", 0)), ("psS", ("n", 1))],
            "psO": [("psO", 0), ("psO", 1)],
            "sqt": [("sqt", 0), ("sqt", 1)],
        }

        xt = sb("xt", [128, NCH, 256])
        xht = sb("xht", [128, NCH, 256], BF16)
        sqt = sb("sqt", [128, NCH, 256])
        k_sb = sb("k_sb", [64, 4, SEQ], BF16)
        v_sb = sb("v_sb", [128, 64, 4, 65], BF16)
        q_t = sb("q_t", [64, 16, 128], BF16)
        e_sb = sb("e_sb", [128, 4096], BF16)
        ctxr = sb("ctxr", [128, NCH, 512], BF16)
        x2r = sb("x2r", [128, NCH, 512])
        yr = ctxr
        wq_sb = sb("wq_sb", [128, NCH, D], BF16)
        wk_sb = sb("wk_sb", [128, NCH, 256], BF16)
        wv_sb = sb("wv_sb", [128, NCH, 256], BF16)
        wo_sb = sb("wo_sb", [128, NCH, D], BF16)
        bq_c = sb("bq_c", [64, 16])
        bk_c = sb("bk_c", [64, 4])
        bo_c = sb("bo_c", [128, 8])
        bvrow = sb("bvrow", [1, 256], BF16)
        btmp = sb("btmp", [16, 128])
        maskw = sb("maskw", [128, 4, 128])
        s16_sb = sb("s16_sb", [16, NCH, 128], BF16)
        identf = sb("identf", [128, 128])
        negBc = sb("negBc", [128, 1])
        nB = sb("nB", [1, 4])
        negB = sb("negB", [1, 1], BF16)
        rowmax = sb("rowmax", [128, 1])
        n2acc = sb("n2acc", [128, 16])
        scr = sb("scr", [64, 4, 128], BF16)
        srow = sb("srow", [1, 512])
        sums_tm = sb("sums_tm", [128, 16])
        recip_hm = sb("recip_hm", [16, 512], BF16)
        ones1f = sb("ones1f", [1, 128])
        ones1b = sb("ones1b", [1, 128], BF16)
        onescol_f = sb("onescol_f", [128, 1])
        onescol_b = sb("onescol_b", [128, 1], BF16)
        rows = ln_rows(nc, ctx, "lnL", width=256)
        rows["onesf_col"] = onescol_f
        rows["konesf_col"] = "onescol_f"

        psS = pst("psS", [128, 2048])
        psO = pst("psO", [128, 2048])
        psA = psS.ap()[:, 0:1024]

        p.v(lambda e: e.memset(ones1f[:], 1.0), writes=["ones1f"])
        p.v(lambda e: e.memset(ones1b[:], 1.0), writes=["ones1b"])
        p.v(lambda e: e.memset(onescol_f[:], 1.0), writes=["onescol_f"])
        p.v(lambda e: e.memset(onescol_b[:], 1.0), writes=["onescol_b"])
        p.v(lambda e: e.memset(v_sb.ap()[:, :, :, 64:65], 1.0), writes=[("v_sb", "ones")])
        p.v(lambda e: e.memset(recip_hm[:], 0.0), writes=["recip_hm"])
        p.v(lambda e: e.memset(ctxr[:, :, :], 0.0), writes=["ctxr"])

        p.dma("sync", lambda e: e.dma_start(out=identf[:], in_=identw[:, :]), writes=["identf"])
        p.dma("sync", lambda e: e.dma_start(out=maskw[:], in_=maskw_in[:, :, :]), writes=["maskw"])
        p.dma("gpsimd", lambda e: e.dma_start(out=s16_sb[:], in_=s16.rearrange("c h t -> h c t")), writes=["s16_sb"])
        p.dma("gpsimd", lambda e: e.dma_start(out=bvrow[:], in_=bvrow_in[:, :]), writes=["bvrow"])
        for cc in range(NCH):
            p.dma("gpsimd", lambda e, cc=cc: e.dma_start(out=wq_sb[:, cc, :], in_=wq_in[128 * cc : 128 * cc + 128, :]), writes=["wq_sb"], tag=f"wq{cc}")
            p.dma("gpsimd", lambda e, cc=cc: e.dma_start(out=wo_sb[:, cc, :], in_=wo_in[128 * cc : 128 * cc + 128, :]), writes=["wo_sb"], tag=f"wo{cc}")
            p.dma("gpsimd", lambda e, cc=cc: e.dma_start(out=wk_sb[:, cc, :], in_=wk_in[128 * cc : 128 * cc + 128, :]), writes=["wk_sb"], tag=f"wk{cc}")
            p.dma("gpsimd", lambda e, cc=cc: e.dma_start(out=wv_sb[:, cc, :], in_=wv_in[128 * cc : 128 * cc + 128, :]), writes=["wv_sb"], tag=f"wv{cc}")
        p.dma("sync", lambda e: e.dma_start(out=btmp[0:16, 0:64], in_=bq16[:, :]), writes=["btmp"])
        p.t(lambda e: e.transpose(psS[0:64, 0:16], btmp[0:16, 0:64], identf[0:16, 0:16]), reads=["btmp", "identf"], writes=["psS"])
        p.v(lambda e: e.tensor_scalar_mul(bq_c[:], psS[0:64, 0:16], 0.125), reads=["psS"], writes=["bq_c"])
        p.dma("sync", lambda e: e.dma_start(out=btmp[0:4, 0:64], in_=bk4[:, :]), writes=["btmp"])
        p.t(lambda e: e.transpose(psS[0:64, 0:4], btmp[0:4, 0:64], identf[0:4, 0:4]), reads=["btmp", "identf"], writes=["psS"])
        p.v(lambda e: e.tensor_copy(bk_c[:], psS[0:64, 0:4]), reads=["psS"], writes=["bk_c"])
        p.dma("sync", lambda e: e.dma_start(out=btmp[0:8, :], in_=bo8[:, :]), writes=["btmp"])
        p.t(lambda e: e.transpose(psS[:, 0:8], btmp[0:8, :], identf[0:8, 0:8]), reads=["btmp", "identf"], writes=["psS"])
        p.v(lambda e: e.tensor_copy(bo_c[:], psS[:, 0:8]), reads=["psS"], writes=["bo_c"])

        for kt4 in range(32):
            p.dma(
                "sync",
                lambda e, kt4=kt4: e.dma_start(out=xt[:, :, :], in_=x2rot.rearrange("(c p) t -> p c t", p=128)[:, :, 256 * kt4 : 256 * kt4 + 256]),
                writes=["xt"],
                tag=f"xt{kt4}",
            )

            def apply_lnL(R, MR, rdeps):
                p.v(
                    lambda e: e.tensor_tensor(sqt[:, :, :], xt[:, :, :], R.unsqueeze(1).broadcast_to([128, NCH, 256]), ALU.mult),
                    reads=["xt", "psS"],
                    writes=["sqt"],
                )
                p.v(
                    lambda e: e.tensor_tensor(xht[:, :, :], sqt[:, :, :], MR.unsqueeze(1).broadcast_to([128, NCH, 256]), ALU.subtract),
                    reads=["sqt", "psS"],
                    writes=["xht"],
                )

            emit_ln_g(
                p,
                [(xt.ap()[:, c, :], "xt") for c in range(NCH)],
                256,
                [(sqt.ap()[:, :, :], xt.ap()[:, :, :], "xt", "sqt")],
                [(sqt.ap()[:, c, :], "sqt") for c in range(NCH)],
                rows, psA, "psS", onescol_f, "onescol_f", onescol_f, "onescol_f",
                ones1f, "ones1f", apply_lnL,
            )
            def mk_k(e, kt4=kt4):
                last = None
                for om in range(4):
                    for ci in range(NCH):
                        last = e.matmul(
                            psO[0:64, 256 * om : 256 * om + 256], wk_sb[:, ci, 64 * om : 64 * om + 64], xht[:, ci, :],
                            start=(ci == 0), stop=(ci == NCH - 1),
                        )
                return last

            p.t(mk_k, reads=["xht", "wk_sb"], writes=["psO"], tag=f"k{kt4}")
            for om in range(4):
                p.s(
                    lambda e, kt4=kt4, om=om: e.activation(
                        out=k_sb[:, om, 256 * kt4 : 256 * kt4 + 256], in_=psO[0:64, 256 * om : 256 * om + 256],
                        func=AF.Identity, bias=bk_c[:, om : om + 1],
                    ),
                    reads=["psO", "bk_c"],
                    writes=["k_sb"],
                )
            for sub in range(2):
                def mk_v(e, kt4=kt4, sub=sub):
                    for ci in range(NCH):
                        e.matmul(psS[:, 1024 + 256 * (sub % 2) : 1280 + 256 * (sub % 2)], xht[:, ci, 128 * sub : 128 * sub + 128], wv_sb[:, ci, :], start=(ci == 0), stop=False)
                    return e.matmul(psS[:, 1024 + 256 * (sub % 2) : 1280 + 256 * (sub % 2)], ones1b[:, 0:128], bvrow[:, :], start=False, stop=True)

                p.t(mk_v, reads=["xht", "wv_sb", "ones1b", "bvrow"], writes=[("psS", 1)], tag=f"v{kt4}_{sub}")
                p.v(
                    lambda e, kt4=kt4, sub=sub: e.tensor_copy(
                        v_sb.ap()[:, 2 * kt4 + sub, :, 0:64],
                        psS.ap()[:, 1024 + 256 * (sub % 2) : 1280 + 256 * (sub % 2)].rearrange("t (g d) -> t g d", g=4),
                    ),
                    reads=[("psS", 1)],
                    writes=[("v_sb", "dat")],
                )

        p.v(lambda e: e.memset(n2acc[:], 0.0), writes=["n2acc"])
        for it in range(8):
            p.dma(
                "sync",
                lambda e, it=it: e.dma_start(out=xt[:, :, :], in_=x2own.rearrange("(c p) t -> p c t", p=128)[:, :, 256 * it : 256 * it + 256]),
                writes=["xt"],
                tag=f"xtq{it}",
            )

            def apply_lnq(R, MR, rdeps):
                p.v(
                    lambda e: e.tensor_tensor(sqt[:, :, :], xt[:, :, :], R.unsqueeze(1).broadcast_to([128, NCH, 256]), ALU.mult),
                    reads=["xt", "psS"],
                    writes=["sqt"],
                )
                p.v(
                    lambda e: e.tensor_tensor(xht[:, :, :], sqt[:, :, :], MR.unsqueeze(1).broadcast_to([128, NCH, 256]), ALU.subtract),
                    reads=["sqt", "psS"],
                    writes=["xht"],
                )

            emit_ln_g(
                p,
                [(xt.ap()[:, c, :], "xt") for c in range(NCH)],
                256,
                [(sqt.ap()[:, :, :], xt.ap()[:, :, :], "xt", "sqt")],
                [(sqt.ap()[:, c, :], "sqt") for c in range(NCH)],
                rows, psA, "psS", onescol_f, "onescol_f", onescol_f, "onescol_f",
                ones1f, "ones1f", apply_lnq,
            )
            for om in range(0, 16, 2):
                par = (om // 2) % 2
                qc = 512 * par

                def mk_q(e, om=om, qc=qc):
                    last = None
                    for ci in range(NCH):
                        last = e.matmul(
                            psO[0:128, qc : qc + 256], wq_sb[:, ci, 64 * om : 64 * om + 128], xht[:, ci, :],
                            start=(ci == 0), stop=(ci == NCH - 1),
                        )
                    return last

                p.t(mk_q, reads=["xht", "wq_sb"], writes=[("psO", par)], tag=f"qn{it}_{om}")
                sc = 2 * par
                p.s(
                    lambda e, om=om, qc=qc, sc=sc: e.activation(
                        out=sqt[0:64, sc, 0:256], in_=psO[0:64, qc : qc + 256], func=AF.Square, scale=0.125, bias=bq_c[:, om : om + 1]
                    ),
                    reads=[("psO", par), "bq_c"],
                    writes=[("sqt", par)],
                )
                p.s(
                    lambda e, om=om, qc=qc, sc=sc: e.activation(
                        out=sqt[0:64, sc + 1, 0:256], in_=psO[64:128, qc : qc + 256], func=AF.Square, scale=0.125, bias=bq_c[:, om + 1 : om + 2]
                    ),
                    reads=[("psO", par), "bq_c"],
                    writes=[("sqt", par)],
                )

                def mk_n2(e, om=om, sc=sc, par=par):
                    last = None
                    base = 768 if par == 0 else 832
                    g4 = (om // 2) // 2
                    for hh in range(2):
                        for half in range(2):
                            col = base + 4 * g4 + 2 * hh + half
                            last = e.matmul(
                                psS[0:128, col : col + 1],
                                sqt[0:64, sc + hh, 128 * half : 128 * half + 128],
                                onescol_f[0:64, :],
                                start=True, stop=True,
                            )
                    return last

                p.t(mk_n2, reads=[("sqt", par), "onescol_f"], writes=[("psS", ("n", par))], tag=f"qn2_{it}_{om}")
            p.v(
                lambda e: e.tensor_tensor(n2acc[:, 0:16], n2acc[:, 0:16], psS[:, 768:784], ALU.max),
                reads=[("psS", ("n", 0)), "n2acc"],
                writes=["n2acc"],
            )
            p.v(
                lambda e: e.tensor_tensor(n2acc[:, 0:16], n2acc[:, 0:16], psS[:, 832:848], ALU.max),
                reads=[("psS", ("n", 1)), "n2acc"],
                writes=["n2acc"],
            )
        p.v(lambda e: e.reduce_max(rowmax[:], n2acc[:], axis=AX.X), reads=["n2acc"], writes=["rowmax"])
        p.t(lambda e: e.transpose(psS[0:1, 256:384], rowmax[:, :], identf[:, :]), reads=["rowmax", "identf"], writes=["psS"])
        p.v(lambda e: e.reduce_max(nB[:, 0:1], psS[0:1, 256:384], axis=AX.X), reads=["psS"], writes=["nB"])
        p.v(lambda e: e.memset(n2acc[:], 0.0), writes=["n2acc"])
        for kt in range(64):
            p.s(lambda e, kt=kt: e.activation(out=scr[:, :, :], in_=k_sb.ap()[:, :, 128 * kt : 128 * kt + 128], func=AF.Square), reads=["k_sb"], writes=["scr"])

            def mk_kn2(e):
                last = None
                for om in range(4):
                    last = e.matmul(psS[0:128, 128 + om : 129 + om], scr[:, om, :], onescol_b[0:64, :], start=True, stop=True)
                return last

            p.t(mk_kn2, reads=["scr", "onescol_b"], writes=["psS"], tag=f"kn2{kt}")
            p.v(lambda e: e.tensor_tensor(n2acc[:, 0:4], n2acc[:, 0:4], psS[:, 128:132], ALU.max), reads=["psS", "n2acc"], writes=["n2acc"])
        p.v(lambda e: e.reduce_max(rowmax[:], n2acc[:, 0:4], axis=AX.X), reads=["n2acc"], writes=["rowmax"])
        p.t(lambda e: e.transpose(psS[0:1, 256:384], rowmax[:, :], identf[:, :]), reads=["rowmax", "identf"], writes=["psS"])
        p.v(lambda e: e.reduce_max(nB[:, 1:2], psS[0:1, 256:384], axis=AX.X), reads=["psS"], writes=["nB"])
        p.s(lambda e: e.activation(out=nB[:, 0:2], in_=nB[:, 0:2], func=AF.Sqrt), reads=["nB"], writes=["nB"])
        p.v(lambda e: e.tensor_mul(nB[:, 2:3], nB[:, 0:1], nB[:, 1:2]), reads=["nB"], writes=["nB"])
        p.s(lambda e: e.activation(out=negB[:, :], in_=nB[:, 2:3], func=AF.Copy, scale=-1.0, bias=-1.0), reads=["nB"], writes=["negB"])
        p.t(lambda e: e.matmul(psS[:, 384:385], ones1b[:, 0:128], negB[:, :], start=True, stop=True), reads=["negB", "ones1b"], writes=["psS"])
        p.v(lambda e: e.tensor_copy(negBc[:], psS[:, 384:385]), reads=["psS"], writes=["negBc"])
        for r in range(int(os.environ.get("L1_REGIONS", "4"))):
            p.dma(
                "sync",
                lambda e, r=r: e.dma_start(out=x2r[:, :, :], in_=x2own.rearrange("(c p) t -> p c t", p=128)[:, :, 512 * r : 512 * r + 512]),
                writes=["x2r"],
                tag=f"x2r{r}",
            )
            for jj in range(int(os.environ.get("L1_QSUBS", "4"))):
                qs = 4 * r + jj
                skip2 = os.environ.get("L1_SKIP2", "") if qs == 1 else ""
                if "xt" not in skip2:
                    p.dma(
                        "sync",
                        lambda e, qs=qs: e.dma_start(out=xt[:, :, 0:128], in_=x2own.rearrange("(c p) t -> p c t", p=128)[:, :, 128 * qs : 128 * qs + 128]),
                        writes=["xt"],
                        tag=f"xtq2_{qs}",
                    )

                def apply_lnq2(R, MR, rdeps):
                    p.v(
                        lambda e: e.tensor_tensor(sqt[:, :, 0:128], xt[:, :, 0:128], R.unsqueeze(1).broadcast_to([128, NCH, 128]), ALU.mult),
                        reads=["xt", "psS"],
                        writes=["sqt"],
                    )
                    p.v(
                        lambda e: e.tensor_tensor(xht[:, :, 0:128], sqt[:, :, 0:128], MR.unsqueeze(1).broadcast_to([128, NCH, 128]), ALU.subtract),
                        reads=["sqt", "psS"],
                        writes=["xht"],
                    )

                if "ln" not in skip2:
                    emit_ln_g(
                        p,
                        [(xt.ap()[:, c, 0:128], "xt") for c in range(NCH)],
                        128,
                        [(sqt.ap()[:, :, 0:128], xt.ap()[:, :, 0:128], "xt", "sqt")],
                        [(sqt.ap()[:, c, 0:128], "sqt") for c in range(NCH)],
                        rows, psA, "psS", onescol_f, "onescol_f", onescol_f, "onescol_f",
                        ones1f, "ones1f", apply_lnq2,
                    )
                for om in range(0, 0 if "qp" in skip2 else 16, 2):
                    qoff = 1024 + 512 * ((om // 2) % 2)

                    def mk_q2(e, om=om, qoff=qoff):
                        last = None
                        for ci in range(NCH):
                            last = e.matmul(
                                psS[0:128, qoff : qoff + 128], wq_sb[:, ci, 64 * om : 64 * om + 128], xht[:, ci, 0:128],
                                start=(ci == 0), stop=(ci == NCH - 1),
                            )
                        return last

                    p.t(mk_q2, reads=["xht", "wq_sb"], writes=[("psS", ("q", (om // 2) % 2))], tag=f"q2{qs}_{om}")
                    p.s(
                        lambda e, om=om, qoff=qoff: e.activation(out=q_t[:, om, :], in_=psS[0:64, qoff : qoff + 128], func=AF.Identity, scale=0.125, bias=bq_c[:, om : om + 1]),
                        reads=[("psS", ("q", (om // 2) % 2)), "bq_c"],
                        writes=["q_t"],
                    )
                    p.s(
                        lambda e, om=om, qoff=qoff: e.activation(out=q_t[:, om + 1, :], in_=psS[64:128, qoff : qoff + 128], func=AF.Identity, scale=0.125, bias=bq_c[:, om + 1 : om + 2]),
                        reads=[("psS", ("q", (om // 2) % 2)), "bq_c"],
                        writes=["q_t"],
                    )
                if "ms" not in skip2:
                    p.v(lambda e: e.memset(psO[:, 0:2048], 0.0), writes=["psO"])
                NKT = 4 * qs + 4

                def mk_st(e, kt):
                    last = None
                    for g in range(4):
                        last = e.matmul(
                            psS[:, 512 * g : 512 * g + 512],
                            k_sb[:, g, 128 * kt : 128 * kt + 128],
                            q_t.ap()[:, 4 * g : 4 * g + 4, :].rearrange("p h t -> p (h t)"),
                            start=True, stop=True,
                        )
                    return last

                def emit_S(kt):
                    eo = 2048 * (kt % 2)
                    p.t(lambda e, kt=kt, f=mk_st: f(e, kt), reads=["k_sb", "q_t"], writes=["psS"], tag=f"st{qs}_{kt}")
                    p.s(
                        lambda e, kt=kt, eo=eo: e.activation(
                            out=e_sb[:, eo : eo + 2048], in_=psS[:, 0:2048], func=AF.Exp, bias=negBc[:, 0:1],
                        ),
                        reads=["psS", "negBc"],
                        writes=[("e_sb", kt % 2)],
                    )
                    if kt >= 4 * qs:
                        p.v(
                            lambda e, kt=kt, qs=qs, eo=eo: e.tensor_tensor(
                                e_sb.ap()[:, eo : eo + 2048].rearrange("a (h b) -> a h b", h=16),
                                e_sb.ap()[:, eo : eo + 2048].rearrange("a (h b) -> a h b", h=16),
                                maskw.ap()[:, kt - 4 * qs, :].unsqueeze(1).broadcast_to([128, 16, 128]),
                                ALU.mult,
                            ),
                            reads=[("e_sb", kt % 2), "maskw"],
                            writes=[("e_sb", kt % 2)],
                        )

                def emit_PV(kt):
                    eo = 2048 * (kt % 2)

                    def mk_pv(e, kt=kt, qs=qs, eo=eo):
                        last = None
                        for g in range(4):
                            last = e.matmul(
                                psO[0:65, 512 * g : 512 * g + 512],
                                v_sb.ap()[:, kt, g, :],
                                e_sb[:, eo + 512 * g : eo + 512 * g + 512],
                                start=(kt == 0), stop=(kt == 4 * qs + 3),
                            )
                        return last

                    p.t(mk_pv, reads=[("v_sb", "dat"), ("v_sb", "ones"), ("e_sb", kt % 2)], writes=["psO"], tag=f"pv{qs}_{kt}")

                emit_S(0)
                for kt in range(1, NKT):
                    emit_S(kt)
                    emit_PV(kt - 1)
                emit_PV(NKT - 1)
                if os.environ.get("L1_DBG") and r == 0 and jj == 0:
                    p.v(lambda e: e.tensor_copy(srow[:, :], psO[64:65, 0:512]), reads=["psO"], writes=["srow"])
                    p.dma(
                        "gpsimd",
                        lambda e: e.dma_start(out=dbgout[0:1, 160:176], in_=srow[0:1, 0:16]),
                        reads=["srow"],
                        writes=["dbgout"],
                    )
                for half in range(4):
                    p.v(
                        lambda e, half=half: e.tensor_copy(srow[:, :], psO[64:65, 512 * half : 512 * half + 512]),
                        reads=["psO"],
                        writes=["srow"],
                    )

                    def mk_str(e, half=half):
                        last = None
                        for h in range(4):
                            last = e.transpose(
                                psS[:, 1024 + 4 * half + h : 1025 + 4 * half + h], srow[0:1, 128 * h : 128 * h + 128], identf[0:1, 0:1]
                            )
                        return last

                    p.t(mk_str, reads=["srow", "identf"], writes=["psS"], tag=f"str{qs}_{half}")
                p.v(lambda e: e.tensor_copy(sums_tm[:, :], psS[:, 1024:1040]), reads=["psS"], writes=["sums_tm"])
                p.v(lambda e: e.reciprocal(sums_tm[:, :], sums_tm[:, :]), reads=["sums_tm"], writes=["sums_tm"])
                p.t(
                    lambda e: e.transpose(psS[0:16, 1024:1152], sums_tm[:, :], identf[:, :]),
                    reads=["sums_tm", "identf"],
                    writes=["psS"],
                )
                p.v(
                    lambda e, jj=jj: e.tensor_copy(recip_hm[:, 128 * jj : 128 * jj + 128], psS[0:16, 1024:1152]),
                    reads=["psS"],
                    writes=["recip_hm"],
                )
                if os.environ.get("L1_DBG") and r == 0 and jj == 0:
                    p.dma(
                        "gpsimd",
                        lambda e: e.dma_start(out=dbgout[0:16, 0:128], in_=recip_hm[:, 0:128]),
                        reads=["recip_hm"],
                        writes=["dbgout"],
                    )
                    p.dma(
                        "gpsimd",
                        lambda e: e.dma_start(out=dbgout[0:16, 128:144], in_=sums_tm[0:16, :]),
                        reads=["sums_tm"],
                        writes=["dbgout"],
                    )
                    p.dma(
                        "gpsimd",
                        lambda e: e.dma_start(out=dbgout[0:1, 144:160], in_=srow[0:1, 0:16]),
                        reads=["srow"],
                        writes=["dbgout"],
                    )
                p.v(
                    lambda e, jj=jj: e.tensor_copy(
                        ctxr.ap()[0:64, :, 128 * jj : 128 * jj + 128],
                        psO.ap()[0:64, 0:2048].rearrange("p (c two t) -> p c two t", two=2, t=128)[:, :, 0, :],
                    ),
                    reads=["psO"],
                    writes=["ctxr"],
                )
                p.v(
                    lambda e, jj=jj: e.tensor_copy(
                        ctxr.ap()[64:128, :, 128 * jj : 128 * jj + 128],
                        psO.ap()[0:64, 0:2048].rearrange("p (c two t) -> p c two t", two=2, t=128)[:, :, 1, :],
                    ),
                    reads=["psO"],
                    writes=["ctxr"],
                )
            for cch in range(NCH):
                p.t(
                    lambda e, cch=cch: e.matmul(psS[:, 0:512], s16_sb[:, cch, :], recip_hm[:, :], start=True, stop=True),
                    reads=["s16_sb", "recip_hm"],
                    writes=["psS"],
                    tag=f"rb{r}_{cch}",
                )
                p.v(
                    lambda e, cch=cch: e.tensor_tensor(ctxr[:, cch, :], ctxr[:, cch, :], psS[:, 0:512], ALU.mult),
                    reads=["ctxr", "psS"],
                    writes=["ctxr"],
                )
            for oj in range(NCH):
                def mk_y(e, oj=oj):
                    bank = psS if oj < 4 else psO
                    last = None
                    for ci in range(NCH):
                        last = e.matmul(
                            bank[:, 512 * (oj % 4) : 512 * (oj % 4) + 512], wo_sb[:, ci, 128 * oj : 128 * oj + 128], ctxr[:, ci, :],
                            start=(ci == 0), stop=(ci == NCH - 1),
                        )
                    return last

                p.t(mk_y, reads=["ctxr", "wo_sb"], writes=["psS" if oj < 4 else "psO"], tag=f"y{r}_{oj}")
            for oj in range(NCH):
                p.s(
                    lambda e, oj=oj: e.activation(
                        out=yr[:, oj, :], in_=(psS if oj < 4 else psO)[:, 512 * (oj % 4) : 512 * (oj % 4) + 512],
                        func=AF.Identity, bias=bo_c[:, oj : oj + 1],
                    ),
                    reads=["psS" if oj < 4 else "psO", "bo_c"],
                    writes=["ctxr"],
                )

            def apply_post_tile(R, MR, off, n):
                p.v(
                    lambda e, off=off, n=n: e.tensor_tensor(
                        sqt[:, :, 0:n], yr[:, :, off : off + n], R.unsqueeze(1).broadcast_to([128, NCH, n]), ALU.mult
                    ),
                    reads=["ctxr", "psS"],
                    writes=["sqt"],
                )
                p.v(
                    lambda e, off=off, n=n: e.tensor_tensor(
                        sqt[:, :, 0:n], sqt[:, :, 0:n], MR.unsqueeze(1).broadcast_to([128, NCH, n]), ALU.subtract
                    ),
                    reads=["sqt", "psS"],
                    writes=["sqt"],
                )
                p.v(
                    lambda e, off=off, n=n: e.tensor_tensor(
                        x2r[:, :, off : off + n], x2r[:, :, off : off + n], sqt[:, :, 0:n], ALU.add
                    ),
                    reads=["sqt", "x2r"],
                    writes=["x2r"],
                )

            for (off, n) in [(0, 128), (128, 128), (256, 128), (384, 128)]:
                emit_ln_g(
                    p,
                    [(yr.ap()[:, c, off : off + n], "ctxr") for c in range(NCH)],
                    n,
                    [(sqt.ap()[:, :, 0:128], yr.ap()[:, :, off : off + n], "ctxr", "sqt")],
                    [(sqt.ap()[:, c, 0:128], "sqt") for c in range(NCH)],
                    rows, psA, "psS", onescol_b, "onescol_b", onescol_f, "onescol_f",
                    ones1f, "ones1f",
                    lambda R, MR, rd, off=off, n=n: apply_post_tile(R, MR, off, n),
                )
            p.dma(
                "sync",
                lambda e, r=r: e.dma_start(
                    out=x3out.rearrange("(c p) t -> p c t", p=128)[:, :, 512 * r : 512 * r + 512], in_=x2r[:, :, :]
                ),
                reads=["x2r"],
                writes=[("x3out", r)],
                tag=f"x3o{r}",
            )

        p.finalize(ctx)
    return nc


def l1_in_maps(inputs, x2_full):
    s16, _ = build_selectors()
    ident = np.eye(128, dtype=np.float32)
    maps = []
    wq = np.ascontiguousarray(inputs["L1_attn_Wq"].astype(np.float32))
    wk = np.ascontiguousarray(inputs["L1_attn_Wk"].astype(np.float32))
    wv = np.ascontiguousarray(inputs["L1_attn_Wv"].astype(np.float32))
    wo = np.ascontiguousarray(inputs["L1_attn_Wo"].astype(np.float32))
    tri = (np.arange(128)[:, None] <= np.arange(128)[None, :]).astype(np.float32)
    for c in range(NCORES):
        b, j = c // 4, c % 4
        qtiles = [4 * i + j for i in range(16)]
        xo = x2_full[b].reshape(64, 128, D)
        x2own = fm(np.ascontiguousarray(xo[qtiles].reshape(TOK, D)))
        maskw = np.zeros((128, 4, 128), np.float32)
        for w in range(4):
            if w < j:
                maskw[:, w, :] = 1.0
            elif w == j:
                maskw[:, w, :] = tri
        maps.append(
            {
                "x2own": x2own,
                "x2rot": fm(np.ascontiguousarray(x2_full[b])),
                "wq_in": wq, "wk_in": wk, "wv_in": wv, "wo_in": wo,
                "bq16": inputs["L1_attn_bq"].astype(np.float32).reshape(16, 64),
                "bk4": inputs["L1_attn_bk"].astype(np.float32).reshape(4, 64),
                "bvrow_in": inputs["L1_attn_bv"].astype(np.float32).reshape(1, 256),
                "bo8": inputs["L1_attn_bo"].astype(np.float32).reshape(8, 128),
                "maskw_in": maskw,
                "s16": s16,
                "identw": ident,
            }
        )
    return maps


def run_launch_l1(inputs, x2_full):
    nc = build_launch_l1()
    maps = l1_in_maps(inputs, x2_full)
    res = run_bass_kernel_spmd(nc, maps, core_ids=list(range(NCORES)))
    out = np.zeros((2, SEQ, D), np.float32)
    for c in range(NCORES):
        b, j = c // 4, c % 4
        qtiles = [4 * i + j for i in range(16)]
        ov = out[b].reshape(64, 128, D)
        ov[qtiles] = res.results[c]["x3out"].T.reshape(16, 128, D)
    return out




def build_launch_d():
    nc = bass.Bass("TRN2", target_bir_lowering=False)
    ap = lambda t: t.ap()
    x3own = ap(nc.dram_tensor("x3own", [D, TOK], F32, kind="ExternalInput"))
    rw_in = ap(nc.dram_tensor("rw_in", [D, 8], F32, kind="ExternalInput"))
    rb_in = ap(nc.dram_tensor("rb_in", [1, 8], F32, kind="ExternalInput"))
    w1_in = ap(nc.dram_tensor("w1_in", [10, D, 512], F32, kind="ExternalInput"))
    w3_in = ap(nc.dram_tensor("w3_in", [10, D, 512], F32, kind="ExternalInput"))
    w2_in = ap(nc.dram_tensor("w2_in", [10, 512, D], F32, kind="ExternalInput"))
    identw = ap(nc.dram_tensor("identw", [128, 128], F32, kind="ExternalInput"))
    out_t = ap(nc.dram_tensor("out_t", [D, TOK], F32, kind="ExternalOutput"))

    with ExitStack() as ctx:
        sb = lambda name, shape, dt=F32: ctx.enter_context(nc.sbuf_tensor(name, shape, dt))
        pst = lambda name, shape, dt=F32: ctx.enter_context(nc.psum_tensor(name, shape, dt))
        p = P(nc)

        x3rs = sb("x3rs", [128, NCH, 512])
        xhA = [sb(f"xh{i}", [128, NCH, 512], BF16) for i in range(2)]
        yacA = [sb(f"yac{i}", [128, NCH, 512]) for i in range(2)]
        growA = [sb(f"grow{i}", [1, 4096], BF16) for i in range(2)]
        sq = sb("sq", [128, NCH, 512])
        h_sb = sb("h_sb", [128, 4, 512], BF16)
        scrab = sb("scrab", [128, 1024], BF16)
        rw_sb = sb("rw_sb", [128, NCH, 8], BF16)
        rb_c = sb("rb_c", [8, 1])
        lg_sb = sb("lg_sb", [8, 512])
        lg_tm = sb("lg_tm", [128, 8])
        mr8 = sb("mr8", [128, 8])
        repl = sb("repl", [128, 8])
        m2 = sb("m2", [128, 1])
        negm1 = sb("negm1", [128, 1])
        sel = sb("sel", [128, 8])
        evals = sb("evals", [128, 8])
        denom = sb("denom", [128, 1])
        w1t = [sb(f"w1t{i}", [128, NCH, 512], BF16) for i in range(2)]
        w3t = [sb(f"w3t{i}", [128, NCH, 512], BF16) for i in range(2)]
        w2t = [sb(f"w2t{i}", [128, 4, D], BF16) for i in range(2)]
        wstg = [sb(f"wstg{i}", [128, NCH, 512]) for i in range(2)]
        identf = sb("identf", [128, 128])
        ones1f = sb("ones1f", [1, 128])
        ones1b = sb("ones1b", [1, 128], BF16)
        onescol_f = sb("onescol_f", [128, 1])
        rows = ln_rows(nc, ctx, "lnD", width=512)
        rows["onesf_col"] = onescol_f
        rows["konesf_col"] = "onescol_f"

        ps1 = pst("ps1", [128, 1024])
        ps2 = pst("ps2", [128, 1024])
        psB = pst("psB", [128, 512])
        psA = pst("psA", [128, 1024])
        psT = pst("psT", [128, 512])

        p.v(lambda e: e.memset(ones1f[:], 1.0), writes=["ones1f"])
        p.v(lambda e: e.memset(ones1b[:], 1.0), writes=["ones1b"])
        p.v(lambda e: e.memset(onescol_f[:], 1.0), writes=["onescol_f"])
        p.dma("sync", lambda e: e.dma_start(out=identf[:], in_=identw[:, :]), writes=["identf"])
        p.dma("gpsimd", lambda e: e.dma_start(out=rw_sb[:], in_=rw_in.rearrange("(c p) m -> p c m", p=128)), writes=["rw_sb"])
        p.dma("sync", lambda e: e.dma_start(out=sq[0:1, 0, 0:8], in_=rb_in[:, :]), writes=["sq"])
        p.t(lambda e: e.transpose(psT[0:8, 0:1], sq.ap()[0:1, 0, 0:8], identf[0:1, 0:1]), reads=["sq", "identf"], writes=["psT"])
        p.v(lambda e: e.tensor_copy(rb_c[:], psT[0:8, 0:1]), reads=["psT"], writes=["rb_c"])

        for rp in range(2):
            for r2 in range(2):
                r = 2 * rp + r2
                B = 512 * r
                x3r, xh, grow = x3rs, xhA[r2], growA[r2]
                kx3, kxh, kgrow = "x3rs", f"xh{r2}", f"grow{r2}"
                p.dma(
                    "sync",
                    lambda e, B=B, x3r=x3r: e.dma_start(out=x3r[:, :, :], in_=x3own.rearrange("(c p) t -> p c t", p=128)[:, :, B : B + 512]),
                    writes=[kx3],
                    tag=f"x3r{r}",
                )

                def apply_lnD(R, MR, rdeps, x3r=x3r, xh=xh, kx3=kx3, kxh=kxh):
                    p.v(
                        lambda e: e.tensor_tensor(sq[:, :, :], x3r[:, :, :], R.unsqueeze(1).broadcast_to([128, NCH, 512]), ALU.mult),
                        reads=[kx3, "psA"],
                        writes=["sq"],
                    )
                    p.v(
                        lambda e: e.tensor_tensor(xh[:, :, :], sq[:, :, :], MR.unsqueeze(1).broadcast_to([128, NCH, 512]), ALU.subtract),
                        reads=["sq", "psA"],
                        writes=[kxh],
                    )

                emit_ln_g(
                    p,
                    [(x3r.ap()[:, c, :], kx3) for c in range(NCH)],
                    512,
                    [(sq.ap()[:, :, :], x3r.ap()[:, :, :], kx3, "sq")],
                    [(sq.ap()[:, c, :], "sq") for c in range(NCH)],
                    rows, psA, "psA", onescol_f, "onescol_f", onescol_f, "onescol_f",
                    ones1f, "ones1f", apply_lnD,
                )

                def mk_lg(e, xh=xh):
                    last = None
                    for ci in range(NCH):
                        last = e.matmul(psT[0:8, 0:512], rw_sb[:, ci, :], xh[:, ci, :], start=(ci == 0), stop=(ci == NCH - 1))
                    return last

                p.t(mk_lg, reads=[kxh, "rw_sb"], writes=["psT"], tag=f"lg{r}")
                p.s(
                    lambda e: e.activation(out=lg_sb[:, :], in_=psT[0:8, 0:512], func=AF.Identity, bias=rb_c[:, :]),
                    reads=["psT", "rb_c"],
                    writes=["lg_sb"],
                )
                for tt in range(4):
                    p.t(
                        lambda e, tt=tt: e.transpose(psT[:, 0:8], lg_sb[:, 128 * tt : 128 * tt + 128], identf[0:8, 0:8]),
                        reads=["lg_sb", "identf"],
                        writes=["psT"],
                    )
                    p.v(lambda e: e.tensor_copy(lg_tm[:, :], psT[:, 0:8]), reads=["psT"], writes=["lg_tm"])
                    p.v(lambda e: e.memset(mr8[:], 3.0e38), writes=["mr8"])
                    p.v(lambda e: e.reduce_max(mr8[:, 0:1], lg_tm[:, :], axis=AX.X), reads=["lg_tm"], writes=["mr8"])
                    p.v(lambda e: e.match_replace(repl[:, :], mr8[:, :], lg_tm[:, :], -1.0e30), reads=["mr8", "lg_tm"], writes=["repl"])
                    p.v(lambda e: e.reduce_max(m2[:, :], repl[:, :], axis=AX.X), reads=["repl"], writes=["m2"])
                    p.v(lambda e: e.tensor_scalar(sel[:, :], lg_tm[:, :], m2[:, :], None, ALU.is_ge), reads=["lg_tm", "m2"], writes=["sel"])
                    p.v(lambda e: e.reduce_max(negm1[:, :], lg_tm[:, :], axis=AX.X, negate=True), reads=["lg_tm"], writes=["negm1"])
                    p.s(
                        lambda e: e.activation(out=evals[:, :], in_=lg_tm[:, :], func=AF.Exp, bias=negm1[:, :]),
                        reads=["lg_tm", "negm1"],
                        writes=["evals"],
                    )
                    p.v(lambda e: e.tensor_mul(evals[:, :], evals[:, :], sel[:, :]), reads=["evals", "sel"], writes=["evals"])
                    p.v(lambda e: e.reduce_sum(denom[:, :], evals[:, :], axis=AX.X), reads=["evals"], writes=["denom"])
                    p.v(lambda e: e.reciprocal(denom[:, :], denom[:, :]), reads=["denom"], writes=["denom"])
                    p.v(lambda e: e.tensor_scalar(evals[:, :], evals[:, :], denom[:, :], None, ALU.mult), reads=["evals", "denom"], writes=["evals"])

                    def mk_gt4(e):
                        last = None
                        for ee in range(8):
                            last = e.transpose(psA[0:1, 128 * ee : 128 * ee + 128], evals[:, ee : ee + 1], identf[:, :])
                        return last

                    p.t(mk_gt4, reads=["evals", "identf"], writes=["psA"], tag=f"gt{r}_{tt}")
                    p.v(
                        lambda e, tt=tt, grow=grow: e.tensor_copy(
                            grow.ap()[:, :].rearrange("o (e t) -> o e t", e=8)[:, :, 128 * tt : 128 * tt + 128],
                            psA.ap()[0:1, 0:1024].rearrange("o (e t) -> o e t", e=8),
                        ),
                        reads=["psA"],
                        writes=[kgrow],
                    )
                p.v(lambda e, yac=yacA[r2]: e.memset(yac[:, :, :], 0.0), writes=[f"yac{r2}"])

            for ee in range(10):
                bb = ee % 2
                p.dma(
                    "gpsimd",
                    lambda e, ee=ee, bb=bb: e.dma_start(out=w1t[bb][:], in_=w1_in[ee].rearrange("(c p) m -> p c m", p=128)),
                    writes=[("w1t", bb)],
                    tag=f"w1l{rp}_{ee}",
                )
                p.dma(
                    "sync",
                    lambda e, ee=ee: e.dma_start(out=wstg[0][:], in_=w3_in[ee].rearrange("(c p) m -> p c m", p=128)),
                    writes=[("wstg", 0)],
                    tag=f"w3l{rp}_{ee}",
                )
                p.s(
                    lambda e, bb=bb: e.activation(out=w3t[bb][:], in_=wstg[0][:], func=AF.Copy),
                    reads=[("wstg", 0)],
                    writes=[("w3t", bb)],
                )
                p.dma(
                    "scalar",
                    lambda e, ee=ee: e.dma_start(
                        out=wstg[1].ap().rearrange("p c m -> p (c m)").rearrange("p (d n) -> p d n", d=4),
                        in_=w2_in[ee].rearrange("(c p) m -> p c m", p=128),
                    ),
                    writes=[("wstg", 1)],
                    tag=f"w2l{rp}_{ee}",
                )
                p.s(
                    lambda e, bb=bb: e.activation(
                        out=w2t[bb][:, :, :],
                        in_=wstg[1].ap().rearrange("p c m -> p (c m)").rearrange("p (d n) -> p d n", d=4),
                        func=AF.Copy,
                    ),
                    reads=[("wstg", 1)],
                    writes=[("w2t", bb)],
                )
                for r2 in range(2):
                    xh, grow, yac = xhA[r2], growA[r2], yacA[r2]
                    kxh, kgrow, kyac = f"xh{r2}", f"grow{r2}", f"yac{r2}"
                    for op2 in range(2):
                        def mk_a(e, op2=op2, bb=bb, xh=xh):
                            last = None
                            for of2 in range(2):
                                of = 2 * op2 + of2
                                for ci in range(NCH):
                                    last = e.matmul(
                                        ps1[:, 512 * of2 : 512 * of2 + 512], w1t[bb][:, ci, 128 * of : 128 * of + 128], xh[:, ci, :],
                                        start=(ci == 0), stop=(ci == NCH - 1),
                                    )
                            return last

                        def mk_b(e, op2=op2, bb=bb, xh=xh):
                            last = None
                            for of2 in range(2):
                                of = 2 * op2 + of2
                                for ci in range(NCH):
                                    last = e.matmul(
                                        ps2[:, 512 * of2 : 512 * of2 + 512], w3t[bb][:, ci, 128 * of : 128 * of + 128], xh[:, ci, :],
                                        start=(ci == 0), stop=(ci == NCH - 1),
                                    )
                            return last

                        p.t(mk_a, reads=[kxh, ("w1t", bb)], writes=["ps1"], tag=f"a{rp}_{ee}_{r2}_{op2}")
                        p.t(mk_b, reads=[kxh, ("w3t", bb)], writes=["ps2"], tag=f"b{rp}_{ee}_{r2}_{op2}")
                        p.s(
                            lambda e: e.activation(out=scrab[:, :], in_=ps1[:, 0:1024], func=AF.Silu),
                            reads=["ps1"],
                            writes=["scrab"],
                        )
                        p.v(
                            lambda e, op2=op2: e.tensor_tensor(
                                h_sb.ap()[:, 2 * op2 : 2 * op2 + 2, :],
                                scrab.ap()[:, :].rearrange("p (c t) -> p c t", c=2),
                                ps2.ap()[:, 0:1024].rearrange("p (c t) -> p c t", c=2),
                                ALU.mult,
                            ),
                            reads=["scrab", "ps2"],
                            writes=["h_sb"],
                        )
                    if ee < 8:
                        def mk_gb(e, ee=ee, grow=grow):
                            return e.matmul(psA[:, 0:512], ones1b[:, 0:128], grow.ap()[:, 512 * ee : 512 * ee + 512], start=True, stop=True)

                        p.t(mk_gb, reads=[kgrow, "ones1b"], writes=["psA"], tag=f"gb{rp}_{ee}_{r2}")
                        p.v(
                            lambda e: e.tensor_tensor(
                                h_sb.ap(), h_sb.ap(), psA.ap()[:, 0:512].unsqueeze(1).broadcast_to([128, 4, 512]), ALU.mult
                            ),
                            reads=["h_sb", "psA"],
                            writes=["h_sb"],
                        )
                    for oj in range(NCH):
                        bk = psB if oj % 2 == 0 else psT
                        bkk = "psB" if oj % 2 == 0 else "psT"

                        def mk_w2(e, oj=oj, bb=bb, bk=bk):
                            last = None
                            for ci in range(4):
                                last = e.matmul(
                                    bk[:, 0:512], w2t[bb][:, ci, 128 * oj : 128 * oj + 128], h_sb[:, ci, :],
                                    start=(ci == 0), stop=(ci == 3),
                                )
                            return last

                        p.t(mk_w2, reads=["h_sb", ("w2t", bb)], writes=[bkk], tag=f"w2{rp}_{ee}_{r2}_{oj}")
                        p.v(
                            lambda e, oj=oj, bk=bk, yac=yac: e.tensor_tensor(yac[:, oj, :], yac[:, oj, :], bk[:, 0:512], ALU.add),
                            reads=[kyac, bkk],
                            writes=[kyac],
                        )

            for r2 in range(2):
                r = 2 * rp + r2
                B = 512 * r
                x3r, yac = x3rs, yacA[r2]
                kx3, kyac = "x3rs", f"yac{r2}"

                p.dma(
                    "sync",
                    lambda e, B=B, x3r=x3r: e.dma_start(out=x3r[:, :, :], in_=x3own.rearrange("(c p) t -> p c t", p=128)[:, :, B : B + 512]),
                    writes=[kx3],
                    tag=f"x3rf{r}",
                )

                def apply_fin(R, MR, rdeps, x3r=x3r, yac=yac, kx3=kx3, kyac=kyac):
                    p.v(
                        lambda e: e.tensor_tensor(sq[:, :, :], yac[:, :, :], R.unsqueeze(1).broadcast_to([128, NCH, 512]), ALU.mult),
                        reads=[kyac, "psA"],
                        writes=["sq"],
                    )
                    p.v(
                        lambda e: e.tensor_tensor(sq[:, :, :], sq[:, :, :], MR.unsqueeze(1).broadcast_to([128, NCH, 512]), ALU.subtract),
                        reads=["sq", "psA"],
                        writes=["sq"],
                    )
                    p.v(
                        lambda e: e.tensor_tensor(x3r[:, :, :], x3r[:, :, :], sq[:, :, :], ALU.add),
                        reads=["sq", kx3],
                        writes=[kx3],
                    )

                emit_ln_g(
                    p,
                    [(yac.ap()[:, c, :], kyac) for c in range(NCH)],
                    512,
                    [(sq.ap()[:, :, :], yac.ap()[:, :, :], kyac, "sq")],
                    [(sq.ap()[:, c, :], "sq") for c in range(NCH)],
                    rows, psA, "psA", onescol_f, "onescol_f", onescol_f, "onescol_f",
                    ones1f, "ones1f", apply_fin,
                )
                p.dma(
                    "sync",
                    lambda e, B=B, x3r=x3r: e.dma_start(
                        out=out_t.rearrange("(c p) t -> p c t", p=128)[:, :, B : B + 512], in_=x3r[:, :, :]
                    ),
                    reads=[kx3],
                    writes=[("out_t", r)],
                    tag=f"o{r}",
                )

        p.finalize(ctx)
    return nc


def d_in_maps(inputs, x3_full):
    ident = np.eye(128, dtype=np.float32)
    w1 = np.concatenate([inputs["L1_ffn_routed_w1"], inputs["L1_ffn_shared_w1"]], axis=0).astype(np.float32)
    w3 = np.concatenate([inputs["L1_ffn_routed_w3"], inputs["L1_ffn_shared_w3"]], axis=0).astype(np.float32)
    w2 = np.concatenate([inputs["L1_ffn_routed_w2"], inputs["L1_ffn_shared_w2"]], axis=0).astype(np.float32)
    maps = []
    for c in range(NCORES):
        b, sown = c // 4, TOK * (c % 4)
        maps.append(
            {
                "x3own": fm(np.ascontiguousarray(x3_full[b, sown : sown + TOK])),
                "rw_in": np.ascontiguousarray(inputs["L1_ffn_router_w"].astype(np.float32)),
                "rb_in": inputs["L1_ffn_router_b"].astype(np.float32).reshape(1, 8),
                "w1_in": np.ascontiguousarray(w1),
                "w3_in": np.ascontiguousarray(w3),
                "w2_in": np.ascontiguousarray(w2),
                "identw": ident,
            }
        )
    return maps


def run_launch_d(inputs, x3_full):
    nc = build_launch_d()
    maps = d_in_maps(inputs, x3_full)
    res = run_bass_kernel_spmd(nc, maps, core_ids=list(range(NCORES)))
    out = np.zeros((2, SEQ, D), np.float32)
    for c in range(NCORES):
        b, sown = c // 4, TOK * (c % 4)
        out[b, sown : sown + TOK] = res.results[c]["out_t"].T
    return out


def build_xpos_tables(positions):
    half = np.arange(0, 64, 2, dtype=np.float64)
    inv_freq = 1.0 / 10000.0 ** (half / 64.0)
    t = positions.astype(np.float64)
    freqs = t[:, None] * inv_freq[None, :]
    freqs = np.concatenate([freqs, freqs], axis=1)
    base = (half + 0.4 * 64) / (1.4 * 64)
    power = (t - 8192 // 2) / 256.0
    scale = base[None, :] ** power[:, None]
    scale = np.concatenate([scale, scale], axis=1)
    cos, sin = np.cos(freqs), np.sin(freqs)
    return (
        np.ascontiguousarray((cos * scale).T.astype(np.float32)),
        np.ascontiguousarray((sin * scale).T.astype(np.float32)),
        np.ascontiguousarray((cos / scale).T.astype(np.float32)),
        np.ascontiguousarray((sin / scale).T.astype(np.float32)),
    )


def build_rot64():
    r = np.zeros((64, 64), np.float32)
    for m in range(32):
        r[m + 32, m] = -1.0
    for m in range(32, 64):
        r[m - 32, m] = 1.0
    return r


def build_tris():
    qq = np.arange(128)[:, None]
    kk = np.arange(128)[None, :]
    triL = np.where(kk < qq, -1.0e30, 0.0).astype(np.float32)
    triD = np.where(kk > qq, -1.0e30, 0.0).astype(np.float32)
    return np.stack([triL, triD])


def fa_in_maps(inputs, stream_full, l):
    wqkv = np.ascontiguousarray(inputs["L0_ffn_wqkv"][l].astype(np.float32))
    wo = np.ascontiguousarray(inputs["L0_ffn_wo"][l].astype(np.float32))
    qs = inputs["L0_ffn_q_scale"][l].astype(np.float32).reshape(1, 64)
    ks = inputs["L0_ffn_k_scale"][l].astype(np.float32).reshape(1, 64)
    rot = build_rot64()
    tris = build_tris()
    ident = np.eye(128, dtype=np.float32)
    maps = []
    for c in range(NCORES):
        b, sown = c // 4, TOK * (c % 4)
        lo = sown - FA_EXT
        fxe = np.zeros((FA_EXT + TOK, D), np.float32)
        a0 = max(lo, 0)
        fxe[a0 - lo :] = stream_full[b, a0 : sown + TOK]
        pos_own = np.arange(sown, sown + TOK)
        pos_ext = np.arange(lo, sown + TOK)
        tqc, tqs_, _, _ = build_xpos_tables(pos_own)
        _, _, tkc, tks_ = build_xpos_tables(pos_ext)
        dead = np.where(pos_ext < 0, -1.0e30, 0.0).astype(np.float32).reshape(1, -1)
        maps.append(
            {
                "fxe": fm(fxe), "wqkv": wqkv, "wo_in": wo, "qs_row": qs, "ks_row": ks,
                "tqc_in": tqc, "tqs_in": tqs_, "tkc_in": tkc, "tks_in": tks_,
                "dead_in": dead, "rot_in": rot, "tri_in": tris, "identw": ident,
            }
        )
    return maps


def fb_in_maps(inputs, xmid_full, l, x1_full=None):
    w1 = np.ascontiguousarray(inputs["L0_ffn_w1"][l].astype(np.float32))
    w2 = np.ascontiguousarray(np.pad(inputs["L0_ffn_w2"][l].astype(np.float32), ((0, 86), (0, 0))))
    b1 = inputs["L0_ffn_b1"][l].astype(np.float32)
    b2 = inputs["L0_ffn_b2"][l].astype(np.float32)
    lng = inputs["L0_ffn_ln_g"][l].astype(np.float32).reshape(1, D)
    lnb = inputs["L0_ffn_ln_b"][l].astype(np.float32).reshape(1, D)
    ident = np.eye(128, dtype=np.float32)
    b1a = np.pad(b1[0:2730], (0, 86)).reshape(22, 128)
    b1g = np.pad(b1[2730:5460], (0, 86)).reshape(22, 128)
    maps = []
    for c in range(NCORES):
        b, sown = c // 4, TOK * (c % 4)
        xme = np.zeros((TOK + 1, D), np.float32)
        a0 = max(sown - 1, 0)
        xme[a0 - (sown - 1) :] = xmid_full[b, a0 : sown + TOK]
        m = {
            "xme": fm(xme), "w1_in": w1, "w2_in": w2, "b1a_in": b1a, "b1g_in": b1g,
            "b2_in": b2.reshape(8, 128), "lng_in": lng, "lnb_in": lnb, "identw": ident,
            "neg1_in": np.full((1, 544), -1.0, np.float32),
        }
        if x1_full is not None:
            m["x1own"] = fm(np.ascontiguousarray(x1_full[b, sown : sown + TOK]))
        maps.append(m)
    return maps


def run_launch_fa(inputs, stream_full, l):
    nc = build_launch_fa(l == 0)
    maps = fa_in_maps(inputs, stream_full, l)
    res = run_bass_kernel_spmd(nc, maps, core_ids=list(range(NCORES)))
    out = np.zeros((2, SEQ, D), np.float32)
    for c in range(NCORES):
        b, sown = c // 4, TOK * (c % 4)
        out[b, sown : sown + TOK] = res.results[c]["xmid_out"].T
    return out


def run_launch_fb(inputs, xmid_full, l, x1_full=None):
    nc = build_launch_fb(x1_full is not None)
    maps = fb_in_maps(inputs, xmid_full, l, x1_full)
    res = run_bass_kernel_spmd(nc, maps, core_ids=list(range(NCORES)))
    out = np.zeros((2, SEQ, D), np.float32)
    for c in range(NCORES):
        b, sown = c // 4, TOK * (c % 4)
        out[b, sown : sown + TOK] = res.results[c]["out_t"].T
    return out


def kernel(**inputs):
    _ = np.asarray(inputs["L0_attn_token_ids"])
    x0 = np.asarray(inputs["L0_attn_input_embeddings"], np.float32)
    x1 = run_launch_a(inputs, x0)
    xm0 = run_launch_fa(inputs, x1, 0)
    xn0 = run_launch_fb(inputs, xm0, 0)
    xm1 = run_launch_fa(inputs, xn0, 1)
    x2 = run_launch_fb(inputs, xm1, 1, x1_full=x1)
    x3 = run_launch_l1(inputs, x2)
    out = run_launch_d(inputs, x3)
    return out


# version 13 (final)
# speedup vs baseline: 1.0124x; 1.0071x over previous
import os
import numpy as np
import ml_dtypes
from contextlib import ExitStack

KA_STAGE = int(os.environ.get("KA_STAGE", "99"))
KA_SUB = int(os.environ.get("KA_SUB", "99"))
KA_TRUNC = int(os.environ.get("KA_TRUNC", "0"))
KA_SBN = int(os.environ.get("KA_SBN", "16"))

import concourse.bass as bass
import concourse.mybir as mybir
from concourse.bass_utils import run_bass_kernel_spmd

F32 = mybir.dt.float32
BF16 = mybir.dt.bfloat16
AF = mybir.ActivationFunctionType
ALU = mybir.AluOpType
AX = mybir.AxisListType

NCORES = 8
D = 1024
NCH = 8
TOK = 2048
SEQ = 8192
BF = ml_dtypes.bfloat16

ENGS = ("tensor", "vector", "scalar", "gpsimd", "sync")


class P:

    RING = 4

    def __init__(self, nc):
        self.nc = nc
        self.ev = {e: [] for e in ENGS}
        self.cnt = {e: 0 for e in ENGS}
        self.dmacnt = {e: 0 for e in ENGS}
        self.semnames = list(ENGS) + [f"{e}D{i}" for e in ("sync", "gpsimd", "scalar", "vector", "tensor") for i in range(self.RING)]
        self.semtot = {n: 0 for n in self.semnames}
        self.lastw = {}
        self.readers = {}

    def _dep(self, need, tok):
        sem, val = tok
        if val > need.get(sem, 0):
            need[sem] = val

    def op(self, eng, emit, reads=(), writes=(), inc=1, tag="", dma=False):
        ag = getattr(self, "alias_groups", None)
        if ag:
            def _ex(keys):
                out = []
                for k in keys:
                    out.extend(ag.get(k, [k]))
                return out
            reads = _ex(list(reads))
            writes = _ex(list(writes))
        etag0 = ("dma", eng, self.dmacnt[eng]) if dma else eng
        need = {}
        for b in reads:
            for (sem, val, weng) in self.lastw.get(b, ()):
                if weng == etag0 == "tensor":
                    continue
                if weng != etag0:
                    self._dep(need, (sem, val))
                elif etag0 != "tensor":
                    self._dep(need, (sem, val))
        for b in writes:
            for (sem, val, weng) in self.lastw.get(b, ()):
                if weng == etag0 == "tensor":
                    continue
                self._dep(need, (sem, val))
            for (sem, val, reng) in self.readers.get(b, ()):
                if reng == etag0 == "tensor":
                    continue
                self._dep(need, (sem, val))
        if dma:
            k = self.dmacnt[eng]
            self.dmacnt[eng] = k + 1
            sem = f"{eng}D{k % self.RING}"
            occ = k // self.RING
            if occ > 0:
                self._dep(need, (sem, 16 * occ))
            myval = 16 * (occ + 1)
            etag = ("dma", eng, k)
        else:
            sem = eng
            self.cnt[eng] += inc
            myval = self.cnt[eng]
            etag = eng
        self.semtot[sem] = max(self.semtot[sem], myval)
        self.seq = getattr(self, "seq", 0) + 1
        self.ev[eng].append((emit, need, sem, 16 if dma else inc, tag, self.seq))
        tok = (sem, myval, etag)
        for b in reads:
            self.readers.setdefault(b, []).append(tok)
        for b in writes:
            self.lastw.setdefault(b, []).append(tok)
            self.readers[b] = []
        return tok

    def t(self, emit, **kw):
        return self.op("tensor", emit, **kw)

    def v(self, emit, **kw):
        return self.op("vector", emit, **kw)

    def s(self, emit, **kw):
        return self.op("scalar", emit, **kw)

    def g(self, emit, **kw):
        return self.op("gpsimd", emit, **kw)

    def dma(self, eng, emit, reads=(), writes=(), tag=""):
        return self.op(eng, emit, reads=reads, writes=writes, tag=tag, dma=True)

    def finalize(self, ctx):
        if os.environ.get("KA_PRINT"):
            print("TOTAL_EVENTS:", getattr(self, "seq", 0))
        if KA_TRUNC > 0:
            for e in ENGS:
                self.ev[e] = [ev for ev in self.ev[e] if ev[5] <= KA_TRUNC]
            self.semtot = {n: 0 for n in self.semnames}
            for e in ENGS:
                for ev in self.ev[e]:
                    sem, inc = ev[2], ev[3]
                    if sem == e:
                        self.semtot[sem] = max(self.semtot[sem], ev[1].get(sem, 0))
            self.semtot = {n: 0 for n in self.semnames}
            for e in ENGS:
                cnt = {}
                for ev in self.ev[e]:
                    sem = ev[2]
                    cnt[sem] = cnt.get(sem, 0) + ev[3]
                for snm, v in cnt.items():
                    self.semtot[snm] = self.semtot.get(snm, 0) + v
        nc = self.nc
        sems = {n: ctx.enter_context(nc.semaphore(n)) for n in self.semnames}
        block = ctx.enter_context(nc.Block())
        totals = dict(self.semtot)
        evs = self.ev

        def runner(ename, engobj):
            done = {n: 0 for n in self.semnames}
            for (emit, need, incsem, inc, tag, _seq) in evs[ename]:
                for n, c2 in need.items():
                    if c2 > done[n]:
                        engobj.wait_ge(sems[n], c2)
                        done[n] = c2
                inst = emit(engobj)
                inst.then_inc(sems[incsem], inc)
            if ename == "sync":
                for n in self.semnames:
                    if totals[n] > done.get(n, 0):
                        engobj.wait_ge(sems[n], totals[n])

        @block.tensor
        def _(e):
            runner("tensor", e)

        @block.vector
        def _(e):
            runner("vector", e)

        @block.scalar
        def _(e):
            runner("scalar", e)

        @block.gpsimd
        def _(e):
            runner("gpsimd", e)

        @block.sync
        def _(e):
            runner("sync", e)


def ln_rows(nc, ctx, prefix, width=544):
    sb = lambda name, shape: ctx.enter_context(nc.sbuf_tensor(name, shape, F32))
    t = {}
    for nm in ("mu", "rstd", "murs", "var"):
        t[nm] = sb(f"{prefix}_{nm}", [1, width])
        t["k" + nm] = f"{prefix}_{nm}"
    t["eps"] = sb(f"{prefix}_eps", [1, 1])
    t["keps"] = f"{prefix}_eps"
    return t


def emit_ln(p, x_ap, xkey, n_tok, sq_ap, sqkey, rows, psA, psAkey, onescol, onescolkey, ones1f, ones1fkey, out_emit, d_model=1024, eps=1e-5):
    nch = x_ap.shape[1]

    def mk_sum(e):
        last = None
        for c in range(nch):
            last = e.matmul(psA[0:1, 0:n_tok], onescol[:, :], x_ap[:, c, :], start=(c == 0), stop=(c == nch - 1))
        return last

    p.t(mk_sum, reads=[xkey, onescolkey], writes=[psAkey])
    p.s(lambda e: e.activation(out=sq_ap, in_=x_ap, func=AF.Square), reads=[xkey], writes=[sqkey])

    def mk_sum2(e):
        last = None
        for c in range(nch):
            last = e.matmul(
                psA[0:1, 512 : 512 + n_tok], onescol_f_of(rows), sq_ap[:, c, :], start=(c == 0), stop=(c == nch - 1)
            )
        return last

    p.t(mk_sum2, reads=[sqkey, rows["konesf_col"]], writes=[psAkey])
    mu, rstd, murs, var = rows["mu"], rows["rstd"], rows["murs"], rows["var"]
    kmu, krstd, kmurs, kvar = rows["kmu"], rows["krstd"], rows["kmurs"], rows["kvar"]
    sc = 1.0 / d_model
    p.s(lambda e: e.activation(out=mu[:, 0:n_tok], in_=psA[0:1, 0:n_tok], func=AF.Copy, scale=sc), reads=[psAkey], writes=[kmu])
    p.s(lambda e: e.activation(out=var[:, 0:n_tok], in_=mu[:, 0:n_tok], func=AF.Square), reads=[kmu], writes=[kvar])
    p.v(
        lambda e: e.scalar_tensor_tensor(var[:, 0:n_tok], psA[0:1, 512 : 512 + n_tok], sc, var[:, 0:n_tok], ALU.mult, ALU.subtract),
        reads=[psAkey, kvar],
        writes=[kvar],
    )
    p.v(lambda e: e.memset(rows["eps"][:], float(eps)), writes=[rows["keps"]])
    p.s(lambda e: e.activation(out=rstd[:, 0:n_tok], in_=var[:, 0:n_tok], func=AF.Sqrt, bias=rows["eps"][:, :]), reads=[kvar, rows["keps"]], writes=[krstd])
    p.v(lambda e: e.reciprocal(rstd[:, 0:n_tok], rstd[:, 0:n_tok]), reads=[krstd], writes=[krstd])
    p.v(lambda e: e.tensor_mul(murs[:, 0:n_tok], mu[:, 0:n_tok], rstd[:, 0:n_tok]), reads=[kmu, krstd], writes=[kmurs])
    p.t(
        lambda e: e.matmul(psA[:, 0:n_tok], ones1f[:, 0:128], rstd[:, 0:n_tok], start=True, stop=True),
        reads=[krstd, ones1fkey],
        writes=[psAkey],
    )
    p.t(
        lambda e: e.matmul(psA[:, 512 : 512 + n_tok], ones1f[:, 0:128], murs[:, 0:n_tok], start=True, stop=True),
        reads=[kmurs, ones1fkey],
        writes=[psAkey],
    )
    out_emit(psA[:, 0:n_tok], psA[:, 512 : 512 + n_tok], [psAkey, psAkey])


def onescol_f_of(rows):
    return rows["onesf_col"][:, :]


KEXT = 2176
NREG = 4
REG = 512


def build_launch_a():
    nc = bass.Bass("TRN2", target_bir_lowering=False)
    ap = lambda t: t.ap()
    x0e = ap(nc.dram_tensor("x0e", [D, KEXT], F32, kind="ExternalInput"))
    wqkvT = ap(nc.dram_tensor("wqkvT", [D, 3 * D], F32, kind="ExternalInput"))
    woutT = ap(nc.dram_tensor("woutT", [D, D], F32, kind="ExternalInput"))
    bqkv24 = ap(nc.dram_tensor("bqkv24", [24, 128], F32, kind="ExternalInput"))
    bqkv_row = ap(nc.dram_tensor("bqkv_row", [1, 3 * D], F32, kind="ExternalInput"))
    bout8 = ap(nc.dram_tensor("bout8", [8, 128], F32, kind="ExternalInput"))
    maskA = ap(nc.dram_tensor("maskA", [16, 128, 128], F32, kind="ExternalInput"))
    maskB = ap(nc.dram_tensor("maskB", [16, 16, 16], F32, kind="ExternalInput"))
    s16 = ap(nc.dram_tensor("s16", [8, 16, 128], F32, kind="ExternalInput"))
    hs16 = ap(nc.dram_tensor("hs16", [8, 128, 16], F32, kind="ExternalInput"))
    identw = ap(nc.dram_tensor("identw", [128, 128], F32, kind="ExternalInput"))
    x1out = ap(nc.dram_tensor("x1out", [D, TOK], F32, kind="ExternalOutput"))

    with ExitStack() as ctx:
        sb = lambda name, shape, dt=F32: ctx.enter_context(nc.sbuf_tensor(name, shape, dt))
        pst = lambda name, shape: ctx.enter_context(nc.psum_tensor(name, shape, F32))
        p = P(nc)

        x0r = sb("x0r", [128, NCH, 544])
        sq = sb("sq", [128, NCH, 544])
        xh = sb("xh", [128, NCH, 544], BF16)
        wq = sb("wq", [128, NCH, 3 * D], BF16)
        wo = sb("wo", [128, NCH, D], BF16)
        bq64 = sb("bq64", [64, 48])
        bout_c = sb("bout_c", [128, 8])
        b48 = sb("b48", [48, 64])
        b8 = sb("b8", [8, 128])
        bvrow = sb("bvrow", [1, D], BF16)
        identf = sb("identf", [128, 128])
        maskA_sb = sb("maskA_sb", [128, 16, 128], BF16)
        maskB_sb = sb("maskB_sb", [16, 16, 16])
        s16_sb = sb("s16_sb", [16, NCH, 128])

        qr = sb("qr", [64, 16, REG], BF16)
        kr = sb("kr", [64, 16, 528], BF16)
        scr = sb("scr", [64, 4, 528])
        vr = sb("vr", [128, 5, 16, 64], BF16)
        eA = sb("eA", [128, 2048], BF16)
        eB = sb("eB", [16, 256], BF16)
        ctxr = sb("ctxr", [128, NCH, REG], BF16)

        yr = sb("yr", [128, NCH, REG], BF16)
        rowmax = sb("rowmax", [128, 1])
        nB = sb("nB", [1, 4])
        negB = sb("negB", [1, 1], BF16)
        negBc = sb("negBc", [128, 1])
        sums_tm = sb("sums_tm", [128, 16])
        bsums_tm = sb("bsums_tm", [16, 16])
        shm = sb("shm", [16, 128])
        recip_hm = sb("recip_hm", [16, REG])
        ones1f = sb("ones1f", [1, 128])
        ones1b = sb("ones1b", [1, 128], BF16)
        onescol_f = sb("onescol_f", [128, 1])
        onescol_b = sb("onescol_b", [128, 1], BF16)
        rows = ln_rows(nc, ctx, "lnA")
        rows2 = rows
        for rr in (rows,):
            rr["onesf_col"] = onescol_f
            rr["konesf_col"] = "onescol_f"

        psO = pst("psO", [128, 2048])
        psA = pst("psA", [128, 1024])
        ps1 = pst("ps1", [128, 512])
        ps2 = pst("ps2", [128, 512])

        p.v(lambda e: e.memset(ones1f[:], 1.0), writes=["ones1f"])
        p.v(lambda e: e.memset(ones1b[:], 1.0), writes=["ones1b"])
        p.v(lambda e: e.memset(onescol_f[:], 1.0), writes=["onescol_f"])
        p.v(lambda e: e.memset(onescol_b[:], 1.0), writes=["onescol_b"])

        p.dma("sync", lambda e: e.dma_start(out=b48[:], in_=bqkv24.rearrange("a (b c) -> (a b) c", b=2)), writes=["b48"])
        p.dma("sync", lambda e: e.dma_start(out=b8[:], in_=bout8[:, :]), writes=["b8"])
        p.dma("gpsimd", lambda e: e.dma_start(out=bvrow[:], in_=bqkv_row[:, 2048:3072]), writes=["bvrow"])
        p.dma("sync", lambda e: e.dma_start(out=identf[:], in_=identw[:, :]), writes=["identf"])
        p.dma("gpsimd", lambda e: e.dma_start(out=maskA_sb[:], in_=maskA.rearrange("j p t -> p j t")), writes=["maskA_sb"])
        p.dma("sync", lambda e: e.dma_start(out=maskB_sb[:], in_=maskB.rearrange("j a b -> a j b")), writes=["maskB_sb"])
        p.dma("sync", lambda e: e.dma_start(out=s16_sb[:], in_=s16.rearrange("c h t -> h c t")), writes=["s16_sb"])

        for cc in range(NCH):
            p.dma(
                "gpsimd",
                lambda e, cc=cc: e.dma_start(out=wq[:, cc, :], in_=wqkvT[128 * cc : 128 * cc + 128, :]),
                writes=["wq"],
                tag=f"wq{cc}",
            )
            p.dma(
                "gpsimd",
                lambda e, cc=cc: e.dma_start(out=wo[:, cc, :], in_=woutT[128 * cc : 128 * cc + 128, :]),
                writes=["wo"],
                tag=f"wo{cc}",
            )
        p.t(lambda e: e.transpose(ps1[0:64, 0:48], b48[:, :], identf[0:48, 0:48]), reads=["b48", "identf"], writes=["ps1"])
        p.v(lambda e: e.tensor_copy(bq64[:], ps1[0:64, 0:48]), reads=["ps1"], writes=["bq64"])
        p.v(lambda e: e.tensor_scalar_mul(bq64[:, 0:16], bq64[:, 0:16], 0.125), reads=["bq64"], writes=["bq64"])
        p.t(lambda e: e.transpose(ps1[:, 0:8], b8[:, :], identf[0:8, 0:8]), reads=["b8", "identf"], writes=["ps1"])
        p.v(lambda e: e.tensor_copy(bout_c[:], ps1[:, 0:8]), reads=["ps1"], writes=["bout_c"])

        for r in range(NREG if KA_STAGE >= 8 else 1):
            r0 = r * REG

            p.dma(
                "sync",
                lambda e, r0=r0: e.dma_start(
                    out=x0r[:, :, :], in_=x0e.rearrange("(c p) t -> p c t", p=128)[:, :, r0 : r0 + 544]
                ),
                writes=["x0r"],
                tag=f"x0r{r}",
            )

            if KA_STAGE < 2:
                continue

            def apply_ln(R, MR, rdeps, off, n):
                p.v(
                    lambda e: e.tensor_tensor(
                        sq[:, :, off : off + n], x0r[:, :, off : off + n],
                        R.unsqueeze(1).broadcast_to([128, NCH, n]), ALU.mult,
                    ),
                    reads=["x0r"] + rdeps[:1],
                    writes=["sq"],
                )
                p.v(
                    lambda e: e.tensor_tensor(
                        xh[:, :, off : off + n], sq[:, :, off : off + n],
                        MR.unsqueeze(1).broadcast_to([128, NCH, n]), ALU.subtract,
                    ),
                    reads=["sq"] + rdeps[1:],
                    writes=["xh"],
                )

            for (off, n) in [(0, 512), (512, 32)]:
                emit_ln(
                    p, x0r.ap()[:, :, off : off + n], "x0r", n, sq.ap()[:, :, off : off + n], "sq",
                    rows, psA, "psA", onescol_f, "onescol_f", ones1f, "ones1f",
                    lambda R, MR, rd, off=off, n=n: apply_ln(R, MR, rd, off, n),
                )

            for om in range(0, 16 if KA_STAGE >= 3 else 0, 2):
                bk = ps1 if (om // 2) % 2 == 0 else ps2
                bkk = "ps1" if (om // 2) % 2 == 0 else "ps2"

                def mk_q(e, om=om, bk=bk):
                    last = None
                    for ci in range(NCH):
                        last = e.matmul(
                            bk[0:128, 0:512], wq[:, ci, 64 * om : 64 * om + 128], xh[:, ci, 8:520],
                            start=(ci == 0), stop=(ci == NCH - 1),
                        )
                    return last

                p.t(mk_q, reads=["xh", "wq"], writes=[bkk], tag=f"q{r}_{om}")
                p.s(
                    lambda e, om=om, bk=bk: e.activation(out=qr[:, om, :], in_=bk[0:64, 0:512], func=AF.Identity, scale=0.125, bias=bq64[:, om : om + 1]),
                    reads=[bkk, "bq64"],
                    writes=["qr"],
                )
                p.s(
                    lambda e, om=om, bk=bk: e.activation(out=qr[:, om + 1, :], in_=bk[64:128, 0:512], func=AF.Identity, scale=0.125, bias=bq64[:, om + 1 : om + 2]),
                    reads=[bkk, "bq64"],
                    writes=["qr"],
                )
            for om in range(0, 16 if KA_STAGE >= 3 else 0, 2):
                for ti, (toff, tn) in enumerate([(0, 512), (512, 16)]):
                    bk = ps1 if (om // 2 + ti) % 2 == 0 else ps2
                    bkk = "ps1" if (om // 2 + ti) % 2 == 0 else "ps2"

                    def mk_k(e, om=om, toff=toff, tn=tn, bk=bk):
                        last = None
                        for ci in range(NCH):
                            last = e.matmul(
                                bk[0:128, 0:tn], wq[:, ci, 1024 + 64 * om : 1024 + 64 * om + 128], xh[:, ci, toff : toff + tn],
                                start=(ci == 0), stop=(ci == NCH - 1),
                            )
                        return last

                    p.t(mk_k, reads=["xh", "wq"], writes=[bkk], tag=f"k{r}_{om}_{toff}")
                    p.s(
                        lambda e, om=om, toff=toff, tn=tn, bk=bk: e.activation(
                            out=kr[:, om, toff : toff + tn], in_=bk[0:64, 0:tn], func=AF.Identity, bias=bq64[:, 16 + om : 17 + om]
                        ),
                        reads=[bkk, "bq64"],
                        writes=["kr"],
                    )
                    p.s(
                        lambda e, om=om, toff=toff, tn=tn, bk=bk: e.activation(
                            out=kr[:, om + 1, toff : toff + tn], in_=bk[64:128, 0:tn], func=AF.Identity, bias=bq64[:, 17 + om : 18 + om]
                        ),
                        reads=[bkk, "bq64"],
                        writes=["kr"],
                    )
            for vt in range(5 if KA_STAGE >= 4 else 0):
                tn = 128 if vt < 4 else 16
                toff = vt * 128
                for ofj in range(2):
                    bk = ps1 if (2 * vt + ofj) % 2 == 0 else ps2
                    bkk = "ps1" if (2 * vt + ofj) % 2 == 0 else "ps2"

                    def mk_v(e, vt=vt, tn=tn, toff=toff, ofj=ofj, bk=bk):
                        for ci in range(NCH):
                            e.matmul(
                                bk[0:tn, 0:512], xh[:, ci, toff : toff + tn],
                                wq[:, ci, 2048 + 512 * ofj : 2048 + 512 * ofj + 512],
                                start=(ci == 0), stop=False,
                            )
                        return e.matmul(
                            bk[0:tn, 0:512], ones1b[:, 0:tn], bvrow[:, 512 * ofj : 512 * ofj + 512],
                            start=False, stop=True,
                        )

                    p.t(mk_v, reads=["xh", "wq", "ones1b", "bvrow"], writes=[bkk], tag=f"v{r}_{vt}_{ofj}")
                    p.v(
                        lambda e, vt=vt, tn=tn, ofj=ofj, bk=bk: e.tensor_copy(
                            vr.ap()[0:tn, vt, 8 * ofj : 8 * ofj + 8, :],
                            bk.ap()[0:tn, 0:512].rearrange("t (h d) -> t h d", h=8),
                        ),
                        reads=[bkk],
                        writes=["vr"],
                    )

            def emit_norm2(src, srckey, ncols, ntiles, into_col):
                for hg in range(4):
                    p.s(
                        lambda e, hg=hg: e.activation(
                            out=scr[:, :, 0:ncols],
                            in_=src[0:64, 4 * hg : 4 * hg + 4, 0:ncols],
                            func=AF.Square,
                        ),
                        reads=[srckey],
                        writes=["scr"],
                    )

                    def mk_n2(e, hg=hg):
                        last = None
                        for hh in range(4):
                            for tt in range(ntiles):
                                tn = 128 if tt < 4 else 16
                                cc = 20 * (4 * hg + hh) + tt
                                last = e.matmul(
                                    ps2[0:tn, cc : cc + 1],
                                    scr[:, hh, 128 * tt : 128 * tt + tn],
                                    onescol_f[0:64, :],
                                    start=True, stop=True,
                                )
                        return last

                    p.t(mk_n2, reads=["scr", "onescol_f"], writes=["ps2"], tag=f"n2_{r}_{hg}_{into_col}")
                p.v(lambda e: e.reduce_max(rowmax[:], ps2[:, 0:320], axis=AX.X), reads=["ps2"], writes=["rowmax"])
                p.t(
                    lambda e: e.transpose(ps2[0:1, 0:128], rowmax[:, :], identf[:, :]),
                    reads=["rowmax", "identf"],
                    writes=["ps2"],
                )
                p.v(
                    lambda e, into_col=into_col: e.reduce_max(nB[:, into_col : into_col + 1], ps2[0:1, 0:128], axis=AX.X),
                    reads=["ps2"],
                    writes=["nB"],
                )

            if KA_STAGE < 5:
                continue
            p.v(lambda e: e.memset(ps2[:, 0:320], 0.0), writes=["ps2"])
            emit_norm2(qr.ap(), "qr", 512, 4, 0)
            p.v(lambda e: e.memset(ps2[:, 0:320], 0.0), writes=["ps2"])
            emit_norm2(kr.ap(), "kr", 528, 5, 1)
            p.s(lambda e: e.activation(out=nB[:, 0:2], in_=nB[:, 0:2], func=AF.Sqrt), reads=["nB"], writes=["nB"])
            p.v(lambda e: e.tensor_mul(nB[:, 2:3], nB[:, 0:1], nB[:, 1:2]), reads=["nB"], writes=["nB"])
            p.s(
                lambda e: e.activation(out=negB[:, :], in_=nB[:, 2:3], func=AF.Copy, scale=-1.0, bias=-1.0),
                reads=["nB"],
                writes=["negB"],
            )
            p.t(
                lambda e: e.matmul(ps2[:, 0:1], ones1b[:, 0:128], negB[:, :], start=True, stop=True),
                reads=["negB", "ones1b"],
                writes=["ps2"],
            )
            p.v(lambda e: e.tensor_copy(negBc[:], ps2[:, 0:1]), reads=["ps2"], writes=["negBc"])

            for j in range(4 if KA_STAGE >= 6 else 0):
                jq = r * 4 + j
                q0 = 128 * j
                k0 = 128 * j

                def mk_sb(e, q0=q0, k0=k0):
                    last = None
                    for h in range(16):
                        last = e.matmul(
                            ps2[0:16, 16 * h : 16 * h + 16],
                            kr[:, h, k0 + 128 : k0 + 144],
                            qr[:, h, q0 + 112 : q0 + 128],
                            start=True, stop=True,
                        )
                    return last

                p.t(mk_sb, reads=["kr", "qr"], writes=["ps2"], tag=f"sB{r}_{j}")
                if KA_SUB < 2:
                    continue
                p.s(
                    lambda e: e.activation(out=eB[:, :], in_=ps2[0:16, 0:256], func=AF.Exp, bias=negBc[0:16, :]),
                    reads=["ps2", "negBc"],
                    writes=["eB"],
                )
                p.v(
                    lambda e, jq=jq: e.tensor_tensor(
                        eB.ap().rearrange("a (h b) -> a h b", h=16),
                        eB.ap().rearrange("a (h b) -> a h b", h=16),
                        maskB_sb.ap()[:, jq, :].unsqueeze(1).broadcast_to([16, 16, 16]),
                        ALU.mult,
                    ),
                    reads=["eB", "maskB_sb"],
                    writes=["eB"],
                )
                if KA_SUB < 3:
                    continue
                for gg in range(2):
                    def mk_sa(e, gg=gg, q0=q0, k0=k0):
                        last = None
                        for hh in range(8):
                            h = 8 * gg + hh
                            last = e.matmul(
                                psA[:, 128 * hh : 128 * hh + 128],
                                kr[:, h, k0 : k0 + 128],
                                qr[:, h, q0 : q0 + 128],
                                start=True, stop=True,
                            )
                        return last

                    p.t(mk_sa, reads=["kr", "qr"], writes=["psA"], tag=f"sA{r}_{j}_{gg}")
                    p.s(
                        lambda e, gg=gg: e.activation(
                            out=eA[:, 1024 * gg : 1024 * gg + 1024], in_=psA[:, 0:1024], func=AF.Exp, bias=negBc[:, :]
                        ),
                        reads=["psA", "negBc"],
                        writes=["eA"],
                    )
                p.v(
                    lambda e, jq=jq: e.tensor_tensor(
                        eA.ap().rearrange("a (h b) -> a h b", h=16),
                        eA.ap().rearrange("a (h b) -> a h b", h=16),
                        maskA_sb.ap()[:, jq, :].unsqueeze(1).broadcast_to([128, 16, 128]),
                        ALU.mult,
                    ),
                    reads=["eA", "maskA_sb"],
                    writes=["eA"],
                )

                if KA_SUB < 4:
                    continue

                def mk_pv(e, j=j):
                    last = None
                    for h in range(16):
                        e.matmul(
                            psO[0:64, 128 * h : 128 * h + 128], vr.ap()[:, j, h, :], eA[:, 128 * h : 128 * h + 128],
                            start=True, stop=False,
                        )
                        last = e.matmul(
                            psO[0:64, 128 * h + 112 : 128 * h + 128], vr.ap()[0:16, j + 1, h, :], eB[:, 16 * h : 16 * h + 16],
                            start=False, stop=True,
                        )
                    return last

                p.t(mk_pv, reads=["vr", "eA", "eB"], writes=["psO"], tag=f"pv{r}_{j}")

                if KA_SUB < 5:
                    continue

                def mk_sums(e):
                    last = None
                    for h in range(16):
                        last = e.matmul(
                            ps2[:, 16 + h : 17 + h], eA[:, 128 * h : 128 * h + 128], onescol_b[:, :], start=True, stop=True
                        )
                    for h in range(16):
                        last = e.matmul(
                            ps2[0:16, 32 + h : 33 + h], eB[:, 16 * h : 16 * h + 16], onescol_b[0:16, :], start=True, stop=True
                        )
                    return last

                p.t(mk_sums, reads=["eA", "eB", "onescol_b"], writes=["ps2"], tag=f"sums{r}_{j}")
                p.v(lambda e: e.tensor_copy(sums_tm[:], ps2[:, 16:32]), reads=["ps2"], writes=["sums_tm"])
                p.v(lambda e: e.tensor_copy(bsums_tm[:], ps2[0:16, 32:48]), reads=["ps2"], writes=["bsums_tm"])
                p.t(
                    lambda e: e.transpose(ps2[0:16, 64:192], sums_tm[:, :], identf[:, :]),
                    reads=["sums_tm", "identf"],
                    writes=["ps2"],
                )
                p.v(lambda e: e.tensor_copy(shm[:, :], ps2[0:16, 64:192]), reads=["ps2"], writes=["shm"])
                p.t(
                    lambda e: e.transpose(ps2[0:16, 192:208], bsums_tm[:, :], identf[0:16, 0:16]),
                    reads=["bsums_tm", "identf"],
                    writes=["ps2"],
                )
                p.v(
                    lambda e: e.tensor_tensor(shm[:, 112:128], shm[:, 112:128], ps2[0:16, 192:208], ALU.add),
                    reads=["ps2", "shm"],
                    writes=["shm"],
                )
                p.v(
                    lambda e, q0=q0: e.reciprocal(recip_hm[:, q0 : q0 + 128], shm[:, :]),
                    reads=["shm"],
                    writes=["recip_hm"],
                )
                if KA_SUB < 6:
                    continue
                p.v(
                    lambda e, q0=q0: e.tensor_copy(
                        ctxr.ap()[0:64, :, q0 : q0 + 128],
                        psO.ap()[0:64, 0:2048].rearrange("p (c two t) -> p c two t", two=2, t=128)[:, :, 0, :],
                    ),
                    reads=["psO"],
                    writes=["ctxr"],
                )
                p.v(
                    lambda e, q0=q0: e.tensor_copy(
                        ctxr.ap()[64:128, :, q0 : q0 + 128],
                        psO.ap()[0:64, 0:2048].rearrange("p (c two t) -> p c two t", two=2, t=128)[:, :, 1, :],
                    ),
                    reads=["psO"],
                    writes=["ctxr"],
                )

            if KA_STAGE < 7:
                continue
            for cch in range(NCH):
                p.t(
                    lambda e, cch=cch: e.matmul(psA[:, 0:512], s16_sb[:, cch, :], recip_hm[:, :], start=True, stop=True),
                    reads=["s16_sb", "recip_hm"],
                    writes=["psA"],
                    tag=f"rb{r}_{cch}",
                )
                p.v(
                    lambda e, cch=cch: e.tensor_tensor(ctxr[:, cch, :], ctxr[:, cch, :], psA[:, 0:512], ALU.mult),
                    reads=["ctxr", "psA"],
                    writes=["ctxr"],
                )

            for oj in range(NCH):
                def mk_y(e, oj=oj):
                    last = None
                    for ci in range(NCH):
                        last = e.matmul(
                            ps1[:, 0:512], wo[:, ci, 128 * oj : 128 * oj + 128], ctxr[:, ci, :],
                            start=(ci == 0), stop=(ci == NCH - 1),
                        )
                    return last

                p.t(mk_y, reads=["ctxr", "wo"], writes=["ps1"], tag=f"y{r}_{oj}")
                p.s(
                    lambda e, oj=oj: e.activation(out=yr[:, oj, :], in_=ps1[:, 0:512], func=AF.Identity, bias=bout_c[:, oj : oj + 1]),
                    reads=["ps1", "bout_c"],
                    writes=["yr"],
                )

            def apply_post(R, MR, rdeps):
                p.v(
                    lambda e: e.tensor_tensor(
                        sq[:, :, 0:512], yr[:, :, :], R.unsqueeze(1).broadcast_to([128, NCH, 512]), ALU.mult
                    ),
                    reads=["yr"] + rdeps[:1],
                    writes=["sq"],
                )
                p.v(
                    lambda e: e.tensor_tensor(
                        sq[:, :, 0:512], sq[:, :, 0:512], MR.unsqueeze(1).broadcast_to([128, NCH, 512]), ALU.subtract
                    ),
                    reads=["sq"] + rdeps[1:],
                    writes=["sq"],
                )
                p.v(
                    lambda e: e.tensor_tensor(x0r[:, :, 8:520], x0r[:, :, 8:520], sq[:, :, 0:512], ALU.add),
                    reads=["sq", "x0r"],
                    writes=["x0r"],
                )

            emit_ln(
                p, yr.ap()[:, :, :], "yr", 512, sq.ap()[:, :, 0:512], "sq",
                rows2, psA, "psA", onescol_b, "onescol_b", ones1f, "ones1f", apply_post,
            )

            p.dma(
                "sync",
                lambda e, r0=r0: e.dma_start(
                    out=x1out.rearrange("(c p) t -> p c t", p=128)[:, :, r0 : r0 + 512], in_=x0r[:, :, 8:520]
                ),
                reads=["x0r"],
                writes=[("x1out", r)],
                tag=f"x1out{r}",
            )

        p.finalize(ctx)
    return nc





def emit_ln_g(p, x_chunks, n_tok, square_jobs, sq_chunks, rows, psA, psAkey,
              onescol, onescolkey, onescol_f, onescol_f_key, ones1f, ones1fkey,
              out_emit, d_model=1024, eps=1e-5):
    assert n_tok <= 512

    def mk_sum(e):
        last = None
        n = len(x_chunks)
        for i, (ap, _k) in enumerate(x_chunks):
            last = e.matmul(psA[0:1, 0:n_tok], onescol[:, :], ap, start=(i == 0), stop=(i == n - 1))
        return last

    p.t(mk_sum, reads=[k for _, k in x_chunks] + [onescolkey], writes=[psAkey])
    for (oap, iap, rkey, wkey) in square_jobs:
        p.s(lambda e, oap=oap, iap=iap: e.activation(out=oap, in_=iap, func=AF.Square), reads=[rkey], writes=[wkey])

    def mk_sum2(e):
        last = None
        n = len(sq_chunks)
        for i, (ap, _k) in enumerate(sq_chunks):
            last = e.matmul(psA[0:1, 512 : 512 + n_tok], onescol_f[:, :], ap, start=(i == 0), stop=(i == n - 1))
        return last

    p.t(mk_sum2, reads=[k for _, k in sq_chunks] + [onescol_f_key], writes=[psAkey])
    mu, rstd, murs, var = rows["mu"], rows["rstd"], rows["murs"], rows["var"]
    kmu, krstd, kmurs, kvar = rows["kmu"], rows["krstd"], rows["kmurs"], rows["kvar"]
    sc = 1.0 / d_model
    p.s(lambda e: e.activation(out=mu[:, 0:n_tok], in_=psA[0:1, 0:n_tok], func=AF.Copy, scale=sc), reads=[psAkey], writes=[kmu])
    p.s(lambda e: e.activation(out=var[:, 0:n_tok], in_=mu[:, 0:n_tok], func=AF.Square), reads=[kmu], writes=[kvar])
    p.v(
        lambda e: e.scalar_tensor_tensor(var[:, 0:n_tok], psA[0:1, 512 : 512 + n_tok], sc, var[:, 0:n_tok], ALU.mult, ALU.subtract),
        reads=[psAkey, kvar],
        writes=[kvar],
    )
    p.v(lambda e: e.memset(rows["eps"][:], float(eps)), writes=[rows["keps"]])
    p.s(lambda e: e.activation(out=rstd[:, 0:n_tok], in_=var[:, 0:n_tok], func=AF.Sqrt, bias=rows["eps"][:, :]), reads=[kvar, rows["keps"]], writes=[krstd])
    p.v(lambda e: e.reciprocal(rstd[:, 0:n_tok], rstd[:, 0:n_tok]), reads=[krstd], writes=[krstd])
    p.v(lambda e: e.tensor_mul(murs[:, 0:n_tok], mu[:, 0:n_tok], rstd[:, 0:n_tok]), reads=[kmu, krstd], writes=[kmurs])
    p.t(
        lambda e: e.matmul(psA[:, 0:n_tok], ones1f[:, 0:128], rstd[:, 0:n_tok], start=True, stop=True),
        reads=[krstd, ones1fkey],
        writes=[psAkey],
    )
    p.t(
        lambda e: e.matmul(psA[:, 512 : 512 + n_tok], ones1f[:, 0:128], murs[:, 0:n_tok], start=True, stop=True),
        reads=[kmurs, ones1fkey],
        writes=[psAkey],
    )
    out_emit(psA[:, 0:n_tok], psA[:, 512 : 512 + n_tok], [psAkey, psAkey])


FA_EXT = 520


def build_launch_fa(entry_ln):
    nc = bass.Bass("TRN2", target_bir_lowering=False)
    ap = lambda t: t.ap()
    fxe = ap(nc.dram_tensor("fxe", [D, FA_EXT + TOK], F32, kind="ExternalInput"))
    wqkv = ap(nc.dram_tensor("wqkv", [D, 1536], F32, kind="ExternalInput"))
    wo_in = ap(nc.dram_tensor("wo_in", [512, D], F32, kind="ExternalInput"))
    qs_row = ap(nc.dram_tensor("qs_row", [1, 64], F32, kind="ExternalInput"))
    ks_row = ap(nc.dram_tensor("ks_row", [1, 64], F32, kind="ExternalInput"))
    tqc_in = ap(nc.dram_tensor("tqc_in", [64, TOK], F32, kind="ExternalInput"))
    tqs_in = ap(nc.dram_tensor("tqs_in", [64, TOK], F32, kind="ExternalInput"))
    tkc_in = ap(nc.dram_tensor("tkc_in", [64, FA_EXT + TOK], F32, kind="ExternalInput"))
    tks_in = ap(nc.dram_tensor("tks_in", [64, FA_EXT + TOK], F32, kind="ExternalInput"))
    dead_in = ap(nc.dram_tensor("dead_in", [1, FA_EXT + TOK], F32, kind="ExternalInput"))
    rot_in = ap(nc.dram_tensor("rot_in", [64, 64], F32, kind="ExternalInput"))
    tri_in = ap(nc.dram_tensor("tri_in", [2, 128, 128], F32, kind="ExternalInput"))
    identw = ap(nc.dram_tensor("identw", [128, 128], F32, kind="ExternalInput"))
    xmid_out = ap(nc.dram_tensor("xmid_out", [D, TOK], F32, kind="ExternalOutput"))

    with ExitStack() as ctx:
        sb = lambda name, shape, dt=F32: ctx.enter_context(nc.sbuf_tensor(name, shape, dt))
        pst = lambda name, shape, dt=F32: ctx.enter_context(nc.psum_tensor(name, shape, dt))
        p = P(nc)
        p.alias_groups = {"psS": [("psS", 0), ("psS", 1)], "sq": [("sq", 0), ("sq", 1), ("sq", "P2"), ("sq", "PT2")]}

        fxr = sb("fxr", [128, NCH, 1032])
        fh = sb("fh", [128, NCH, 1032], BF16)
        sq = sb("sq", [128, NCH, 512])
        wq_sb = sb("wq_sb", [128, NCH, 1536], BF16)
        wo_sb = sb("wo_sb", [128, 4, 1024], BF16)
        tqc = sb("tqc", [64, 512])
        tqs = sb("tqs", [64, 512])
        tkc = sb("tkc", [64, 1024])
        tks = sb("tks", [64, 1024])
        dead_r = sb("dead_r", [1, 1024])
        rot64 = sb("rot64", [64, 64], BF16)
        triL = sb("triL", [128, 128])
        triD = sb("triD", [128, 128])
        identf = sb("identf", [128, 128])
        identb = sb("identb", [128, 128], BF16)
        qs_col = sb("qs_col", [64, 1])
        ks_col = sb("ks_col", [64, 1])
        qraw = sb("qraw", [64, 8, 512], BF16)
        kraw = sb("kraw", [64, 8, 1024], BF16)
        qfin = sb("qfin", [65, 8, 512], BF16)
        kfin = sb("kfin", [65, 8, 1024], BF16)
        scr = sb("scr", [1, 128])
        u_sbR = [sb(f"u_sbR{i}", [64, 1024], BF16) for i in range(2)]
        rrowR = [sb(f"rrowR{i}", [1, 1024]) for i in range(2)]
        vr = sb("vr", [128, 8, 8, 64], BF16)
        P_sb = sb("P_sb", [128, 4, 640], BF16)
        PT_sb = sb("PT_sb", [128, 2560], BF16)
        negm = sb("negm", [128, 4])
        ssum = sb("ssum", [128, 4])
        negmB = sb("negmB", [128, 4])
        ssumB = sb("ssumB", [128, 4])
        o_fm = sb("o_fm", [128, 4, 512], BF16)
        ones1f = sb("ones1f", [1, 128])
        onescol_f = sb("onescol_f", [128, 1])
        onescol_b = sb("onescol_b", [128, 1], BF16)
        rows = ln_rows(nc, ctx, "lnF", width=512)
        rows["onesf_col"] = onescol_f
        rows["konesf_col"] = "onescol_f"

        psS = pst("psS", [128, 2560])
        psPT = pst("psPT", [128, 2048], BF16)
        psO = pst("psO", [128, 512])
        psA = psS.ap()[:, 0:1024]
        psN = psS.ap()[:, 1024:2048]

        p.v(lambda e: e.memset(ones1f[:], 1.0), writes=["ones1f"])
        p.v(lambda e: e.memset(onescol_f[:], 1.0), writes=["onescol_f"])
        p.v(lambda e: e.memset(onescol_b[:], 1.0), writes=["onescol_b"])
        p.v(lambda e: e.memset(qfin.ap()[64:65, :, :], 1.0), writes=[("qfin", "ones")])

        p.dma("sync", lambda e: e.dma_start(out=identf[:], in_=identw[:, :]), writes=["identf"])
        p.dma("gpsimd", lambda e: e.dma_start(out=identb[:], in_=identw[:, :]), writes=["identb"])
        p.dma("gpsimd", lambda e: e.dma_start(out=rot64[:], in_=rot_in[:, :]), writes=["rot64"])
        p.dma("sync", lambda e: e.dma_start(out=triL[:], in_=tri_in[0, :, :]), writes=["triL"])
        p.dma("sync", lambda e: e.dma_start(out=triD[:], in_=tri_in[1, :, :]), writes=["triD"])

        for cc in range(NCH):
            p.dma("gpsimd", lambda e, cc=cc: e.dma_start(out=wq_sb[:, cc, :], in_=wqkv[128 * cc : 128 * cc + 128, :]), writes=["wq_sb"], tag=f"wq{cc}")
        for cc in range(4):
            p.dma("gpsimd", lambda e, cc=cc: e.dma_start(out=wo_sb[:, cc, :], in_=wo_in[128 * cc : 128 * cc + 128, :]), writes=["wo_sb"], tag=f"wo{cc}")
        p.dma("sync", lambda e: e.dma_start(out=scr[0:1, 0:64], in_=qs_row[:, :]), writes=["scr"])
        p.dma("sync", lambda e: e.dma_start(out=scr[0:1, 64:128], in_=ks_row[:, :]), writes=["scr"])
        p.t(lambda e: e.transpose(psN[0:64, 0:1], scr[0:1, 0:64], identf[0:1, 0:1]), reads=["scr", "identf"], writes=["psS"])
        p.v(lambda e: e.tensor_scalar_mul(qs_col[:], psN[0:64, 0:1], 8.0), reads=["psS"], writes=["qs_col"])
        p.t(lambda e: e.transpose(psN[0:64, 0:1], scr[0:1, 64:128], identf[0:1, 0:1]), reads=["scr", "identf"], writes=["psS"])
        p.v(lambda e: e.tensor_copy(ks_col[:], psN[0:64, 0:1]), reads=["psS"], writes=["ks_col"])

        for r in range(4):
            B = 512 * r
            c0 = B

            p.dma(
                "sync",
                lambda e, c0=c0: e.dma_start(out=fxr[:, :, :], in_=fxe.rearrange("(c p) t -> p c t", p=128)[:, :, c0 : c0 + 1032]),
                writes=["fxr"],
                tag=f"fxr{r}",
            )
            p.dma("sync", lambda e, B=B: e.dma_start(out=tqc[:, :], in_=tqc_in[:, B : B + 512]), writes=["tqc"], tag=f"tqc{r}")
            p.dma("sync", lambda e, B=B: e.dma_start(out=tqs[:, :], in_=tqs_in[:, B : B + 512]), writes=["tqs"], tag=f"tqs{r}")
            p.dma("sync", lambda e, c0=c0: e.dma_start(out=tkc[:, :], in_=tkc_in[:, c0 + 8 : c0 + 1032]), writes=["tkc"], tag=f"tkc{r}")
            p.dma("sync", lambda e, c0=c0: e.dma_start(out=tks[:, :], in_=tks_in[:, c0 + 8 : c0 + 1032]), writes=["tks"], tag=f"tks{r}")
            p.dma("sync", lambda e, c0=c0: e.dma_start(out=dead_r[:, :], in_=dead_in[:, c0 + 8 : c0 + 1032]), writes=["dead_r"], tag=f"dead{r}")

            if entry_ln:
                def apply_lnF(R, MR, rdeps, off, n):
                    p.v(
                        lambda e: e.tensor_tensor(
                            sq[:, :, 0:n], fxr[:, :, off : off + n], R.unsqueeze(1).broadcast_to([128, NCH, n]), ALU.mult
                        ),
                        reads=["fxr", "psS"],
                        writes=["sq"],
                    )
                    p.v(
                        lambda e: e.tensor_tensor(
                            fxr[:, :, off : off + n], sq[:, :, 0:n], MR.unsqueeze(1).broadcast_to([128, NCH, n]), ALU.subtract
                        ),
                        reads=["sq", "psS"],
                        writes=["fxr"],
                    )

                for (off, n) in [(0, 512), (512, 512), (1024, 8)]:
                    emit_ln_g(
                        p,
                        [(fxr.ap()[:, c, off : off + n], "fxr") for c in range(NCH)],
                        n,
                        [(sq.ap()[:, :, 0:n], fxr.ap()[:, :, off : off + n], "fxr", "sq")],
                        [(sq.ap()[:, c, 0:n], "sq") for c in range(NCH)],
                        rows, psA, "psS", onescol_f, "onescol_f", onescol_f, "onescol_f",
                        ones1f, "ones1f",
                        lambda R, MR, rd, off=off, n=n: apply_lnF(R, MR, rd, off, n),
                    )
            p.v(lambda e: e.tensor_copy(fh[:, :, :], fxr[:, :, :]), reads=["fxr"], writes=["fh"])

            def hap(c, a, n):
                off = a if c < 4 else a - 1
                return fh.ap()[:, c, off : off + n]

            for vt in range(8):
                def mk_v(e, vt=vt):
                    last = None
                    for ci in range(NCH):
                        last = e.matmul(
                            psS[0:128, 0:512], hap(ci, 8 + 128 * vt, 128), wq_sb[:, ci, 1024:1536],
                            start=(ci == 0), stop=(ci == NCH - 1),
                        )
                    return last

                p.t(mk_v, reads=["fh", "wq_sb"], writes=["psS"], tag=f"v{r}_{vt}")
                p.v(
                    lambda e, vt=vt: e.tensor_copy(
                        vr.ap()[:, vt, :, :], psS.ap()[0:128, 0:512].rearrange("t (h d) -> t h d", h=8)
                    ),
                    reads=["psS"],
                    writes=["vr"],
                )
            for om in range(0, 8, 2):
                def mk_qr(e, om=om):
                    last = None
                    for ci in range(NCH):
                        last = e.matmul(
                            psN[0:128, 0:512], wq_sb[:, ci, 64 * om : 64 * om + 128], hap(ci, 520, 512),
                            start=(ci == 0), stop=(ci == NCH - 1),
                        )
                    return last

                p.t(mk_qr, reads=["fh", "wq_sb"], writes=["psS"], tag=f"qr{r}_{om}")
                p.s(lambda e, om=om: e.activation(out=qraw[:, om, :], in_=psN[0:64, 0:512], func=AF.Copy), reads=["psS"], writes=["qraw"])
                p.s(lambda e, om=om: e.activation(out=qraw[:, om + 1, :], in_=psN[64:128, 0:512], func=AF.Copy), reads=["psS"], writes=["qraw"])
                for tt in range(2):
                    def mk_kr(e, om=om, tt=tt):
                        last = None
                        for ci in range(NCH):
                            last = e.matmul(
                                psN[0:128, 0:512], wq_sb[:, ci, 512 + 64 * om : 512 + 64 * om + 128], hap(ci, 8 + 512 * tt, 512),
                                start=(ci == 0), stop=(ci == NCH - 1),
                            )
                        return last

                    p.t(mk_kr, reads=["fh", "wq_sb"], writes=["psS"], tag=f"kr{r}_{om}_{tt}")
                    p.s(
                        lambda e, om=om, tt=tt: e.activation(out=kraw[:, om, 512 * tt : 512 * tt + 512], in_=psN[0:64, 0:512], func=AF.Copy),
                        reads=["psS"],
                        writes=["kraw"],
                    )
                    p.s(
                        lambda e, om=om, tt=tt: e.activation(out=kraw[:, om + 1, 512 * tt : 512 * tt + 512], in_=psN[64:128, 0:512], func=AF.Copy),
                        reads=["psS"],
                        writes=["kraw"],
                    )

            sqflat = sq.ap()[0:64, :, :].rearrange("p c t -> p (c t)")
            scrR = [sqflat[:, 0:1024], sqflat[:, 1024:2048]]

            def rot_pipe2(idx, src3, srckey, nh, n, sc_col, sc_key, tcos, tcos_k, tsin, tsin_k, dst3, dstkey):
                pp = idx % 2
                W = nh * n
                scr2, u2, rr2 = scrR[pp], u_sbR[pp], rrowR[pp]
                kscr, ku, krr = ("sq", pp), f"u_sbR{pp}", f"rrowR{pp}"
                pcol = 1024 * pp
                kps = ("psS", pp)
                scr2v = scr2[:, 0:W].rearrange("p (c t) -> p c t", c=nh)
                u2v = u_sbR[pp].ap()[:, 0:W].rearrange("p (c t) -> p c t", c=nh)
                psmatv = psS.ap()[0:64, pcol : pcol + W].rearrange("p (c t) -> p c t", c=nh)
                src2 = src3.rearrange("p c t -> p (c t)")
                cosb = tcos.ap()[:, 0:n].unsqueeze(1).broadcast_to([64, nh, n])
                sinb = tsin.ap()[:, 0:n].unsqueeze(1).broadcast_to([64, nh, n])

                p.s(lambda e: e.activation(out=scr2v, in_=src3, func=AF.Square), reads=[srckey], writes=[kscr])

                def mk_ss(e):
                    last = None
                    for j2 in range(2):
                        last = e.matmul(psS[0:1, pcol + 512 * j2 : pcol + 512 * j2 + 512], onescol_f[0:64, :], scr2[:, 512 * j2 : 512 * j2 + 512], start=True, stop=True)
                    return last

                p.t(mk_ss, reads=[kscr, "onescol_f"], writes=[kps], tag=f"ss{r}_{idx}{srckey}")
                p.s(lambda e: e.activation(out=rr2[:, 0:W], in_=psS[0:1, pcol : pcol + W], func=AF.Sqrt), reads=[kps], writes=[krr])
                p.v(lambda e: e.tensor_scalar_max(rr2[:, 0:W], rr2[:, 0:W], 1e-12), reads=[krr], writes=[krr])
                p.v(lambda e: e.reciprocal(rr2[:, 0:W], rr2[:, 0:W]), reads=[krr], writes=[krr])

                def mk_bc(e):
                    last = None
                    for j2 in range(2):
                        last = e.matmul(psS[0:64, pcol + 512 * j2 : pcol + 512 * j2 + 512], ones1f[:, 0:64], rr2[:, 512 * j2 : 512 * j2 + 512], start=True, stop=True)
                    return last

                p.t(mk_bc, reads=[krr, "ones1f"], writes=[kps], tag=f"bc{r}_{idx}{srckey}")
                p.v(
                    lambda e: e.scalar_tensor_tensor(u2[:, 0:W], src2, sc_col[:, :], psS[0:64, pcol : pcol + W], ALU.mult, ALU.mult),
                    reads=[srckey, sc_key, kps],
                    writes=[ku],
                )

                def mk_rot(e):
                    last = None
                    for j2 in range(2):
                        last = e.matmul(psS[0:64, pcol + 512 * j2 : pcol + 512 * j2 + 512], rot64[:, :], u2[:, 512 * j2 : 512 * j2 + 512], start=True, stop=True)
                    return last

                p.t(mk_rot, reads=[ku, "rot64"], writes=[kps], tag=f"rot{r}_{idx}{srckey}")
                p.v(lambda e: e.tensor_tensor(scr2v, u2v, cosb, ALU.mult), reads=[ku, tcos_k], writes=[kscr])
                p.v(lambda e: e.tensor_tensor(u2v, psmatv, sinb, ALU.mult), reads=[kps, tsin_k], writes=[ku])
                p.v(lambda e: e.tensor_tensor(dst3, scr2v, u2v, ALU.add), reads=[kscr, ku], writes=[dstkey])

            pipes = []
            for hp in range(4):
                pipes.append((qraw.ap()[:, 2 * hp : 2 * hp + 2, :], "qraw", 2, 512, qs_col, "qs_col", tqc, "tqc", tqs, "tqs", qfin.ap()[0:64, 2 * hp : 2 * hp + 2, :], "qfin"))
            for h in range(8):
                pipes.append((kraw.ap()[:, h : h + 1, :], "kraw", 1, 1024, ks_col, "ks_col", tkc, "tkc", tks, "tks", kfin.ap()[0:64, h : h + 1, :], "kfin"))
            for idx, args in enumerate(pipes):
                rot_pipe2(idx, *args)
            p.v(
                lambda e: e.tensor_copy(kfin.ap()[64:65, :, :], dead_r.ap()[:, :].unsqueeze(1).broadcast_to([1, 8, 1024])),
                reads=["dead_r"],
                writes=[("kfin", "dead")],
            )

            sqb = sq.ap().rearrange("p c t -> p (c t)").bitcast(BF16)
            P_sb2 = sqb[:, 0:2560].rearrange("p (c t) -> p c t", c=4)
            PT_sb2 = sqb[:, 2560:5120]
            for j in range(4):
                q0 = 128 * j
                for hg in range(2):
                    par = (2 * j + hg) % 2
                    Pq = P_sb.ap() if par == 0 else P_sb2
                    PTq = PT_sb.ap() if par == 0 else PT_sb2
                    kP = "P_sb" if par == 0 else ("sq", "P2")
                    kPT = "PT_sb" if par == 0 else ("sq", "PT2")
                    negmq = negm if par == 0 else negmB
                    ssumq = ssum if par == 0 else ssumB
                    kneg = "negm" if par == 0 else "negmB"
                    ksum = "ssum" if par == 0 else "ssumB"
                    def mk_S(e, j=j, hg=hg, q0=q0):
                        last = None
                        for hh in range(4):
                            h = 4 * hg + hh
                            a = 640 * hh
                            while a < 640 * hh + 640:
                                n = min(512 - a % 512, 640 * hh + 640 - a)
                                last = e.matmul(
                                    psS[:, a : a + n],
                                    qfin[:, h, q0 : q0 + 128],
                                    kfin[:, h, 128 * j + (a - 640 * hh) : 128 * j + (a - 640 * hh) + n],
                                    start=True, stop=True,
                                )
                                a += n
                        return last

                    p.t(mk_S, reads=["qfin", ("qfin", "ones"), "kfin", ("kfin", "dead")], writes=["psS"], tag=f"S{r}_{j}_{hg}")
                    p.v(
                        lambda e: e.tensor_tensor(
                            psS.ap().rearrange("p (hh k) -> p hh k", hh=4)[:, :, 0:128],
                            psS.ap().rearrange("p (hh k) -> p hh k", hh=4)[:, :, 0:128],
                            triL.ap().unsqueeze(1).broadcast_to([128, 4, 128]),
                            ALU.add,
                        ),
                        reads=["psS", "triL"],
                        writes=["psS"],
                    )
                    p.v(
                        lambda e: e.tensor_tensor(
                            psS.ap().rearrange("p (hh k) -> p hh k", hh=4)[:, :, 512:640],
                            psS.ap().rearrange("p (hh k) -> p hh k", hh=4)[:, :, 512:640],
                            triD.ap().unsqueeze(1).broadcast_to([128, 4, 128]),
                            ALU.add,
                        ),
                        reads=["psS", "triD"],
                        writes=["psS"],
                    )
                    p.v(
                        lambda e, negmq=negmq: e.reduce_max(negmq[:, :], psS.ap().rearrange("p (hh k) -> p hh k", hh=4), axis=AX.X, negate=True),
                        reads=["psS"],
                        writes=[kneg],
                    )
                    for hh in range(4):
                        p.s(
                            lambda e, hh=hh, Pq=Pq, negmq=negmq: e.activation(
                                out=Pq[:, hh, :], in_=psS[:, 640 * hh : 640 * hh + 640], func=AF.Exp, bias=negmq[:, hh : hh + 1]
                            ),
                            reads=["psS", kneg],
                            writes=[kP],
                        )
                    p.v(
                        lambda e, Pq=Pq, ssumq=ssumq: e.reduce_sum(ssumq[:, :], Pq, axis=AX.X),
                        reads=[kP],
                        writes=[ksum],
                    )
                    p.v(lambda e, ssumq=ssumq: e.reciprocal(ssumq[:, :], ssumq[:, :]), reads=[ksum], writes=[ksum])
                    p.v(
                        lambda e, Pq=Pq, ssumq=ssumq: e.tensor_tensor(
                            Pq, Pq, ssumq.ap()[:, :].unsqueeze(2).broadcast_to([128, 4, 640]), ALU.mult
                        ),
                        reads=[kP, ksum],
                        writes=[kP],
                    )
                    for wv in range(2):
                        def mk_T(e, wv=wv, Pq=Pq):
                            last = None
                            for i in range(10):
                                g = 10 * wv + i
                                hh, kc = g // 5, g % 5
                                last = e.transpose(
                                    psPT[:, 128 * i : 128 * i + 128],
                                    Pq[:, hh, 128 * kc : 128 * kc + 128],
                                    identb[:, :],
                                )
                            return last

                        p.t(mk_T, reads=[kP, "identb"], writes=["psPT"], tag=f"T{r}_{j}_{hg}_{wv}")
                        p.v(
                            lambda e, wv=wv, PTq=PTq: e.tensor_copy(PTq[:, 1280 * wv : 1280 * wv + 1280], psPT[:, 0:1280]),
                            reads=["psPT"],
                            writes=[kPT],
                        )

                    def mk_PV(e, j=j, hg=hg, PTq=PTq):
                        last = None
                        for hh in range(4):
                            h = 4 * hg + hh
                            for kc in range(5):
                                g = 5 * hh + kc
                                last = e.matmul(
                                    psO[0:64, 128 * hh : 128 * hh + 128],
                                    vr.ap()[:, j + kc, h, :],
                                    PTq[:, 128 * g : 128 * g + 128],
                                    start=(kc == 0), stop=(kc == 4),
                                )
                        return last

                    p.t(mk_PV, reads=["vr", kPT], writes=["psO"], tag=f"PV{r}_{j}_{hg}")
                    for po in range(2):
                        p.v(
                            lambda e, po=po, hg=hg, q0=q0: e.tensor_copy(
                                o_fm.ap()[64 * po : 64 * po + 64, 2 * hg : 2 * hg + 2, q0 : q0 + 128],
                                psO.ap()[0:64, 0:512].rearrange("p (c two t) -> p c two t", c=2, two=2, t=128)[:, :, po, :],
                            ),
                            reads=["psO"],
                            writes=["o_fm"],
                        )

            for oj in range(NCH):
                def mk_wo(e, oj=oj):
                    last = None
                    for ci in range(4):
                        last = e.matmul(
                            psN[:, 0:512], wo_sb[:, ci, 128 * oj : 128 * oj + 128], o_fm[:, ci, :],
                            start=(ci == 0), stop=(ci == 3),
                        )
                    return last

                p.t(mk_wo, reads=["o_fm", "wo_sb"], writes=["psS"], tag=f"wo{r}_{oj}")
                p.v(
                    lambda e, oj=oj: e.tensor_tensor(fxr[:, oj, 520:1032], fxr[:, oj, 520:1032], psN[:, 0:512], ALU.add),
                    reads=["fxr", "psS"],
                    writes=["fxr"],
                )
            p.dma(
                "sync",
                lambda e, B=B: e.dma_start(
                    out=xmid_out.rearrange("(c p) t -> p c t", p=128)[:, :, B : B + 512], in_=fxr[:, :, 520:1032]
                ),
                reads=["fxr"],
                writes=[("xmid_out", r)],
                tag=f"xmo{r}",
            )

        p.finalize(ctx)
    return nc




def build_launch_fb(final_combine):
    nc = bass.Bass("TRN2", target_bir_lowering=False)
    ap = lambda t: t.ap()
    xme = ap(nc.dram_tensor("xme", [D, TOK + 1], F32, kind="ExternalInput"))
    w1_in = ap(nc.dram_tensor("w1_in", [D, 5460], F32, kind="ExternalInput"))
    w2_in = ap(nc.dram_tensor("w2_in", [2816, D], F32, kind="ExternalInput"))
    b1a_in = ap(nc.dram_tensor("b1a_in", [22, 128], F32, kind="ExternalInput"))
    b1g_in = ap(nc.dram_tensor("b1g_in", [22, 128], F32, kind="ExternalInput"))
    b2_in = ap(nc.dram_tensor("b2_in", [8, 128], F32, kind="ExternalInput"))
    lng_in = ap(nc.dram_tensor("lng_in", [1, D], F32, kind="ExternalInput"))
    lnb_in = ap(nc.dram_tensor("lnb_in", [1, D], F32, kind="ExternalInput"))
    identw = ap(nc.dram_tensor("identw", [128, 128], F32, kind="ExternalInput"))
    neg1_in = ap(nc.dram_tensor("neg1_in", [1, 544], F32, kind="ExternalInput"))
    if final_combine:
        x1own = ap(nc.dram_tensor("x1own", [D, TOK], F32, kind="ExternalInput"))
    out_t = ap(nc.dram_tensor("out_t", [D, TOK], F32, kind="ExternalOutput"))

    with ExitStack() as ctx:
        sb = lambda name, shape, dt=F32: ctx.enter_context(nc.sbuf_tensor(name, shape, dt))
        pst = lambda name, shape, dt=F32: ctx.enter_context(nc.psum_tensor(name, shape, dt))
        p = P(nc)

        xmr = sb("xmr", [128, NCH, 513])
        sq = sb("sq", [128, NCH, 512])
        xh = sb("xh", [128, NCH, 512], BF16)
        w1ta = [sb(f"w1ta{i}", [128, NCH, 128], BF16) for i in range(2)]
        w1tg = [sb(f"w1tg{i}", [128, NCH, 128], BF16) for i in range(2)]
        w2t = [sb(f"w2t{i}", [128, 22, 128], BF16) for i in range(2)]
        h_sb = sb("h_sb", [128, 22, 512], BF16)
        b1a_c = sb("b1a_c", [128, 22])
        b1g_c = sb("b1g_c", [128, 22])
        b2_c = sb("b2_c", [128, 8])
        gb2 = sb("gb2", [2, NCH, 128])
        gcol = sb("gcol", [128, 8])
        rows2v = sb("rows2v", [2, 544])
        btmp = sb("btmp", [22, 128])
        identf = sb("identf", [128, 128])
        xnr = sb("xnr", [128, NCH, 512])
        x1r = sb("x1r", [128, NCH, 512])
        ones1f = sb("ones1f", [1, 128])
        onescol_f = sb("onescol_f", [128, 1])
        rows = ln_rows(nc, ctx, "lnB")
        rows["onesf_col"] = onescol_f
        rows["konesf_col"] = "onescol_f"

        ps1 = pst("ps1", [128, 512])
        ps2 = pst("ps2", [128, 512])
        psA = pst("psA", [128, 1024])
        psB = pst("psB", [128, 512])

        p.v(lambda e: e.memset(ones1f[:], 1.0), writes=["ones1f"])
        p.v(lambda e: e.memset(onescol_f[:], 1.0), writes=["onescol_f"])
        p.dma("sync", lambda e: e.dma_start(out=rows2v.ap()[1:2, :], in_=neg1_in[:, :]), writes=[("rows2v", "neg")])

        p.dma("sync", lambda e: e.dma_start(out=identf[:], in_=identw[:, :]), writes=["identf"])
        p.dma("sync", lambda e: e.dma_start(out=btmp[:], in_=b1a_in[:, :]), writes=["btmp"])
        p.t(lambda e: e.transpose(psA[:, 0:22], btmp[:, :], identf[0:22, 0:22]), reads=["btmp", "identf"], writes=["psA"])
        p.v(lambda e: e.tensor_copy(b1a_c[:], psA[:, 0:22]), reads=["psA"], writes=["b1a_c"])
        p.dma("sync", lambda e: e.dma_start(out=btmp[:], in_=b1g_in[:, :]), writes=["btmp"])
        p.t(lambda e: e.transpose(psA[:, 0:22], btmp[:, :], identf[0:22, 0:22]), reads=["btmp", "identf"], writes=["psA"])
        p.v(lambda e: e.tensor_copy(b1g_c[:], psA[:, 0:22]), reads=["psA"], writes=["b1g_c"])
        p.dma("sync", lambda e: e.dma_start(out=btmp[0:8, :], in_=b2_in[:, :]), writes=["btmp"])
        p.t(lambda e: e.transpose(psA[:, 0:8], btmp[0:8, :], identf[0:8, 0:8]), reads=["btmp", "identf"], writes=["psA"])
        p.v(lambda e: e.tensor_copy(b2_c[:], psA[:, 0:8]), reads=["psA"], writes=["b2_c"])
        p.dma("sync", lambda e: e.dma_start(out=gb2.ap()[0:1, :, :], in_=lng_in.rearrange("o (c m) -> o c m", c=NCH)), writes=[("gb2", "g")])
        p.dma("sync", lambda e: e.dma_start(out=gb2.ap()[1:2, :, :], in_=lnb_in.rearrange("o (c m) -> o c m", c=NCH)), writes=[("gb2", "b")])
        lng8 = sb("lng8", [8, 128])
        p.dma("sync", lambda e: e.dma_start(out=lng8[:], in_=lng_in.rearrange("o (c m) -> (o c) m", c=NCH)), writes=["lng8"])
        p.t(lambda e: e.transpose(psA[:, 0:8], lng8[:, :], identf[0:8, 0:8]), reads=["lng8", "identf"], writes=["psA"])
        p.v(lambda e: e.tensor_copy(gcol[:], psA[:, 0:8]), reads=["psA"], writes=["gcol"])

        for r in range(4):
            B = 512 * r
            p.dma(
                "sync",
                lambda e, B=B: e.dma_start(out=xmr[:, :, :], in_=xme.rearrange("(c p) t -> p c t", p=128)[:, :, B : B + 513]),
                writes=["xmr"],
                tag=f"xmr{r}",
            )
            if final_combine:
                p.dma(
                    "sync",
                    lambda e, B=B: e.dma_start(out=x1r[:, :, :], in_=x1own.rearrange("(c p) t -> p c t", p=128)[:, :, B : B + 512]),
                    writes=["x1r"],
                    tag=f"x1r{r}",
                )

            def h2ap(c, n=512):
                off = 1 if c < 4 else 0
                return xmr.ap()[:, c, off : off + n]

            def apply_lnG(R, MR, rdeps):
                p.v(
                    lambda e: e.tensor_tensor(sq[:, 0:4, :], xmr[:, 0:4, 1:513], R.unsqueeze(1).broadcast_to([128, 4, 512]), ALU.mult),
                    reads=["xmr", "psA"],
                    writes=["sq"],
                )
                p.v(
                    lambda e: e.tensor_tensor(sq[:, 4:8, :], xmr[:, 4:8, 0:512], R.unsqueeze(1).broadcast_to([128, 4, 512]), ALU.mult),
                    reads=["xmr", "psA"],
                    writes=["sq"],
                )
                p.v(lambda e: e.tensor_copy(rows2v[0:1, 0:512], rows["murs"][:, 0:512]), reads=[rows["kmurs"]], writes=[("rows2v", "m")])
                for cch in range(NCH):
                    p.t(
                        lambda e, cch=cch: e.matmul(psB[:, 0:512], gb2[:, cch, :], rows2v[:, 0:512], start=True, stop=True),
                        reads=[("gb2", "g"), ("gb2", "b"), ("rows2v", "m"), ("rows2v", "neg")],
                        writes=["psB"],
                        tag=f"gbc{r}_{cch}",
                    )
                    p.v(
                        lambda e, cch=cch: e.scalar_tensor_tensor(
                            xh[:, cch, :], sq[:, cch, :], gcol[:, cch : cch + 1], psB[:, 0:512], ALU.mult, ALU.subtract
                        ),
                        reads=["sq", "gcol", "psB"],
                        writes=["xh"],
                    )

            emit_ln_g(
                p,
                [(h2ap(c), "xmr") for c in range(NCH)],
                512,
                [
                    (sq.ap()[:, 0:4, :], xmr.ap()[:, 0:4, 1:513], "xmr", "sq"),
                    (sq.ap()[:, 4:8, :], xmr.ap()[:, 4:8, 0:512], "xmr", "sq"),
                ],
                [(sq.ap()[:, c, :], "sq") for c in range(NCH)],
                rows, psA, "psA", onescol_f, "onescol_f", onescol_f, "onescol_f",
                ones1f, "ones1f", apply_lnG,
            )

            for oj in range(22):
                mm = 128 if oj < 21 else 42
                bb = oj % 2
                p.dma(
                    "gpsimd",
                    lambda e, oj=oj, mm=mm, bb=bb: e.dma_start(
                        out=w1ta[bb].ap()[:, :, 0:mm],
                        in_=w1_in.rearrange("(c p) m -> p c m", p=128)[:, :, 128 * oj : 128 * oj + mm],
                    ),
                    writes=[("w1ta", bb)],
                    tag=f"w1a{r}_{oj}",
                )
                p.dma(
                    "gpsimd",
                    lambda e, oj=oj, mm=mm, bb=bb: e.dma_start(
                        out=w1tg[bb].ap()[:, :, 0:mm],
                        in_=w1_in.rearrange("(c p) m -> p c m", p=128)[:, :, 2730 + 128 * oj : 2730 + 128 * oj + mm],
                    ),
                    writes=[("w1tg", bb)],
                    tag=f"w1g{r}_{oj}",
                )

                def mk_a(e, oj=oj, mm=mm, bb=bb):
                    last = None
                    for ci in range(NCH):
                        last = e.matmul(
                            ps1[0:mm, 0:512], w1ta[bb][:, ci, 0:mm], xh[:, ci, :],
                            start=(ci == 0), stop=(ci == NCH - 1),
                        )
                    return last

                def mk_g(e, oj=oj, mm=mm, bb=bb):
                    last = None
                    for ci in range(NCH):
                        last = e.matmul(
                            ps2[0:mm, 0:512], w1tg[bb][:, ci, 0:mm], xh[:, ci, :],
                            start=(ci == 0), stop=(ci == NCH - 1),
                        )
                    return last

                p.t(mk_a, reads=["xh", ("w1ta", bb)], writes=["ps1"], tag=f"a{r}_{oj}")
                p.t(mk_g, reads=["xh", ("w1tg", bb)], writes=["ps2"], tag=f"g{r}_{oj}")
                p.s(
                    lambda e, oj=oj, mm=mm: e.activation(out=sq[0:mm, 0, :], in_=ps1[0:mm, 0:512], func=AF.Identity, bias=b1a_c[0:mm, oj : oj + 1]),
                    reads=["ps1", "b1a_c"],
                    writes=["sq"],
                )
                p.s(
                    lambda e, oj=oj, mm=mm: e.activation(out=sq[0:mm, 1, :], in_=ps2[0:mm, 0:512], func=AF.Gelu, bias=b1g_c[0:mm, oj : oj + 1]),
                    reads=["ps2", "b1g_c"],
                    writes=["sq"],
                )
                p.v(
                    lambda e, oj=oj, mm=mm: e.tensor_tensor(h_sb[0:mm, oj, :], sq[0:mm, 0, :], sq[0:mm, 1, :], ALU.mult),
                    reads=["sq"],
                    writes=["h_sb"],
                )

            for oj in range(NCH):
                bb = oj % 2
                p.dma(
                    "gpsimd",
                    lambda e, oj=oj, bb=bb: e.dma_start(
                        out=w2t[bb].ap()[:, 0:22, :],
                        in_=w2_in.rearrange("(c p) m -> p c m", p=128)[:, :, 128 * oj : 128 * oj + 128],
                    ),
                    writes=[("w2t", bb)],
                    tag=f"w2l{r}_{oj}",
                )

                def mk_w2(e, oj=oj, bb=bb):
                    last = None
                    for ci in range(22):
                        rr2 = 128 if ci < 21 else 42
                        last = e.matmul(
                            ps1[:, 0:512], w2t[bb][0:rr2, ci, :], h_sb[0:rr2, ci, :],
                            start=(ci == 0), stop=(ci == 21),
                        )
                    return last

                p.t(mk_w2, reads=["h_sb", ("w2t", bb)], writes=["ps1"], tag=f"w2{r}_{oj}")
                p.s(
                    lambda e, oj=oj: e.activation(out=sq[:, 0, :], in_=ps1[:, 0:512], func=AF.Identity, bias=b2_c[:, oj : oj + 1]),
                    reads=["ps1", "b2_c"],
                    writes=["sq"],
                )
                p.v(
                    lambda e, oj=oj: e.tensor_tensor(xnr[:, oj, :], xmr[:, oj, 1:513], sq[:, 0, :], ALU.add),
                    reads=["xmr", "sq"],
                    writes=["xnr"],
                )

            if final_combine:
                def apply_fin(R, MR, rdeps):
                    p.v(
                        lambda e: e.tensor_tensor(sq[:, :, :], xnr[:, :, :], R.unsqueeze(1).broadcast_to([128, NCH, 512]), ALU.mult),
                        reads=["xnr", "psA"],
                        writes=["sq"],
                    )
                    p.v(
                        lambda e: e.tensor_tensor(sq[:, :, :], sq[:, :, :], MR.unsqueeze(1).broadcast_to([128, NCH, 512]), ALU.subtract),
                        reads=["sq", "psA"],
                        writes=["sq"],
                    )
                    p.v(
                        lambda e: e.tensor_tensor(xnr[:, :, :], x1r[:, :, :], sq[:, :, :], ALU.add),
                        reads=["sq", "x1r"],
                        writes=["xnr"],
                    )

                emit_ln_g(
                    p,
                    [(xnr.ap()[:, c, :], "xnr") for c in range(NCH)],
                    512,
                    [(sq.ap()[:, :, :], xnr.ap()[:, :, :], "xnr", "sq")],
                    [(sq.ap()[:, c, :], "sq") for c in range(NCH)],
                    rows, psA, "psA", onescol_f, "onescol_f", onescol_f, "onescol_f",
                    ones1f, "ones1f", apply_fin,
                )

            p.dma(
                "sync",
                lambda e, B=B: e.dma_start(
                    out=out_t.rearrange("(c p) t -> p c t", p=128)[:, :, B : B + 512], in_=xnr[:, :, :]
                ),
                reads=["xnr"],
                writes=[("out_t", r)],
                tag=f"out{r}",
            )

        p.finalize(ctx)
    return nc




def fm(x):
    return np.ascontiguousarray(x.T)


def build_attn0_masks(s):
    mA = np.zeros((16, 128, 128), np.float32)
    mB = np.zeros((16, 16, 16), np.float32)
    for j in range(16):
        qt = s + 128 * j
        kk = np.arange(128)[:, None]
        qq = np.arange(128)[None, :]
        key = qt - 8 + kk
        q = qt + qq
        mA[j] = ((np.abs(key - q) <= 8) & (key >= 0) & (key < SEQ)).astype(np.float32)
        kk2 = np.arange(16)[:, None]
        qi = np.arange(16)[None, :]
        key2 = qt + 120 + kk2
        q2 = qt + 112 + qi
        mB[j] = ((np.abs(key2 - q2) <= 8) & (key2 >= 0) & (key2 < SEQ)).astype(np.float32)
    return mA, mB


def build_selectors():
    s16 = np.zeros((8, 16, 128), np.float32)
    hs16 = np.zeros((8, 128, 16), np.float32)
    for c in range(8):
        for pp in range(128):
            h = 2 * c + (1 if pp >= 64 else 0)
            s16[c, h, pp] = 1.0
            hs16[c, pp, h] = 1.0
    return s16, hs16


def run_launch_a(inputs, x0_full):
    nc = build_launch_a()
    w_qkv = inputs["L0_attn_w_qkv"]
    w_out = inputs["L0_attn_w_out"]
    wqkvT = np.ascontiguousarray(w_qkv.T.astype(np.float32))
    woutT = np.ascontiguousarray(w_out.T.astype(np.float32))
    bqkv = inputs["L0_attn_b_qkv"].astype(np.float32)
    bout = inputs["L0_attn_b_out"].astype(np.float32)
    s16, hs16 = build_selectors()
    ident = np.eye(128, dtype=np.float32)
    in_maps = []
    for c in range(NCORES):
        b, s = c // 4, TOK * (c % 4)
        lo, hi = s - 8, s - 8 + KEXT
        x0e = np.zeros((KEXT, D), np.float32)
        a0, a1 = max(lo, 0), min(hi, SEQ)
        x0e[a0 - lo : a1 - lo] = x0_full[b, a0:a1]
        mA, mB = build_attn0_masks(s)
        in_maps.append(
            {
                "x0e": fm(x0e),
                "wqkvT": wqkvT,
                "woutT": woutT,
                "bqkv24": bqkv.reshape(24, 128),
                "bqkv_row": bqkv.reshape(1, 3 * D),
                "bout8": bout.reshape(8, 128),
                "maskA": mA,
                "maskB": mB,
                "s16": s16,
                "hs16": hs16,
                "identw": ident,
            }
        )
    res = run_bass_kernel_spmd(nc, in_maps, core_ids=list(range(NCORES)))
    x1 = np.zeros((2, SEQ, D), np.float32)
    for c in range(NCORES):
        b, s = c // 4, TOK * (c % 4)
        x1[b, s : s + TOK] = res.results[c]["x1out"].T
    return x1





def build_launch_l1():
    nc = bass.Bass("TRN2", target_bir_lowering=False)
    ap = lambda t: t.ap()
    x2own = ap(nc.dram_tensor("x2own", [D, TOK], F32, kind="ExternalInput"))
    x2rot = ap(nc.dram_tensor("x2rot", [D, SEQ], F32, kind="ExternalInput"))
    wq_in = ap(nc.dram_tensor("wq_in", [D, D], F32, kind="ExternalInput"))
    wk_in = ap(nc.dram_tensor("wk_in", [D, 256], F32, kind="ExternalInput"))
    wv_in = ap(nc.dram_tensor("wv_in", [D, 256], F32, kind="ExternalInput"))
    wo_in = ap(nc.dram_tensor("wo_in", [D, D], F32, kind="ExternalInput"))
    bq16 = ap(nc.dram_tensor("bq16", [16, 64], F32, kind="ExternalInput"))
    bk4 = ap(nc.dram_tensor("bk4", [4, 64], F32, kind="ExternalInput"))
    bvrow_in = ap(nc.dram_tensor("bvrow_in", [1, 256], F32, kind="ExternalInput"))
    bo8 = ap(nc.dram_tensor("bo8", [8, 128], F32, kind="ExternalInput"))
    maskw_in = ap(nc.dram_tensor("maskw_in", [128, 4, 128], F32, kind="ExternalInput"))
    s16 = ap(nc.dram_tensor("s16", [8, 16, 128], F32, kind="ExternalInput"))
    identw = ap(nc.dram_tensor("identw", [128, 128], F32, kind="ExternalInput"))
    x3out = ap(nc.dram_tensor("x3out", [D, TOK], F32, kind="ExternalOutput"))
    dbgout = ap(nc.dram_tensor("dbgout", [16, 256], F32, kind="ExternalOutput"))

    with ExitStack() as ctx:
        sb = lambda name, shape, dt=F32: ctx.enter_context(nc.sbuf_tensor(name, shape, dt))
        pst = lambda name, shape, dt=F32: ctx.enter_context(nc.psum_tensor(name, shape, dt))
        p = P(nc)
        p.alias_groups = {
            "psS": [("psS", 0), ("psS", 1), ("psS", ("q", 0)), ("psS", ("q", 1)), ("psS", ("n", 0)), ("psS", ("n", 1))],
            "psO": [("psO", 0), ("psO", 1)],
            "sqt": [("sqt", 0), ("sqt", 1)],
        }

        xt = sb("xt", [128, NCH, 256])
        xht = sb("xht", [128, NCH, 256], BF16)
        sqt = sb("sqt", [128, NCH, 256])
        k_sb = sb("k_sb", [64, 4, SEQ], BF16)
        v_sb = sb("v_sb", [128, 64, 4, 65], BF16)
        q_t = sb("q_t", [64, 16, 128], BF16)
        e_sb = sb("e_sb", [128, 4096], BF16)
        ctxr = sb("ctxr", [128, NCH, 512], BF16)
        x2r = sb("x2r", [128, NCH, 512])
        yr = ctxr
        wq_sb = sb("wq_sb", [128, NCH, D], BF16)
        wk_sb = sb("wk_sb", [128, NCH, 256], BF16)
        wv_sb = sb("wv_sb", [128, NCH, 256], BF16)
        wo_sb = sb("wo_sb", [128, NCH, D], BF16)
        bq_c = sb("bq_c", [64, 16])
        bk_c = sb("bk_c", [64, 4])
        bo_c = sb("bo_c", [128, 8])
        bvrow = sb("bvrow", [1, 256], BF16)
        btmp = sb("btmp", [16, 128])
        maskw = sb("maskw", [128, 4, 128])
        s16_sb = sb("s16_sb", [16, NCH, 128], BF16)
        identf = sb("identf", [128, 128])
        negBc = sb("negBc", [128, 1])
        nB = sb("nB", [1, 4])
        negB = sb("negB", [1, 1], BF16)
        rowmax = sb("rowmax", [128, 1])
        n2acc = sb("n2acc", [128, 16])
        scr = sb("scr", [64, 4, 128], BF16)
        srow = sb("srow", [1, 512])
        sums_tm = sb("sums_tm", [128, 16])
        recip_hm = sb("recip_hm", [16, 512], BF16)
        ones1f = sb("ones1f", [1, 128])
        ones1b = sb("ones1b", [1, 128], BF16)
        onescol_f = sb("onescol_f", [128, 1])
        onescol_b = sb("onescol_b", [128, 1], BF16)
        rows = ln_rows(nc, ctx, "lnL", width=256)
        rows["onesf_col"] = onescol_f
        rows["konesf_col"] = "onescol_f"

        psS = pst("psS", [128, 2048])
        psO = pst("psO", [128, 2048])
        psA = psS.ap()[:, 0:1024]

        p.v(lambda e: e.memset(ones1f[:], 1.0), writes=["ones1f"])
        p.v(lambda e: e.memset(ones1b[:], 1.0), writes=["ones1b"])
        p.v(lambda e: e.memset(onescol_f[:], 1.0), writes=["onescol_f"])
        p.v(lambda e: e.memset(onescol_b[:], 1.0), writes=["onescol_b"])
        p.v(lambda e: e.memset(v_sb.ap()[:, :, :, 64:65], 1.0), writes=[("v_sb", "ones")])
        p.v(lambda e: e.memset(recip_hm[:], 0.0), writes=["recip_hm"])
        p.v(lambda e: e.memset(ctxr[:, :, :], 0.0), writes=["ctxr"])

        p.dma("sync", lambda e: e.dma_start(out=identf[:], in_=identw[:, :]), writes=["identf"])
        p.dma("sync", lambda e: e.dma_start(out=maskw[:], in_=maskw_in[:, :, :]), writes=["maskw"])
        p.dma("gpsimd", lambda e: e.dma_start(out=s16_sb[:], in_=s16.rearrange("c h t -> h c t")), writes=["s16_sb"])
        p.dma("gpsimd", lambda e: e.dma_start(out=bvrow[:], in_=bvrow_in[:, :]), writes=["bvrow"])
        for cc in range(NCH):
            p.dma("gpsimd", lambda e, cc=cc: e.dma_start(out=wq_sb[:, cc, :], in_=wq_in[128 * cc : 128 * cc + 128, :]), writes=["wq_sb"], tag=f"wq{cc}")
            p.dma("gpsimd", lambda e, cc=cc: e.dma_start(out=wo_sb[:, cc, :], in_=wo_in[128 * cc : 128 * cc + 128, :]), writes=["wo_sb"], tag=f"wo{cc}")
            p.dma("gpsimd", lambda e, cc=cc: e.dma_start(out=wk_sb[:, cc, :], in_=wk_in[128 * cc : 128 * cc + 128, :]), writes=["wk_sb"], tag=f"wk{cc}")
            p.dma("gpsimd", lambda e, cc=cc: e.dma_start(out=wv_sb[:, cc, :], in_=wv_in[128 * cc : 128 * cc + 128, :]), writes=["wv_sb"], tag=f"wv{cc}")
        p.dma("sync", lambda e: e.dma_start(out=btmp[0:16, 0:64], in_=bq16[:, :]), writes=["btmp"])
        p.t(lambda e: e.transpose(psS[0:64, 0:16], btmp[0:16, 0:64], identf[0:16, 0:16]), reads=["btmp", "identf"], writes=["psS"])
        p.v(lambda e: e.tensor_scalar_mul(bq_c[:], psS[0:64, 0:16], 0.125), reads=["psS"], writes=["bq_c"])
        p.dma("sync", lambda e: e.dma_start(out=btmp[0:4, 0:64], in_=bk4[:, :]), writes=["btmp"])
        p.t(lambda e: e.transpose(psS[0:64, 0:4], btmp[0:4, 0:64], identf[0:4, 0:4]), reads=["btmp", "identf"], writes=["psS"])
        p.v(lambda e: e.tensor_copy(bk_c[:], psS[0:64, 0:4]), reads=["psS"], writes=["bk_c"])
        p.dma("sync", lambda e: e.dma_start(out=btmp[0:8, :], in_=bo8[:, :]), writes=["btmp"])
        p.t(lambda e: e.transpose(psS[:, 0:8], btmp[0:8, :], identf[0:8, 0:8]), reads=["btmp", "identf"], writes=["psS"])
        p.v(lambda e: e.tensor_copy(bo_c[:], psS[:, 0:8]), reads=["psS"], writes=["bo_c"])

        for kt4 in range(32):
            p.dma(
                "sync",
                lambda e, kt4=kt4: e.dma_start(out=xt[:, :, :], in_=x2rot.rearrange("(c p) t -> p c t", p=128)[:, :, 256 * kt4 : 256 * kt4 + 256]),
                writes=["xt"],
                tag=f"xt{kt4}",
            )

            def apply_lnL(R, MR, rdeps):
                p.v(
                    lambda e: e.tensor_tensor(sqt[:, :, :], xt[:, :, :], R.unsqueeze(1).broadcast_to([128, NCH, 256]), ALU.mult),
                    reads=["xt", "psS"],
                    writes=["sqt"],
                )
                p.v(
                    lambda e: e.tensor_tensor(xht[:, :, :], sqt[:, :, :], MR.unsqueeze(1).broadcast_to([128, NCH, 256]), ALU.subtract),
                    reads=["sqt", "psS"],
                    writes=["xht"],
                )

            emit_ln_g(
                p,
                [(xt.ap()[:, c, :], "xt") for c in range(NCH)],
                256,
                [(sqt.ap()[:, :, :], xt.ap()[:, :, :], "xt", "sqt")],
                [(sqt.ap()[:, c, :], "sqt") for c in range(NCH)],
                rows, psA, "psS", onescol_f, "onescol_f", onescol_f, "onescol_f",
                ones1f, "ones1f", apply_lnL,
            )
            def mk_k(e, kt4=kt4):
                last = None
                for op2 in range(2):
                    for ci in range(NCH):
                        last = e.matmul(
                            psO[0:128, 256 * op2 : 256 * op2 + 256], wk_sb[:, ci, 128 * op2 : 128 * op2 + 128], xht[:, ci, :],
                            start=(ci == 0), stop=(ci == NCH - 1),
                        )
                return last

            p.t(mk_k, reads=["xht", "wk_sb"], writes=["psO"], tag=f"k{kt4}")
            for om in range(4):
                p.s(
                    lambda e, kt4=kt4, om=om: e.activation(
                        out=k_sb[:, om, 256 * kt4 : 256 * kt4 + 256],
                        in_=psO[64 * (om % 2) : 64 * (om % 2) + 64, 256 * (om // 2) : 256 * (om // 2) + 256],
                        func=AF.Identity, bias=bk_c[:, om : om + 1],
                    ),
                    reads=["psO", "bk_c"],
                    writes=["k_sb"],
                )
            for sub in range(2):
                def mk_v(e, kt4=kt4, sub=sub):
                    for ci in range(NCH):
                        e.matmul(psS[:, 1024 + 256 * (sub % 2) : 1280 + 256 * (sub % 2)], xht[:, ci, 128 * sub : 128 * sub + 128], wv_sb[:, ci, :], start=(ci == 0), stop=False)
                    return e.matmul(psS[:, 1024 + 256 * (sub % 2) : 1280 + 256 * (sub % 2)], ones1b[:, 0:128], bvrow[:, :], start=False, stop=True)

                p.t(mk_v, reads=["xht", "wv_sb", "ones1b", "bvrow"], writes=[("psS", 1)], tag=f"v{kt4}_{sub}")
                p.v(
                    lambda e, kt4=kt4, sub=sub: e.tensor_copy(
                        v_sb.ap()[:, 2 * kt4 + sub, :, 0:64],
                        psS.ap()[:, 1024 + 256 * (sub % 2) : 1280 + 256 * (sub % 2)].rearrange("t (g d) -> t g d", g=4),
                    ),
                    reads=[("psS", 1)],
                    writes=[("v_sb", "dat")],
                )

        p.v(lambda e: e.memset(n2acc[:], 0.0), writes=["n2acc"])
        for it in range(8):
            p.dma(
                "sync",
                lambda e, it=it: e.dma_start(out=xt[:, :, :], in_=x2own.rearrange("(c p) t -> p c t", p=128)[:, :, 256 * it : 256 * it + 256]),
                writes=["xt"],
                tag=f"xtq{it}",
            )

            def apply_lnq(R, MR, rdeps):
                p.v(
                    lambda e: e.tensor_tensor(sqt[:, :, :], xt[:, :, :], R.unsqueeze(1).broadcast_to([128, NCH, 256]), ALU.mult),
                    reads=["xt", "psS"],
                    writes=["sqt"],
                )
                p.v(
                    lambda e: e.tensor_tensor(xht[:, :, :], sqt[:, :, :], MR.unsqueeze(1).broadcast_to([128, NCH, 256]), ALU.subtract),
                    reads=["sqt", "psS"],
                    writes=["xht"],
                )

            emit_ln_g(
                p,
                [(xt.ap()[:, c, :], "xt") for c in range(NCH)],
                256,
                [(sqt.ap()[:, :, :], xt.ap()[:, :, :], "xt", "sqt")],
                [(sqt.ap()[:, c, :], "sqt") for c in range(NCH)],
                rows, psA, "psS", onescol_f, "onescol_f", onescol_f, "onescol_f",
                ones1f, "ones1f", apply_lnq,
            )
            for om in range(0, 16, 2):
                par = (om // 2) % 2
                qc = 512 * par

                def mk_q(e, om=om, qc=qc):
                    last = None
                    for ci in range(NCH):
                        last = e.matmul(
                            psO[0:128, qc : qc + 256], wq_sb[:, ci, 64 * om : 64 * om + 128], xht[:, ci, :],
                            start=(ci == 0), stop=(ci == NCH - 1),
                        )
                    return last

                p.t(mk_q, reads=["xht", "wq_sb"], writes=[("psO", par)], tag=f"qn{it}_{om}")
                sc = 2 * par
                p.s(
                    lambda e, om=om, qc=qc, sc=sc: e.activation(
                        out=sqt[0:64, sc, 0:256], in_=psO[0:64, qc : qc + 256], func=AF.Square, scale=0.125, bias=bq_c[:, om : om + 1]
                    ),
                    reads=[("psO", par), "bq_c"],
                    writes=[("sqt", par)],
                )
                p.s(
                    lambda e, om=om, qc=qc, sc=sc: e.activation(
                        out=sqt[0:64, sc + 1, 0:256], in_=psO[64:128, qc : qc + 256], func=AF.Square, scale=0.125, bias=bq_c[:, om + 1 : om + 2]
                    ),
                    reads=[("psO", par), "bq_c"],
                    writes=[("sqt", par)],
                )

                def mk_n2(e, om=om, sc=sc, par=par):
                    last = None
                    base = 768 if par == 0 else 832
                    g4 = (om // 2) // 2
                    for hh in range(2):
                        for half in range(2):
                            col = base + 4 * g4 + 2 * hh + half
                            last = e.matmul(
                                psS[0:128, col : col + 1],
                                sqt[0:64, sc + hh, 128 * half : 128 * half + 128],
                                onescol_f[0:64, :],
                                start=True, stop=True,
                            )
                    return last

                p.t(mk_n2, reads=[("sqt", par), "onescol_f"], writes=[("psS", ("n", par))], tag=f"qn2_{it}_{om}")
            p.v(
                lambda e: e.tensor_tensor(n2acc[:, 0:16], n2acc[:, 0:16], psS[:, 768:784], ALU.max),
                reads=[("psS", ("n", 0)), "n2acc"],
                writes=["n2acc"],
            )
            p.v(
                lambda e: e.tensor_tensor(n2acc[:, 0:16], n2acc[:, 0:16], psS[:, 832:848], ALU.max),
                reads=[("psS", ("n", 1)), "n2acc"],
                writes=["n2acc"],
            )
        p.v(lambda e: e.reduce_max(rowmax[:], n2acc[:], axis=AX.X), reads=["n2acc"], writes=["rowmax"])
        p.t(lambda e: e.transpose(psS[0:1, 256:384], rowmax[:, :], identf[:, :]), reads=["rowmax", "identf"], writes=["psS"])
        p.v(lambda e: e.reduce_max(nB[:, 0:1], psS[0:1, 256:384], axis=AX.X), reads=["psS"], writes=["nB"])
        p.v(lambda e: e.memset(n2acc[:], 0.0), writes=["n2acc"])
        for kt in range(64):
            p.s(lambda e, kt=kt: e.activation(out=scr[:, :, :], in_=k_sb.ap()[:, :, 128 * kt : 128 * kt + 128], func=AF.Square), reads=["k_sb"], writes=["scr"])

            def mk_kn2(e):
                last = None
                for om in range(4):
                    last = e.matmul(psS[0:128, 128 + om : 129 + om], scr[:, om, :], onescol_b[0:64, :], start=True, stop=True)
                return last

            p.t(mk_kn2, reads=["scr", "onescol_b"], writes=["psS"], tag=f"kn2{kt}")
            p.v(lambda e: e.tensor_tensor(n2acc[:, 0:4], n2acc[:, 0:4], psS[:, 128:132], ALU.max), reads=["psS", "n2acc"], writes=["n2acc"])
        p.v(lambda e: e.reduce_max(rowmax[:], n2acc[:, 0:4], axis=AX.X), reads=["n2acc"], writes=["rowmax"])
        p.t(lambda e: e.transpose(psS[0:1, 256:384], rowmax[:, :], identf[:, :]), reads=["rowmax", "identf"], writes=["psS"])
        p.v(lambda e: e.reduce_max(nB[:, 1:2], psS[0:1, 256:384], axis=AX.X), reads=["psS"], writes=["nB"])
        p.s(lambda e: e.activation(out=nB[:, 0:2], in_=nB[:, 0:2], func=AF.Sqrt), reads=["nB"], writes=["nB"])
        p.v(lambda e: e.tensor_mul(nB[:, 2:3], nB[:, 0:1], nB[:, 1:2]), reads=["nB"], writes=["nB"])
        p.s(lambda e: e.activation(out=negB[:, :], in_=nB[:, 2:3], func=AF.Copy, scale=-1.0, bias=-1.0), reads=["nB"], writes=["negB"])
        p.t(lambda e: e.matmul(psS[:, 384:385], ones1b[:, 0:128], negB[:, :], start=True, stop=True), reads=["negB", "ones1b"], writes=["psS"])
        p.v(lambda e: e.tensor_copy(negBc[:], psS[:, 384:385]), reads=["psS"], writes=["negBc"])
        for r in range(int(os.environ.get("L1_REGIONS", "4"))):
            p.dma(
                "sync",
                lambda e, r=r: e.dma_start(out=x2r[:, :, :], in_=x2own.rearrange("(c p) t -> p c t", p=128)[:, :, 512 * r : 512 * r + 512]),
                writes=["x2r"],
                tag=f"x2r{r}",
            )
            for jj in range(int(os.environ.get("L1_QSUBS", "4"))):
                qs = 4 * r + jj
                skip2 = os.environ.get("L1_SKIP2", "") if qs == 1 else ""
                if "xt" not in skip2:
                    p.dma(
                        "sync",
                        lambda e, qs=qs: e.dma_start(out=xt[:, :, 0:128], in_=x2own.rearrange("(c p) t -> p c t", p=128)[:, :, 128 * qs : 128 * qs + 128]),
                        writes=["xt"],
                        tag=f"xtq2_{qs}",
                    )

                def apply_lnq2(R, MR, rdeps):
                    p.v(
                        lambda e: e.tensor_tensor(sqt[:, :, 0:128], xt[:, :, 0:128], R.unsqueeze(1).broadcast_to([128, NCH, 128]), ALU.mult),
                        reads=["xt", "psS"],
                        writes=["sqt"],
                    )
                    p.v(
                        lambda e: e.tensor_tensor(xht[:, :, 0:128], sqt[:, :, 0:128], MR.unsqueeze(1).broadcast_to([128, NCH, 128]), ALU.subtract),
                        reads=["sqt", "psS"],
                        writes=["xht"],
                    )

                if "ln" not in skip2:
                    emit_ln_g(
                        p,
                        [(xt.ap()[:, c, 0:128], "xt") for c in range(NCH)],
                        128,
                        [(sqt.ap()[:, :, 0:128], xt.ap()[:, :, 0:128], "xt", "sqt")],
                        [(sqt.ap()[:, c, 0:128], "sqt") for c in range(NCH)],
                        rows, psA, "psS", onescol_f, "onescol_f", onescol_f, "onescol_f",
                        ones1f, "ones1f", apply_lnq2,
                    )
                for om in range(0, 0 if "qp" in skip2 else 16, 2):
                    qoff = 1024 + 512 * ((om // 2) % 2)

                    def mk_q2(e, om=om, qoff=qoff):
                        last = None
                        for ci in range(NCH):
                            last = e.matmul(
                                psS[0:128, qoff : qoff + 128], wq_sb[:, ci, 64 * om : 64 * om + 128], xht[:, ci, 0:128],
                                start=(ci == 0), stop=(ci == NCH - 1),
                            )
                        return last

                    p.t(mk_q2, reads=["xht", "wq_sb"], writes=[("psS", ("q", (om // 2) % 2))], tag=f"q2{qs}_{om}")
                    p.s(
                        lambda e, om=om, qoff=qoff: e.activation(out=q_t[:, om, :], in_=psS[0:64, qoff : qoff + 128], func=AF.Identity, scale=0.125, bias=bq_c[:, om : om + 1]),
                        reads=[("psS", ("q", (om // 2) % 2)), "bq_c"],
                        writes=["q_t"],
                    )
                    p.s(
                        lambda e, om=om, qoff=qoff: e.activation(out=q_t[:, om + 1, :], in_=psS[64:128, qoff : qoff + 128], func=AF.Identity, scale=0.125, bias=bq_c[:, om + 1 : om + 2]),
                        reads=[("psS", ("q", (om // 2) % 2)), "bq_c"],
                        writes=["q_t"],
                    )
                NKT = 4 * qs + 4

                def mk_st(e, kt):
                    last = None
                    for g in range(4):
                        last = e.matmul(
                            psS[:, 512 * g : 512 * g + 512],
                            k_sb[:, g, 128 * kt : 128 * kt + 128],
                            q_t.ap()[:, 4 * g : 4 * g + 4, :].rearrange("p h t -> p (h t)"),
                            start=True, stop=True,
                        )
                    return last

                def emit_S(kt):
                    eo = 2048 * (kt % 2)
                    p.t(lambda e, kt=kt, f=mk_st: f(e, kt), reads=["k_sb", "q_t"], writes=["psS"], tag=f"st{qs}_{kt}")
                    p.s(
                        lambda e, kt=kt, eo=eo: e.activation(
                            out=e_sb[:, eo : eo + 2048], in_=psS[:, 0:2048], func=AF.Exp, bias=negBc[:, 0:1],
                        ),
                        reads=["psS", "negBc"],
                        writes=[("e_sb", kt % 2)],
                    )
                    if kt >= 4 * qs:
                        p.v(
                            lambda e, kt=kt, qs=qs, eo=eo: e.tensor_tensor(
                                e_sb.ap()[:, eo : eo + 2048].rearrange("a (h b) -> a h b", h=16),
                                e_sb.ap()[:, eo : eo + 2048].rearrange("a (h b) -> a h b", h=16),
                                maskw.ap()[:, kt - 4 * qs, :].unsqueeze(1).broadcast_to([128, 16, 128]),
                                ALU.mult,
                            ),
                            reads=[("e_sb", kt % 2), "maskw"],
                            writes=[("e_sb", kt % 2)],
                        )

                def emit_PV(kt):
                    eo = 2048 * (kt % 2)

                    def mk_pv(e, kt=kt, qs=qs, eo=eo):
                        last = None
                        for g in range(4):
                            last = e.matmul(
                                psO[0:65, 512 * g : 512 * g + 512],
                                v_sb.ap()[:, kt, g, :],
                                e_sb[:, eo + 512 * g : eo + 512 * g + 512],
                                start=(kt == 0), stop=(kt == 4 * qs + 3),
                            )
                        return last

                    p.t(mk_pv, reads=[("v_sb", "dat"), ("v_sb", "ones"), ("e_sb", kt % 2)], writes=["psO"], tag=f"pv{qs}_{kt}")

                emit_S(0)
                for kt in range(1, NKT):
                    emit_S(kt)
                    emit_PV(kt - 1)
                emit_PV(NKT - 1)
                if os.environ.get("L1_DBG") and r == 0 and jj == 0:
                    p.v(lambda e: e.tensor_copy(srow[:, :], psO[64:65, 0:512]), reads=["psO"], writes=["srow"])
                    p.dma(
                        "gpsimd",
                        lambda e: e.dma_start(out=dbgout[0:1, 160:176], in_=srow[0:1, 0:16]),
                        reads=["srow"],
                        writes=["dbgout"],
                    )
                for half in range(4):
                    p.v(
                        lambda e, half=half: e.tensor_copy(srow[:, :], psO[64:65, 512 * half : 512 * half + 512]),
                        reads=["psO"],
                        writes=["srow"],
                    )

                    def mk_str(e, half=half):
                        last = None
                        for h in range(4):
                            last = e.transpose(
                                psS[:, 1024 + 4 * half + h : 1025 + 4 * half + h], srow[0:1, 128 * h : 128 * h + 128], identf[0:1, 0:1]
                            )
                        return last

                    p.t(mk_str, reads=["srow", "identf"], writes=["psS"], tag=f"str{qs}_{half}")
                p.v(lambda e: e.tensor_copy(sums_tm[:, :], psS[:, 1024:1040]), reads=["psS"], writes=["sums_tm"])
                p.v(lambda e: e.reciprocal(sums_tm[:, :], sums_tm[:, :]), reads=["sums_tm"], writes=["sums_tm"])
                p.t(
                    lambda e: e.transpose(psS[0:16, 1024:1152], sums_tm[:, :], identf[:, :]),
                    reads=["sums_tm", "identf"],
                    writes=["psS"],
                )
                p.v(
                    lambda e, jj=jj: e.tensor_copy(recip_hm[:, 128 * jj : 128 * jj + 128], psS[0:16, 1024:1152]),
                    reads=["psS"],
                    writes=["recip_hm"],
                )
                if os.environ.get("L1_DBG") and r == 0 and jj == 0:
                    p.dma(
                        "gpsimd",
                        lambda e: e.dma_start(out=dbgout[0:16, 0:128], in_=recip_hm[:, 0:128]),
                        reads=["recip_hm"],
                        writes=["dbgout"],
                    )
                    p.dma(
                        "gpsimd",
                        lambda e: e.dma_start(out=dbgout[0:16, 128:144], in_=sums_tm[0:16, :]),
                        reads=["sums_tm"],
                        writes=["dbgout"],
                    )
                    p.dma(
                        "gpsimd",
                        lambda e: e.dma_start(out=dbgout[0:1, 144:160], in_=srow[0:1, 0:16]),
                        reads=["srow"],
                        writes=["dbgout"],
                    )
                p.v(
                    lambda e, jj=jj: e.tensor_copy(
                        ctxr.ap()[0:64, :, 128 * jj : 128 * jj + 128],
                        psO.ap()[0:64, 0:2048].rearrange("p (c two t) -> p c two t", two=2, t=128)[:, :, 0, :],
                    ),
                    reads=["psO"],
                    writes=["ctxr"],
                )
                p.v(
                    lambda e, jj=jj: e.tensor_copy(
                        ctxr.ap()[64:128, :, 128 * jj : 128 * jj + 128],
                        psO.ap()[0:64, 0:2048].rearrange("p (c two t) -> p c two t", two=2, t=128)[:, :, 1, :],
                    ),
                    reads=["psO"],
                    writes=["ctxr"],
                )
            for cch in range(NCH):
                p.t(
                    lambda e, cch=cch: e.matmul(psS[:, 0:512], s16_sb[:, cch, :], recip_hm[:, :], start=True, stop=True),
                    reads=["s16_sb", "recip_hm"],
                    writes=["psS"],
                    tag=f"rb{r}_{cch}",
                )
                p.v(
                    lambda e, cch=cch: e.tensor_tensor(ctxr[:, cch, :], ctxr[:, cch, :], psS[:, 0:512], ALU.mult),
                    reads=["ctxr", "psS"],
                    writes=["ctxr"],
                )
            for oj in range(NCH):
                def mk_y(e, oj=oj):
                    bank = psS if oj < 4 else psO
                    last = None
                    for ci in range(NCH):
                        last = e.matmul(
                            bank[:, 512 * (oj % 4) : 512 * (oj % 4) + 512], wo_sb[:, ci, 128 * oj : 128 * oj + 128], ctxr[:, ci, :],
                            start=(ci == 0), stop=(ci == NCH - 1),
                        )
                    return last

                p.t(mk_y, reads=["ctxr", "wo_sb"], writes=["psS" if oj < 4 else "psO"], tag=f"y{r}_{oj}")
            for oj in range(NCH):
                p.s(
                    lambda e, oj=oj: e.activation(
                        out=yr[:, oj, :], in_=(psS if oj < 4 else psO)[:, 512 * (oj % 4) : 512 * (oj % 4) + 512],
                        func=AF.Identity, bias=bo_c[:, oj : oj + 1],
                    ),
                    reads=["psS" if oj < 4 else "psO", "bo_c"],
                    writes=["ctxr"],
                )

            def apply_post_tile(R, MR, off, n):
                p.v(
                    lambda e, off=off, n=n: e.tensor_tensor(
                        sqt[:, :, 0:n], yr[:, :, off : off + n], R.unsqueeze(1).broadcast_to([128, NCH, n]), ALU.mult
                    ),
                    reads=["ctxr", "psS"],
                    writes=["sqt"],
                )
                p.v(
                    lambda e, off=off, n=n: e.tensor_tensor(
                        sqt[:, :, 0:n], sqt[:, :, 0:n], MR.unsqueeze(1).broadcast_to([128, NCH, n]), ALU.subtract
                    ),
                    reads=["sqt", "psS"],
                    writes=["sqt"],
                )
                p.v(
                    lambda e, off=off, n=n: e.tensor_tensor(
                        x2r[:, :, off : off + n], x2r[:, :, off : off + n], sqt[:, :, 0:n], ALU.add
                    ),
                    reads=["sqt", "x2r"],
                    writes=["x2r"],
                )

            for (off, n) in [(0, 128), (128, 128), (256, 128), (384, 128)]:
                emit_ln_g(
                    p,
                    [(yr.ap()[:, c, off : off + n], "ctxr") for c in range(NCH)],
                    n,
                    [(sqt.ap()[:, :, 0:128], yr.ap()[:, :, off : off + n], "ctxr", "sqt")],
                    [(sqt.ap()[:, c, 0:128], "sqt") for c in range(NCH)],
                    rows, psA, "psS", onescol_b, "onescol_b", onescol_f, "onescol_f",
                    ones1f, "ones1f",
                    lambda R, MR, rd, off=off, n=n: apply_post_tile(R, MR, off, n),
                )
            p.dma(
                "sync",
                lambda e, r=r: e.dma_start(
                    out=x3out.rearrange("(c p) t -> p c t", p=128)[:, :, 512 * r : 512 * r + 512], in_=x2r[:, :, :]
                ),
                reads=["x2r"],
                writes=[("x3out", r)],
                tag=f"x3o{r}",
            )

        p.finalize(ctx)
    return nc


def l1_in_maps(inputs, x2_full):
    s16, _ = build_selectors()
    ident = np.eye(128, dtype=np.float32)
    maps = []
    wq = np.ascontiguousarray(inputs["L1_attn_Wq"].astype(np.float32))
    wk = np.ascontiguousarray(inputs["L1_attn_Wk"].astype(np.float32))
    wv = np.ascontiguousarray(inputs["L1_attn_Wv"].astype(np.float32))
    wo = np.ascontiguousarray(inputs["L1_attn_Wo"].astype(np.float32))
    tri = (np.arange(128)[:, None] <= np.arange(128)[None, :]).astype(np.float32)
    for c in range(NCORES):
        b, j = c // 4, c % 4
        qtiles = [4 * i + j for i in range(16)]
        xo = x2_full[b].reshape(64, 128, D)
        x2own = fm(np.ascontiguousarray(xo[qtiles].reshape(TOK, D)))
        maskw = np.zeros((128, 4, 128), np.float32)
        for w in range(4):
            if w < j:
                maskw[:, w, :] = 1.0
            elif w == j:
                maskw[:, w, :] = tri
        maps.append(
            {
                "x2own": x2own,
                "x2rot": fm(np.ascontiguousarray(x2_full[b])),
                "wq_in": wq, "wk_in": wk, "wv_in": wv, "wo_in": wo,
                "bq16": inputs["L1_attn_bq"].astype(np.float32).reshape(16, 64),
                "bk4": inputs["L1_attn_bk"].astype(np.float32).reshape(4, 64),
                "bvrow_in": inputs["L1_attn_bv"].astype(np.float32).reshape(1, 256),
                "bo8": inputs["L1_attn_bo"].astype(np.float32).reshape(8, 128),
                "maskw_in": maskw,
                "s16": s16,
                "identw": ident,
            }
        )
    return maps


def run_launch_l1(inputs, x2_full):
    nc = build_launch_l1()
    maps = l1_in_maps(inputs, x2_full)
    res = run_bass_kernel_spmd(nc, maps, core_ids=list(range(NCORES)))
    out = np.zeros((2, SEQ, D), np.float32)
    for c in range(NCORES):
        b, j = c // 4, c % 4
        qtiles = [4 * i + j for i in range(16)]
        ov = out[b].reshape(64, 128, D)
        ov[qtiles] = res.results[c]["x3out"].T.reshape(16, 128, D)
    return out




def build_launch_d():
    nc = bass.Bass("TRN2", target_bir_lowering=False)
    ap = lambda t: t.ap()
    x3own = ap(nc.dram_tensor("x3own", [D, TOK], F32, kind="ExternalInput"))
    rw_in = ap(nc.dram_tensor("rw_in", [D, 8], F32, kind="ExternalInput"))
    rb_in = ap(nc.dram_tensor("rb_in", [1, 8], F32, kind="ExternalInput"))
    w1_in = ap(nc.dram_tensor("w1_in", [10, D, 512], F32, kind="ExternalInput"))
    w3_in = ap(nc.dram_tensor("w3_in", [10, D, 512], F32, kind="ExternalInput"))
    w2_in = ap(nc.dram_tensor("w2_in", [10, 512, D], F32, kind="ExternalInput"))
    identw = ap(nc.dram_tensor("identw", [128, 128], F32, kind="ExternalInput"))
    out_t = ap(nc.dram_tensor("out_t", [D, TOK], F32, kind="ExternalOutput"))

    with ExitStack() as ctx:
        sb = lambda name, shape, dt=F32: ctx.enter_context(nc.sbuf_tensor(name, shape, dt))
        pst = lambda name, shape, dt=F32: ctx.enter_context(nc.psum_tensor(name, shape, dt))
        p = P(nc)

        x3rs = sb("x3rs", [128, NCH, 512])
        xhA = [sb(f"xh{i}", [128, NCH, 512], BF16) for i in range(2)]
        yacA = [sb(f"yac{i}", [128, NCH, 512]) for i in range(2)]
        growA = [sb(f"grow{i}", [1, 4096], BF16) for i in range(2)]
        sq = sb("sq", [128, NCH, 512])
        h_sb = sb("h_sb", [128, 4, 512], BF16)
        scrab = sb("scrab", [128, 1024], BF16)
        rw_sb = sb("rw_sb", [128, NCH, 8], BF16)
        rb_c = sb("rb_c", [8, 1])
        lg_sb = sb("lg_sb", [8, 512])
        lg_tm = sb("lg_tm", [128, 8])
        mr8 = sb("mr8", [128, 8])
        repl = sb("repl", [128, 8])
        m2 = sb("m2", [128, 1])
        negm1 = sb("negm1", [128, 1])
        sel = sb("sel", [128, 8])
        evals = sb("evals", [128, 8])
        denom = sb("denom", [128, 1])
        w1t = [sb(f"w1t{i}", [128, NCH, 512], BF16) for i in range(2)]
        w3t = [sb(f"w3t{i}", [128, NCH, 512], BF16) for i in range(2)]
        w2t = [sb(f"w2t{i}", [128, 4, D], BF16) for i in range(2)]
        wstg = [sb(f"wstg{i}", [128, NCH, 512]) for i in range(2)]
        identf = sb("identf", [128, 128])
        ones1f = sb("ones1f", [1, 128])
        ones1b = sb("ones1b", [1, 128], BF16)
        onescol_f = sb("onescol_f", [128, 1])
        rows = ln_rows(nc, ctx, "lnD", width=512)
        rows["onesf_col"] = onescol_f
        rows["konesf_col"] = "onescol_f"

        ps1 = pst("ps1", [128, 1024])
        ps2 = pst("ps2", [128, 1024])
        psB = pst("psB", [128, 512])
        psA = pst("psA", [128, 1024])
        psT = pst("psT", [128, 512])

        p.v(lambda e: e.memset(ones1f[:], 1.0), writes=["ones1f"])
        p.v(lambda e: e.memset(ones1b[:], 1.0), writes=["ones1b"])
        p.v(lambda e: e.memset(onescol_f[:], 1.0), writes=["onescol_f"])
        p.dma("sync", lambda e: e.dma_start(out=identf[:], in_=identw[:, :]), writes=["identf"])
        p.dma("gpsimd", lambda e: e.dma_start(out=rw_sb[:], in_=rw_in.rearrange("(c p) m -> p c m", p=128)), writes=["rw_sb"])
        p.dma("sync", lambda e: e.dma_start(out=sq[0:1, 0, 0:8], in_=rb_in[:, :]), writes=["sq"])
        p.t(lambda e: e.transpose(psT[0:8, 0:1], sq.ap()[0:1, 0, 0:8], identf[0:1, 0:1]), reads=["sq", "identf"], writes=["psT"])
        p.v(lambda e: e.tensor_copy(rb_c[:], psT[0:8, 0:1]), reads=["psT"], writes=["rb_c"])

        for rp in range(2):
            for r2 in range(2):
                r = 2 * rp + r2
                B = 512 * r
                x3r, xh, grow = x3rs, xhA[r2], growA[r2]
                kx3, kxh, kgrow = "x3rs", f"xh{r2}", f"grow{r2}"
                p.dma(
                    "sync",
                    lambda e, B=B, x3r=x3r: e.dma_start(out=x3r[:, :, :], in_=x3own.rearrange("(c p) t -> p c t", p=128)[:, :, B : B + 512]),
                    writes=[kx3],
                    tag=f"x3r{r}",
                )

                def apply_lnD(R, MR, rdeps, x3r=x3r, xh=xh, kx3=kx3, kxh=kxh):
                    p.v(
                        lambda e: e.tensor_tensor(sq[:, :, :], x3r[:, :, :], R.unsqueeze(1).broadcast_to([128, NCH, 512]), ALU.mult),
                        reads=[kx3, "psA"],
                        writes=["sq"],
                    )
                    p.v(
                        lambda e: e.tensor_tensor(xh[:, :, :], sq[:, :, :], MR.unsqueeze(1).broadcast_to([128, NCH, 512]), ALU.subtract),
                        reads=["sq", "psA"],
                        writes=[kxh],
                    )

                emit_ln_g(
                    p,
                    [(x3r.ap()[:, c, :], kx3) for c in range(NCH)],
                    512,
                    [(sq.ap()[:, :, :], x3r.ap()[:, :, :], kx3, "sq")],
                    [(sq.ap()[:, c, :], "sq") for c in range(NCH)],
                    rows, psA, "psA", onescol_f, "onescol_f", onescol_f, "onescol_f",
                    ones1f, "ones1f", apply_lnD,
                )

                def mk_lg(e, xh=xh):
                    last = None
                    for ci in range(NCH):
                        last = e.matmul(psT[0:8, 0:512], rw_sb[:, ci, :], xh[:, ci, :], start=(ci == 0), stop=(ci == NCH - 1))
                    return last

                p.t(mk_lg, reads=[kxh, "rw_sb"], writes=["psT"], tag=f"lg{r}")
                p.s(
                    lambda e: e.activation(out=lg_sb[:, :], in_=psT[0:8, 0:512], func=AF.Identity, bias=rb_c[:, :]),
                    reads=["psT", "rb_c"],
                    writes=["lg_sb"],
                )
                for tt in range(4):
                    p.t(
                        lambda e, tt=tt: e.transpose(psT[:, 0:8], lg_sb[:, 128 * tt : 128 * tt + 128], identf[0:8, 0:8]),
                        reads=["lg_sb", "identf"],
                        writes=["psT"],
                    )
                    p.v(lambda e: e.tensor_copy(lg_tm[:, :], psT[:, 0:8]), reads=["psT"], writes=["lg_tm"])
                    p.v(lambda e: e.memset(mr8[:], 3.0e38), writes=["mr8"])
                    p.v(lambda e: e.reduce_max(mr8[:, 0:1], lg_tm[:, :], axis=AX.X), reads=["lg_tm"], writes=["mr8"])
                    p.v(lambda e: e.match_replace(repl[:, :], mr8[:, :], lg_tm[:, :], -1.0e30), reads=["mr8", "lg_tm"], writes=["repl"])
                    p.v(lambda e: e.reduce_max(m2[:, :], repl[:, :], axis=AX.X), reads=["repl"], writes=["m2"])
                    p.v(lambda e: e.tensor_scalar(sel[:, :], lg_tm[:, :], m2[:, :], None, ALU.is_ge), reads=["lg_tm", "m2"], writes=["sel"])
                    p.v(lambda e: e.reduce_max(negm1[:, :], lg_tm[:, :], axis=AX.X, negate=True), reads=["lg_tm"], writes=["negm1"])
                    p.s(
                        lambda e: e.activation(out=evals[:, :], in_=lg_tm[:, :], func=AF.Exp, bias=negm1[:, :]),
                        reads=["lg_tm", "negm1"],
                        writes=["evals"],
                    )
                    p.v(lambda e: e.tensor_mul(evals[:, :], evals[:, :], sel[:, :]), reads=["evals", "sel"], writes=["evals"])
                    p.v(lambda e: e.reduce_sum(denom[:, :], evals[:, :], axis=AX.X), reads=["evals"], writes=["denom"])
                    p.v(lambda e: e.reciprocal(denom[:, :], denom[:, :]), reads=["denom"], writes=["denom"])
                    p.v(lambda e: e.tensor_scalar(evals[:, :], evals[:, :], denom[:, :], None, ALU.mult), reads=["evals", "denom"], writes=["evals"])

                    def mk_gt4(e):
                        last = None
                        for ee in range(8):
                            last = e.transpose(psA[0:1, 128 * ee : 128 * ee + 128], evals[:, ee : ee + 1], identf[:, :])
                        return last

                    p.t(mk_gt4, reads=["evals", "identf"], writes=["psA"], tag=f"gt{r}_{tt}")
                    p.v(
                        lambda e, tt=tt, grow=grow: e.tensor_copy(
                            grow.ap()[:, :].rearrange("o (e t) -> o e t", e=8)[:, :, 128 * tt : 128 * tt + 128],
                            psA.ap()[0:1, 0:1024].rearrange("o (e t) -> o e t", e=8),
                        ),
                        reads=["psA"],
                        writes=[kgrow],
                    )
                p.v(lambda e, yac=yacA[r2]: e.memset(yac[:, :, :], 0.0), writes=[f"yac{r2}"])

            for ee in range(10):
                bb = ee % 2
                p.dma(
                    "gpsimd",
                    lambda e, ee=ee, bb=bb: e.dma_start(out=w1t[bb][:], in_=w1_in[ee].rearrange("(c p) m -> p c m", p=128)),
                    writes=[("w1t", bb)],
                    tag=f"w1l{rp}_{ee}",
                )
                p.dma(
                    "sync",
                    lambda e, ee=ee: e.dma_start(out=wstg[0][:], in_=w3_in[ee].rearrange("(c p) m -> p c m", p=128)),
                    writes=[("wstg", 0)],
                    tag=f"w3l{rp}_{ee}",
                )
                p.s(
                    lambda e, bb=bb: e.activation(out=w3t[bb][:], in_=wstg[0][:], func=AF.Copy),
                    reads=[("wstg", 0)],
                    writes=[("w3t", bb)],
                )
                p.dma(
                    "scalar",
                    lambda e, ee=ee: e.dma_start(
                        out=wstg[1].ap().rearrange("p c m -> p (c m)").rearrange("p (d n) -> p d n", d=4),
                        in_=w2_in[ee].rearrange("(c p) m -> p c m", p=128),
                    ),
                    writes=[("wstg", 1)],
                    tag=f"w2l{rp}_{ee}",
                )
                p.s(
                    lambda e, bb=bb: e.activation(
                        out=w2t[bb][:, :, :],
                        in_=wstg[1].ap().rearrange("p c m -> p (c m)").rearrange("p (d n) -> p d n", d=4),
                        func=AF.Copy,
                    ),
                    reads=[("wstg", 1)],
                    writes=[("w2t", bb)],
                )
                for r2 in range(2):
                    xh, grow, yac = xhA[r2], growA[r2], yacA[r2]
                    kxh, kgrow, kyac = f"xh{r2}", f"grow{r2}", f"yac{r2}"
                    for op2 in range(2):
                        def mk_a(e, op2=op2, bb=bb, xh=xh):
                            last = None
                            for of2 in range(2):
                                of = 2 * op2 + of2
                                for ci in range(NCH):
                                    last = e.matmul(
                                        ps1[:, 512 * of2 : 512 * of2 + 512], w1t[bb][:, ci, 128 * of : 128 * of + 128], xh[:, ci, :],
                                        start=(ci == 0), stop=(ci == NCH - 1),
                                    )
                            return last

                        def mk_b(e, op2=op2, bb=bb, xh=xh):
                            last = None
                            for of2 in range(2):
                                of = 2 * op2 + of2
                                for ci in range(NCH):
                                    last = e.matmul(
                                        ps2[:, 512 * of2 : 512 * of2 + 512], w3t[bb][:, ci, 128 * of : 128 * of + 128], xh[:, ci, :],
                                        start=(ci == 0), stop=(ci == NCH - 1),
                                    )
                            return last

                        p.t(mk_a, reads=[kxh, ("w1t", bb)], writes=["ps1"], tag=f"a{rp}_{ee}_{r2}_{op2}")
                        p.t(mk_b, reads=[kxh, ("w3t", bb)], writes=["ps2"], tag=f"b{rp}_{ee}_{r2}_{op2}")
                        p.s(
                            lambda e: e.activation(out=scrab[:, :], in_=ps1[:, 0:1024], func=AF.Silu),
                            reads=["ps1"],
                            writes=["scrab"],
                        )
                        p.v(
                            lambda e, op2=op2: e.tensor_tensor(
                                h_sb.ap()[:, 2 * op2 : 2 * op2 + 2, :],
                                scrab.ap()[:, :].rearrange("p (c t) -> p c t", c=2),
                                ps2.ap()[:, 0:1024].rearrange("p (c t) -> p c t", c=2),
                                ALU.mult,
                            ),
                            reads=["scrab", "ps2"],
                            writes=["h_sb"],
                        )
                    if ee < 8:
                        def mk_gb(e, ee=ee, grow=grow):
                            return e.matmul(psA[:, 0:512], ones1b[:, 0:128], grow.ap()[:, 512 * ee : 512 * ee + 512], start=True, stop=True)

                        p.t(mk_gb, reads=[kgrow, "ones1b"], writes=["psA"], tag=f"gb{rp}_{ee}_{r2}")
                        p.v(
                            lambda e: e.tensor_tensor(
                                h_sb.ap(), h_sb.ap(), psA.ap()[:, 0:512].unsqueeze(1).broadcast_to([128, 4, 512]), ALU.mult
                            ),
                            reads=["h_sb", "psA"],
                            writes=["h_sb"],
                        )
                    for oj in range(NCH):
                        bk = psB if oj % 2 == 0 else psT
                        bkk = "psB" if oj % 2 == 0 else "psT"

                        def mk_w2(e, oj=oj, bb=bb, bk=bk):
                            last = None
                            for ci in range(4):
                                last = e.matmul(
                                    bk[:, 0:512], w2t[bb][:, ci, 128 * oj : 128 * oj + 128], h_sb[:, ci, :],
                                    start=(ci == 0), stop=(ci == 3),
                                )
                            return last

                        p.t(mk_w2, reads=["h_sb", ("w2t", bb)], writes=[bkk], tag=f"w2{rp}_{ee}_{r2}_{oj}")
                        p.v(
                            lambda e, oj=oj, bk=bk, yac=yac: e.tensor_tensor(yac[:, oj, :], yac[:, oj, :], bk[:, 0:512], ALU.add),
                            reads=[kyac, bkk],
                            writes=[kyac],
                        )

            for r2 in range(2):
                r = 2 * rp + r2
                B = 512 * r
                x3r, yac = x3rs, yacA[r2]
                kx3, kyac = "x3rs", f"yac{r2}"

                p.dma(
                    "sync",
                    lambda e, B=B, x3r=x3r: e.dma_start(out=x3r[:, :, :], in_=x3own.rearrange("(c p) t -> p c t", p=128)[:, :, B : B + 512]),
                    writes=[kx3],
                    tag=f"x3rf{r}",
                )

                def apply_fin(R, MR, rdeps, x3r=x3r, yac=yac, kx3=kx3, kyac=kyac):
                    p.v(
                        lambda e: e.tensor_tensor(sq[:, :, :], yac[:, :, :], R.unsqueeze(1).broadcast_to([128, NCH, 512]), ALU.mult),
                        reads=[kyac, "psA"],
                        writes=["sq"],
                    )
                    p.v(
                        lambda e: e.tensor_tensor(sq[:, :, :], sq[:, :, :], MR.unsqueeze(1).broadcast_to([128, NCH, 512]), ALU.subtract),
                        reads=["sq", "psA"],
                        writes=["sq"],
                    )
                    p.v(
                        lambda e: e.tensor_tensor(x3r[:, :, :], x3r[:, :, :], sq[:, :, :], ALU.add),
                        reads=["sq", kx3],
                        writes=[kx3],
                    )

                emit_ln_g(
                    p,
                    [(yac.ap()[:, c, :], kyac) for c in range(NCH)],
                    512,
                    [(sq.ap()[:, :, :], yac.ap()[:, :, :], kyac, "sq")],
                    [(sq.ap()[:, c, :], "sq") for c in range(NCH)],
                    rows, psA, "psA", onescol_f, "onescol_f", onescol_f, "onescol_f",
                    ones1f, "ones1f", apply_fin,
                )
                p.dma(
                    "sync",
                    lambda e, B=B, x3r=x3r: e.dma_start(
                        out=out_t.rearrange("(c p) t -> p c t", p=128)[:, :, B : B + 512], in_=x3r[:, :, :]
                    ),
                    reads=[kx3],
                    writes=[("out_t", r)],
                    tag=f"o{r}",
                )

        p.finalize(ctx)
    return nc


def d_in_maps(inputs, x3_full):
    ident = np.eye(128, dtype=np.float32)
    w1 = np.concatenate([inputs["L1_ffn_routed_w1"], inputs["L1_ffn_shared_w1"]], axis=0).astype(np.float32)
    w3 = np.concatenate([inputs["L1_ffn_routed_w3"], inputs["L1_ffn_shared_w3"]], axis=0).astype(np.float32)
    w2 = np.concatenate([inputs["L1_ffn_routed_w2"], inputs["L1_ffn_shared_w2"]], axis=0).astype(np.float32)
    maps = []
    for c in range(NCORES):
        b, sown = c // 4, TOK * (c % 4)
        maps.append(
            {
                "x3own": fm(np.ascontiguousarray(x3_full[b, sown : sown + TOK])),
                "rw_in": np.ascontiguousarray(inputs["L1_ffn_router_w"].astype(np.float32)),
                "rb_in": inputs["L1_ffn_router_b"].astype(np.float32).reshape(1, 8),
                "w1_in": np.ascontiguousarray(w1),
                "w3_in": np.ascontiguousarray(w3),
                "w2_in": np.ascontiguousarray(w2),
                "identw": ident,
            }
        )
    return maps


def run_launch_d(inputs, x3_full):
    nc = build_launch_d()
    maps = d_in_maps(inputs, x3_full)
    res = run_bass_kernel_spmd(nc, maps, core_ids=list(range(NCORES)))
    out = np.zeros((2, SEQ, D), np.float32)
    for c in range(NCORES):
        b, sown = c // 4, TOK * (c % 4)
        out[b, sown : sown + TOK] = res.results[c]["out_t"].T
    return out


def build_xpos_tables(positions):
    half = np.arange(0, 64, 2, dtype=np.float64)
    inv_freq = 1.0 / 10000.0 ** (half / 64.0)
    t = positions.astype(np.float64)
    freqs = t[:, None] * inv_freq[None, :]
    freqs = np.concatenate([freqs, freqs], axis=1)
    base = (half + 0.4 * 64) / (1.4 * 64)
    power = (t - 8192 // 2) / 256.0
    scale = base[None, :] ** power[:, None]
    scale = np.concatenate([scale, scale], axis=1)
    cos, sin = np.cos(freqs), np.sin(freqs)
    return (
        np.ascontiguousarray((cos * scale).T.astype(np.float32)),
        np.ascontiguousarray((sin * scale).T.astype(np.float32)),
        np.ascontiguousarray((cos / scale).T.astype(np.float32)),
        np.ascontiguousarray((sin / scale).T.astype(np.float32)),
    )


def build_rot64():
    r = np.zeros((64, 64), np.float32)
    for m in range(32):
        r[m + 32, m] = -1.0
    for m in range(32, 64):
        r[m - 32, m] = 1.0
    return r


def build_tris():
    qq = np.arange(128)[:, None]
    kk = np.arange(128)[None, :]
    triL = np.where(kk < qq, -1.0e30, 0.0).astype(np.float32)
    triD = np.where(kk > qq, -1.0e30, 0.0).astype(np.float32)
    return np.stack([triL, triD])


def fa_in_maps(inputs, stream_full, l):
    wqkv = np.ascontiguousarray(inputs["L0_ffn_wqkv"][l].astype(np.float32))
    wo = np.ascontiguousarray(inputs["L0_ffn_wo"][l].astype(np.float32))
    qs = inputs["L0_ffn_q_scale"][l].astype(np.float32).reshape(1, 64)
    ks = inputs["L0_ffn_k_scale"][l].astype(np.float32).reshape(1, 64)
    rot = build_rot64()
    tris = build_tris()
    ident = np.eye(128, dtype=np.float32)
    maps = []
    for c in range(NCORES):
        b, sown = c // 4, TOK * (c % 4)
        lo = sown - FA_EXT
        fxe = np.zeros((FA_EXT + TOK, D), np.float32)
        a0 = max(lo, 0)
        fxe[a0 - lo :] = stream_full[b, a0 : sown + TOK]
        pos_own = np.arange(sown, sown + TOK)
        pos_ext = np.arange(lo, sown + TOK)
        tqc, tqs_, _, _ = build_xpos_tables(pos_own)
        _, _, tkc, tks_ = build_xpos_tables(pos_ext)
        dead = np.where(pos_ext < 0, -1.0e30, 0.0).astype(np.float32).reshape(1, -1)
        maps.append(
            {
                "fxe": fm(fxe), "wqkv": wqkv, "wo_in": wo, "qs_row": qs, "ks_row": ks,
                "tqc_in": tqc, "tqs_in": tqs_, "tkc_in": tkc, "tks_in": tks_,
                "dead_in": dead, "rot_in": rot, "tri_in": tris, "identw": ident,
            }
        )
    return maps


def fb_in_maps(inputs, xmid_full, l, x1_full=None):
    w1 = np.ascontiguousarray(inputs["L0_ffn_w1"][l].astype(np.float32))
    w2 = np.ascontiguousarray(np.pad(inputs["L0_ffn_w2"][l].astype(np.float32), ((0, 86), (0, 0))))
    b1 = inputs["L0_ffn_b1"][l].astype(np.float32)
    b2 = inputs["L0_ffn_b2"][l].astype(np.float32)
    lng = inputs["L0_ffn_ln_g"][l].astype(np.float32).reshape(1, D)
    lnb = inputs["L0_ffn_ln_b"][l].astype(np.float32).reshape(1, D)
    ident = np.eye(128, dtype=np.float32)
    b1a = np.pad(b1[0:2730], (0, 86)).reshape(22, 128)
    b1g = np.pad(b1[2730:5460], (0, 86)).reshape(22, 128)
    maps = []
    for c in range(NCORES):
        b, sown = c // 4, TOK * (c % 4)
        xme = np.zeros((TOK + 1, D), np.float32)
        a0 = max(sown - 1, 0)
        xme[a0 - (sown - 1) :] = xmid_full[b, a0 : sown + TOK]
        m = {
            "xme": fm(xme), "w1_in": w1, "w2_in": w2, "b1a_in": b1a, "b1g_in": b1g,
            "b2_in": b2.reshape(8, 128), "lng_in": lng, "lnb_in": lnb, "identw": ident,
            "neg1_in": np.full((1, 544), -1.0, np.float32),
        }
        if x1_full is not None:
            m["x1own"] = fm(np.ascontiguousarray(x1_full[b, sown : sown + TOK]))
        maps.append(m)
    return maps


def run_launch_fa(inputs, stream_full, l):
    nc = build_launch_fa(l == 0)
    maps = fa_in_maps(inputs, stream_full, l)
    res = run_bass_kernel_spmd(nc, maps, core_ids=list(range(NCORES)))
    out = np.zeros((2, SEQ, D), np.float32)
    for c in range(NCORES):
        b, sown = c // 4, TOK * (c % 4)
        out[b, sown : sown + TOK] = res.results[c]["xmid_out"].T
    return out


def run_launch_fb(inputs, xmid_full, l, x1_full=None):
    nc = build_launch_fb(x1_full is not None)
    maps = fb_in_maps(inputs, xmid_full, l, x1_full)
    res = run_bass_kernel_spmd(nc, maps, core_ids=list(range(NCORES)))
    out = np.zeros((2, SEQ, D), np.float32)
    for c in range(NCORES):
        b, sown = c // 4, TOK * (c % 4)
        out[b, sown : sown + TOK] = res.results[c]["out_t"].T
    return out


def kernel(**inputs):
    _ = np.asarray(inputs["L0_attn_token_ids"])
    x0 = np.asarray(inputs["L0_attn_input_embeddings"], np.float32)
    x1 = run_launch_a(inputs, x0)
    xm0 = run_launch_fa(inputs, x1, 0)
    xn0 = run_launch_fb(inputs, xm0, 0)
    xm1 = run_launch_fa(inputs, xn0, 1)
    x2 = run_launch_fb(inputs, xm1, 1, x1_full=x1)
    x3 = run_launch_l1(inputs, x2)
    out = run_launch_d(inputs, x3)
    return out
